# version 04
# baseline (speedup 1.0000x reference)
import numpy as np
import concourse.bass as bass
import concourse.mybir as mybir
from concourse.bass_utils import run_bass_kernel_spmd
from contextlib import ExitStack

F32 = mybir.dt.float32
BF16 = mybir.dt.bfloat16
I32 = mybir.dt.int32
AF = mybir.ActivationFunctionType
ALU = mybir.AluOpType
AX = mybir.AxisListType

D = 2048
HID = 8192
NCORE = 8
ATANH2 = float(2.0 * np.arctanh(0.5))
RG8 = [list(range(NCORE))]
RG4 = [[0, 1, 2, 3], [4, 5, 6, 7]]
ENGS = ("sy", "sq", "v", "t", "g")
RSQ2 = float(128.0 ** -0.5)


class Seq:
    def __init__(self, nc, es):
        self.nc = nc
        self.es = es
        self.sems = {e: es.enter_context(nc.semaphore(f"sq_{e}")) for e in ENGS}
        self.ops = {e: [] for e in ENGS}
        self.count = {e: 0 for e in ENGS}
        self.ccount = {}

    def _chan_sem(self, key):
        if key not in self.ccount:
            self.sems[key] = self.es.enter_context(self.nc.semaphore(f"sq_{key}"))
            self.ccount[key] = 0
        return key

    def raw(self, eng, fn, deps=(), dma=False, chan=None):
        deps = [d for d in deps if d is not None]
        if dma:
            key = self._chan_sem(f"{eng}.{chan or 'd'}")
            if self.ccount[key] > 0:
                deps.append((key, self.ccount[key]))
            self.ccount[key] += 16
            self.ops[eng].append((fn, deps, key, 16))
            return (key, self.ccount[key])
        if self.count[eng] > 0:
            deps.append((eng, self.count[eng]))
        self.count[eng] += 1
        self.ops[eng].append((fn, deps, eng, 1))
        return (eng, self.count[eng])

    def cc(self, build, deps):
        key = self._chan_sem("g.cc")
        deps = list(deps) + ([(key, self.ccount[key])] if self.ccount[key] else [])
        self.ccount[key] += 1
        val = self.ccount[key]

        def f(e):
            build(e).then_inc(self.sems[key], 1)
            return e.nop()
        self.raw("g", f, [d for d in deps if d is not None])
        return (key, val)

    def barrier(self):
        marks = {e: self.count[e] for e in ENGS}
        cmarks = dict(self.ccount)
        out = {}
        for e in ENGS:
            deps = [(o, marks[o]) for o in ENGS if o != e and marks[o] > 0]
            deps += [(k, v) for k, v in cmarks.items() if v > 0]
            out[e] = self.raw(e, lambda en: en.nop(), deps=deps)
        return out

    def emit(self, block):
        sections = {"sy": block.sync, "sq": block.scalar, "v": block.vector,
                    "t": block.tensor, "g": block.gpsimd}
        for eng_name in ENGS:
            def make(eng_name=eng_name):
                def body(en):
                    last = {}
                    done = 0
                    for fn, waits, skey, inc in self.ops[eng_name]:
                        for (oe, val) in waits:
                            if last.get(oe, 0) >= val:
                                continue
                            if oe == eng_name and val > done:
                                raise RuntimeError("forward same-engine dep")
                            en.wait_ge(self.sems[oe], val)
                            last[oe] = val
                        ins = fn(en)
                        if isinstance(ins, (list, tuple)):
                            ins = ins[-1]
                        ins.then_inc(self.sems[skey], inc)
                        if skey == eng_name:
                            done += inc
                return body
            sections[eng_name](make())


class Buf:
    def __init__(self):
        self.writer = None
        self.readers = []


ATTN_CFG = {"groups": None, "chunks": None, "vh": None, "rs": True, "proj": True}


def build_program(T, upto=9):
    LT = T // 4
    NT = LT // 128
    NG = T // 512
    assert T % 2048 == 0
    TBU = min(LT, 1024)
    NBU = LT // TBU
    NTU = TBU // 128

    nc = bass.Bass("TRN2", target_bir_lowering=False)

    x0 = nc.dram_tensor("x0", [LT, D], F32, kind="ExternalInput").ap()
    w1Ts = nc.dram_tensor("w1Ts", [2, 128, HID], F32, kind="ExternalInput").ap()
    w2Ts = nc.dram_tensor("w2Ts", [2, 128, HID], F32, kind="ExternalInput").ap()
    w3Ts = nc.dram_tensor("w3Ts", [8, 128, D], F32, kind="ExternalInput").ap()
    g1in = nc.dram_tensor("g1in", [1, D], F32, kind="ExternalInput").ap()
    g2in = nc.dram_tensor("g2in", [1, D], F32, kind="ExternalInput").ap()
    g3in = nc.dram_tensor("g3in", [1, HID], F32, kind="ExternalInput").ap()
    wqkvin = nc.dram_tensor("wqkvin", [D, 2048], F32, kind="ExternalInput").ap()
    wzin = nc.dram_tensor("wzin", [D, 1024], F32, kind="ExternalInput").ap()
    wbain = nc.dram_tensor("wbain", [128, 16, 16], F32, kind="ExternalInput").ap()
    convdin = nc.dram_tensor("convdin", [128, 16, 4, 128], F32, kind="ExternalInput").ap()
    woutin = nc.dram_tensor("woutin", [1024, D], F32, kind="ExternalInput").ap()
    alogin = nc.dram_tensor("alogin", [8, 1], F32, kind="ExternalInput").ap()
    dtbin = nc.dram_tensor("dtbin", [8, 1], F32, kind="ExternalInput").ap()
    normwin = nc.dram_tensor("normwin", [1, 128], F32, kind="ExternalInput").ap()
    identin = nc.dram_tensor("identin", [128, 128], F32, kind="ExternalInput").ap()
    masksin = nc.dram_tensor("masksin", [2, 128, 128], F32, kind="ExternalInput").ap()
    sel8in = nc.dram_tensor("sel8in", [8, 128], F32, kind="ExternalInput").ap()
    out = nc.dram_tensor("out", [LT, D], F32, kind="ExternalOutput").ap()

    agt_in = nc.dram_tensor("agt_in", [24, 128, 2048], BF16).ap()
    agt_out = nc.dram_tensor("agt_out", [NCORE, 24, 128, 2048], BF16).ap()
    ar_in = nc.dram_tensor("ar_in", [1, 4], F32).ap()
    ar_out = nc.dram_tensor("ar_out", [1, 4], F32).ap()
    ag1_in = nc.dram_tensor("ag1_in", [16, 128, LT], BF16).ap()
    ag1_out = nc.dram_tensor("ag1_out", [4, 16, 128, LT], BF16).ap()
    rs_in = nc.dram_tensor("rs_in", [T, D], BF16).ap()
    rs_out = nc.dram_tensor("rs_out", [LT, D], BF16).ap()
    zsp = nc.dram_tensor("zsp", [T // 128, 128, 1024], BF16).ap()
    betagsp = nc.dram_tensor("betagsp", [2, 8, T + 1], F32).ap()
    hgsp = nc.dram_tensor("hgsp", [NT, 128, HID], BF16).ap()
    x1sp = nc.dram_tensor("x1sp", [LT, D], F32).ap()
    x2sp = nc.dram_tensor("x2sp", [LT, D], F32).ap()

    es = ExitStack()
    seq = Seq(nc, es)

    def E(eng, fn, r=(), w=(), x=(), dma=False, chan=None):
        deps = []
        for b in r:
            if b.writer is not None:
                deps.append(b.writer)
        for b in w:
            deps.extend(b.readers if b.readers else ([b.writer] if b.writer else []))
        deps.extend(x)
        h = seq.raw(eng, fn, deps, dma=dma, chan=chan)
        for b in r:
            b.readers.append(h)
        for b in w:
            b.writer = h
            b.readers = []
        return h

    ident = es.enter_context(nc.sbuf_tensor("ident", [128, 128], F32))
    identb = es.enter_context(nc.sbuf_tensor("identb", [128, 128], BF16))
    mnegst = es.enter_context(nc.sbuf_tensor("mnegst", [128, 128], F32))
    mtriu = es.enter_context(nc.sbuf_tensor("mtriu", [128, 128], F32))
    sel8 = es.enter_context(nc.sbuf_tensor("sel8", [8, 128], F32))
    onesrow = es.enter_context(nc.sbuf_tensor("onesrow", [1, 128], F32))
    onescol = es.enter_context(nc.sbuf_tensor("onescol", [128, 1], F32))
    betaP = es.enter_context(nc.sbuf_tensor("betaP", [128, 1], F32))
    betaN = es.enter_context(nc.sbuf_tensor("betaN", [128, 1], F32))
    probef = es.enter_context(nc.sbuf_tensor("probef", [128, 1], F32))
    probei = es.enter_context(nc.sbuf_tensor("probei", [128, 1], I32))
    normw_t = es.enter_context(nc.sbuf_tensor("normw_t", [128, 128], F32))
    scw = es.enter_context(nc.sbuf_tensor("scw", [128, 8], F32))
    cbuf = es.enter_context(nc.sbuf_tensor("cbuf", [128, 48], F32))
    PSB = es.enter_context(nc.psum_tensor("PSB", [128, 512], F32))
    B_PSB = Buf()
    B_cb = [Buf() for _ in range(48)]
    B_const = Buf()

    def cb(j):
        return cbuf[:, j:j + 1]

    E("sy", lambda e: e.dma_start(out=ident[:], in_=identin), w=[B_const], dma=True)
    E("sy", lambda e: e.dma_start(out=mnegst[:], in_=masksin[0]), w=[B_const], dma=True)
    E("sy", lambda e: e.dma_start(out=mtriu[:], in_=masksin[1]), w=[B_const], dma=True)
    E("sy", lambda e: e.dma_start(out=sel8[:], in_=sel8in), w=[B_const], dma=True)
    E("sy", lambda e: e.dma_start(out=normw_t[0:1, :], in_=normwin), w=[B_const], dma=True)
    E("v", lambda e: e.memset(onesrow[:], 1.0), w=[B_const])
    E("v", lambda e: e.memset(onescol[:], 1.0), w=[B_const])
    E("v", lambda e: e.tensor_copy(identb[:], ident[:]), r=[B_const], w=[B_const])
    E("v", lambda e: e.memset(probef[:], 1.5), w=[B_const])
    E("v", lambda e: e.tensor_copy(probei[:], probef[:]), r=[B_const], w=[B_const])
    E("v", lambda e: e.tensor_scalar(betaP[:], probei[:], -2.0, -256.5, ALU.add, ALU.mult),
      r=[B_const], w=[B_const])
    E("v", lambda e: e.tensor_scalar(betaN[:], probei[:], -2.0, 256.0, ALU.add, ALU.mult),
      r=[B_const], w=[B_const])
    E("t", lambda e: e.matmul(PSB[:, 0:128], onesrow[:], normw_t[0:1, :], start=True, stop=True),
      r=[B_const], w=[B_PSB])
    cdep = E("v", lambda e: e.tensor_copy(normw_t[:], PSB[:, 0:128]), r=[B_PSB], w=[B_const])

    wtiles = []
    for wi, src in ((0, w1Ts), (1, w2Ts)):
        for ch in range(2):
            for c4 in range(4):
                wtiles.append((src[ch, :, c4 * 2048:(c4 + 1) * 2048], wi * 8 + ch * 4 + c4, wi))
    for rc in range(8):
        wtiles.append((w3Ts[rc, :, :], 16 + rc, 2))

    B_scw = Buf()
    _stk1 = ExitStack()
    wld = _stk1.enter_context(nc.sbuf_tensor("wld", [128, 2, 2048], F32))
    wq1 = _stk1.enter_context(nc.sbuf_tensor("wq1", [128, 2048], F32))
    wq2 = _stk1.enter_context(nc.sbuf_tensor("wq2", [128, 2048], F32))
    wqi = _stk1.enter_context(nc.sbuf_tensor("wqi", [128, 2048], I32))
    wqb = _stk1.enter_context(nc.sbuf_tensor("wqb", [128, 2048], BF16))
    wdmp = _stk1.enter_context(nc.sbuf_tensor("wdmp", [128, 2048], F32))
    acc3 = _stk1.enter_context(nc.sbuf_tensor("acc3", [128, 4], F32))
    arb = _stk1.enter_context(nc.sbuf_tensor("arb", [1, 4], F32))
    if True:
        B_wld = [Buf(), Buf()]
        B_acc = Buf()
        B_tmp = Buf()
        B_arb = Buf()
        E("v", lambda e: e.memset(acc3[:], 0.0), w=[B_acc])
        for i, (src, entry, m) in enumerate(wtiles):
            b = wld[:, i % 2, :]
            E("sy", lambda e, s=src, b=b: e.dma_start(out=b, in_=s), w=[B_wld[i % 2]], dma=True)
            E("sq", lambda e, b=b, m=m: e.activation(wdmp[:], b, AF.Abs, accum_out=cb(8 + m)),
              r=[B_wld[i % 2]], w=[B_tmp, B_cb[8 + m]])
            E("v", lambda e, m=m: e.tensor_tensor(acc3[:, m:m + 1], acc3[:, m:m + 1], cb(8 + m), ALU.add),
              r=[B_cb[8 + m], B_acc], w=[B_acc])
        E("t", lambda e: e.matmul(PSB[:4, 0:1], acc3[:], onescol[:], start=True, stop=True),
          r=[B_acc, B_const], w=[B_PSB])
        E("v", lambda e: e.tensor_copy(acc3[:4, 0:1], PSB[:4, 0:1]), r=[B_PSB], w=[B_acc])
        E("t", lambda e: e.transpose(PSB[0:1, 128:132], acc3[:4, 0:1], ident[0:4, 0:4]),
          r=[B_acc, B_const], w=[B_PSB])
        E("v", lambda e: e.tensor_copy(arb[:], PSB[0:1, 128:132]), r=[B_PSB], w=[B_arb])
        h0 = E("sy", lambda e: e.dma_start(out=ar_in, in_=arb[:]), r=[B_arb], dma=True)
        h = seq.cc(lambda e: e.collective_compute("AllReduce", ALU.add, replica_groups=RG8,
                                                  ins=[ar_in.opt()], outs=[ar_out.opt()]), [h0])
        E("sy", lambda e: e.dma_start(out=arb[:], in_=ar_out), w=[B_arb], x=[h], dma=True)
        E("t", lambda e: e.matmul(PSB[:, 128:132], onesrow[:], arb[:], start=True, stop=True),
          r=[B_arb, B_const], w=[B_PSB])
        E("v", lambda e: e.tensor_scalar(cbuf[:, 0:4], PSB[:, 128:132], 1.0 / (2048.0 * 8192.0),
                                         None, ALU.mult), r=[B_PSB], w=[B_cb[0]])
        E("v", lambda e: e.tensor_scalar(cbuf[:, 4:8], cbuf[:, 0:4], 1e-8, ATANH2, ALU.add, ALU.mult),
          r=[B_cb[0]], w=[B_cb[4]])
        E("v", lambda e: e.reciprocal(scw[:, 0:4], cbuf[:, 4:8]), r=[B_cb[4]], w=[B_scw])
        E("v", lambda e: e.tensor_scalar(cbuf[:, 8:12], cbuf[:, 0:4], -1.0, 1.0, ALU.mult, ALU.add),
          r=[B_cb[0]], w=[B_cb[8]])
        E("v", lambda e: e.reciprocal(cbuf[:, 12:16], cbuf[:, 8:12]), r=[B_cb[8]], w=[B_cb[12]])
        E("v", lambda e: e.tensor_scalar(cbuf[:, 16:20], cbuf[:, 0:4], 1.0, None, ALU.add),
          r=[B_cb[0]], w=[B_cb[16]])
        E("v", lambda e: e.tensor_tensor(cbuf[:, 20:24], cbuf[:, 16:20], cbuf[:, 12:16], ALU.mult),
          r=[B_cb[16], B_cb[12]], w=[B_cb[20]])
        E("sq", lambda e: e.activation(cbuf[:, 24:28], cbuf[:, 20:24], AF.Ln), r=[B_cb[20]], w=[B_cb[24]])
        E("v", lambda e: e.tensor_scalar(scw[:, 4:8], cbuf[:, 24:28], 0.5, None, ALU.mult),
          r=[B_cb[24], B_scw], w=[B_scw])

        prev_store = None
        for i, (src, entry, m) in enumerate(wtiles):
            b = wld[:, i % 2, :]
            E("sy", lambda e, s=src, b=b: e.dma_start(out=b, in_=s), w=[B_wld[i % 2]], dma=True)
            E("sq", lambda e, b=b, m=m: e.activation(wq1[:], b, AF.Copy, scale=scw[:, m:m + 1]),
              r=[B_wld[i % 2], B_scw], w=[B_tmp])
            E("v", lambda e: e.tensor_scalar(wq2[:], wq1[:], 1.4, -1.4, ALU.min, ALU.max),
              r=[B_tmp], w=[B_tmp])
            E("sq", lambda e: e.activation(wq1[:], wq2[:], AF.Identity, bias=betaP[:, 0:1]),
              r=[B_tmp, B_const], w=[B_tmp])
            E("v", lambda e: e.tensor_copy(wqi[:], wq1[:]), r=[B_tmp], w=[B_tmp])
            E("v", lambda e: e.tensor_scalar(wqb[:], wqi[:], betaN[:, 0:1], None, ALU.add),
              r=[B_tmp, B_const], w=[B_tmp])
            prev_store = E("sy", lambda e, entry=entry: e.dma_start(out=agt_in[entry], in_=wqb[:]),
                           r=[B_tmp], x=[prev_store], dma=True)
        g_agt = seq.cc(lambda e: e.collective_compute("AllGather", ALU.bypass, replica_groups=RG8,
                                                      ins=[agt_in.opt()], outs=[agt_out.opt()]), [prev_store])
        _ = g_agt
    _stk1.close()
    seq.barrier()

    def w_up_tile(wi, ic, oc):
        return agt_out[ic // 2, wi * 8 + (ic % 2) * 4 + oc // 4, :, (oc % 4) * 512:(oc % 4) * 512 + 512]

    def w_dn_tile(icd, od):
        return agt_out[icd // 8, 16 + icd % 8, :, od * 512:od * 512 + 512]

    def ln_chain(src_ap, r_bufs, dst_ap, w_bufs, dmp_ap, dmp_buf, base, eps, x=()):
        E("v", lambda e: e.tensor_reduce(cb(base), src_ap, AX.X, ALU.add),
          r=r_bufs, w=[B_cb[base]], x=list(x))
        E("v", lambda e: e.tensor_scalar(cb(base + 1), cb(base), -1.0 / D, None, ALU.mult),
          r=[B_cb[base]], w=[B_cb[base + 1]])
        E("sq", lambda e: e.activation(dmp_ap, src_ap, AF.Square, bias=cb(base + 1),
                                       accum_out=cb(base + 2)),
          r=r_bufs + [B_cb[base + 1]], w=[dmp_buf, B_cb[base + 2]])
        E("v", lambda e: e.tensor_scalar(cb(base + 3), cb(base + 2), 1.0 / D, eps, ALU.mult, ALU.add),
          r=[B_cb[base + 2]], w=[B_cb[base + 3]])
        E("sq", lambda e: e.activation(cb(base + 4), cb(base + 3), AF.Sqrt),
          r=[B_cb[base + 3]], w=[B_cb[base + 4]])
        E("v", lambda e: e.reciprocal(cb(base + 5), cb(base + 4)), r=[B_cb[base + 4]], w=[B_cb[base + 5]])
        return E("v", lambda e: e.tensor_scalar(dst_ap, src_ap, cb(base + 1), cb(base + 5),
                                                ALU.add, ALU.mult),
                 r=r_bufs + [B_cb[base + 1], B_cb[base + 5]], w=w_bufs)

    c1s = es.enter_context(nc.sbuf_tensor("c1s", [128, NT], F32))
    c2s = es.enter_context(nc.sbuf_tensor("c2s", [128, NT], F32))
    c3s = es.enter_context(nc.sbuf_tensor("c3s", [128, NT], F32))
    ssqr = es.enter_context(nc.sbuf_tensor("ssqr", [128, NT], F32))
    amaxr = es.enter_context(nc.sbuf_tensor("amaxr", [128, NT], F32))
    B_c1s = [Buf() for _ in range(NT)]
    B_c2s = [Buf() for _ in range(NT)]
    B_c3s = [Buf() for _ in range(NT)]
    B_ssqr = [Buf() for _ in range(NT)]
    B_amaxr = [Buf() for _ in range(NT)]

    def emit_ffn(call):
        x_src = x0 if call == 1 else x2sp
        ag1_last = [None]
        out_last = [None]
        hsp_done = {}
        _stk2 = ExitStack()
        gt3 = _stk2.enter_context(nc.sbuf_tensor(f"f{call}_gt3", [128, HID], BF16))
        g12t = _stk2.enter_context(nc.sbuf_tensor(f"f{call}_g12t", [128, 2, D], BF16))
        xin = _stk2.enter_context(nc.sbuf_tensor(f"f{call}_xin", [128, 2, D], F32))
        xln = _stk2.enter_context(nc.sbuf_tensor(f"f{call}_xln", [128, D], F32))
        ytmp = _stk2.enter_context(nc.sbuf_tensor(f"f{call}_ytmp", [128, D], F32))
        kint = _stk2.enter_context(nc.sbuf_tensor(f"f{call}_kint", [128, D], I32))
        kbf = _stk2.enter_context(nc.sbuf_tensor(f"f{call}_kbf", [128, 2, D], BF16))
        fdmp = _stk2.enter_context(nc.sbuf_tensor(f"f{call}_fdmp", [128, D], F32))
        k1T = _stk2.enter_context(nc.sbuf_tensor(f"f{call}_k1T", [128, 16, TBU], BF16))
        k2T = _stk2.enter_context(nc.sbuf_tensor(f"f{call}_k2T", [128, 16, TBU], BF16))
        wu = _stk2.enter_context(nc.sbuf_tensor(f"f{call}_wu", [128, 16, 2, 512], BF16))
        silu1 = _stk2.enter_context(nc.sbuf_tensor(f"f{call}_silu1", [128, 512], F32))
        htile = _stk2.enter_context(nc.sbuf_tensor(f"f{call}_htile", [128, 512], BF16))
        hgt = _stk2.enter_context(nc.sbuf_tensor(f"f{call}_hgt", [128, 2, 512], BF16))
        ps1a = _stk2.enter_context(nc.psum_tensor(f"f{call}_ps1a", [128, 512], F32))
        ps1b = _stk2.enter_context(nc.psum_tensor(f"f{call}_ps1b", [128, 512], F32))
        ps2a = _stk2.enter_context(nc.psum_tensor(f"f{call}_ps2a", [128, 512], F32))
        ps2b = _stk2.enter_context(nc.psum_tensor(f"f{call}_ps2b", [128, 512], F32))
        if True:

            B_g = Buf()
            B_gs = Buf()
            with nc.sbuf_tensor(f"f{call}_gstg", [1, 512], F32) as gstg:
                jobs = ([(g3in, gt3[:, j * 512:(j + 1) * 512], j) for j in range(16)] +
                        [(g1in, g12t[:, 0, j * 512:(j + 1) * 512], j) for j in range(4)] +
                        [(g2in, g12t[:, 1, j * 512:(j + 1) * 512], j) for j in range(4)])
                for srcT, dst, j in jobs:
                    E("sy", lambda e, srcT=srcT, j=j: e.dma_start(
                        out=gstg[:], in_=srcT[0:1, j * 512:(j + 1) * 512]), w=[B_gs], dma=True)
                    E("t", lambda e: e.matmul(PSB[:, 0:512], onesrow[:], gstg[:],
                                              start=True, stop=True),
                      r=[B_const, B_gs], w=[B_PSB])
                    E("v", lambda e, dst=dst: e.tensor_copy(dst, PSB[:, 0:512]),
                      r=[B_PSB], w=[B_g])

            B_xin = [Buf(), Buf()]
            B_xln = Buf()
            B_y = Buf()
            B_ki = Buf()
            B_kb = [Buf(), Buf()]
            B_kT = [Buf(), Buf()]
            B_wu = [[Buf() for _ in range(16)] for _ in range(2)]
            B_psy = [[Buf(), Buf()], [Buf(), Buf()]]
            B_si = Buf()
            B_ht = Buf()
            B_hg = [Buf(), Buf()]
            B_fd = Buf()
            E("v", lambda e: e.memset(ssqr[:], 0.0), w=B_ssqr)
            E("g", lambda e: e.memset(amaxr[:], 0.0), w=B_amaxr)

            for tb in range(NBU):
                for tl in range(NTU):
                    tt = tb * NTU + tl
                    xb = xin[:, tt % 2, :]
                    E("sy", lambda e, tt=tt, xb=xb: e.dma_start(
                        out=xb, in_=x_src[tt * 128:(tt + 1) * 128, :]), w=[B_xin[tt % 2]], dma=True)
                    ln_chain(xb, [B_xin[tt % 2]], xln[:], [B_xln], fdmp[:], B_fd, 0, 1e-5)
                    E("sq", lambda e: e.activation(fdmp[:], xln[:], AF.Square, accum_out=cb(6)),
                      r=[B_xln], w=[B_fd, B_cb[6]])
                    E("v", lambda e: e.tensor_scalar(cb(7), cb(6), 1.0 / D, 1e-8, ALU.mult, ALU.add),
                      r=[B_cb[6]], w=[B_cb[7]])
                    E("sq", lambda e: e.activation(cb(8), cb(7), AF.Sqrt), r=[B_cb[7]], w=[B_cb[8]])
                    E("v", lambda e: e.reciprocal(cb(9), cb(8)), r=[B_cb[8]], w=[B_cb[9]])
                    for gi in range(2):
                        kb = kbf[:, gi, :]
                        kT = (k1T, k2T)[gi]
                        cs = (c1s, c2s)[gi]
                        Bcs = (B_c1s, B_c2s)[gi][tt]
                        E("v", lambda e, gi=gi: e.scalar_tensor_tensor(
                            ytmp[:], xln[:], cb(9), g12t[:, gi, :], ALU.mult, ALU.mult),
                          r=[B_xln, B_cb[9], B_g], w=[B_y])
                        E("v", lambda e: e.tensor_reduce(cb(10), ytmp[:], AX.X, ALU.max,
                                                         apply_absolute_value=True),
                          r=[B_y], w=[B_cb[10]])
                        E("v", lambda e: e.tensor_scalar(cb(11), cb(10), 1e-4, None, ALU.max),
                          r=[B_cb[10]], w=[B_cb[11]])
                        E("v", lambda e: e.reciprocal(cb(12), cb(11)), r=[B_cb[11]], w=[B_cb[12]])
                        E("v", lambda e: e.tensor_scalar(cb(13), cb(12), 127.0, None, ALU.mult),
                          r=[B_cb[12]], w=[B_cb[13]])
                        E("v", lambda e, cs=cs, tt=tt, gi=gi: e.scalar_tensor_tensor(
                            cs[:, tt:tt + 1], cb(11), 1.0 / 127.0, scw[:, 4 + gi:5 + gi],
                            ALU.mult, ALU.mult), r=[B_cb[11], B_scw], w=[Bcs])
                        E("sq", lambda e: e.activation(ytmp[:], ytmp[:], AF.Identity, scale=cb(13),
                                                       bias=betaP[:, 0:1]),
                          r=[B_y, B_cb[13], B_const], w=[B_y])
                        E("v", lambda e: e.tensor_copy(kint[:], ytmp[:]), r=[B_y], w=[B_ki])
                        E("v", lambda e, kb=kb: e.tensor_scalar(kb, kint[:], betaN[:, 0:1], None, ALU.add),
                          r=[B_ki, B_const], w=[B_kb[gi]])
                        for ic in range(16):
                            E("sy", lambda e, kb=kb, kT=kT, ic=ic, tl=tl: e.dma_start(
                                out=kT[:, ic, tl * 128:(tl + 1) * 128],
                                in_=kb[:, ic * 128:(ic + 1) * 128], transpose=True),
                              r=[B_kb[gi]], w=[B_kT[gi]], dma=True)
                for oc in range(16):
                    for wi in range(2):
                        for ic in range(16):
                            E("sq", lambda e, wi=wi, ic=ic, oc=oc: e.dma_start(
                                out=wu[:, ic, wi, :], in_=w_up_tile(wi, ic, oc)),
                              w=[B_wu[wi][ic]], x=[g_agt], dma=True)
                    for tl in range(NTU):
                        tt = tb * NTU + tl
                        psy1 = (ps1a, ps1b)[tl % 2]
                        psy2 = (ps2a, ps2b)[tl % 2]

                        def mk_mm(tl=tl, psy1=psy1, psy2=psy2):
                            def f(e):
                                ins = []
                                for ic in range(16):
                                    ins.append(e.matmul(psy1[:], k1T[:, ic, tl * 128:(tl + 1) * 128],
                                                        wu[:, ic, 0, :], start=(ic == 0), stop=(ic == 15)))
                                for ic in range(16):
                                    ins.append(e.matmul(psy2[:], k2T[:, ic, tl * 128:(tl + 1) * 128],
                                                        wu[:, ic, 1, :], start=(ic == 0), stop=(ic == 15)))
                                return ins
                            return f
                        E("t", mk_mm(), r=[B_kT[0], B_kT[1]] + [B_wu[wi][ic] for wi in range(2)
                                                                for ic in range(16)],
                          w=[B_psy[0][tl % 2], B_psy[1][tl % 2]])
                        E("sq", lambda e, psy1=psy1, tt=tt: e.activation(
                            silu1[:], psy1[:], AF.Silu, scale=c1s[:, tt:tt + 1]),
                          r=[B_psy[0][tl % 2], B_c1s[tt]], w=[B_si])
                        E("v", lambda e, psy2=psy2, tt=tt: e.scalar_tensor_tensor(
                            htile[:], psy2[:], c2s[:, tt:tt + 1], silu1[:], ALU.mult, ALU.mult),
                          r=[B_psy[1][tl % 2], B_c2s[tt], B_si], w=[B_ht])
                        E("sq", lambda e: e.activation(fdmp[:, 0:512], htile[:], AF.Square,
                                                       accum_out=cb(14)),
                          r=[B_ht], w=[B_fd, B_cb[14]])
                        E("v", lambda e, tt=tt: e.tensor_tensor(
                            ssqr[:, tt:tt + 1], ssqr[:, tt:tt + 1], cb(14), ALU.add),
                          r=[B_cb[14], B_ssqr[tt]], w=[B_ssqr[tt]])
                        hb = hgt[:, tl % 2, :]
                        E("v", lambda e, oc=oc, hb=hb: e.tensor_tensor(
                            hb, htile[:], gt3[:, oc * 512:(oc + 1) * 512], ALU.mult),
                          r=[B_ht, B_g], w=[B_hg[tl % 2]])
                        E("v", lambda e, hb=hb: e.tensor_reduce(cb(15), hb, AX.X, ALU.max,
                                                                apply_absolute_value=True),
                          r=[B_hg[tl % 2]], w=[B_cb[15]])
                        E("v", lambda e, tt=tt: e.tensor_tensor(
                            amaxr[:, tt:tt + 1], amaxr[:, tt:tt + 1], cb(15), ALU.max),
                          r=[B_cb[15], B_amaxr[tt]], w=[B_amaxr[tt]])
                        hsp_done[(oc, tt)] = E("sy", lambda e, oc=oc, tt=tt, hb=hb: e.dma_start(
                            out=hgsp[tt, :, oc * 512:(oc + 1) * 512], in_=hb),
                          r=[B_hg[tl % 2]], dma=True)

        _stk2.close()
        seq.barrier()
        _stk3 = ExitStack()
        khT = _stk3.enter_context(nc.sbuf_tensor(f"f{call}_khT", [128, 64, 512], BF16))
        wdt = _stk3.enter_context(nc.sbuf_tensor(f"f{call}_wdt", [128, 64, 512], BF16))
        hgl = _stk3.enter_context(nc.sbuf_tensor(f"f{call}_hgl", [128, 2048], BF16))
        uh = _stk3.enter_context(nc.sbuf_tensor(f"f{call}_uh", [128, 2048], F32))
        kih = _stk3.enter_context(nc.sbuf_tensor(f"f{call}_kih", [128, 2048], I32))
        khb = _stk3.enter_context(nc.sbuf_tensor(f"f{call}_khb", [128, 2048], BF16))
        oacc = _stk3.enter_context(nc.sbuf_tensor(f"f{call}_oacc", [128, 512], F32))
        xres = _stk3.enter_context(nc.sbuf_tensor(f"f{call}_xres", [128, D], F32))
        x1t = _stk3.enter_context(nc.sbuf_tensor(f"f{call}_x1t", [128, D], F32))
        cdmp = _stk3.enter_context(nc.sbuf_tensor(f"f{call}_cdmp", [128, D], F32))
        x2nb = _stk3.enter_context(nc.sbuf_tensor(f"f{call}_x2nb", [128, D], BF16))
        x2nT = _stk3.enter_context(nc.sbuf_tensor(f"f{call}_x2nT", [128, 16, 128], BF16))
        psoa = _stk3.enter_context(nc.psum_tensor(f"f{call}_psoa", [128, 512], F32))
        psob = _stk3.enter_context(nc.psum_tensor(f"f{call}_psob", [128, 512], F32))
        if True:
            B_khT = Buf()
            B_wd = Buf()
            B_hgl = Buf()
            B_u = Buf()
            B_k2 = Buf()
            B_k3 = Buf()
            B_oa = Buf()
            B_xr = Buf()
            B_pso = [Buf(), Buf()]
            B_x1t = Buf()
            B_x2n = Buf()
            B_x2T = Buf()
            B_cd = Buf()
            for bd in range(LT // 512):
                for tl in range(4):
                    tt = bd * 4 + tl
                    E("v", lambda e, tt=tt: e.tensor_scalar(cb(16), ssqr[:, tt:tt + 1], 1.0 / HID,
                                                            1e-8, ALU.mult, ALU.add),
                      r=[B_ssqr[tt]], w=[B_cb[16]])
                    E("sq", lambda e: e.activation(cb(17), cb(16), AF.Sqrt), r=[B_cb[16]], w=[B_cb[17]])
                    E("v", lambda e: e.reciprocal(cb(18), cb(17)), r=[B_cb[17]], w=[B_cb[18]])
                    E("v", lambda e, tt=tt: e.tensor_tensor(cb(19), amaxr[:, tt:tt + 1], cb(18), ALU.mult),
                      r=[B_amaxr[tt], B_cb[18]], w=[B_cb[19]])
                    E("v", lambda e: e.tensor_scalar(cb(20), cb(19), 1e-4, None, ALU.max),
                      r=[B_cb[19]], w=[B_cb[20]])
                    E("v", lambda e: e.reciprocal(cb(21), cb(20)), r=[B_cb[20]], w=[B_cb[21]])
                    E("v", lambda e: e.tensor_scalar(cb(22), cb(21), 127.0, None, ALU.mult),
                      r=[B_cb[21]], w=[B_cb[22]])
                    E("v", lambda e: e.tensor_tensor(cb(23), cb(22), cb(18), ALU.mult),
                      r=[B_cb[22], B_cb[18]], w=[B_cb[23]])
                    E("v", lambda e, tt=tt: e.scalar_tensor_tensor(
                        c3s[:, tt:tt + 1], cb(20), 0.5 / 127.0, scw[:, 6:7], ALU.mult, ALU.mult),
                      r=[B_cb[20], B_scw], w=[B_c3s[tt]])
                    for q4 in range(4):
                        E("sy", lambda e, tt=tt, q4=q4: e.dma_start(
                            out=hgl[:], in_=hgsp[tt, :, q4 * 2048:(q4 + 1) * 2048]),
                          w=[B_hgl], x=[hsp_done[(q4 * 4 + jj, tt)] for jj in range(4)], dma=True)
                        E("sq", lambda e: e.activation(uh[:], hgl[:], AF.Identity, scale=cb(23),
                                                       bias=betaP[:, 0:1]),
                          r=[B_hgl, B_cb[23], B_const], w=[B_u])
                        E("v", lambda e: e.tensor_copy(kih[:], uh[:]), r=[B_u], w=[B_k2])
                        E("v", lambda e: e.tensor_scalar(khb[:], kih[:], betaN[:, 0:1], None, ALU.add),
                          r=[B_k2, B_const], w=[B_k3])
                        for ic in range(16):
                            icd = q4 * 16 + ic
                            E("sy", lambda e, icd=icd, ic=ic, tl=tl: e.dma_start(
                                out=khT[:, icd, tl * 128:(tl + 1) * 128],
                                in_=khb[:, ic * 128:(ic + 1) * 128], transpose=True),
                              r=[B_k3], w=[B_khT], dma=True)

                ostores = {}
                for od in range(4):
                    wls = []
                    for icd in range(64):
                        wls.append(E("sq", lambda e, icd=icd, od=od: e.dma_start(
                            out=wdt[:, icd, :], in_=w_dn_tile(icd, od)),
                          w=[B_wd], x=[g_agt], dma=True))
                    for tl in range(4):
                        tt = bd * 4 + tl
                        pso = (psoa, psob)[tl % 2]

                        def mk_dn(tl=tl, pso=pso):
                            def f(e):
                                ins = []
                                for icd in range(64):
                                    ins.append(e.matmul(pso[:], khT[:, icd, tl * 128:(tl + 1) * 128],
                                                        wdt[:, icd, :], start=(icd == 0), stop=(icd == 63)))
                                return ins
                            return f
                        E("t", mk_dn(), r=[B_khT, B_wd], w=[B_pso[tl % 2]], x=wls)
                        E("sy", lambda e, tt=tt, od=od: e.dma_start(
                            out=xres[:, od * 512:(od + 1) * 512],
                            in_=x_src[tt * 128:(tt + 1) * 128, od * 512:(od + 1) * 512]),
                          w=[B_xr], dma=True)
                        E("v", lambda e, tt=tt, od=od, pso=pso: e.scalar_tensor_tensor(
                            oacc[:], pso[:], c3s[:, tt:tt + 1], xres[:, od * 512:(od + 1) * 512],
                            ALU.mult, ALU.add), r=[B_pso[tl % 2], B_c3s[tt], B_xr], w=[B_oa])
                        dst = x1sp if call == 1 else out
                        st = E("sy", lambda e, tt=tt, od=od, dst=dst: e.dma_start(
                            out=dst[tt * 128:(tt + 1) * 128, od * 512:(od + 1) * 512], in_=oacc[:]),
                          r=[B_oa], dma=True)
                        ostores[(od, tl)] = st
                        out_last[0] = st
                if call == 1:
                    for tl in range(4):
                        tt = bd * 4 + tl
                        E("sy", lambda e, tt=tt: e.dma_start(
                            out=x1t[:], in_=x1sp[tt * 128:(tt + 1) * 128, :]),
                          w=[B_x1t], x=[ostores[(od, tl)] for od in range(4)], dma=True)
                        ln_chain(x1t[:], [B_x1t], x2nb[:], [B_x2n], cdmp[:], B_cd, 24, 1e-5)
                        for ic in range(16):
                            E("sy", lambda e, ic=ic: e.dma_start(
                                out=x2nT[:, ic, :], in_=x2nb[:, ic * 128:(ic + 1) * 128], transpose=True),
                              r=[B_x2n], w=[B_x2T], dma=True)
                        ag1_last[0] = E("sy", lambda e, tt=tt: e.dma_start(
                            out=ag1_in[:, :, tt * 128:(tt + 1) * 128].rearrange("a p b -> p a b"),
                            in_=x2nT[:]), r=[B_x2T], dma=True)
        _stk3.close()
        seq.barrier()
        return ag1_last[0] if call == 1 else out_last[0]

    def emit_zsweep():
        zstores = {}
        _stk5 = ExitStack()
        wzsb = _stk5.enter_context(nc.sbuf_tensor("wzsb", [128, 16, 1024], BF16))
        wbasb = _stk5.enter_context(nc.sbuf_tensor("wbasb", [128, 16, 16], BF16))
        batile = _stk5.enter_context(nc.sbuf_tensor("batile", [128, 16], F32))
        zt2 = _stk5.enter_context(nc.sbuf_tensor("zt2", [128, 1024], BF16))
        xTz = _stk5.enter_context(nc.sbuf_tensor("xTz", [128, 16, 128], BF16))
        bafm = _stk5.enter_context(nc.sbuf_tensor("bafm", [8, 2, T], F32))
        bg2 = _stk5.enter_context(nc.sbuf_tensor("bg2", [8, 2, T], F32))
        zcol = _stk5.enter_context(nc.sbuf_tensor("zcol", [8, 1], F32))
        psZ1 = _stk5.enter_context(nc.psum_tensor("psZ1", [128, 512], F32))
        psZ2 = _stk5.enter_context(nc.psum_tensor("psZ2", [128, 512], F32))
        psZ3 = _stk5.enter_context(nc.psum_tensor("psZ3", [128, 512], F32))
        B_zt2 = Buf(); B_bat = Buf(); B_xTz = Buf(); B_ba = Buf(); B_bg = Buf(); B_zc = Buf()
        B_pz1 = Buf(); B_pz2 = Buf(); B_pz3 = Buf()
        wz_l = []
        for ic in range(16):
            wz_l.append(E("g", lambda e, ic=ic: e.dma_start(
                out=wzsb[:, ic, :], in_=wzin[ic * 128:(ic + 1) * 128, :]), dma=True))
        wba_l = E("g", lambda e: e.dma_start(
            out=wbasb[:], in_=wbain), dma=True)
        for tt2 in range(T // 128):
            E("sy", lambda e, tt2=tt2: e.dma_start(
                out=xTz[:],
                in_=ag1_out[(tt2 * 128) // LT, :, :,
                            (tt2 * 128) % LT:(tt2 * 128) % LT + 128].rearrange("a p b -> p a b")),
              w=[B_xTz], x=[g_ag1], dma=True)

            def mk_zh(half):
                def f(e):
                    ins = []
                    for ic in range(16):
                        ins.append(e.matmul((psZ1 if half == 0 else psZ2)[:],
                                            xTz[:, ic, :],
                                            wzsb[:, ic, half * 512:half * 512 + 512],
                                            start=(ic == 0), stop=(ic == 15)))
                    return ins
                return f
            E("t", mk_zh(0), r=[B_xTz], w=[B_pz1], x=wz_l)
            E("t", mk_zh(1), r=[B_xTz], w=[B_pz2], x=wz_l)

            def mk_ba(e):
                ins = []
                for ic in range(16):
                    ins.append(e.matmul(psZ3[:, 0:16], xTz[:, ic, :],
                                        wbasb[:, ic, :], start=(ic == 0), stop=(ic == 15)))
                return ins
            E("t", mk_ba, r=[B_xTz], w=[B_pz3], x=[wba_l])
            E("v", lambda e: e.tensor_copy(zt2[:, 0:512], psZ1[:]), r=[B_pz1], w=[B_zt2])
            E("v", lambda e: e.tensor_copy(zt2[:, 512:1024], psZ2[:]), r=[B_pz2], w=[B_zt2])
            zstores[tt2] = E("sy", lambda e, tt2=tt2: e.dma_start(out=zsp[tt2], in_=zt2[:]),
                             r=[B_zt2], dma=True)
            E("v", lambda e: e.tensor_copy(batile[:], psZ3[:, 0:16]), r=[B_pz3], w=[B_bat])
            for half in range(2):
                E("t", lambda e, half=half: e.transpose(psZ3[0:8, 128 + half * 128:256 + half * 128],
                                                        batile[:, half * 8:half * 8 + 8], ident[:]),
                  r=[B_bat, B_const], w=[B_pz3])
                E("v", lambda e, tt2=tt2, half=half: e.tensor_copy(
                    bafm[:, half, tt2 * 128:(tt2 + 1) * 128],
                    psZ3[0:8, 128 + half * 128:256 + half * 128]),
                  r=[B_pz3], w=[B_ba])
        l_dtb = E("sy", lambda e: e.dma_start(out=cbuf[0:8, 28:29], in_=dtbin), dma=True)
        l_alog = E("sy", lambda e: e.dma_start(out=cbuf[0:8, 29:30], in_=alogin), dma=True)
        E("sq", lambda e: e.activation(bg2[:, 0, :], bafm[:, 0, :], AF.Sigmoid), r=[B_ba], w=[B_bg])
        E("v", lambda e: e.tensor_scalar(bafm[:, 1, :], bafm[:, 1, :], cbuf[0:8, 28:29],
                                         None, ALU.add), r=[B_ba], w=[B_ba], x=[l_dtb])
        E("sq", lambda e: e.activation(bg2[:, 1, :], bafm[:, 1, :], AF.Abs), r=[B_ba], w=[B_bg])
        E("sq", lambda e: e.activation(bg2[:, 1, :], bg2[:, 1, :], AF.Exp, scale=-1.0),
          r=[B_bg], w=[B_bg])
        E("sq", lambda e: e.activation(bg2[:, 1, :], bg2[:, 1, :], AF.Ln, bias=1.0),
          r=[B_bg], w=[B_bg])
        E("v", lambda e: e.tensor_scalar(bafm[:, 1, :], bafm[:, 1, :], 0.0, None, ALU.max),
          r=[B_ba], w=[B_ba])
        E("v", lambda e: e.tensor_tensor(bg2[:, 1, :], bg2[:, 1, :], bafm[:, 1, :], ALU.add),
          r=[B_bg, B_ba], w=[B_bg])
        E("sq", lambda e: e.activation(cbuf[0:8, 30:31], cbuf[0:8, 29:30], AF.Exp),
          x=[l_alog], w=[B_cb[30]])
        E("v", lambda e: e.tensor_scalar(cbuf[0:8, 31:32], cbuf[0:8, 30:31], -1.0, None,
                                         ALU.mult), r=[B_cb[30]], w=[B_cb[31]])
        E("v", lambda e: e.tensor_scalar(bg2[:, 1, :], bg2[:, 1, :], cbuf[0:8, 31:32], None,
                                         ALU.mult), r=[B_bg, B_cb[31]], w=[B_bg])
        E("v", lambda e: e.tensor_tensor_scan(bg2[:, 1, :], bg2[:, 1, :], bafm[:, 1, :], 0.0,
                                              ALU.add, ALU.bypass), r=[B_bg, B_ba], w=[B_bg])
        E("v", lambda e: e.memset(zcol[:], 0.0), w=[B_zc])
        E("sy", lambda e: e.dma_start(out=betagsp[0, :, 0:1], in_=zcol[:], allow_slow_non_contiguous=True), r=[B_zc], dma=True)
        E("sy", lambda e: e.dma_start(out=betagsp[1, :, 0:1], in_=zcol[:], allow_slow_non_contiguous=True), r=[B_zc], dma=True)
        E("sy", lambda e: e.dma_start(out=betagsp[0, :, 1:T + 1], in_=bg2[:, 0, :]),
          r=[B_bg], dma=True)
        bgst = E("sy", lambda e: e.dma_start(out=betagsp[1, :, 1:T + 1], in_=bg2[:, 1, :]),
                 r=[B_bg], dma=True)
        _stk5.close()
        seq.barrier()
        return zstores, bgst

    def emit_attention(zstores, bgst):
        _stk4 = ExitStack()
        wq = _stk4.enter_context(nc.sbuf_tensor("wq", [128, 16, 2048], BF16))
        wo = _stk4.enter_context(nc.sbuf_tensor("wo", [128, 8, 2048], BF16))
        cdg = _stk4.enter_context(nc.sbuf_tensor("cdg", [128, 16, 4, 128], BF16))
        xT = _stk4.enter_context(nc.sbuf_tensor("xT", [128, 1, 16, 512], BF16))
        mxs = _stk4.enter_context(nc.sbuf_tensor("mxs", [128, 515], BF16))
        tails = _stk4.enter_context(nc.sbuf_tensor("tails", [128, 16, 4], BF16))
        qkv = _stk4.enter_context(nc.sbuf_tensor("qkv", [128, 2, 16, 512], BF16))
        bg8 = _stk4.enter_context(nc.sbuf_tensor("bg8", [8, 2, 129], F32))
        hrow = _stk4.enter_context(nc.sbuf_tensor("hrow", [8, 2, 128], F32))
        hcolT = _stk4.enter_context(nc.sbuf_tensor("hcolT", [128, 4, 8], F32))
        grow = _stk4.enter_context(nc.sbuf_tensor("grow", [1, 128], F32))
        gamr = _stk4.enter_context(nc.sbuf_tensor("gamr", [8, 1], F32))
        gamd = _stk4.enter_context(nc.sbuf_tensor("gamd", [8, 8], F32))
        gamc = _stk4.enter_context(nc.sbuf_tensor("gamc", [128, 8], F32))
        lrow = _stk4.enter_context(nc.sbuf_tensor("lrow", [1, 1024], F32))
        rcolk = _stk4.enter_context(nc.sbuf_tensor("rcolk", [128, 4], F32))
        qn = _stk4.enter_context(nc.sbuf_tensor("qn", [128, 4, 128], BF16))
        kn = _stk4.enter_context(nc.sbuf_tensor("kn", [128, 4, 128], BF16))
        sq2t = _stk4.enter_context(nc.sbuf_tensor("sq2t", [128, 128], F32))
        ktok = _stk4.enter_context(nc.sbuf_tensor("ktok", [128, 4, 128], BF16))
        vtok = _stk4.enter_context(nc.sbuf_tensor("vtok", [128, 8, 128], BF16))
        kbgt = _stk4.enter_context(nc.sbuf_tensor("kbgt", [128, 128], BF16))
        vbt = _stk4.enter_context(nc.sbuf_tensor("vbt", [128, 128], BF16))
        kdect = _stk4.enter_context(nc.sbuf_tensor("kdect", [128, 128], BF16))
        dect = _stk4.enter_context(nc.sbuf_tensor("dect", [128, 2, 128], F32))
        dmneg = _stk4.enter_context(nc.sbuf_tensor("dmneg", [128, 128], F32))
        dmTin = _stk4.enter_context(nc.sbuf_tensor("dmTin", [128, 128], F32))
        Amat = _stk4.enter_context(nc.sbuf_tensor("Amat", [128, 128], BF16))
        Bmat = _stk4.enter_context(nc.sbuf_tensor("Bmat", [128, 128], BF16))
        Apw = _stk4.enter_context(nc.sbuf_tensor("Apw", [128, 2, 128], BF16))
        Bpw = _stk4.enter_context(nc.sbuf_tensor("Bpw", [128, 2, 128], BF16))
        Prod = _stk4.enter_context(nc.sbuf_tensor("Prod", [128, 2, 128], BF16))
        vcorr = _stk4.enter_context(nc.sbuf_tensor("vcorr", [128, 128], BF16))
        kcdf = _stk4.enter_context(nc.sbuf_tensor("kcdf", [128, 128], BF16))
        vnew = _stk4.enter_context(nc.sbuf_tensor("vnew", [128, 128], BF16))
        intraT = _stk4.enter_context(nc.sbuf_tensor("intraT", [128, 128], BF16))
        o1s = _stk4.enter_context(nc.sbuf_tensor("o1s", [128, 128], F32))
        oc2 = _stk4.enter_context(nc.sbuf_tensor("oc2", [128, 128], F32))
        xg = _stk4.enter_context(nc.sbuf_tensor("xg", [128, 128], F32))
        nrm = _stk4.enter_context(nc.sbuf_tensor("nrm", [128, 128], BF16))
        nrmT = _stk4.enter_context(nc.sbuf_tensor("nrmT", [128, 8, 128], BF16))
        stm = _stk4.enter_context(nc.sbuf_tensor("stm", [128, 2, 8, 128], F32))
        stb = _stk4.enter_context(nc.sbuf_tensor("stb", [128, 128], BF16))
        ztile = _stk4.enter_context(nc.sbuf_tensor("ztile", [128, 1024], BF16))
        szt = _stk4.enter_context(nc.sbuf_tensor("szt", [128, 128], F32))
        rst = _stk4.enter_context(nc.sbuf_tensor("rst", [128, 512], BF16))
        psP = _stk4.enter_context(nc.psum_tensor("psP", [128, 512], F32))
        psC = _stk4.enter_context(nc.psum_tensor("psC", [128, 512], F32))
        psK1 = _stk4.enter_context(nc.psum_tensor("psK1", [128, 512], F32))
        psK2 = _stk4.enter_context(nc.psum_tensor("psK2", [128, 512], F32))
        psE = _stk4.enter_context(nc.psum_tensor("psE", [128, 512], F32))
        psF = _stk4.enter_context(nc.psum_tensor("psF", [128, 512], F32))
        psG = _stk4.enter_context(nc.psum_tensor("psG", [128, 512], F32))
        if True:

            BN = {}
            for nm in ("psP", "psC", "psK1", "psK2", "hrow", "hcolT", "grow", "gamr", "gamd",
                       "gamc", "lrow", "rcolk", "sq2t", "kbgt", "vbt", "kdect", "dect", "dmneg",
                       "dmTin", "Amat", "Bmat", "vcorr", "kcdf", "vnew", "intraT", "o1s", "oc2",
                       "xg", "nrm", "stb", "ztile", "szt", "mxs", "tails", "rst", "wcst"):
                BN[nm] = Buf()
            _bE = Buf(); _bF = Buf(); _bG = Buf()
            B_psE = [_bE, _bE, _bE, _bE]
            B_psF = [_bF, _bF, _bF, _bF]
            B_psG = [_bG, _bG, _bG, _bG]
            B_xT = [Buf()]
            B_qkv = [Buf(), Buf()]
            B_qn = [Buf() for _ in range(4)]
            B_kn = [Buf() for _ in range(4)]
            B_ktok = [Buf() for _ in range(4)]
            B_vtok = [Buf() for _ in range(8)]
            B_Apw = [Buf(), Buf()]
            B_Bpw = [Buf(), Buf()]
            B_Prod = [Buf(), Buf()]
            B_st = [[Buf() for _ in range(8)], [Buf() for _ in range(8)]]
            B_nrmT = [Buf() for _ in range(8)]
            B_bg8 = Buf()

            pe = lambda q, w=128: psE[:, q * 128:q * 128 + w]
            pf = lambda q, w=128: psF[:, q * 128:q * 128 + w]
            pg = lambda q, w=128: psG[:, q * 128:q * 128 + w]

            B_wo = Buf()
            for ic in range(16):
                E("g", lambda e, ic=ic: e.dma_start(out=wq[:, ic, :],
                                                    in_=wqkvin[ic * 128:(ic + 1) * 128, :]),
                  w=[BN["wcst"]], dma=True)
            for kc in range(8):
                E("g", lambda e, kc=kc: e.dma_start(out=wo[:, kc, :],
                                                    in_=woutin[kc * 128:(kc + 1) * 128, :]),
                  w=[B_wo], dma=True)
            cd_l = E("g", lambda e: e.dma_start(
                out=cdg[:], in_=convdin), dma=True)
            E("v", lambda e: e.memset(tails[:], 0.0), w=[BN["tails"]])

            for vh in range(8):
                E("v", lambda e, vh=vh: e.memset(stm[:, 0, vh, :], 0.0), w=[B_st[0][vh]])

            rs_done = []
            NGx = ATTN_CFG["groups"] if ATTN_CFG["groups"] is not None else NG
            for gidx in range(NGx):
                gb = gidx % 2
                E("sy", lambda e, gidx=gidx, gb=gb: e.dma_start(
                    out=xT[:, 0].rearrange("p a b -> p a b"),
                    in_=ag1_out[(gidx * 512) // LT, :, :,
                                (gidx * 512) % LT:(gidx * 512) % LT + 512].rearrange("a p b -> p a b")),
                  w=[B_xT[0]], x=[g_ag1], dma=True)
                for oc in range(16):
                    def mk_p(oc=oc, gb=gb):
                        def f(e):
                            ins = []
                            for ic in range(16):
                                ins.append(e.matmul(psP[:], wq[:, ic, oc * 128:(oc + 1) * 128],
                                                    xT[:, 0, ic, :], start=(ic == 0), stop=(ic == 15)))
                            return ins
                        return f
                    E("t", mk_p(), r=[B_xT[0], BN["wcst"]], w=[BN["psP"]])
                    E("v", lambda e: e.tensor_copy(mxs[:, 3:515], psP[:]), r=[BN["psP"]], w=[BN["mxs"]])
                    E("v", lambda e, oc=oc: e.tensor_copy(mxs[:, 0:3], tails[:, oc, 0:3]),
                      r=[BN["tails"]], w=[BN["mxs"]])
                    E("v", lambda e, oc=oc: e.tensor_copy(tails[:, oc, 0:3], mxs[:, 512:515]),
                      r=[BN["mxs"]], w=[BN["tails"]])

                    def mk_c(oc=oc):
                        def f(e):
                            ins = []
                            for k in range(4):
                                ins.append(e.matmul(psC[:], cdg[:, oc, k, :], mxs[:, k:k + 512],
                                                    start=(k == 0), stop=(k == 3)))
                            return ins
                        return f
                    E("t", mk_c(), r=[BN["mxs"]], w=[BN["psC"]], x=[cd_l])
                    E("sq", lambda e, oc=oc, gb=gb: e.activation(qkv[:, gb, oc, :], psC[:], AF.Silu),
                      r=[BN["psC"]], w=[B_qkv[gb]])

                for ch in range(ATTN_CFG["chunks"] if ATTN_CFG["chunks"] is not None else 4):
                    j = gidx * 4 + ch
                    t0 = j * 128
                    sl = slice(ch * 128, ch * 128 + 128)
                    pm = j % 2
                    pn = 1 - pm
                    Bq = B_qkv[gb]
                    E("sy", lambda e, t0=t0: e.dma_start(out=bg8[:, 0, 0:129],
                                                         in_=betagsp[1, :, t0:t0 + 129]),
                      w=[B_bg8], x=[bgst], dma=True)
                    E("sy", lambda e, t0=t0: e.dma_start(out=bg8[:, 1, 0:128],
                                                         in_=betagsp[0, :, t0 + 1:t0 + 129]),
                      w=[B_bg8], x=[bgst], dma=True)
                    E("v", lambda e: e.tensor_scalar(
                        hrow[:, 0, :], bg8[:, 0, 1:129], bg8[:, 0, 0:1], None, ALU.subtract),
                      r=[B_bg8], w=[BN["hrow"]])
                    E("v", lambda e: e.tensor_scalar(hrow[:, 1, :], hrow[:, 0, :],
                                                     hrow[:, 0, 127:128], None, ALU.subtract),
                      r=[BN["hrow"]], w=[BN["hrow"]])
                    E("sq", lambda e: e.activation(gamr[:], hrow[:, 0, 127:128], AF.Exp),
                      r=[BN["hrow"]], w=[BN["gamr"]])
                    E("t", lambda e: e.transpose(pg(0, 8), hrow[:, 0, :], ident[0:8, 0:8]),
                      r=[BN["hrow"], B_const], w=[B_psG[0]])
                    E("v", lambda e: e.tensor_copy(hcolT[:, 0, :], psG[0:128, 0:8]),
                      r=[B_psG[0]], w=[BN["hcolT"]])
                    E("t", lambda e: e.transpose(pg(0, 8), hrow[:, 1, :], ident[0:8, 0:8]),
                      r=[BN["hrow"], B_const], w=[B_psG[0]])
                    E("v", lambda e: e.tensor_copy(hcolT[:, 1, :], psG[0:128, 0:8]),
                      r=[B_psG[0]], w=[BN["hcolT"]])
                    E("t", lambda e: e.transpose(pg(0, 8), bg8[:, 1, 0:128], ident[0:8, 0:8]),
                      r=[B_bg8, B_const], w=[B_psG[0]])
                    E("v", lambda e: e.tensor_copy(hcolT[:, 2, :], psG[0:128, 0:8]),
                      r=[B_psG[0]], w=[BN["hcolT"]])
                    E("sq", lambda e: e.activation(hcolT[:, 3, :], hcolT[:, 0, :], AF.Exp),
                      r=[BN["hcolT"]], w=[BN["hcolT"]])
                    E("sq", lambda e: e.activation(hcolT[:, 1, :], hcolT[:, 1, :], AF.Exp, scale=-1.0),
                      r=[BN["hcolT"]], w=[BN["hcolT"]])
                    E("v", lambda e: e.tensor_scalar(gamd[:], identb[0:8, 0:8], gamr[:], None, ALU.mult),
                      r=[BN["gamr"], B_const], w=[BN["gamd"]])
                    E("t", lambda e: e.matmul(pg(0, 8), sel8[:], gamd[:], start=True, stop=True),
                      r=[BN["gamd"], B_const], w=[B_psG[0]])
                    E("v", lambda e: e.tensor_copy(gamc[:], psG[0:128, 0:8]), r=[B_psG[0]], w=[BN["gamc"]])
                    for kh in range(4):
                        for qk in range(2):
                            src_oc = kh if qk == 0 else 4 + kh
                            ps_t = psE if qk == 0 else psF
                            Bp = B_psE[0] if qk == 0 else B_psF[0]
                            E("sq", lambda e, src_oc=src_oc, gb=gb, sl=sl: e.activation(
                                sq2t[:], qkv[:, gb, src_oc, sl], AF.Square), r=[Bq], w=[BN["sq2t"]])
                            E("t", lambda e, kh=kh, ps_t=ps_t: e.matmul(
                                ps_t[0:1, kh * 128:kh * 128 + 128], onescol[:], sq2t[:],
                                start=True, stop=True),
                              r=[BN["sq2t"], B_const], w=[Bp])
                    E("v", lambda e: e.tensor_copy(lrow[0:1, 0:512], psE[0:1, 0:512]),
                      r=[B_psE[0]], w=[BN["lrow"]])
                    E("v", lambda e: e.tensor_copy(lrow[0:1, 512:1024], psF[0:1, 0:512]),
                      r=[B_psF[0]], w=[BN["lrow"]])
                    E("v", lambda e: e.tensor_scalar(lrow[:], lrow[:], 1e-6, None, ALU.add),
                      r=[BN["lrow"]], w=[BN["lrow"]])
                    E("sq", lambda e: e.activation(lrow[:], lrow[:], AF.Sqrt),
                      r=[BN["lrow"]], w=[BN["lrow"]])
                    E("v", lambda e: e.reciprocal(lrow[:], lrow[:]), r=[BN["lrow"]], w=[BN["lrow"]])
                    for kh in range(4):
                        E("t", lambda e, kh=kh: e.matmul(
                            pg(1)[:, kh:kh + 1], lrow[0:1, 512 + kh * 128:512 + (kh + 1) * 128],
                            onescol[0:1, 0:1], start=True, stop=True),
                          r=[BN["lrow"], B_const], w=[B_psG[1]])
                    E("v", lambda e: e.tensor_copy(rcolk[:], psG[:, 128:132]),
                      r=[B_psG[1]], w=[BN["rcolk"]])
                    for kh in range(4):
                        for qk in range(2):
                            src_oc = kh if qk == 0 else 4 + kh
                            dstt = (qn if qk == 0 else kn)
                            Bd = (B_qn if qk == 0 else B_kn)[kh]
                            E("t", lambda e, kh=kh, qk=qk: e.matmul(
                                pg(2), onesrow[:], lrow[0:1, qk * 512 + kh * 128:qk * 512 + (kh + 1) * 128],
                                start=True, stop=True), r=[BN["lrow"], B_const], w=[B_psG[2]])
                            scl = RSQ2 if qk == 0 else 1.0
                            E("v", lambda e, src_oc=src_oc, dstt=dstt, kh=kh, gb=gb, sl=sl, scl=scl:
                              e.scalar_tensor_tensor(dstt[:, kh, :], qkv[:, gb, src_oc, sl], scl,
                                                     pg(2), ALU.mult, ALU.mult),
                              r=[Bq, B_psG[2]], w=[Bd])
                    for kh in range(4):
                        E("sy", lambda e, kh=kh, gb=gb, sl=sl: e.dma_start(
                            out=ktok[:, kh, :], in_=qkv[:, gb, 4 + kh, sl], transpose=True),
                          r=[Bq], w=[B_ktok[kh]], dma=True)
                    for vh in range(8):
                        E("sy", lambda e, vh=vh, gb=gb, sl=sl: e.dma_start(
                            out=vtok[:, vh, :], in_=qkv[:, gb, 8 + vh, sl], transpose=True),
                          r=[Bq], w=[B_vtok[vh]], dma=True)
                    for kh in range(4):
                        E("t", lambda e, kh=kh: e.matmul(
                            psK1[:, kh * 128:(kh + 1) * 128], kn[:, kh, :], kn[:, kh, :],
                            start=True, stop=True), r=[B_kn[kh]], w=[BN["psK1"]])
                        E("t", lambda e, kh=kh: e.matmul(
                            psK2[:, kh * 128:(kh + 1) * 128], kn[:, kh, :], qn[:, kh, :],
                            start=True, stop=True), r=[B_kn[kh], B_qn[kh]], w=[BN["psK2"]])
                    E("sy", lambda e, j=j: e.dma_start(out=ztile[:], in_=zsp[j]),
                      w=[BN["ztile"]], x=[zstores[j]], dma=True)

                    _ven = [0]

                    def _VE(*aa, **kk):
                        _ven[0] += 1
                        if ATTN_CFG.get("vhsteps") is not None and _ven[0] > ATTN_CFG["vhsteps"]:
                            return None
                        return E(*aa, **kk)
                    for vh in range(ATTN_CFG["vh"] if ATTN_CFG["vh"] is not None else 8):
                        kh = vh // 2
                        _VE("t", lambda e, vh=vh: e.transpose(psG[0:1, 384:512],
                                                            hcolT[:, 0, vh:vh + 1], ident[:]),
                          r=[BN["hcolT"], B_const], w=[B_psG[3]])
                        _VE("v", lambda e: e.tensor_copy(grow[:], psG[0:1, 384:512]),
                          r=[B_psG[3]], w=[BN["grow"]])
                        _VE("t", lambda e: e.matmul(psG[:, 384:512], onesrow[:], grow[:],
                                                  start=True, stop=True),
                          r=[BN["grow"], B_const], w=[B_psG[3]])
                        _VE("v", lambda e, vh=vh: e.tensor_scalar(
                            dect[:, 0, :], psG[:, 384:512], hcolT[:, 0, vh:vh + 1], 0.0,
                            ALU.subtract, ALU.max), r=[B_psG[3], BN["hcolT"]], w=[BN["dect"]])
                        _VE("v", lambda e, vh=vh: e.tensor_scalar(
                            dect[:, 1, :], psG[:, 384:512], hcolT[:, 0, vh:vh + 1], 0.0,
                            ALU.subtract, ALU.min), r=[B_psG[3], BN["hcolT"]], w=[BN["dect"]])
                        _VE("sq", lambda e: e.activation(dect[:, 0, :], dect[:, 0, :], AF.Exp, scale=-1.0),
                          r=[BN["dect"]], w=[BN["dect"]])
                        _VE("sq", lambda e: e.activation(dect[:, 1, :], dect[:, 1, :], AF.Exp),
                          r=[BN["dect"]], w=[BN["dect"]])
                        _VE("v", lambda e: e.tensor_tensor(dmneg[:], dect[:, 0, :], mnegst[:], ALU.mult),
                          r=[BN["dect"], B_const], w=[BN["dmneg"]])
                        _VE("v", lambda e: e.tensor_tensor(dmTin[:], dect[:, 1, :], mtriu[:], ALU.mult),
                          r=[BN["dect"], B_const], w=[BN["dmTin"]])
                        _VE("v", lambda e, kh=kh, vh=vh: e.scalar_tensor_tensor(
                            Amat[:], psK1[:, kh * 128:(kh + 1) * 128], hcolT[:, 2, vh:vh + 1],
                            dmneg[:], ALU.mult, ALU.mult),
                          r=[BN["psK1"], BN["hcolT"], BN["dmneg"]], w=[BN["Amat"]])
                        _VE("sy", lambda e: e.dma_start(out=Bmat[:], in_=Amat[:], transpose=True),
                          r=[BN["Amat"]], w=[BN["Bmat"]], dma=True)
                        _VE("v", lambda e: e.tensor_tensor(Prod[:, 0, :], identb[:], Bmat[:], ALU.add),
                          r=[BN["Bmat"], B_const], w=[B_Prod[0]])
                        lastP = 0
                        AsrcAP, BsrcAP = Amat[:], Bmat[:]
                        AsrcB, BsrcB = BN["Amat"], BN["Bmat"]
                        for lev in range(1, 7):
                            pA = lev % 2
                            _VE("t", lambda e, B_=BsrcAP, A_=AsrcAP: e.matmul(pe(1), B_, A_,
                                                                            start=True, stop=True),
                              r=[AsrcB, BsrcB], w=[B_psE[1]])
                            _VE("sq", lambda e, pA=pA: e.activation(Apw[:, pA, :], pe(1), AF.Copy),
                              r=[B_psE[1]], w=[B_Apw[pA]])
                            if lev < 6:
                                _VE("t", lambda e, B_=BsrcAP, A_=AsrcAP: e.matmul(pe(2), A_, B_,
                                                                                start=True, stop=True),
                                  r=[AsrcB, BsrcB], w=[B_psE[2]])
                                _VE("sq", lambda e, pA=pA: e.activation(Bpw[:, pA, :], pe(2), AF.Copy),
                                  r=[B_psE[2]], w=[B_Bpw[pA]])
                            _VE("t", lambda e, pA=pA, lastP=lastP: e.matmul(
                                pe(3), Apw[:, pA, :], Prod[:, lastP, :], start=True, stop=True),
                              r=[B_Apw[pA], B_Prod[lastP]], w=[B_psE[3]])
                            _VE("v", lambda e, lastP=lastP: e.tensor_tensor(
                                Prod[:, 1 - lastP, :], Prod[:, lastP, :], pe(3), ALU.add),
                              r=[B_psE[3], B_Prod[lastP]], w=[B_Prod[1 - lastP]])
                            lastP = 1 - lastP
                            AsrcAP, BsrcAP = Apw[:, pA, :], Bpw[:, pA, :]
                            AsrcB, BsrcB = B_Apw[pA], B_Bpw[pA]
                        _VE("v", lambda e, kh=kh, vh=vh: e.tensor_tensor(
                            cb(32), rcolk[:, kh:kh + 1], hcolT[:, 2, vh:vh + 1], ALU.mult),
                          r=[BN["rcolk"], BN["hcolT"]], w=[B_cb[32]])
                        _VE("v", lambda e, vh=vh: e.tensor_tensor(
                            cb(33), cb(32), hcolT[:, 3, vh:vh + 1], ALU.mult),
                          r=[B_cb[32], BN["hcolT"]], w=[B_cb[33]])
                        _VE("v", lambda e, kh=kh, vh=vh: e.tensor_tensor(
                            cb(34), rcolk[:, kh:kh + 1], hcolT[:, 1, vh:vh + 1], ALU.mult),
                          r=[BN["rcolk"], BN["hcolT"]], w=[B_cb[34]])
                        _VE("g", lambda e, kh=kh: e.tensor_scalar(kbgt[:], ktok[:, kh, :], cb(33),
                                                                None, ALU.mult),
                          r=[B_ktok[kh], B_cb[33]], w=[BN["kbgt"]])
                        _VE("g", lambda e, vh=vh: e.tensor_scalar(vbt[:], vtok[:, vh, :],
                                                                hcolT[:, 2, vh:vh + 1], None, ALU.mult),
                          r=[B_vtok[vh], BN["hcolT"]], w=[BN["vbt"]])
                        _VE("g", lambda e, kh=kh: e.tensor_scalar(kdect[:], ktok[:, kh, :], cb(34),
                                                                None, ALU.mult),
                          r=[B_ktok[kh], B_cb[34]], w=[BN["kdect"]])
                        _VE("t", lambda e, lastP=lastP: e.matmul(pf(0), Prod[:, lastP, :], vbt[:],
                                                               start=True, stop=True),
                          r=[B_Prod[lastP], BN["vbt"]], w=[B_psF[0]])
                        _VE("v", lambda e: e.tensor_copy(vcorr[:], pf(0)), r=[B_psF[0]], w=[BN["vcorr"]])
                        _VE("t", lambda e, lastP=lastP: e.matmul(pf(1), kbgt[:], Prod[:, lastP, :],
                                                               start=True, stop=True),
                          r=[B_Prod[lastP], BN["kbgt"]], w=[B_psF[1]])
                        _VE("sq", lambda e: e.activation(kcdf[:], pf(1), AF.Copy),
                          r=[B_psF[1]], w=[BN["kcdf"]])
                        _VE("sq", lambda e, pm=pm, vh=vh: e.activation(stb[:], stm[:, pm, vh, :], AF.Copy),
                          r=[B_st[pm][vh]], w=[BN["stb"]])
                        _VE("t", lambda e: e.matmul(pf(2), kcdf[:], stb[:], start=True, stop=True),
                          r=[BN["kcdf"], BN["stb"]], w=[B_psF[2]])
                        _VE("v", lambda e: e.tensor_tensor(vnew[:], vcorr[:], pf(2), ALU.subtract),
                          r=[BN["vcorr"], B_psF[2]], w=[BN["vnew"]])
                        _VE("t", lambda e, kh=kh: e.matmul(pf(3), qn[:, kh, :], stb[:],
                                                         start=True, stop=True),
                          r=[B_qn[kh], BN["stb"]], w=[B_psF[3]])
                        _VE("sq", lambda e, vh=vh: e.activation(o1s[:], pf(3), AF.Copy,
                                                              scale=hcolT[:, 3, vh:vh + 1]),
                          r=[B_psF[3], BN["hcolT"]], w=[BN["o1s"]])
                        _VE("v", lambda e, kh=kh: e.tensor_tensor(
                            intraT[:], psK2[:, kh * 128:(kh + 1) * 128], dmTin[:], ALU.mult),
                          r=[BN["psK2"], BN["dmTin"]], w=[BN["intraT"]])
                        _VE("t", lambda e: e.matmul(pg(3), intraT[:], vnew[:], start=True, stop=True),
                          r=[BN["intraT"], BN["vnew"]], w=[B_psG[3]])
                        _VE("v", lambda e: e.tensor_tensor(oc2[:], o1s[:], pg(3), ALU.add),
                          r=[BN["o1s"], B_psG[3]], w=[BN["oc2"]])
                        _VE("sq", lambda e, vh=vh: e.activation(szt[:], ztile[:, vh * 128:(vh + 1) * 128],
                                                              AF.Silu), r=[BN["ztile"]], w=[BN["szt"]])
                        _VE("v", lambda e: e.tensor_tensor(xg[:], oc2[:], szt[:], ALU.mult),
                          r=[BN["oc2"], BN["szt"]], w=[BN["xg"]])
                        _VE("sq", lambda e: e.activation(sq2t[:], xg[:], AF.Square, accum_out=cb(35)),
                          r=[BN["xg"]], w=[BN["sq2t"], B_cb[35]])
                        _VE("v", lambda e: e.tensor_scalar(cb(36), cb(35), 1.0 / 128.0, 1e-6,
                                                         ALU.mult, ALU.add), r=[B_cb[35]], w=[B_cb[36]])
                        _VE("sq", lambda e: e.activation(cb(37), cb(36), AF.Sqrt),
                          r=[B_cb[36]], w=[B_cb[37]])
                        _VE("v", lambda e: e.reciprocal(cb(38), cb(37)), r=[B_cb[37]], w=[B_cb[38]])
                        _VE("v", lambda e: e.scalar_tensor_tensor(nrm[:], xg[:], cb(38), normw_t[:],
                                                                ALU.mult, ALU.mult),
                          r=[BN["xg"], B_cb[38], B_const], w=[BN["nrm"]])
                        _VE("sy", lambda e, vh=vh: e.dma_start(out=nrmT[:, vh, :], in_=nrm[:],
                                                             transpose=True),
                          r=[BN["nrm"]], w=[B_nrmT[vh]], dma=True)
                        _VE("t", lambda e: e.matmul(pg(3), kdect[:], vnew[:], start=True, stop=True),
                          r=[BN["kdect"], BN["vnew"]], w=[B_psG[3]])
                        _VE("v", lambda e, pm=pm, pn=pn, vh=vh: e.scalar_tensor_tensor(
                            stm[:, pn, vh, :], stm[:, pm, vh, :], gamc[:, vh:vh + 1], pg(3),
                            ALU.mult, ALU.add),
                          r=[B_psG[3], B_st[pm][vh], BN["gamc"]], w=[B_st[pn][vh]])
                    for od in range(4 if (ATTN_CFG["rs"] and (ATTN_CFG["vh"] is None or ATTN_CFG["vh"] == 8)) else 0):
                        ps_t = psK1 if od % 2 == 0 else psK2
                        Bp = BN["psK1"] if od % 2 == 0 else BN["psK2"]

                        def mk_wo(od=od, ps_t=ps_t):
                            def f(e):
                                ins = []
                                for vh in range(8):
                                    ins.append(e.matmul(ps_t[:], nrmT[:, vh, :],
                                                        wo[:, vh, od * 512:(od + 1) * 512],
                                                        start=(vh == 0), stop=(vh == 7)))
                                return ins
                            return f
                        E("t", mk_wo(), r=[B_nrmT[v] for v in range(8)] + [B_wo], w=[Bp])
                        E("v", lambda e, od=od, ps_t=ps_t: e.tensor_copy(rst[:, 0:512], ps_t[:]),
                          r=[Bp], w=[BN["rst"]])
                        rs_done.append(E("sy", lambda e, t0=t0, od=od: e.dma_start(
                            out=rs_in[t0:t0 + 128, od * 512:(od + 1) * 512], in_=rst[:, 0:512]),
                          r=[BN["rst"]], dma=True))
            if ATTN_CFG["rs"] and rs_done:
                g_rs = seq.cc(lambda e: e.collective_compute("ReduceScatter", ALU.add, replica_groups=RG4,
                                                             ins=[rs_in.opt()], outs=[rs_out.opt()]),
                              [rs_done[-1]])
            else:
                g_rs = E("g", lambda e: e.nop())
            _stk4.close()
            seq.barrier()
            return g_rs

    def emit_x2():
        _stk6 = ExitStack()
        xa = _stk6.enter_context(nc.sbuf_tensor("xa", [128, D], F32))
        xb2 = _stk6.enter_context(nc.sbuf_tensor("xb2", [128, D], BF16))
        xc = _stk6.enter_context(nc.sbuf_tensor("xc", [128, D], F32))
        if True:
            B_a, B_b, B_c2 = Buf(), Buf(), Buf()
            for tt in range(NT):
                E("sy", lambda e, tt=tt: e.dma_start(out=xa[:], in_=x1sp[tt * 128:(tt + 1) * 128, :]),
                  w=[B_a], dma=True)
                E("sy", lambda e, tt=tt: e.dma_start(out=xb2[:], in_=rs_out[tt * 128:(tt + 1) * 128, :]),
                  w=[B_b], dma=True)
                E("v", lambda e: e.tensor_tensor(xc[:], xa[:], xb2[:], ALU.add),
                  r=[B_a, B_b], w=[B_c2])
                E("sy", lambda e, tt=tt: e.dma_start(out=x2sp[tt * 128:(tt + 1) * 128, :], in_=xc[:]),
                  r=[B_c2], dma=True)
        _stk6.close()
        seq.barrier()

    def finish_dummy():
        with nc.sbuf_tensor("dz", [128, D], F32) as dz:
            B_dz = Buf()
            E("v", lambda e: e.memset(dz[:], 0.0), w=[B_dz])
            for tt in range(NT):
                E("sy", lambda e, tt=tt: e.dma_start(out=out[tt * 128:(tt + 1) * 128, :], in_=dz[:]),
                  r=[B_dz], dma=True)
        seq.barrier()
        with nc.Block() as block:
            seq.emit(block)
        es.close()
        return nc

    if upto == 0:
        return finish_dummy()
    ag1_dep = emit_ffn(1)
    if upto == 1:
        return finish_dummy()
    g_ag1 = seq.cc(lambda e: e.collective_compute("AllGather", ALU.bypass, replica_groups=RG4,
                                                  ins=[ag1_in.opt()], outs=[ag1_out.opt()]), [ag1_dep])
    seq.barrier()
    if upto == 2:
        return finish_dummy()
    zstores, bgst = emit_zsweep()
    if upto == 3:
        return finish_dummy()
    emit_attention(zstores, bgst)
    if upto == 4:
        return finish_dummy()
    emit_x2()
    if upto == 5:
        return finish_dummy()
    emit_ffn(2)
    seq.barrier()

    with nc.Block() as block:
        seq.emit(block)
    es.close()
    return nc


_prog_cache = {}


def kernel(L0_attn_hidden, L0_attn_W_qkv, L0_attn_W_z, L0_attn_W_b, L0_attn_W_a,
           L0_attn_conv_weight, L0_attn_dt_bias, L0_attn_A_log, L0_attn_norm_w,
           L0_attn_W_out, L0_ffn_w1, L0_ffn_g1, L0_ffn_w2, L0_ffn_g2, L0_ffn_w3, L0_ffn_g3):
    hid = np.ascontiguousarray(np.asarray(L0_attn_hidden, np.float32))
    B, T, _ = hid.shape
    LT = T // 4
    w1T = np.ascontiguousarray(np.asarray(L0_ffn_w1, np.float32).T)
    w2T = np.ascontiguousarray(np.asarray(L0_ffn_w2, np.float32).T)
    w3T = np.ascontiguousarray(np.asarray(L0_ffn_w3, np.float32).T)
    Wqkv = np.asarray(L0_attn_W_qkv, np.float32)
    Wz = np.asarray(L0_attn_W_z, np.float32)
    Wb = np.asarray(L0_attn_W_b, np.float32)
    Wa = np.asarray(L0_attn_W_a, np.float32)
    convw = np.asarray(L0_attn_conv_weight, np.float32)
    Wout = np.asarray(L0_attn_W_out, np.float32)
    ident = np.eye(128, dtype=np.float32)
    ones = np.ones((128, 128), np.float32)
    masks = np.stack([-np.tril(ones, -1), np.triu(ones)]).astype(np.float32)
    sel8 = np.ones((8, 128), np.float32)

    in_maps = []
    for c in range(NCORE):
        bb, hg = c // 4, c % 4
        kh0, vh0 = 4 * hg, 8 * hg
        qcols = slice(kh0 * 128, kh0 * 128 + 512)
        kcols = slice(2048 + kh0 * 128, 2048 + kh0 * 128 + 512)
        vcols = slice(4096 + vh0 * 128, 4096 + vh0 * 128 + 1024)
        wqkv_sl = np.concatenate([Wqkv[:, qcols], Wqkv[:, kcols], Wqkv[:, vcols]], axis=1)
        conv_sl = np.concatenate([convw[kh0 * 128:kh0 * 128 + 512],
                                  convw[2048 + kh0 * 128:2048 + kh0 * 128 + 512],
                                  convw[4096 + vh0 * 128:4096 + vh0 * 128 + 1024]], axis=0)
        convd = np.zeros((128, 16, 4, 128), np.float32)
        for oc in range(16):
            for k in range(4):
                np.fill_diagonal(convd[:, oc, k, :], conv_sl[oc * 128:(oc + 1) * 128, k])
        wba_full = np.concatenate([Wb[:, vh0:vh0 + 8], Wa[:, vh0:vh0 + 8]], axis=1)
        wba = np.ascontiguousarray(wba_full.reshape(16, 128, 16).transpose(1, 0, 2))
        in_maps.append({
            "x0": np.ascontiguousarray(hid[bb, (c % 4) * LT:(c % 4) * LT + LT, :]),
            "w1Ts": np.ascontiguousarray(w1T[c * 256:(c + 1) * 256].reshape(2, 128, 8192)),
            "w2Ts": np.ascontiguousarray(w2T[c * 256:(c + 1) * 256].reshape(2, 128, 8192)),
            "w3Ts": np.ascontiguousarray(w3T[c * 1024:(c + 1) * 1024].reshape(8, 128, 2048)),
            "g1in": np.asarray(L0_ffn_g1, np.float32).reshape(1, -1),
            "g2in": np.asarray(L0_ffn_g2, np.float32).reshape(1, -1),
            "g3in": np.asarray(L0_ffn_g3, np.float32).reshape(1, -1),
            "wqkvin": np.ascontiguousarray(wqkv_sl),
            "wzin": np.ascontiguousarray(Wz[:, vh0 * 128:vh0 * 128 + 1024]),
            "wbain": wba,
            "convdin": convd,
            "woutin": np.ascontiguousarray(Wout[vh0 * 128:vh0 * 128 + 1024, :]),
            "alogin": np.ascontiguousarray(np.asarray(L0_attn_A_log, np.float32)[vh0:vh0 + 8]
                                           .reshape(8, 1)),
            "dtbin": np.ascontiguousarray(np.asarray(L0_attn_dt_bias, np.float32)[vh0:vh0 + 8]
                                          .reshape(8, 1)),
            "normwin": np.asarray(L0_attn_norm_w, np.float32).reshape(1, 128),
            "identin": ident,
            "masksin": masks,
            "sel8in": sel8,
        })

    key = T
    if key not in _prog_cache:
        _prog_cache[key] = build_program(T)
    nc = _prog_cache[key]
    res = run_bass_kernel_spmd(nc, in_maps, core_ids=list(range(NCORE)))
    outp = np.empty((B, T, D), np.float32)
    for c in range(NCORE):
        outp[c // 4, (c % 4) * LT:(c % 4) * LT + LT, :] = res.results[c]["out"]
    return outp


# version 06
# speedup vs baseline: 1.1811x; 1.1811x over previous
import numpy as np
import concourse.bass as bass
import concourse.mybir as mybir
from concourse.bass_utils import run_bass_kernel_spmd
from contextlib import ExitStack

F32 = mybir.dt.float32
BF16 = mybir.dt.bfloat16
I32 = mybir.dt.int32
AF = mybir.ActivationFunctionType
ALU = mybir.AluOpType
AX = mybir.AxisListType

D = 2048
HID = 8192
NCORE = 8
ATANH2 = float(2.0 * np.arctanh(0.5))
RG8 = [list(range(NCORE))]
RG4 = [[0, 1, 2, 3], [4, 5, 6, 7]]
ENGS = ("sy", "sq", "v", "t", "g")
RSQ2 = float(128.0 ** -0.5)


class Seq:
    def __init__(self, nc, es):
        self.nc = nc
        self.es = es
        self.sems = {e: es.enter_context(nc.semaphore(f"sq_{e}")) for e in ENGS}
        self.ops = {e: [] for e in ENGS}
        self.count = {e: 0 for e in ENGS}
        self.ccount = {}

    def _chan_sem(self, key):
        if key not in self.ccount:
            self.sems[key] = self.es.enter_context(self.nc.semaphore(f"sq_{key}"))
            self.ccount[key] = 0
        return key

    def raw(self, eng, fn, deps=(), dma=False, chan=None, serial=True):
        deps = [d for d in deps if d is not None]
        if dma:
            key = self._chan_sem(f"{eng}.{chan or 'd'}")
            if serial and self.ccount[key] > 0:
                deps.append((key, self.ccount[key]))
            self.ccount[key] += 16
            self.ops[eng].append((fn, deps, key, 16))
            return (key, self.ccount[key])
        if self.count[eng] > 0:
            deps.append((eng, self.count[eng]))
        self.count[eng] += 1
        self.ops[eng].append((fn, deps, eng, 1))
        return (eng, self.count[eng])

    def cc(self, build, deps):
        key = self._chan_sem("g.cc")
        deps = list(deps) + ([(key, self.ccount[key])] if self.ccount[key] else [])
        self.ccount[key] += 1
        val = self.ccount[key]

        def f(e):
            build(e).then_inc(self.sems[key], 1)
            return e.nop()
        self.raw("g", f, [d for d in deps if d is not None])
        return (key, val)

    def barrier(self):
        marks = {e: self.count[e] for e in ENGS}
        cmarks = dict(self.ccount)
        out = {}
        for e in ENGS:
            deps = [(o, marks[o]) for o in ENGS if o != e and marks[o] > 0]
            deps += [(k, v) for k, v in cmarks.items() if v > 0]
            out[e] = self.raw(e, lambda en: en.nop(), deps=deps)
        return out

    def emit(self, block):
        sections = {"sy": block.sync, "sq": block.scalar, "v": block.vector,
                    "t": block.tensor, "g": block.gpsimd}
        for eng_name in ENGS:
            def make(eng_name=eng_name):
                def body(en):
                    last = {}
                    done = 0
                    for fn, waits, skey, inc in self.ops[eng_name]:
                        wmax = {}
                        for (oe, val) in waits:
                            if wmax.get(oe, -1) < val:
                                wmax[oe] = val
                        for (oe, val) in wmax.items():
                            if last.get(oe, 0) >= val:
                                continue
                            if oe == eng_name and val > done:
                                raise RuntimeError("forward same-engine dep")
                            en.wait_ge(self.sems[oe], val)
                            last[oe] = val
                        ins = fn(en)
                        if isinstance(ins, (list, tuple)):
                            ins = ins[-1]
                        ins.then_inc(self.sems[skey], inc)
                        if skey == eng_name:
                            done += inc
                return body
            sections[eng_name](make())


class Buf:
    def __init__(self):
        self.writer = None
        self.readers = []


ATTN_CFG = {"groups": None, "chunks": None, "vh": None, "rs": True, "proj": True}


def build_program(T, upto=9):
    LT = T // 4
    NT = LT // 128
    NG = T // 512
    assert T % 2048 == 0
    TBU = min(LT, 1024)
    NBU = LT // TBU
    NTU = TBU // 128

    nc = bass.Bass("TRN2", target_bir_lowering=False)

    x0 = nc.dram_tensor("x0", [LT, D], F32, kind="ExternalInput").ap()
    w1Ts = nc.dram_tensor("w1Ts", [2, 128, HID], F32, kind="ExternalInput").ap()
    w2Ts = nc.dram_tensor("w2Ts", [2, 128, HID], F32, kind="ExternalInput").ap()
    w3Ts = nc.dram_tensor("w3Ts", [8, 128, D], F32, kind="ExternalInput").ap()
    g1in = nc.dram_tensor("g1in", [1, D], F32, kind="ExternalInput").ap()
    g2in = nc.dram_tensor("g2in", [1, D], F32, kind="ExternalInput").ap()
    g3in = nc.dram_tensor("g3in", [1, HID], F32, kind="ExternalInput").ap()
    wqkvin = nc.dram_tensor("wqkvin", [D, 2048], F32, kind="ExternalInput").ap()
    wzin = nc.dram_tensor("wzin", [D, 1024], F32, kind="ExternalInput").ap()
    wbain = nc.dram_tensor("wbain", [128, 16, 16], F32, kind="ExternalInput").ap()
    convdin = nc.dram_tensor("convdin", [128, 16, 4, 128], F32, kind="ExternalInput").ap()
    woutin = nc.dram_tensor("woutin", [1024, D], F32, kind="ExternalInput").ap()
    alogin = nc.dram_tensor("alogin", [8, 1], F32, kind="ExternalInput").ap()
    dtbin = nc.dram_tensor("dtbin", [8, 1], F32, kind="ExternalInput").ap()
    normwin = nc.dram_tensor("normwin", [1, 128], F32, kind="ExternalInput").ap()
    identin = nc.dram_tensor("identin", [128, 128], F32, kind="ExternalInput").ap()
    masksin = nc.dram_tensor("masksin", [2, 128, 128], F32, kind="ExternalInput").ap()
    sel8in = nc.dram_tensor("sel8in", [8, 128], F32, kind="ExternalInput").ap()
    out = nc.dram_tensor("out", [LT, D], F32, kind="ExternalOutput").ap()

    agt_in = nc.dram_tensor("agt_in", [24, 128, 2048], BF16).ap()
    agt_out = nc.dram_tensor("agt_out", [NCORE, 24, 128, 2048], BF16).ap()
    ar_in = nc.dram_tensor("ar_in", [1, 4], F32).ap()
    ar_out = nc.dram_tensor("ar_out", [1, 4], F32).ap()
    ag1_in = nc.dram_tensor("ag1_in", [16, 128, LT], BF16).ap()
    ag1_out = nc.dram_tensor("ag1_out", [4, 16, 128, LT], BF16).ap()
    rs_in = nc.dram_tensor("rs_in", [T, D], BF16).ap()
    rs_out = nc.dram_tensor("rs_out", [LT, D], BF16).ap()
    zsp = nc.dram_tensor("zsp", [T // 128, 128, 1024], BF16).ap()
    betagsp = nc.dram_tensor("betagsp", [2, 8, T + 1], F32).ap()
    hgsp = nc.dram_tensor("hgsp", [NT, 128, HID], BF16).ap()
    x1sp = nc.dram_tensor("x1sp", [LT, D], F32).ap()
    x2sp = nc.dram_tensor("x2sp", [LT, D], F32).ap()

    es = ExitStack()
    seq = Seq(nc, es)

    def E(eng, fn, r=(), w=(), x=(), dma=False, chan=None, serial=True):
        deps = []
        for b in r:
            if b.writer is not None:
                deps.append(b.writer)
        for b in w:
            deps.extend(b.readers if b.readers else ([b.writer] if b.writer else []))
        deps.extend(x)
        h = seq.raw(eng, fn, deps, dma=dma, chan=chan, serial=serial)
        for b in r:
            b.readers.append(h)
        for b in w:
            b.writer = h
            b.readers = []
        return h

    ident = es.enter_context(nc.sbuf_tensor("ident", [128, 128], F32))
    identb = es.enter_context(nc.sbuf_tensor("identb", [128, 128], BF16))
    mnegst = es.enter_context(nc.sbuf_tensor("mnegst", [128, 128], F32))
    mtriu = es.enter_context(nc.sbuf_tensor("mtriu", [128, 128], F32))
    sel8 = es.enter_context(nc.sbuf_tensor("sel8", [8, 128], F32))
    onesrow = es.enter_context(nc.sbuf_tensor("onesrow", [1, 128], F32))
    onescol = es.enter_context(nc.sbuf_tensor("onescol", [128, 1], F32))
    betaP = es.enter_context(nc.sbuf_tensor("betaP", [128, 1], F32))
    betaN = es.enter_context(nc.sbuf_tensor("betaN", [128, 1], F32))
    probef = es.enter_context(nc.sbuf_tensor("probef", [128, 1], F32))
    probei = es.enter_context(nc.sbuf_tensor("probei", [128, 1], I32))
    normw_t = es.enter_context(nc.sbuf_tensor("normw_t", [128, 128], F32))
    scw = es.enter_context(nc.sbuf_tensor("scw", [128, 8], F32))
    cbuf = es.enter_context(nc.sbuf_tensor("cbuf", [128, 48], F32))
    PSB = es.enter_context(nc.psum_tensor("PSB", [128, 512], F32))
    B_PSB = Buf()
    B_cb = [Buf() for _ in range(48)]
    B_const = Buf()

    def cb(j):
        return cbuf[:, j:j + 1]

    E("sy", lambda e: e.dma_start(out=ident[:], in_=identin), w=[B_const], dma=True)
    E("sy", lambda e: e.dma_start(out=mnegst[:], in_=masksin[0]), w=[B_const], dma=True)
    E("sy", lambda e: e.dma_start(out=mtriu[:], in_=masksin[1]), w=[B_const], dma=True)
    E("sy", lambda e: e.dma_start(out=sel8[:], in_=sel8in), w=[B_const], dma=True)
    E("sy", lambda e: e.dma_start(out=normw_t[0:1, :], in_=normwin), w=[B_const], dma=True)
    E("v", lambda e: e.memset(onesrow[:], 1.0), w=[B_const])
    E("v", lambda e: e.memset(onescol[:], 1.0), w=[B_const])
    E("v", lambda e: e.tensor_copy(identb[:], ident[:]), r=[B_const], w=[B_const])
    E("v", lambda e: e.memset(probef[:], 1.5), w=[B_const])
    E("v", lambda e: e.tensor_copy(probei[:], probef[:]), r=[B_const], w=[B_const])
    E("v", lambda e: e.tensor_scalar(betaP[:], probei[:], -2.0, -256.5, ALU.add, ALU.mult),
      r=[B_const], w=[B_const])
    E("v", lambda e: e.tensor_scalar(betaN[:], probei[:], -2.0, 256.0, ALU.add, ALU.mult),
      r=[B_const], w=[B_const])
    E("t", lambda e: e.matmul(PSB[:, 0:128], onesrow[:], normw_t[0:1, :], start=True, stop=True),
      r=[B_const], w=[B_PSB])
    cdep = E("v", lambda e: e.tensor_copy(normw_t[:], PSB[:, 0:128]), r=[B_PSB], w=[B_const])

    wtiles = []
    for wi, src in ((0, w1Ts), (1, w2Ts)):
        for ch in range(2):
            for c4 in range(4):
                wtiles.append((src[ch, :, c4 * 2048:(c4 + 1) * 2048], wi * 8 + ch * 4 + c4, wi))
    for rc in range(8):
        wtiles.append((w3Ts[rc, :, :], 16 + rc, 2))

    B_scw = Buf()
    _stk1 = ExitStack()
    wld = _stk1.enter_context(nc.sbuf_tensor("wld", [128, 2, 2048], F32))
    wq1 = _stk1.enter_context(nc.sbuf_tensor("wq1", [128, 2048], F32))
    wq2 = _stk1.enter_context(nc.sbuf_tensor("wq2", [128, 2048], F32))
    wqi = _stk1.enter_context(nc.sbuf_tensor("wqi", [128, 2048], I32))
    wqb = _stk1.enter_context(nc.sbuf_tensor("wqb", [128, 2048], BF16))
    wdmp = _stk1.enter_context(nc.sbuf_tensor("wdmp", [128, 2048], F32))
    acc3 = _stk1.enter_context(nc.sbuf_tensor("acc3", [128, 4], F32))
    arb = _stk1.enter_context(nc.sbuf_tensor("arb", [1, 4], F32))
    if True:
        B_wld = [Buf(), Buf()]
        B_acc = Buf()
        B_tmp = Buf()
        B_arb = Buf()
        E("v", lambda e: e.memset(acc3[:], 0.0), w=[B_acc])
        for i, (src, entry, m) in enumerate(wtiles):
            b = wld[:, i % 2, :]
            E("sy", lambda e, s=src, b=b: e.dma_start(out=b, in_=s), w=[B_wld[i % 2]], dma=True)
            E("sq", lambda e, b=b, m=m: e.activation(wdmp[:], b, AF.Abs, accum_out=cb(8 + m)),
              r=[B_wld[i % 2]], w=[B_tmp, B_cb[8 + m]])
            E("v", lambda e, m=m: e.tensor_tensor(acc3[:, m:m + 1], acc3[:, m:m + 1], cb(8 + m), ALU.add),
              r=[B_cb[8 + m], B_acc], w=[B_acc])
        E("t", lambda e: e.matmul(PSB[:4, 0:1], acc3[:], onescol[:], start=True, stop=True),
          r=[B_acc, B_const], w=[B_PSB])
        E("v", lambda e: e.tensor_copy(acc3[:4, 0:1], PSB[:4, 0:1]), r=[B_PSB], w=[B_acc])
        E("t", lambda e: e.transpose(PSB[0:1, 128:132], acc3[:4, 0:1], ident[0:4, 0:4]),
          r=[B_acc, B_const], w=[B_PSB])
        E("v", lambda e: e.tensor_copy(arb[:], PSB[0:1, 128:132]), r=[B_PSB], w=[B_arb])
        h0 = E("sy", lambda e: e.dma_start(out=ar_in, in_=arb[:]), r=[B_arb], dma=True)
        h = seq.cc(lambda e: e.collective_compute("AllReduce", ALU.add, replica_groups=RG8,
                                                  ins=[ar_in.opt()], outs=[ar_out.opt()]), [h0])
        E("sy", lambda e: e.dma_start(out=arb[:], in_=ar_out), w=[B_arb], x=[h], dma=True)
        E("t", lambda e: e.matmul(PSB[:, 128:132], onesrow[:], arb[:], start=True, stop=True),
          r=[B_arb, B_const], w=[B_PSB])
        E("v", lambda e: e.tensor_scalar(cbuf[:, 0:4], PSB[:, 128:132], 1.0 / (2048.0 * 8192.0),
                                         None, ALU.mult), r=[B_PSB], w=[B_cb[0]])
        E("v", lambda e: e.tensor_scalar(cbuf[:, 4:8], cbuf[:, 0:4], 1e-8, ATANH2, ALU.add, ALU.mult),
          r=[B_cb[0]], w=[B_cb[4]])
        E("v", lambda e: e.reciprocal(scw[:, 0:4], cbuf[:, 4:8]), r=[B_cb[4]], w=[B_scw])
        E("v", lambda e: e.tensor_scalar(cbuf[:, 8:12], cbuf[:, 0:4], -1.0, 1.0, ALU.mult, ALU.add),
          r=[B_cb[0]], w=[B_cb[8]])
        E("v", lambda e: e.reciprocal(cbuf[:, 12:16], cbuf[:, 8:12]), r=[B_cb[8]], w=[B_cb[12]])
        E("v", lambda e: e.tensor_scalar(cbuf[:, 16:20], cbuf[:, 0:4], 1.0, None, ALU.add),
          r=[B_cb[0]], w=[B_cb[16]])
        E("v", lambda e: e.tensor_tensor(cbuf[:, 20:24], cbuf[:, 16:20], cbuf[:, 12:16], ALU.mult),
          r=[B_cb[16], B_cb[12]], w=[B_cb[20]])
        E("sq", lambda e: e.activation(cbuf[:, 24:28], cbuf[:, 20:24], AF.Ln), r=[B_cb[20]], w=[B_cb[24]])
        E("v", lambda e: e.tensor_scalar(scw[:, 4:8], cbuf[:, 24:28], 0.5, None, ALU.mult),
          r=[B_cb[24], B_scw], w=[B_scw])

        prev_store = None
        for i, (src, entry, m) in enumerate(wtiles):
            b = wld[:, i % 2, :]
            E("sy", lambda e, s=src, b=b: e.dma_start(out=b, in_=s), w=[B_wld[i % 2]], dma=True)
            E("sq", lambda e, b=b, m=m: e.activation(wq1[:], b, AF.Copy, scale=scw[:, m:m + 1]),
              r=[B_wld[i % 2], B_scw], w=[B_tmp])
            E("v", lambda e: e.tensor_scalar(wq2[:], wq1[:], 1.4, -1.4, ALU.min, ALU.max),
              r=[B_tmp], w=[B_tmp])
            E("sq", lambda e: e.activation(wq1[:], wq2[:], AF.Identity, bias=betaP[:, 0:1]),
              r=[B_tmp, B_const], w=[B_tmp])
            E("v", lambda e: e.tensor_copy(wqi[:], wq1[:]), r=[B_tmp], w=[B_tmp])
            E("v", lambda e: e.tensor_scalar(wqb[:], wqi[:], betaN[:, 0:1], None, ALU.add),
              r=[B_tmp, B_const], w=[B_tmp])
            prev_store = E("sy", lambda e, entry=entry: e.dma_start(out=agt_in[entry], in_=wqb[:]),
                           r=[B_tmp], x=[prev_store], dma=True)
        g_agt = seq.cc(lambda e: e.collective_compute("AllGather", ALU.bypass, replica_groups=RG8,
                                                      ins=[agt_in.opt()], outs=[agt_out.opt()]), [prev_store])
        _ = g_agt
    _stk1.close()
    seq.barrier()

    def w_up_tile(wi, ic, oc):
        return agt_out[ic // 2, wi * 8 + (ic % 2) * 4 + oc // 4, :, (oc % 4) * 512:(oc % 4) * 512 + 512]

    def w_dn_tile(icd, od):
        return agt_out[icd // 8, 16 + icd % 8, :, od * 512:od * 512 + 512]

    def ln_chain(src_ap, r_bufs, dst_ap, w_bufs, dmp_ap, dmp_buf, base, eps, x=()):
        E("v", lambda e: e.tensor_reduce(cb(base), src_ap, AX.X, ALU.add),
          r=r_bufs, w=[B_cb[base]], x=list(x))
        E("v", lambda e: e.tensor_scalar(cb(base + 1), cb(base), -1.0 / D, None, ALU.mult),
          r=[B_cb[base]], w=[B_cb[base + 1]])
        E("sq", lambda e: e.activation(dmp_ap, src_ap, AF.Square, bias=cb(base + 1),
                                       accum_out=cb(base + 2)),
          r=r_bufs + [B_cb[base + 1]], w=[dmp_buf, B_cb[base + 2]])
        E("v", lambda e: e.tensor_scalar(cb(base + 3), cb(base + 2), 1.0 / D, eps, ALU.mult, ALU.add),
          r=[B_cb[base + 2]], w=[B_cb[base + 3]])
        E("sq", lambda e: e.activation(cb(base + 4), cb(base + 3), AF.Ln),
          r=[B_cb[base + 3]], w=[B_cb[base + 4]])
        E("sq", lambda e: e.activation(cb(base + 5), cb(base + 4), AF.Exp, scale=-0.5),
          r=[B_cb[base + 4]], w=[B_cb[base + 5]])
        return E("v", lambda e: e.tensor_scalar(dst_ap, src_ap, cb(base + 1), cb(base + 5),
                                                ALU.add, ALU.mult),
                 r=r_bufs + [B_cb[base + 1], B_cb[base + 5]], w=w_bufs)

    last_sh = [None]
    last_so = [None]
    c1s = es.enter_context(nc.sbuf_tensor("c1s", [128, NT], F32))
    c2s = es.enter_context(nc.sbuf_tensor("c2s", [128, NT], F32))
    c3s = es.enter_context(nc.sbuf_tensor("c3s", [128, NT], F32))
    ssqr = es.enter_context(nc.sbuf_tensor("ssqr", [128, NT], F32))
    amaxr = es.enter_context(nc.sbuf_tensor("amaxr", [128, NT], F32))
    B_c1s = [Buf() for _ in range(NT)]
    B_c2s = [Buf() for _ in range(NT)]
    B_c3s = [Buf() for _ in range(NT)]
    B_ssqr = [Buf() for _ in range(NT)]
    B_amaxr = [Buf() for _ in range(NT)]

    def emit_ffn(call):
        x_src = x0 if call == 1 else x2sp
        ag1_last = [None]
        out_last = [None]
        hsp_done = {}
        _stk2 = ExitStack()
        gt3 = _stk2.enter_context(nc.sbuf_tensor(f"f{call}_gt3", [128, HID], BF16))
        g12t = _stk2.enter_context(nc.sbuf_tensor(f"f{call}_g12t", [128, 2, D], BF16))
        xin = _stk2.enter_context(nc.sbuf_tensor(f"f{call}_xin", [128, 2, D], F32))
        xln = _stk2.enter_context(nc.sbuf_tensor(f"f{call}_xln", [128, D], F32))
        ytmp = _stk2.enter_context(nc.sbuf_tensor(f"f{call}_ytmp", [128, D], F32))
        kint = _stk2.enter_context(nc.sbuf_tensor(f"f{call}_kint", [128, D], I32))
        kbf = _stk2.enter_context(nc.sbuf_tensor(f"f{call}_kbf", [128, 2, D], BF16))
        fdmp = _stk2.enter_context(nc.sbuf_tensor(f"f{call}_fdmp", [128, D], F32))
        k1T = _stk2.enter_context(nc.sbuf_tensor(f"f{call}_k1T", [128, 16, TBU], BF16))
        k2T = _stk2.enter_context(nc.sbuf_tensor(f"f{call}_k2T", [128, 16, TBU], BF16))
        wu0 = _stk2.enter_context(nc.sbuf_tensor(f"f{call}_wu0", [128, 8, 2, 512], BF16))
        wu1 = _stk2.enter_context(nc.sbuf_tensor(f"f{call}_wu1", [128, 8, 2, 512], BF16))
        wut = (wu0, wu1)
        silu1 = _stk2.enter_context(nc.sbuf_tensor(f"f{call}_silu1", [128, 512], F32))
        htile = _stk2.enter_context(nc.sbuf_tensor(f"f{call}_htile", [128, 512], BF16))
        hgt = _stk2.enter_context(nc.sbuf_tensor(f"f{call}_hgt", [128, 2, 512], BF16))
        ps1a = _stk2.enter_context(nc.psum_tensor(f"f{call}_ps1a", [128, 512], F32))
        ps1b = _stk2.enter_context(nc.psum_tensor(f"f{call}_ps1b", [128, 512], F32))
        ps2a = _stk2.enter_context(nc.psum_tensor(f"f{call}_ps2a", [128, 512], F32))
        ps2b = _stk2.enter_context(nc.psum_tensor(f"f{call}_ps2b", [128, 512], F32))
        if True:

            B_g = Buf()
            B_gs = Buf()
            with nc.sbuf_tensor(f"f{call}_gstg", [1, 512], F32) as gstg:
                jobs = ([(g3in, gt3[:, j * 512:(j + 1) * 512], j) for j in range(16)] +
                        [(g1in, g12t[:, 0, j * 512:(j + 1) * 512], j) for j in range(4)] +
                        [(g2in, g12t[:, 1, j * 512:(j + 1) * 512], j) for j in range(4)])
                for srcT, dst, j in jobs:
                    E("sy", lambda e, srcT=srcT, j=j: e.dma_start(
                        out=gstg[:], in_=srcT[0:1, j * 512:(j + 1) * 512]), w=[B_gs], dma=True)
                    E("t", lambda e: e.matmul(PSB[:, 0:512], onesrow[:], gstg[:],
                                              start=True, stop=True),
                      r=[B_const, B_gs], w=[B_PSB])
                    E("v", lambda e, dst=dst: e.tensor_copy(dst, PSB[:, 0:512]),
                      r=[B_PSB], w=[B_g])

            B_xin = [Buf(), Buf()]
            B_xln = Buf()
            B_y = Buf()
            B_ki = Buf()
            B_kb = [Buf(), Buf()]
            B_kT = [Buf(), Buf()]
            last_tk = {0: None, 1: None}
            B_wu = [Buf(), Buf()]
            last_wu = {0: None, 1: None}
            B_psy = [[Buf(), Buf()], [Buf(), Buf()]]
            B_si = Buf()
            B_ht = Buf()
            B_hg = [Buf(), Buf()]
            B_fd = Buf()
            E("v", lambda e: e.memset(ssqr[:], 0.0), w=B_ssqr)
            E("g", lambda e: e.memset(amaxr[:], 0.0), w=B_amaxr)

            for tb in range(NBU):
                for tl in range(NTU):
                    tt = tb * NTU + tl
                    xb = xin[:, tt % 2, :]
                    E("sy", lambda e, tt=tt, xb=xb: e.dma_start(
                        out=xb, in_=x_src[tt * 128:(tt + 1) * 128, :]), w=[B_xin[tt % 2]], dma=True,
                      chan="x" + str(tt % 2))
                    ln_chain(xb, [B_xin[tt % 2]], xln[:], [B_xln], fdmp[:], B_fd, 0, 1e-5)
                    E("sq", lambda e: e.activation(fdmp[:], xln[:], AF.Square, accum_out=cb(6)),
                      r=[B_xln], w=[B_fd, B_cb[6]])
                    E("v", lambda e: e.tensor_scalar(cb(7), cb(6), 1.0 / D, 1e-8, ALU.mult, ALU.add),
                      r=[B_cb[6]], w=[B_cb[7]])
                    E("sq", lambda e: e.activation(cb(8), cb(7), AF.Ln), r=[B_cb[7]], w=[B_cb[8]])
                    E("sq", lambda e: e.activation(cb(9), cb(8), AF.Exp, scale=-0.5),
                      r=[B_cb[8]], w=[B_cb[9]])
                    for gi in range(2):
                        kb = kbf[:, gi, :]
                        kT = (k1T, k2T)[gi]
                        cs = (c1s, c2s)[gi]
                        Bcs = (B_c1s, B_c2s)[gi][tt]
                        E("v", lambda e, gi=gi: e.scalar_tensor_tensor(
                            ytmp[:], xln[:], cb(9), g12t[:, gi, :], ALU.mult, ALU.mult),
                          r=[B_xln, B_cb[9], B_g], w=[B_y])
                        E("v", lambda e: e.tensor_reduce(cb(10), ytmp[:], AX.X, ALU.max,
                                                         apply_absolute_value=True),
                          r=[B_y], w=[B_cb[10]])
                        E("v", lambda e: e.tensor_scalar(cb(11), cb(10), 1e-4, None, ALU.max),
                          r=[B_cb[10]], w=[B_cb[11]])
                        E("v", lambda e: e.reciprocal(cb(12), cb(11)), r=[B_cb[11]], w=[B_cb[12]])
                        E("v", lambda e: e.tensor_scalar(cb(13), cb(12), 127.0, None, ALU.mult),
                          r=[B_cb[12]], w=[B_cb[13]])
                        E("v", lambda e, cs=cs, tt=tt, gi=gi: e.scalar_tensor_tensor(
                            cs[:, tt:tt + 1], cb(11), 1.0 / 127.0, scw[:, 4 + gi:5 + gi],
                            ALU.mult, ALU.mult), r=[B_cb[11], B_scw], w=[Bcs])
                        E("sq", lambda e: e.activation(ytmp[:], ytmp[:], AF.Identity, scale=cb(13),
                                                       bias=betaP[:, 0:1]),
                          r=[B_y, B_cb[13], B_const], w=[B_y])
                        E("v", lambda e: e.tensor_copy(kint[:], ytmp[:]), r=[B_y], w=[B_ki])
                        E("v", lambda e, kb=kb: e.tensor_scalar(kb, kint[:], betaN[:, 0:1], None, ALU.add),
                          r=[B_ki, B_const], w=[B_kb[gi]])
                        for ic in range(16):
                            h_tk = E("sy", lambda e, kb=kb, kT=kT, ic=ic, tl=tl: e.dma_start(
                                out=kT[:, ic, tl * 128:(tl + 1) * 128],
                                in_=kb[:, ic * 128:(ic + 1) * 128], transpose=True),
                              r=[B_kb[gi]] if ic == 15 else [], w=[B_kT[gi]],
                              x=([last_tk[gi]] if ic == 0 else []) +
                                ([B_kb[gi].writer] if ic < 15 else []),
                              dma=True, chan="tk" + str(gi), serial=False)
                        last_tk[gi] = h_tk
                for oc in range(16):
                    for wi in range(2):
                        for ch in range(2):
                            h_wu = E("sq", lambda e, wi=wi, oc=oc, ch=ch: e.dma_start(
                                out=wut[wi][:, :, ch, :],
                                in_=agt_out[:, wi * 8 + ch * 4 + oc // 4, :,
                                            (oc % 4) * 512:(oc % 4) * 512 + 512]
                                    .rearrange("r p n -> p r n")),
                              w=[B_wu[wi]],
                              x=[g_agt] + ([last_wu[wi]] if ch == 0 else []),
                              dma=True, chan="wu" + str(wi), serial=False)
                            last_wu[wi] = h_wu
                    for tl in range(NTU):
                        tt = tb * NTU + tl
                        psy1 = (ps1a, ps1b)[tl % 2]
                        psy2 = (ps2a, ps2b)[tl % 2]

                        def mk_mm(tl=tl, psy1=psy1, psy2=psy2):
                            def f(e):
                                ins = []
                                for ic in range(16):
                                    ins.append(e.matmul(psy1[:], k1T[:, ic, tl * 128:(tl + 1) * 128],
                                                        wu0[:, ic // 2, ic % 2, :],
                                                        start=(ic == 0), stop=(ic == 15)))
                                for ic in range(16):
                                    ins.append(e.matmul(psy2[:], k2T[:, ic, tl * 128:(tl + 1) * 128],
                                                        wu1[:, ic // 2, ic % 2, :],
                                                        start=(ic == 0), stop=(ic == 15)))
                                return ins
                            return f
                        E("t", mk_mm(), r=[B_kT[0], B_kT[1], B_wu[0], B_wu[1]],
                          w=[B_psy[0][tl % 2], B_psy[1][tl % 2]])
                        E("sq", lambda e, psy1=psy1, tt=tt: e.activation(
                            silu1[:], psy1[:], AF.Silu, scale=c1s[:, tt:tt + 1]),
                          r=[B_psy[0][tl % 2], B_c1s[tt]], w=[B_si])
                        E("v", lambda e, psy2=psy2, tt=tt: e.scalar_tensor_tensor(
                            htile[:], psy2[:], c2s[:, tt:tt + 1], silu1[:], ALU.mult, ALU.mult),
                          r=[B_psy[1][tl % 2], B_c2s[tt], B_si], w=[B_ht])
                        E("sq", lambda e: e.activation(fdmp[:, 0:512], htile[:], AF.Square,
                                                       accum_out=cb(14)),
                          r=[B_ht], w=[B_fd, B_cb[14]])
                        E("v", lambda e, tt=tt: e.tensor_tensor(
                            ssqr[:, tt:tt + 1], ssqr[:, tt:tt + 1], cb(14), ALU.add),
                          r=[B_cb[14], B_ssqr[tt]], w=[B_ssqr[tt]])
                        hb = hgt[:, tl % 2, :]
                        E("v", lambda e, oc=oc, hb=hb: e.tensor_tensor(
                            hb, htile[:], gt3[:, oc * 512:(oc + 1) * 512], ALU.mult),
                          r=[B_ht, B_g], w=[B_hg[tl % 2]])
                        E("v", lambda e, hb=hb: e.tensor_reduce(cb(15), hb, AX.X, ALU.max,
                                                                apply_absolute_value=True),
                          r=[B_hg[tl % 2]], w=[B_cb[15]])
                        E("v", lambda e, tt=tt: e.tensor_tensor(
                            amaxr[:, tt:tt + 1], amaxr[:, tt:tt + 1], cb(15), ALU.max),
                          r=[B_cb[15], B_amaxr[tt]], w=[B_amaxr[tt]])
                        hsp_done[(oc, tt)] = E("sy", lambda e, oc=oc, tt=tt, hb=hb: e.dma_start(
                            out=hgsp[tt, :, oc * 512:(oc + 1) * 512], in_=hb),
                          r=[B_hg[tl % 2]],
                          x=[last_sh[0]] if (oc == 0 and tt == tb * NTU) else [],
                          dma=True, chan="sh")
                        last_sh[0] = hsp_done[(oc, tt)]

        _stk2.close()
        seq.barrier()
        _stk3 = ExitStack()
        khT = _stk3.enter_context(nc.sbuf_tensor(f"f{call}_khT", [128, 64, 512], BF16))
        wdt = _stk3.enter_context(nc.sbuf_tensor(f"f{call}_wdt", [128, 8, 8, 512], BF16))
        hgl = _stk3.enter_context(nc.sbuf_tensor(f"f{call}_hgl", [128, 2048], BF16))
        uh = _stk3.enter_context(nc.sbuf_tensor(f"f{call}_uh", [128, 2048], F32))
        kih = _stk3.enter_context(nc.sbuf_tensor(f"f{call}_kih", [128, 2048], I32))
        khb = _stk3.enter_context(nc.sbuf_tensor(f"f{call}_khb", [128, 2048], BF16))
        oacc = _stk3.enter_context(nc.sbuf_tensor(f"f{call}_oacc", [128, 512], F32))
        xres = _stk3.enter_context(nc.sbuf_tensor(f"f{call}_xres", [128, D], F32))
        x1t = _stk3.enter_context(nc.sbuf_tensor(f"f{call}_x1t", [128, D], F32))
        cdmp = _stk3.enter_context(nc.sbuf_tensor(f"f{call}_cdmp", [128, D], F32))
        x2nb = _stk3.enter_context(nc.sbuf_tensor(f"f{call}_x2nb", [128, D], BF16))
        x2nT = _stk3.enter_context(nc.sbuf_tensor(f"f{call}_x2nT", [128, 16, 128], BF16))
        psoa = _stk3.enter_context(nc.psum_tensor(f"f{call}_psoa", [128, 512], F32))
        psob = _stk3.enter_context(nc.psum_tensor(f"f{call}_psob", [128, 512], F32))
        if True:
            B_khT = Buf()
            B_wd = Buf()
            last_wd = [None]
            last_tkh = [None]
            last_tx = [None]
            B_hgl = Buf()
            B_u = Buf()
            B_k2 = Buf()
            B_k3 = Buf()
            B_oa = Buf()
            B_xr = Buf()
            B_pso = [Buf(), Buf()]
            B_x1t = Buf()
            B_x2n = Buf()
            B_x2T = Buf()
            B_cd = Buf()
            for bd in range(LT // 512):
                for tl in range(4):
                    tt = bd * 4 + tl
                    E("v", lambda e, tt=tt: e.tensor_scalar(cb(16), ssqr[:, tt:tt + 1], 1.0 / HID,
                                                            1e-8, ALU.mult, ALU.add),
                      r=[B_ssqr[tt]], w=[B_cb[16]])
                    E("sq", lambda e: e.activation(cb(17), cb(16), AF.Ln), r=[B_cb[16]], w=[B_cb[17]])
                    E("sq", lambda e: e.activation(cb(18), cb(17), AF.Exp, scale=-0.5),
                      r=[B_cb[17]], w=[B_cb[18]])
                    E("v", lambda e, tt=tt: e.tensor_tensor(cb(19), amaxr[:, tt:tt + 1], cb(18), ALU.mult),
                      r=[B_amaxr[tt], B_cb[18]], w=[B_cb[19]])
                    E("v", lambda e: e.tensor_scalar(cb(20), cb(19), 1e-4, None, ALU.max),
                      r=[B_cb[19]], w=[B_cb[20]])
                    E("v", lambda e: e.reciprocal(cb(21), cb(20)), r=[B_cb[20]], w=[B_cb[21]])
                    E("v", lambda e: e.tensor_scalar(cb(22), cb(21), 127.0, None, ALU.mult),
                      r=[B_cb[21]], w=[B_cb[22]])
                    E("v", lambda e: e.tensor_tensor(cb(23), cb(22), cb(18), ALU.mult),
                      r=[B_cb[22], B_cb[18]], w=[B_cb[23]])
                    E("v", lambda e, tt=tt: e.scalar_tensor_tensor(
                        c3s[:, tt:tt + 1], cb(20), 0.5 / 127.0, scw[:, 6:7], ALU.mult, ALU.mult),
                      r=[B_cb[20], B_scw], w=[B_c3s[tt]])
                    for q4 in range(4):
                        E("sy", lambda e, tt=tt, q4=q4: e.dma_start(
                            out=hgl[:], in_=hgsp[tt, :, q4 * 2048:(q4 + 1) * 2048]),
                          w=[B_hgl], x=[hsp_done[(15, NT - 1)]], dma=True, chan="hl")
                        E("sq", lambda e: e.activation(uh[:], hgl[:], AF.Identity, scale=cb(23),
                                                       bias=betaP[:, 0:1]),
                          r=[B_hgl, B_cb[23], B_const], w=[B_u])
                        E("v", lambda e: e.tensor_copy(kih[:], uh[:]), r=[B_u], w=[B_k2])
                        E("v", lambda e: e.tensor_scalar(khb[:], kih[:], betaN[:, 0:1], None, ALU.add),
                          r=[B_k2, B_const], w=[B_k3])
                        for ic in range(16):
                            icd = q4 * 16 + ic
                            h_kh = E("sy", lambda e, icd=icd, ic=ic, tl=tl: e.dma_start(
                                out=khT[:, icd, tl * 128:(tl + 1) * 128],
                                in_=khb[:, ic * 128:(ic + 1) * 128], transpose=True),
                              r=[B_k3] if ic == 15 else [], w=[B_khT],
                              x=([last_tkh[0]] if (tl == 0 and q4 == 0 and ic == 0) else []) +
                                ([B_k3.writer] if ic < 15 else []),
                              dma=True, chan="tkh", serial=False)

                last_tkh[0] = B_khT.writer
                ostores = {}
                for od in range(4):
                    wls = []
                    for rc in range(8):
                        h_wd = E("sq", lambda e, od=od, rc=rc: e.dma_start(
                            out=wdt[:, :, rc, :],
                            in_=agt_out[:, 16 + rc, :, od * 512:od * 512 + 512]
                                .rearrange("r p n -> p r n")),
                          w=[B_wd], x=[g_agt] + ([last_wd[0]] if rc == 0 else []),
                          dma=True, chan="wd", serial=False)
                        wls.append(h_wd)
                    last_wd[0] = wls[-1]
                    for tl in range(4):
                        tt = bd * 4 + tl
                        pso = (psoa, psob)[tl % 2]

                        def mk_dn(tl=tl, pso=pso):
                            def f(e):
                                ins = []
                                for icd in range(64):
                                    ins.append(e.matmul(pso[:], khT[:, icd, tl * 128:(tl + 1) * 128],
                                                        wdt[:, icd // 8, icd % 8, :],
                                                        start=(icd == 0), stop=(icd == 63)))
                                return ins
                            return f
                        E("t", mk_dn(), r=[B_khT, B_wd], w=[B_pso[tl % 2]], x=wls)
                        E("sy", lambda e, tt=tt, od=od: e.dma_start(
                            out=xres[:, od * 512:(od + 1) * 512],
                            in_=x_src[tt * 128:(tt + 1) * 128, od * 512:(od + 1) * 512]),
                          w=[B_xr], dma=True, chan="xr")
                        E("v", lambda e, tt=tt, od=od, pso=pso: e.scalar_tensor_tensor(
                            oacc[:], pso[:], c3s[:, tt:tt + 1], xres[:, od * 512:(od + 1) * 512],
                            ALU.mult, ALU.add), r=[B_pso[tl % 2], B_c3s[tt], B_xr], w=[B_oa])
                        dst = x1sp if call == 1 else out
                        st = E("sy", lambda e, tt=tt, od=od, dst=dst: e.dma_start(
                            out=dst[tt * 128:(tt + 1) * 128, od * 512:(od + 1) * 512], in_=oacc[:]),
                          r=[B_oa], x=[last_so[0]] if (od == 0 and tl == 0) else [],
                          dma=True, chan="so", serial=False)
                        last_so[0] = st
                        ostores[(od, tl)] = st
                        out_last[0] = st
                if call == 1:
                    for tl in range(4):
                        tt = bd * 4 + tl
                        E("sy", lambda e, tt=tt: e.dma_start(
                            out=x1t[:], in_=x1sp[tt * 128:(tt + 1) * 128, :]),
                          w=[B_x1t], x=[ostores[(3, 3)]], dma=True, chan="x1")
                        ln_chain(x1t[:], [B_x1t], x2nb[:], [B_x2n], cdmp[:], B_cd, 24, 1e-5)
                        for ic in range(16):
                            h_tx = E("sy", lambda e, ic=ic: e.dma_start(
                                out=x2nT[:, ic, :], in_=x2nb[:, ic * 128:(ic + 1) * 128], transpose=True),
                              r=[B_x2n] if ic == 15 else [], w=[B_x2T],
                              x=([last_tx[0]] if ic == 0 else []) +
                                ([B_x2n.writer] if ic < 15 else []),
                              dma=True, chan="tx", serial=False)
                        last_tx[0] = h_tx
                        ag1_last[0] = E("sy", lambda e, tt=tt: e.dma_start(
                            out=ag1_in[:, :, tt * 128:(tt + 1) * 128].rearrange("a p b -> p a b"),
                            in_=x2nT[:]), r=[B_x2T], dma=True, chan="sa", serial=False)
        _stk3.close()
        seq.barrier()
        return ag1_last[0] if call == 1 else out_last[0]

    def emit_zsweep():
        zstores = {}
        _stk5 = ExitStack()
        wzsb = _stk5.enter_context(nc.sbuf_tensor("wzsb", [128, 16, 1024], BF16))
        wbasb = _stk5.enter_context(nc.sbuf_tensor("wbasb", [128, 16, 16], BF16))
        batile = _stk5.enter_context(nc.sbuf_tensor("batile", [128, 16], F32))
        zt2 = _stk5.enter_context(nc.sbuf_tensor("zt2", [128, 1024], BF16))
        xTz = _stk5.enter_context(nc.sbuf_tensor("xTz", [128, 16, 128], BF16))
        bafm = _stk5.enter_context(nc.sbuf_tensor("bafm", [8, 2, T], F32))
        bg2 = _stk5.enter_context(nc.sbuf_tensor("bg2", [8, 2, T], F32))
        zcol = _stk5.enter_context(nc.sbuf_tensor("zcol", [8, 1], F32))
        psZ1 = _stk5.enter_context(nc.psum_tensor("psZ1", [128, 512], F32))
        psZ2 = _stk5.enter_context(nc.psum_tensor("psZ2", [128, 512], F32))
        psZ3 = _stk5.enter_context(nc.psum_tensor("psZ3", [128, 512], F32))
        B_zt2 = Buf(); B_bat = Buf(); B_xTz = Buf(); B_ba = Buf(); B_bg = Buf(); B_zc = Buf()
        B_pz1 = Buf(); B_pz2 = Buf(); B_pz3 = Buf()
        wz_l = []
        for ic in range(16):
            wz_l.append(E("g", lambda e, ic=ic: e.dma_start(
                out=wzsb[:, ic, :], in_=wzin[ic * 128:(ic + 1) * 128, :]), dma=True))
        wba_l = E("g", lambda e: e.dma_start(
            out=wbasb[:], in_=wbain), dma=True)
        for tt2 in range(T // 128):
            E("sy", lambda e, tt2=tt2: e.dma_start(
                out=xTz[:],
                in_=ag1_out[(tt2 * 128) // LT, :, :,
                            (tt2 * 128) % LT:(tt2 * 128) % LT + 128].rearrange("a p b -> p a b")),
              w=[B_xTz], x=[g_ag1], dma=True, chan="zx")

            def mk_zh(half):
                def f(e):
                    ins = []
                    for ic in range(16):
                        ins.append(e.matmul((psZ1 if half == 0 else psZ2)[:],
                                            xTz[:, ic, :],
                                            wzsb[:, ic, half * 512:half * 512 + 512],
                                            start=(ic == 0), stop=(ic == 15)))
                    return ins
                return f
            E("t", mk_zh(0), r=[B_xTz], w=[B_pz1], x=wz_l)
            E("t", mk_zh(1), r=[B_xTz], w=[B_pz2], x=wz_l)

            def mk_ba(e):
                ins = []
                for ic in range(16):
                    ins.append(e.matmul(psZ3[:, 0:16], xTz[:, ic, :],
                                        wbasb[:, ic, :], start=(ic == 0), stop=(ic == 15)))
                return ins
            E("t", mk_ba, r=[B_xTz], w=[B_pz3], x=[wba_l])
            E("sq", lambda e: e.activation(zt2[:, 0:512], psZ1[:], AF.Silu), r=[B_pz1], w=[B_zt2])
            E("sq", lambda e: e.activation(zt2[:, 512:1024], psZ2[:], AF.Silu), r=[B_pz2], w=[B_zt2])
            zstores[tt2] = E("sy", lambda e, tt2=tt2: e.dma_start(out=zsp[tt2], in_=zt2[:]),
                             r=[B_zt2], dma=True, chan="sz", serial=False)
            E("v", lambda e: e.tensor_copy(batile[:], psZ3[:, 0:16]), r=[B_pz3], w=[B_bat])
            for half in range(2):
                E("t", lambda e, half=half: e.transpose(psZ3[0:8, 128 + half * 128:256 + half * 128],
                                                        batile[:, half * 8:half * 8 + 8], ident[:]),
                  r=[B_bat, B_const], w=[B_pz3])
                E("v", lambda e, tt2=tt2, half=half: e.tensor_copy(
                    bafm[:, half, tt2 * 128:(tt2 + 1) * 128],
                    psZ3[0:8, 128 + half * 128:256 + half * 128]),
                  r=[B_pz3], w=[B_ba])
        l_dtb = E("sy", lambda e: e.dma_start(out=cbuf[0:8, 28:29], in_=dtbin), dma=True)
        l_alog = E("sy", lambda e: e.dma_start(out=cbuf[0:8, 29:30], in_=alogin), dma=True)
        E("sq", lambda e: e.activation(bg2[:, 0, :], bafm[:, 0, :], AF.Sigmoid), r=[B_ba], w=[B_bg])
        E("v", lambda e: e.tensor_scalar(bafm[:, 1, :], bafm[:, 1, :], cbuf[0:8, 28:29],
                                         None, ALU.add), r=[B_ba], w=[B_ba], x=[l_dtb])
        E("sq", lambda e: e.activation(bg2[:, 1, :], bafm[:, 1, :], AF.Abs), r=[B_ba], w=[B_bg])
        E("sq", lambda e: e.activation(bg2[:, 1, :], bg2[:, 1, :], AF.Exp, scale=-1.0),
          r=[B_bg], w=[B_bg])
        E("sq", lambda e: e.activation(bg2[:, 1, :], bg2[:, 1, :], AF.Ln, bias=1.0),
          r=[B_bg], w=[B_bg])
        E("v", lambda e: e.tensor_scalar(bafm[:, 1, :], bafm[:, 1, :], 0.0, None, ALU.max),
          r=[B_ba], w=[B_ba])
        E("v", lambda e: e.tensor_tensor(bg2[:, 1, :], bg2[:, 1, :], bafm[:, 1, :], ALU.add),
          r=[B_bg, B_ba], w=[B_bg])
        E("sq", lambda e: e.activation(cbuf[0:8, 30:31], cbuf[0:8, 29:30], AF.Exp),
          x=[l_alog], w=[B_cb[30]])
        E("v", lambda e: e.tensor_scalar(cbuf[0:8, 31:32], cbuf[0:8, 30:31], -1.0, None,
                                         ALU.mult), r=[B_cb[30]], w=[B_cb[31]])
        E("v", lambda e: e.tensor_scalar(bg2[:, 1, :], bg2[:, 1, :], cbuf[0:8, 31:32], None,
                                         ALU.mult), r=[B_bg, B_cb[31]], w=[B_bg])
        E("v", lambda e: e.tensor_tensor_scan(bg2[:, 1, :], bg2[:, 1, :], bafm[:, 1, :], 0.0,
                                              ALU.add, ALU.bypass), r=[B_bg, B_ba], w=[B_bg])
        E("v", lambda e: e.memset(zcol[:], 0.0), w=[B_zc])
        E("sy", lambda e: e.dma_start(out=betagsp[0, :, 0:1], in_=zcol[:], allow_slow_non_contiguous=True), r=[B_zc], dma=True)
        E("sy", lambda e: e.dma_start(out=betagsp[1, :, 0:1], in_=zcol[:], allow_slow_non_contiguous=True), r=[B_zc], dma=True)
        E("sy", lambda e: e.dma_start(out=betagsp[0, :, 1:T + 1], in_=bg2[:, 0, :]),
          r=[B_bg], dma=True)
        bgst = E("sy", lambda e: e.dma_start(out=betagsp[1, :, 1:T + 1], in_=bg2[:, 1, :]),
                 r=[B_bg], dma=True)
        _stk5.close()
        seq.barrier()
        return zstores, bgst

    def emit_attention(zstores, bgst):
        _stk4 = ExitStack()
        wq = _stk4.enter_context(nc.sbuf_tensor("wq", [128, 16, 2048], BF16))
        wo = _stk4.enter_context(nc.sbuf_tensor("wo", [128, 8, 2048], BF16))
        cdg = _stk4.enter_context(nc.sbuf_tensor("cdg", [128, 16, 4, 128], BF16))
        xT = _stk4.enter_context(nc.sbuf_tensor("xT", [128, 1, 16, 512], BF16))
        mxs = _stk4.enter_context(nc.sbuf_tensor("mxs", [128, 515], BF16))
        tails = _stk4.enter_context(nc.sbuf_tensor("tails", [128, 16, 4], BF16))
        qkv = _stk4.enter_context(nc.sbuf_tensor("qkv", [128, 2, 16, 512], BF16))
        bg8 = _stk4.enter_context(nc.sbuf_tensor("bg8", [8, 2, 129], F32))
        hrow = _stk4.enter_context(nc.sbuf_tensor("hrow", [8, 2, 128], F32))
        hcolT = _stk4.enter_context(nc.sbuf_tensor("hcolT", [128, 4, 8], F32))
        grow = _stk4.enter_context(nc.sbuf_tensor("grow", [1, 128], F32))
        gamr = _stk4.enter_context(nc.sbuf_tensor("gamr", [8, 1], F32))
        gamd = _stk4.enter_context(nc.sbuf_tensor("gamd", [8, 8], F32))
        gamc = _stk4.enter_context(nc.sbuf_tensor("gamc", [128, 8], F32))
        lrow = _stk4.enter_context(nc.sbuf_tensor("lrow", [1, 1024], F32))
        rcolk = _stk4.enter_context(nc.sbuf_tensor("rcolk", [128, 4], F32))
        qn = _stk4.enter_context(nc.sbuf_tensor("qn", [128, 4, 128], BF16))
        kn = _stk4.enter_context(nc.sbuf_tensor("kn", [128, 4, 128], BF16))
        sq2t = _stk4.enter_context(nc.sbuf_tensor("sq2t", [128, 128], F32))
        ktok = _stk4.enter_context(nc.sbuf_tensor("ktok", [128, 4, 128], BF16))
        vtok = _stk4.enter_context(nc.sbuf_tensor("vtok", [128, 8, 128], BF16))
        kbgt = _stk4.enter_context(nc.sbuf_tensor("kbgt", [128, 128], BF16))
        vbt = _stk4.enter_context(nc.sbuf_tensor("vbt", [128, 128], BF16))
        kdect = _stk4.enter_context(nc.sbuf_tensor("kdect", [128, 128], BF16))
        dect = _stk4.enter_context(nc.sbuf_tensor("dect", [128, 2, 128], F32))
        dmneg = _stk4.enter_context(nc.sbuf_tensor("dmneg", [128, 128], F32))
        dmTin = _stk4.enter_context(nc.sbuf_tensor("dmTin", [128, 128], F32))
        Amat = _stk4.enter_context(nc.sbuf_tensor("Amat", [128, 128], BF16))
        Bmat = _stk4.enter_context(nc.sbuf_tensor("Bmat", [128, 128], BF16))
        Apw = _stk4.enter_context(nc.sbuf_tensor("Apw", [128, 2, 128], BF16))
        Bpw = _stk4.enter_context(nc.sbuf_tensor("Bpw", [128, 2, 128], BF16))
        Prod = _stk4.enter_context(nc.sbuf_tensor("Prod", [128, 2, 128], BF16))
        vcorr = _stk4.enter_context(nc.sbuf_tensor("vcorr", [128, 128], BF16))
        kcdf = _stk4.enter_context(nc.sbuf_tensor("kcdf", [128, 128], BF16))
        vnew = _stk4.enter_context(nc.sbuf_tensor("vnew", [128, 128], BF16))
        intraT = _stk4.enter_context(nc.sbuf_tensor("intraT", [128, 128], BF16))
        o1s = _stk4.enter_context(nc.sbuf_tensor("o1s", [128, 128], F32))
        oc2 = _stk4.enter_context(nc.sbuf_tensor("oc2", [128, 128], F32))
        xg = _stk4.enter_context(nc.sbuf_tensor("xg", [128, 128], F32))
        nrm = _stk4.enter_context(nc.sbuf_tensor("nrm", [128, 128], BF16))
        nrmT = _stk4.enter_context(nc.sbuf_tensor("nrmT", [128, 8, 128], BF16))
        stm = _stk4.enter_context(nc.sbuf_tensor("stm", [128, 2, 8, 128], F32))
        stb = _stk4.enter_context(nc.sbuf_tensor("stb", [128, 128], BF16))
        ztile = _stk4.enter_context(nc.sbuf_tensor("ztile", [128, 1024], BF16))
        szt = _stk4.enter_context(nc.sbuf_tensor("szt", [128, 128], F32))
        rst = _stk4.enter_context(nc.sbuf_tensor("rst", [128, 512], BF16))
        psP = _stk4.enter_context(nc.psum_tensor("psP", [128, 512], F32))
        psC = _stk4.enter_context(nc.psum_tensor("psC", [128, 512], F32))
        psK1 = _stk4.enter_context(nc.psum_tensor("psK1", [128, 512], F32))
        psK2 = _stk4.enter_context(nc.psum_tensor("psK2", [128, 512], F32))
        psE = _stk4.enter_context(nc.psum_tensor("psE", [128, 512], F32))
        psF = _stk4.enter_context(nc.psum_tensor("psF", [128, 512], F32))
        psG = _stk4.enter_context(nc.psum_tensor("psG", [128, 512], F32))
        if True:

            BN = {}
            for nm in ("psP", "psC", "psK1", "psK2", "hrow", "hcolT", "grow", "gamr", "gamd",
                       "gamc", "lrow", "rcolk", "sq2t", "kbgt", "vbt", "kdect", "dect", "dmneg",
                       "dmTin", "Amat", "Bmat", "vcorr", "kcdf", "vnew", "intraT", "o1s", "oc2",
                       "xg", "nrm", "stb", "ztile", "szt", "mxs", "tails", "rst", "wcst"):
                BN[nm] = Buf()
            _bE = Buf(); _bF = Buf(); _bG = Buf()
            B_psE = [_bE, _bE, _bE, _bE]
            B_psF = [_bF, _bF, _bF, _bF]
            B_psG = [_bG, _bG, _bG, _bG]
            B_xT = [Buf()]
            B_qkv = [Buf(), Buf()]
            B_qn = [Buf() for _ in range(4)]
            last_tkv = [None]
            last_tb = [None]
            last_tn = [None]
            B_kn = [Buf() for _ in range(4)]
            B_kvtok = Buf()
            B_ktok = [B_kvtok] * 4
            B_vtok = [B_kvtok] * 8
            B_Apw = [Buf(), Buf()]
            B_Bpw = [Buf(), Buf()]
            B_Prod = [Buf(), Buf()]
            B_st = [[Buf() for _ in range(8)], [Buf() for _ in range(8)]]
            _bn = Buf()
            B_nrmT = [_bn] * 8
            B_bg8 = Buf()

            pe = lambda q, w=128: psE[:, q * 128:q * 128 + w]
            pf = lambda q, w=128: psF[:, q * 128:q * 128 + w]
            pg = lambda q, w=128: psG[:, q * 128:q * 128 + w]

            B_wo = Buf()
            for ic in range(16):
                E("g", lambda e, ic=ic: e.dma_start(out=wq[:, ic, :],
                                                    in_=wqkvin[ic * 128:(ic + 1) * 128, :]),
                  w=[BN["wcst"]], dma=True)
            for kc in range(8):
                E("g", lambda e, kc=kc: e.dma_start(out=wo[:, kc, :],
                                                    in_=woutin[kc * 128:(kc + 1) * 128, :]),
                  w=[B_wo], dma=True)
            cd_l = E("g", lambda e: e.dma_start(
                out=cdg[:], in_=convdin), dma=True)
            E("v", lambda e: e.memset(tails[:], 0.0), w=[BN["tails"]])

            for vh in range(8):
                E("v", lambda e, vh=vh: e.memset(stm[:, 0, vh, :], 0.0), w=[B_st[0][vh]])

            rs_done = []
            NGx = ATTN_CFG["groups"] if ATTN_CFG["groups"] is not None else NG
            for gidx in range(NGx):
                gb = gidx % 2
                E("sy", lambda e, gidx=gidx, gb=gb: e.dma_start(
                    out=xT[:, 0].rearrange("p a b -> p a b"),
                    in_=ag1_out[(gidx * 512) // LT, :, :,
                                (gidx * 512) % LT:(gidx * 512) % LT + 512].rearrange("a p b -> p a b")),
                  w=[B_xT[0]], x=[g_ag1], dma=True, chan="zx")
                for oc in range(16):
                    def mk_p(oc=oc, gb=gb):
                        def f(e):
                            ins = []
                            for ic in range(16):
                                ins.append(e.matmul(psP[:], wq[:, ic, oc * 128:(oc + 1) * 128],
                                                    xT[:, 0, ic, :], start=(ic == 0), stop=(ic == 15)))
                            return ins
                        return f
                    E("t", mk_p(), r=[B_xT[0], BN["wcst"]], w=[BN["psP"]])
                    E("v", lambda e: e.tensor_copy(mxs[:, 3:515], psP[:]), r=[BN["psP"]], w=[BN["mxs"]])
                    E("v", lambda e, oc=oc: e.tensor_copy(mxs[:, 0:3], tails[:, oc, 0:3]),
                      r=[BN["tails"]], w=[BN["mxs"]])
                    E("v", lambda e, oc=oc: e.tensor_copy(tails[:, oc, 0:3], mxs[:, 512:515]),
                      r=[BN["mxs"]], w=[BN["tails"]])

                    def mk_c(oc=oc):
                        def f(e):
                            ins = []
                            for k in range(4):
                                ins.append(e.matmul(psC[:], cdg[:, oc, k, :], mxs[:, k:k + 512],
                                                    start=(k == 0), stop=(k == 3)))
                            return ins
                        return f
                    E("t", mk_c(), r=[BN["mxs"]], w=[BN["psC"]], x=[cd_l])
                    E("sq", lambda e, oc=oc, gb=gb: e.activation(qkv[:, gb, oc, :], psC[:], AF.Silu),
                      r=[BN["psC"]], w=[B_qkv[gb]])

                for ch in range(ATTN_CFG["chunks"] if ATTN_CFG["chunks"] is not None else 4):
                    j = gidx * 4 + ch
                    t0 = j * 128
                    sl = slice(ch * 128, ch * 128 + 128)
                    pm = j % 2
                    pn = 1 - pm
                    Bq = B_qkv[gb]
                    E("sy", lambda e, t0=t0: e.dma_start(out=bg8[:, 0, 0:129],
                                                         in_=betagsp[1, :, t0:t0 + 129]),
                      w=[B_bg8], x=[bgst], dma=True, chan="bg")
                    E("sy", lambda e, t0=t0: e.dma_start(out=bg8[:, 1, 0:128],
                                                         in_=betagsp[0, :, t0 + 1:t0 + 129]),
                      w=[B_bg8], x=[bgst], dma=True, chan="bg")
                    E("v", lambda e: e.tensor_scalar(
                        hrow[:, 0, :], bg8[:, 0, 1:129], bg8[:, 0, 0:1], None, ALU.subtract),
                      r=[B_bg8], w=[BN["hrow"]])
                    E("v", lambda e: e.tensor_scalar(hrow[:, 1, :], hrow[:, 0, :],
                                                     hrow[:, 0, 127:128], None, ALU.subtract),
                      r=[BN["hrow"]], w=[BN["hrow"]])
                    E("sq", lambda e: e.activation(gamr[:], hrow[:, 0, 127:128], AF.Exp),
                      r=[BN["hrow"]], w=[BN["gamr"]])
                    E("t", lambda e: e.transpose(pg(0, 8), hrow[:, 0, :], ident[0:8, 0:8]),
                      r=[BN["hrow"], B_const], w=[B_psG[0]])
                    E("v", lambda e: e.tensor_copy(hcolT[:, 0, :], psG[0:128, 0:8]),
                      r=[B_psG[0]], w=[BN["hcolT"]])
                    E("t", lambda e: e.transpose(pg(0, 8), hrow[:, 1, :], ident[0:8, 0:8]),
                      r=[BN["hrow"], B_const], w=[B_psG[0]])
                    E("v", lambda e: e.tensor_copy(hcolT[:, 1, :], psG[0:128, 0:8]),
                      r=[B_psG[0]], w=[BN["hcolT"]])
                    E("t", lambda e: e.transpose(pg(0, 8), bg8[:, 1, 0:128], ident[0:8, 0:8]),
                      r=[B_bg8, B_const], w=[B_psG[0]])
                    E("v", lambda e: e.tensor_copy(hcolT[:, 2, :], psG[0:128, 0:8]),
                      r=[B_psG[0]], w=[BN["hcolT"]])
                    E("sq", lambda e: e.activation(hcolT[:, 3, :], hcolT[:, 0, :], AF.Exp),
                      r=[BN["hcolT"]], w=[BN["hcolT"]])
                    E("sq", lambda e: e.activation(hcolT[:, 1, :], hcolT[:, 1, :], AF.Exp, scale=-1.0),
                      r=[BN["hcolT"]], w=[BN["hcolT"]])
                    E("v", lambda e: e.tensor_scalar(gamd[:], identb[0:8, 0:8], gamr[:], None, ALU.mult),
                      r=[BN["gamr"], B_const], w=[BN["gamd"]])
                    E("t", lambda e: e.matmul(pg(0, 8), sel8[:], gamd[:], start=True, stop=True),
                      r=[BN["gamd"], B_const], w=[B_psG[0]])
                    E("v", lambda e: e.tensor_copy(gamc[:], psG[0:128, 0:8]), r=[B_psG[0]], w=[BN["gamc"]])
                    for kh in range(4):
                        for qk in range(2):
                            src_oc = kh if qk == 0 else 4 + kh
                            ps_t = psE if qk == 0 else psF
                            Bp = B_psE[0] if qk == 0 else B_psF[0]
                            E("sq", lambda e, src_oc=src_oc, gb=gb, sl=sl: e.activation(
                                sq2t[:], qkv[:, gb, src_oc, sl], AF.Square), r=[Bq], w=[BN["sq2t"]])
                            E("t", lambda e, kh=kh, ps_t=ps_t: e.matmul(
                                ps_t[0:1, kh * 128:kh * 128 + 128], onescol[:], sq2t[:],
                                start=True, stop=True),
                              r=[BN["sq2t"], B_const], w=[Bp])
                    E("v", lambda e: e.tensor_copy(lrow[0:1, 0:512], psE[0:1, 0:512]),
                      r=[B_psE[0]], w=[BN["lrow"]])
                    E("v", lambda e: e.tensor_copy(lrow[0:1, 512:1024], psF[0:1, 0:512]),
                      r=[B_psF[0]], w=[BN["lrow"]])
                    E("v", lambda e: e.tensor_scalar(lrow[:], lrow[:], 1e-6, None, ALU.add),
                      r=[BN["lrow"]], w=[BN["lrow"]])
                    E("sq", lambda e: e.activation(lrow[:], lrow[:], AF.Ln),
                      r=[BN["lrow"]], w=[BN["lrow"]])
                    E("sq", lambda e: e.activation(lrow[:], lrow[:], AF.Exp, scale=-0.5),
                      r=[BN["lrow"]], w=[BN["lrow"]])
                    for kh in range(4):
                        E("t", lambda e, kh=kh: e.matmul(
                            pg(1)[:, kh:kh + 1], lrow[0:1, 512 + kh * 128:512 + (kh + 1) * 128],
                            onescol[0:1, 0:1], start=True, stop=True),
                          r=[BN["lrow"], B_const], w=[B_psG[1]])
                    E("v", lambda e: e.tensor_copy(rcolk[:], psG[:, 128:132]),
                      r=[B_psG[1]], w=[BN["rcolk"]])
                    for kh in range(4):
                        for qk in range(2):
                            src_oc = kh if qk == 0 else 4 + kh
                            dstt = (qn if qk == 0 else kn)
                            Bd = (B_qn if qk == 0 else B_kn)[kh]
                            E("t", lambda e, kh=kh, qk=qk: e.matmul(
                                pg(2), onesrow[:], lrow[0:1, qk * 512 + kh * 128:qk * 512 + (kh + 1) * 128],
                                start=True, stop=True), r=[BN["lrow"], B_const], w=[B_psG[2]])
                            scl = RSQ2 if qk == 0 else 1.0
                            E("v", lambda e, src_oc=src_oc, dstt=dstt, kh=kh, gb=gb, sl=sl, scl=scl:
                              e.scalar_tensor_tensor(dstt[:, kh, :], qkv[:, gb, src_oc, sl], scl,
                                                     pg(2), ALU.mult, ALU.mult),
                              r=[Bq, B_psG[2]], w=[Bd])
                    for kh in range(4):
                        E("sy", lambda e, kh=kh, gb=gb, sl=sl: e.dma_start(
                            out=ktok[:, kh, :], in_=qkv[:, gb, 4 + kh, sl], transpose=True),
                          r=[], w=[B_kvtok],
                          x=([last_tkv[0]] if kh == 0 else []) + [Bq.writer],
                          dma=True, chan="tkv", serial=False)
                    tpkv = None
                    for vh in range(8):
                        tpkv = E("sy", lambda e, vh=vh, gb=gb, sl=sl: e.dma_start(
                            out=vtok[:, vh, :], in_=qkv[:, gb, 8 + vh, sl], transpose=True),
                          r=[Bq] if vh == 7 else [], w=[B_kvtok],
                          x=[Bq.writer] if vh < 7 else [], dma=True, chan="tkv", serial=False)
                    last_tkv[0] = tpkv
                    for kh in range(4):
                        E("t", lambda e, kh=kh: e.matmul(
                            psK1[:, kh * 128:(kh + 1) * 128], kn[:, kh, :], kn[:, kh, :],
                            start=True, stop=True), r=[B_kn[kh]], w=[BN["psK1"]])
                        E("t", lambda e, kh=kh: e.matmul(
                            psK2[:, kh * 128:(kh + 1) * 128], kn[:, kh, :], qn[:, kh, :],
                            start=True, stop=True), r=[B_kn[kh], B_qn[kh]], w=[BN["psK2"]])
                    E("sy", lambda e, j=j: e.dma_start(out=ztile[:], in_=zsp[j]),
                      w=[BN["ztile"]], dma=True, chan="zt")

                    _ven = [0]

                    def _VE(*aa, **kk):
                        _ven[0] += 1
                        if ATTN_CFG.get("vhsteps") is not None and _ven[0] > ATTN_CFG["vhsteps"]:
                            return None
                        return E(*aa, **kk)
                    for vh in range(ATTN_CFG["vh"] if ATTN_CFG["vh"] is not None else 8):
                        kh = vh // 2
                        _VE("t", lambda e, vh=vh: e.transpose(psG[0:1, 384:512],
                                                            hcolT[:, 0, vh:vh + 1], ident[:]),
                          r=[BN["hcolT"], B_const], w=[B_psG[3]])
                        _VE("v", lambda e: e.tensor_copy(grow[:], psG[0:1, 384:512]),
                          r=[B_psG[3]], w=[BN["grow"]])
                        _VE("t", lambda e: e.matmul(psG[:, 384:512], onesrow[:], grow[:],
                                                  start=True, stop=True),
                          r=[BN["grow"], B_const], w=[B_psG[3]])
                        _VE("v", lambda e, vh=vh: e.tensor_scalar(
                            dect[:, 0, :], psG[:, 384:512], hcolT[:, 0, vh:vh + 1], 0.0,
                            ALU.subtract, ALU.max), r=[B_psG[3], BN["hcolT"]], w=[BN["dect"]])
                        _VE("v", lambda e, vh=vh: e.tensor_scalar(
                            dect[:, 1, :], psG[:, 384:512], hcolT[:, 0, vh:vh + 1], 0.0,
                            ALU.subtract, ALU.min), r=[B_psG[3], BN["hcolT"]], w=[BN["dect"]])
                        _VE("sq", lambda e: e.activation(dect[:, 0, :], dect[:, 0, :], AF.Exp, scale=-1.0),
                          r=[BN["dect"]], w=[BN["dect"]])
                        _VE("sq", lambda e: e.activation(dect[:, 1, :], dect[:, 1, :], AF.Exp),
                          r=[BN["dect"]], w=[BN["dect"]])
                        _VE("v", lambda e: e.tensor_tensor(dmneg[:], dect[:, 0, :], mnegst[:], ALU.mult),
                          r=[BN["dect"], B_const], w=[BN["dmneg"]])
                        _VE("v", lambda e: e.tensor_tensor(dmTin[:], dect[:, 1, :], mtriu[:], ALU.mult),
                          r=[BN["dect"], B_const], w=[BN["dmTin"]])
                        _VE("v", lambda e, kh=kh, vh=vh: e.scalar_tensor_tensor(
                            Amat[:], psK1[:, kh * 128:(kh + 1) * 128], hcolT[:, 2, vh:vh + 1],
                            dmneg[:], ALU.mult, ALU.mult),
                          r=[BN["psK1"], BN["hcolT"], BN["dmneg"]], w=[BN["Amat"]])
                        _VE("sy", lambda e: e.dma_start(out=Bmat[:], in_=Amat[:], transpose=True),
                          r=[BN["Amat"]], w=[BN["Bmat"]], dma=True)
                        _VE("v", lambda e: e.tensor_tensor(Prod[:, 0, :], identb[:], Bmat[:], ALU.add),
                          r=[BN["Bmat"], B_const], w=[B_Prod[0]])
                        lastP = 0
                        AsrcAP, BsrcAP = Amat[:], Bmat[:]
                        AsrcB, BsrcB = BN["Amat"], BN["Bmat"]
                        for lev in range(1, 7):
                            pA = lev % 2
                            _VE("t", lambda e, B_=BsrcAP, A_=AsrcAP: e.matmul(pe(1), B_, A_,
                                                                            start=True, stop=True),
                              r=[AsrcB, BsrcB], w=[B_psE[1]])
                            _VE("sq", lambda e, pA=pA: e.activation(Apw[:, pA, :], pe(1), AF.Copy),
                              r=[B_psE[1]], w=[B_Apw[pA]])
                            if lev < 6:
                                _VE("t", lambda e, B_=BsrcAP, A_=AsrcAP: e.matmul(pe(2), A_, B_,
                                                                                start=True, stop=True),
                                  r=[AsrcB, BsrcB], w=[B_psE[2]])
                                _VE("sq", lambda e, pA=pA: e.activation(Bpw[:, pA, :], pe(2), AF.Copy),
                                  r=[B_psE[2]], w=[B_Bpw[pA]])
                            _VE("t", lambda e, pA=pA, lastP=lastP: e.matmul(
                                pe(3), Apw[:, pA, :], Prod[:, lastP, :], start=True, stop=True),
                              r=[B_Apw[pA], B_Prod[lastP]], w=[B_psE[3]])
                            _VE("v", lambda e, lastP=lastP: e.tensor_tensor(
                                Prod[:, 1 - lastP, :], Prod[:, lastP, :], pe(3), ALU.add),
                              r=[B_psE[3], B_Prod[lastP]], w=[B_Prod[1 - lastP]])
                            lastP = 1 - lastP
                            AsrcAP, BsrcAP = Apw[:, pA, :], Bpw[:, pA, :]
                            AsrcB, BsrcB = B_Apw[pA], B_Bpw[pA]
                        _VE("v", lambda e, kh=kh, vh=vh: e.tensor_tensor(
                            cb(32), rcolk[:, kh:kh + 1], hcolT[:, 2, vh:vh + 1], ALU.mult),
                          r=[BN["rcolk"], BN["hcolT"]], w=[B_cb[32]])
                        _VE("v", lambda e, vh=vh: e.tensor_tensor(
                            cb(33), cb(32), hcolT[:, 3, vh:vh + 1], ALU.mult),
                          r=[B_cb[32], BN["hcolT"]], w=[B_cb[33]])
                        _VE("v", lambda e, kh=kh, vh=vh: e.tensor_tensor(
                            cb(34), rcolk[:, kh:kh + 1], hcolT[:, 1, vh:vh + 1], ALU.mult),
                          r=[BN["rcolk"], BN["hcolT"]], w=[B_cb[34]])
                        _VE("g", lambda e, kh=kh: e.tensor_scalar(kbgt[:], ktok[:, kh, :], cb(33),
                                                                None, ALU.mult),
                          r=[B_ktok[kh], B_cb[33]], w=[BN["kbgt"]], x=[tpkv])
                        _VE("g", lambda e, vh=vh: e.tensor_scalar(vbt[:], vtok[:, vh, :],
                                                                hcolT[:, 2, vh:vh + 1], None, ALU.mult),
                          r=[B_vtok[vh], BN["hcolT"]], w=[BN["vbt"]], x=[tpkv])
                        _VE("g", lambda e, kh=kh: e.tensor_scalar(kdect[:], ktok[:, kh, :], cb(34),
                                                                None, ALU.mult),
                          r=[B_ktok[kh], B_cb[34]], w=[BN["kdect"]], x=[tpkv])
                        _VE("t", lambda e, lastP=lastP: e.matmul(pf(0), Prod[:, lastP, :], vbt[:],
                                                               start=True, stop=True),
                          r=[B_Prod[lastP], BN["vbt"]], w=[B_psF[0]])
                        _VE("v", lambda e: e.tensor_copy(vcorr[:], pf(0)), r=[B_psF[0]], w=[BN["vcorr"]])
                        _VE("t", lambda e, lastP=lastP: e.matmul(pf(1), kbgt[:], Prod[:, lastP, :],
                                                               start=True, stop=True),
                          r=[B_Prod[lastP], BN["kbgt"]], w=[B_psF[1]])
                        _VE("sq", lambda e: e.activation(kcdf[:], pf(1), AF.Copy),
                          r=[B_psF[1]], w=[BN["kcdf"]])
                        _VE("sq", lambda e, pm=pm, vh=vh: e.activation(stb[:], stm[:, pm, vh, :], AF.Copy),
                          r=[B_st[pm][vh]], w=[BN["stb"]])
                        _VE("t", lambda e: e.matmul(pf(2), kcdf[:], stb[:], start=True, stop=True),
                          r=[BN["kcdf"], BN["stb"]], w=[B_psF[2]])
                        _VE("v", lambda e: e.tensor_tensor(vnew[:], vcorr[:], pf(2), ALU.subtract),
                          r=[BN["vcorr"], B_psF[2]], w=[BN["vnew"]])
                        _VE("t", lambda e, kh=kh: e.matmul(pf(3), qn[:, kh, :], stb[:],
                                                         start=True, stop=True),
                          r=[B_qn[kh], BN["stb"]], w=[B_psF[3]])
                        _VE("sq", lambda e, vh=vh: e.activation(o1s[:], pf(3), AF.Copy,
                                                              scale=hcolT[:, 3, vh:vh + 1]),
                          r=[B_psF[3], BN["hcolT"]], w=[BN["o1s"]])
                        _VE("v", lambda e, kh=kh: e.tensor_tensor(
                            intraT[:], psK2[:, kh * 128:(kh + 1) * 128], dmTin[:], ALU.mult),
                          r=[BN["psK2"], BN["dmTin"]], w=[BN["intraT"]])
                        _VE("t", lambda e: e.matmul(pg(3), intraT[:], vnew[:], start=True, stop=True),
                          r=[BN["intraT"], BN["vnew"]], w=[B_psG[3]])
                        _VE("v", lambda e: e.tensor_tensor(oc2[:], o1s[:], pg(3), ALU.add),
                          r=[BN["o1s"], B_psG[3]], w=[BN["oc2"]])
                        _VE("v", lambda e, vh=vh: e.tensor_tensor(
                            xg[:], oc2[:], ztile[:, vh * 128:(vh + 1) * 128], ALU.mult),
                          r=[BN["oc2"], BN["ztile"]], w=[BN["xg"]])
                        _VE("sq", lambda e: e.activation(sq2t[:], xg[:], AF.Square, accum_out=cb(35)),
                          r=[BN["xg"]], w=[BN["sq2t"], B_cb[35]])
                        _VE("v", lambda e: e.tensor_scalar(cb(36), cb(35), 1.0 / 128.0, 1e-6,
                                                         ALU.mult, ALU.add), r=[B_cb[35]], w=[B_cb[36]])
                        _VE("sq", lambda e: e.activation(cb(37), cb(36), AF.Ln),
                          r=[B_cb[36]], w=[B_cb[37]])
                        _VE("sq", lambda e: e.activation(cb(38), cb(37), AF.Exp, scale=-0.5),
                          r=[B_cb[37]], w=[B_cb[38]])
                        _VE("v", lambda e: e.scalar_tensor_tensor(nrm[:], xg[:], cb(38), normw_t[:],
                                                                ALU.mult, ALU.mult),
                          r=[BN["xg"], B_cb[38], B_const], w=[BN["nrm"]])
                        _VE("sy", lambda e, vh=vh: e.dma_start(out=nrmT[:, vh, :], in_=nrm[:],
                                                             transpose=True),
                          r=[BN["nrm"]], w=[B_nrmT[vh]], dma=True)
                        _VE("t", lambda e: e.matmul(pg(3), kdect[:], vnew[:], start=True, stop=True),
                          r=[BN["kdect"], BN["vnew"]], w=[B_psG[3]])
                        _VE("v", lambda e, pm=pm, pn=pn, vh=vh: e.scalar_tensor_tensor(
                            stm[:, pn, vh, :], stm[:, pm, vh, :], gamc[:, vh:vh + 1], pg(3),
                            ALU.mult, ALU.add),
                          r=[B_psG[3], B_st[pm][vh], BN["gamc"]], w=[B_st[pn][vh]])
                    for od in range(4 if (ATTN_CFG["rs"] and (ATTN_CFG["vh"] is None or ATTN_CFG["vh"] == 8)) else 0):
                        ps_t = psK1 if od % 2 == 0 else psK2
                        Bp = BN["psK1"] if od % 2 == 0 else BN["psK2"]

                        def mk_wo(od=od, ps_t=ps_t):
                            def f(e):
                                ins = []
                                for vh in range(8):
                                    ins.append(e.matmul(ps_t[:], nrmT[:, vh, :],
                                                        wo[:, vh, od * 512:(od + 1) * 512],
                                                        start=(vh == 0), stop=(vh == 7)))
                                return ins
                            return f
                        E("t", mk_wo(), r=[B_nrmT[v] for v in range(8)] + [B_wo], w=[Bp])
                        E("v", lambda e, od=od, ps_t=ps_t: e.tensor_copy(rst[:, 0:512], ps_t[:]),
                          r=[Bp], w=[BN["rst"]])
                        rs_done.append(E("sy", lambda e, t0=t0, od=od: e.dma_start(
                            out=rs_in[t0:t0 + 128, od * 512:(od + 1) * 512], in_=rst[:, 0:512]),
                          r=[BN["rst"]], dma=True, chan="sr", serial=False))
            if ATTN_CFG["rs"] and rs_done:
                g_rs = seq.cc(lambda e: e.collective_compute("ReduceScatter", ALU.add, replica_groups=RG4,
                                                             ins=[rs_in.opt()], outs=[rs_out.opt()]),
                              [rs_done[-1]])
            else:
                g_rs = E("g", lambda e: e.nop())
            _stk4.close()
            seq.barrier()
            return g_rs

    def emit_x2():
        _stk6 = ExitStack()
        xa = _stk6.enter_context(nc.sbuf_tensor("xa", [128, D], F32))
        xb2 = _stk6.enter_context(nc.sbuf_tensor("xb2", [128, D], BF16))
        xc = _stk6.enter_context(nc.sbuf_tensor("xc", [128, D], F32))
        if True:
            B_a, B_b, B_c2 = Buf(), Buf(), Buf()
            for tt in range(NT):
                E("sy", lambda e, tt=tt: e.dma_start(out=xa[:], in_=x1sp[tt * 128:(tt + 1) * 128, :]),
                  w=[B_a], dma=True, chan="xa")
                E("sy", lambda e, tt=tt: e.dma_start(out=xb2[:], in_=rs_out[tt * 128:(tt + 1) * 128, :]),
                  w=[B_b], dma=True, chan="xb")
                E("v", lambda e: e.tensor_tensor(xc[:], xa[:], xb2[:], ALU.add),
                  r=[B_a, B_b], w=[B_c2])
                E("sy", lambda e, tt=tt: e.dma_start(out=x2sp[tt * 128:(tt + 1) * 128, :], in_=xc[:]),
                  r=[B_c2], dma=True)
        _stk6.close()
        seq.barrier()

    def finish_dummy():
        with nc.sbuf_tensor("dz", [128, D], F32) as dz:
            B_dz = Buf()
            E("v", lambda e: e.memset(dz[:], 0.0), w=[B_dz])
            for tt in range(NT):
                E("sy", lambda e, tt=tt: e.dma_start(out=out[tt * 128:(tt + 1) * 128, :], in_=dz[:]),
                  r=[B_dz], dma=True)
        seq.barrier()
        with nc.Block() as block:
            seq.emit(block)
        es.close()
        return nc

    if upto == 0:
        return finish_dummy()
    ag1_dep = emit_ffn(1)
    if upto == 1:
        return finish_dummy()
    g_ag1 = seq.cc(lambda e: e.collective_compute("AllGather", ALU.bypass, replica_groups=RG4,
                                                  ins=[ag1_in.opt()], outs=[ag1_out.opt()]), [ag1_dep])
    seq.barrier()
    if upto == 2:
        return finish_dummy()
    zstores, bgst = emit_zsweep()
    if upto == 3:
        return finish_dummy()
    emit_attention(zstores, bgst)
    if upto == 4:
        return finish_dummy()
    emit_x2()
    if upto == 5:
        return finish_dummy()
    emit_ffn(2)
    seq.barrier()

    with nc.Block() as block:
        seq.emit(block)
    es.close()
    return nc


_prog_cache = {}


def kernel(L0_attn_hidden, L0_attn_W_qkv, L0_attn_W_z, L0_attn_W_b, L0_attn_W_a,
           L0_attn_conv_weight, L0_attn_dt_bias, L0_attn_A_log, L0_attn_norm_w,
           L0_attn_W_out, L0_ffn_w1, L0_ffn_g1, L0_ffn_w2, L0_ffn_g2, L0_ffn_w3, L0_ffn_g3):
    hid = np.ascontiguousarray(np.asarray(L0_attn_hidden, np.float32))
    B, T, _ = hid.shape
    LT = T // 4
    w1T = np.ascontiguousarray(np.asarray(L0_ffn_w1, np.float32).T)
    w2T = np.ascontiguousarray(np.asarray(L0_ffn_w2, np.float32).T)
    w3T = np.ascontiguousarray(np.asarray(L0_ffn_w3, np.float32).T)
    Wqkv = np.asarray(L0_attn_W_qkv, np.float32)
    Wz = np.asarray(L0_attn_W_z, np.float32)
    Wb = np.asarray(L0_attn_W_b, np.float32)
    Wa = np.asarray(L0_attn_W_a, np.float32)
    convw = np.asarray(L0_attn_conv_weight, np.float32)
    Wout = np.asarray(L0_attn_W_out, np.float32)
    ident = np.eye(128, dtype=np.float32)
    ones = np.ones((128, 128), np.float32)
    masks = np.stack([-np.tril(ones, -1), np.triu(ones)]).astype(np.float32)
    sel8 = np.ones((8, 128), np.float32)

    in_maps = []
    for c in range(NCORE):
        bb, hg = c // 4, c % 4
        kh0, vh0 = 4 * hg, 8 * hg
        qcols = slice(kh0 * 128, kh0 * 128 + 512)
        kcols = slice(2048 + kh0 * 128, 2048 + kh0 * 128 + 512)
        vcols = slice(4096 + vh0 * 128, 4096 + vh0 * 128 + 1024)
        wqkv_sl = np.concatenate([Wqkv[:, qcols], Wqkv[:, kcols], Wqkv[:, vcols]], axis=1)
        conv_sl = np.concatenate([convw[kh0 * 128:kh0 * 128 + 512],
                                  convw[2048 + kh0 * 128:2048 + kh0 * 128 + 512],
                                  convw[4096 + vh0 * 128:4096 + vh0 * 128 + 1024]], axis=0)
        convd = np.zeros((128, 16, 4, 128), np.float32)
        for oc in range(16):
            for k in range(4):
                np.fill_diagonal(convd[:, oc, k, :], conv_sl[oc * 128:(oc + 1) * 128, k])
        wba_full = np.concatenate([Wb[:, vh0:vh0 + 8], Wa[:, vh0:vh0 + 8]], axis=1)
        wba = np.ascontiguousarray(wba_full.reshape(16, 128, 16).transpose(1, 0, 2))
        in_maps.append({
            "x0": np.ascontiguousarray(hid[bb, (c % 4) * LT:(c % 4) * LT + LT, :]),
            "w1Ts": np.ascontiguousarray(w1T[c * 256:(c + 1) * 256].reshape(2, 128, 8192)),
            "w2Ts": np.ascontiguousarray(w2T[c * 256:(c + 1) * 256].reshape(2, 128, 8192)),
            "w3Ts": np.ascontiguousarray(w3T[c * 1024:(c + 1) * 1024].reshape(8, 128, 2048)),
            "g1in": np.asarray(L0_ffn_g1, np.float32).reshape(1, -1),
            "g2in": np.asarray(L0_ffn_g2, np.float32).reshape(1, -1),
            "g3in": np.asarray(L0_ffn_g3, np.float32).reshape(1, -1),
            "wqkvin": np.ascontiguousarray(wqkv_sl),
            "wzin": np.ascontiguousarray(Wz[:, vh0 * 128:vh0 * 128 + 1024]),
            "wbain": wba,
            "convdin": convd,
            "woutin": np.ascontiguousarray(Wout[vh0 * 128:vh0 * 128 + 1024, :]),
            "alogin": np.ascontiguousarray(np.asarray(L0_attn_A_log, np.float32)[vh0:vh0 + 8]
                                           .reshape(8, 1)),
            "dtbin": np.ascontiguousarray(np.asarray(L0_attn_dt_bias, np.float32)[vh0:vh0 + 8]
                                          .reshape(8, 1)),
            "normwin": np.asarray(L0_attn_norm_w, np.float32).reshape(1, 128),
            "identin": ident,
            "masksin": masks,
            "sel8in": sel8,
        })

    key = T
    if key not in _prog_cache:
        _prog_cache[key] = build_program(T)
    nc = _prog_cache[key]
    res = run_bass_kernel_spmd(nc, in_maps, core_ids=list(range(NCORE)))
    outp = np.empty((B, T, D), np.float32)
    for c in range(NCORE):
        outp[c // 4, (c % 4) * LT:(c % 4) * LT + LT, :] = res.results[c]["out"]
    return outp


# version 07
# speedup vs baseline: 1.3898x; 1.1767x over previous
import numpy as np
import concourse.bass as bass
import concourse.mybir as mybir
from concourse.bass_utils import run_bass_kernel_spmd
from contextlib import ExitStack

F32 = mybir.dt.float32
BF16 = mybir.dt.bfloat16
I32 = mybir.dt.int32
AF = mybir.ActivationFunctionType
ALU = mybir.AluOpType
AX = mybir.AxisListType

D = 2048
HID = 8192
NCORE = 8
ATANH2 = float(2.0 * np.arctanh(0.5))
RG8 = [list(range(NCORE))]
RG4 = [[0, 1, 2, 3], [4, 5, 6, 7]]
ENGS = ("sy", "sq", "v", "t", "g")
RSQ2 = float(128.0 ** -0.5)


class Seq:
    def __init__(self, nc, es):
        self.nc = nc
        self.es = es
        self.sems = {e: es.enter_context(nc.semaphore(f"sq_{e}")) for e in ENGS}
        self.ops = {e: [] for e in ENGS}
        self.count = {e: 0 for e in ENGS}
        self.ccount = {}

    def _chan_sem(self, key):
        if key not in self.ccount:
            self.sems[key] = self.es.enter_context(self.nc.semaphore(f"sq_{key}"))
            self.ccount[key] = 0
        return key

    def raw(self, eng, fn, deps=(), dma=False, chan=None, serial=True):
        deps = [d for d in deps if d is not None]
        if dma:
            key = self._chan_sem(f"{eng}.{chan or 'd'}")
            if serial and self.ccount[key] > 0:
                deps.append((key, self.ccount[key]))
            self.ccount[key] += 16
            self.ops[eng].append((fn, deps, key, 16))
            return (key, self.ccount[key])
        if self.count[eng] > 0:
            deps.append((eng, self.count[eng]))
        self.count[eng] += 1
        self.ops[eng].append((fn, deps, eng, 1))
        return (eng, self.count[eng])

    def cc(self, build, deps):
        key = self._chan_sem("g.cc")
        deps = list(deps) + ([(key, self.ccount[key])] if self.ccount[key] else [])
        self.ccount[key] += 1
        val = self.ccount[key]

        def f(e):
            build(e).then_inc(self.sems[key], 1)
            return e.nop()
        self.raw("g", f, [d for d in deps if d is not None])
        return (key, val)

    def barrier(self):
        marks = {e: self.count[e] for e in ENGS}
        cmarks = dict(self.ccount)
        out = {}
        for e in ENGS:
            deps = [(o, marks[o]) for o in ENGS if o != e and marks[o] > 0]
            deps += [(k, v) for k, v in cmarks.items() if v > 0]
            out[e] = self.raw(e, lambda en: en.nop(), deps=deps)
        return out

    def emit(self, block):
        sections = {"sy": block.sync, "sq": block.scalar, "v": block.vector,
                    "t": block.tensor, "g": block.gpsimd}
        for eng_name in ENGS:
            def make(eng_name=eng_name):
                def body(en):
                    last = {}
                    done = 0
                    for fn, waits, skey, inc in self.ops[eng_name]:
                        wmax = {}
                        for (oe, val) in waits:
                            if wmax.get(oe, -1) < val:
                                wmax[oe] = val
                        for (oe, val) in wmax.items():
                            if last.get(oe, 0) >= val:
                                continue
                            if oe == eng_name and val > done:
                                raise RuntimeError("forward same-engine dep")
                            en.wait_ge(self.sems[oe], val)
                            last[oe] = val
                        ins = fn(en)
                        if isinstance(ins, (list, tuple)):
                            ins = ins[-1]
                        ins.then_inc(self.sems[skey], inc)
                        if skey == eng_name:
                            done += inc
                return body
            sections[eng_name](make())


class Buf:
    def __init__(self):
        self.writer = None
        self.readers = []


ATTN_CFG = {"groups": None, "chunks": None, "vh": None, "rs": True, "proj": True}


def build_program(T, upto=9):
    LT = T // 4
    NT = LT // 128
    NG = T // 512
    assert T % 2048 == 0
    TBU = min(LT, 1024)
    NBU = LT // TBU
    NTU = TBU // 128

    nc = bass.Bass("TRN2", target_bir_lowering=False)

    x0 = nc.dram_tensor("x0", [LT, D], F32, kind="ExternalInput").ap()
    w1Ts = nc.dram_tensor("w1Ts", [2, 128, HID], F32, kind="ExternalInput").ap()
    w2Ts = nc.dram_tensor("w2Ts", [2, 128, HID], F32, kind="ExternalInput").ap()
    w3Ts = nc.dram_tensor("w3Ts", [8, 128, D], F32, kind="ExternalInput").ap()
    g1in = nc.dram_tensor("g1in", [1, D], F32, kind="ExternalInput").ap()
    g2in = nc.dram_tensor("g2in", [1, D], F32, kind="ExternalInput").ap()
    g3in = nc.dram_tensor("g3in", [1, HID], F32, kind="ExternalInput").ap()
    wqkvin = nc.dram_tensor("wqkvin", [D, 2048], F32, kind="ExternalInput").ap()
    wzin = nc.dram_tensor("wzin", [D, 1024], F32, kind="ExternalInput").ap()
    wbain = nc.dram_tensor("wbain", [128, 16, 16], F32, kind="ExternalInput").ap()
    convdin = nc.dram_tensor("convdin", [128, 16, 4, 128], F32, kind="ExternalInput").ap()
    woutin = nc.dram_tensor("woutin", [1024, D], F32, kind="ExternalInput").ap()
    alogin = nc.dram_tensor("alogin", [8, 1], F32, kind="ExternalInput").ap()
    dtbin = nc.dram_tensor("dtbin", [8, 1], F32, kind="ExternalInput").ap()
    normwin = nc.dram_tensor("normwin", [1, 128], F32, kind="ExternalInput").ap()
    identin = nc.dram_tensor("identin", [128, 128], F32, kind="ExternalInput").ap()
    masksin = nc.dram_tensor("masksin", [2, 128, 128], F32, kind="ExternalInput").ap()
    sel8in = nc.dram_tensor("sel8in", [8, 128], F32, kind="ExternalInput").ap()
    out = nc.dram_tensor("out", [LT, D], F32, kind="ExternalOutput").ap()

    agt_in = nc.dram_tensor("agt_in", [24, 128, 2048], BF16).ap()
    agt_out = nc.dram_tensor("agt_out", [NCORE, 24, 128, 2048], BF16).ap()
    ar_in = nc.dram_tensor("ar_in", [1, 4], F32).ap()
    ar_out = nc.dram_tensor("ar_out", [1, 4], F32).ap()
    ag1_in = nc.dram_tensor("ag1_in", [16, 128, LT], BF16).ap()
    ag1_out = nc.dram_tensor("ag1_out", [4, 16, 128, LT], BF16).ap()
    rs_in = nc.dram_tensor("rs_in", [T, D], BF16).ap()
    rs_out = nc.dram_tensor("rs_out", [LT, D], BF16).ap()
    zsp = nc.dram_tensor("zsp", [T // 128, 128, 1024], BF16).ap()
    betagsp = nc.dram_tensor("betagsp", [2, 8, T + 1], F32).ap()
    hgsp = nc.dram_tensor("hgsp", [NT, 128, HID], BF16).ap()
    x1sp = nc.dram_tensor("x1sp", [LT, D], F32).ap()
    x2sp = nc.dram_tensor("x2sp", [LT, D], F32).ap()

    es = ExitStack()
    seq = Seq(nc, es)

    def E(eng, fn, r=(), w=(), x=(), dma=False, chan=None, serial=True):
        deps = []
        for b in r:
            if b.writer is not None:
                deps.append(b.writer)
        for b in w:
            deps.extend(b.readers if b.readers else ([b.writer] if b.writer else []))
        deps.extend(x)
        h = seq.raw(eng, fn, deps, dma=dma, chan=chan, serial=serial)
        for b in r:
            b.readers.append(h)
        for b in w:
            b.writer = h
            b.readers = []
        return h

    ident = es.enter_context(nc.sbuf_tensor("ident", [128, 128], F32))
    identb = es.enter_context(nc.sbuf_tensor("identb", [128, 128], BF16))
    mnegst = es.enter_context(nc.sbuf_tensor("mnegst", [128, 128], F32))
    mtriu = es.enter_context(nc.sbuf_tensor("mtriu", [128, 128], F32))
    sel8 = es.enter_context(nc.sbuf_tensor("sel8", [8, 128], F32))
    onesrow = es.enter_context(nc.sbuf_tensor("onesrow", [1, 128], F32))
    onescol = es.enter_context(nc.sbuf_tensor("onescol", [128, 1], F32))
    betaP = es.enter_context(nc.sbuf_tensor("betaP", [128, 1], F32))
    betaN = es.enter_context(nc.sbuf_tensor("betaN", [128, 1], F32))
    probef = es.enter_context(nc.sbuf_tensor("probef", [128, 1], F32))
    probei = es.enter_context(nc.sbuf_tensor("probei", [128, 1], I32))
    normw_t = es.enter_context(nc.sbuf_tensor("normw_t", [128, 128], F32))
    scw = es.enter_context(nc.sbuf_tensor("scw", [128, 8], F32))
    cbuf = es.enter_context(nc.sbuf_tensor("cbuf", [128, 48], F32))
    _stkP = ExitStack()
    PSB = _stkP.enter_context(nc.psum_tensor("PSB", [128, 512], F32))
    B_PSB = Buf()
    B_cb = [Buf() for _ in range(48)]
    B_const = Buf()

    def cb(j):
        return cbuf[:, j:j + 1]

    E("sy", lambda e: e.dma_start(out=ident[:], in_=identin), w=[B_const], dma=True)
    E("sy", lambda e: e.dma_start(out=mnegst[:], in_=masksin[0]), w=[B_const], dma=True)
    E("sy", lambda e: e.dma_start(out=mtriu[:], in_=masksin[1]), w=[B_const], dma=True)
    E("sy", lambda e: e.dma_start(out=sel8[:], in_=sel8in), w=[B_const], dma=True)
    E("sy", lambda e: e.dma_start(out=normw_t[0:1, :], in_=normwin), w=[B_const], dma=True)
    E("v", lambda e: e.memset(onesrow[:], 1.0), w=[B_const])
    E("v", lambda e: e.memset(onescol[:], 1.0), w=[B_const])
    E("v", lambda e: e.tensor_copy(identb[:], ident[:]), r=[B_const], w=[B_const])
    E("v", lambda e: e.memset(probef[:], 1.5), w=[B_const])
    E("v", lambda e: e.tensor_copy(probei[:], probef[:]), r=[B_const], w=[B_const])
    E("v", lambda e: e.tensor_scalar(betaP[:], probei[:], -2.0, -256.5, ALU.add, ALU.mult),
      r=[B_const], w=[B_const])
    E("v", lambda e: e.tensor_scalar(betaN[:], probei[:], -2.0, 256.0, ALU.add, ALU.mult),
      r=[B_const], w=[B_const])
    E("t", lambda e: e.matmul(PSB[:, 0:128], onesrow[:], normw_t[0:1, :], start=True, stop=True),
      r=[B_const], w=[B_PSB])
    cdep = E("v", lambda e: e.tensor_copy(normw_t[:], PSB[:, 0:128]), r=[B_PSB], w=[B_const])

    wtiles = []
    for wi, src in ((0, w1Ts), (1, w2Ts)):
        for ch in range(2):
            for c4 in range(4):
                wtiles.append((src[ch, :, c4 * 2048:(c4 + 1) * 2048], wi * 8 + ch * 4 + c4, wi))
    for rc in range(8):
        wtiles.append((w3Ts[rc, :, :], 16 + rc, 2))

    B_scw = Buf()
    _stk1 = ExitStack()
    wld = _stk1.enter_context(nc.sbuf_tensor("wld", [128, 2, 2048], F32))
    wq1 = _stk1.enter_context(nc.sbuf_tensor("wq1", [128, 2048], F32))
    wq2 = _stk1.enter_context(nc.sbuf_tensor("wq2", [128, 2048], F32))
    wqi = _stk1.enter_context(nc.sbuf_tensor("wqi", [128, 2048], I32))
    wqb = _stk1.enter_context(nc.sbuf_tensor("wqb", [128, 2048], BF16))
    wdmp = _stk1.enter_context(nc.sbuf_tensor("wdmp", [128, 2048], F32))
    acc3 = _stk1.enter_context(nc.sbuf_tensor("acc3", [128, 4], F32))
    arb = _stk1.enter_context(nc.sbuf_tensor("arb", [1, 4], F32))
    if True:
        B_wld = [Buf(), Buf()]
        B_acc = Buf()
        B_tmp = Buf()
        B_arb = Buf()
        E("v", lambda e: e.memset(acc3[:], 0.0), w=[B_acc])
        for i, (src, entry, m) in enumerate(wtiles):
            b = wld[:, i % 2, :]
            E("sy", lambda e, s=src, b=b: e.dma_start(out=b, in_=s), w=[B_wld[i % 2]], dma=True)
            E("sq", lambda e, b=b, m=m: e.activation(wdmp[:], b, AF.Abs, accum_out=cb(8 + m)),
              r=[B_wld[i % 2]], w=[B_tmp, B_cb[8 + m]])
            E("v", lambda e, m=m: e.tensor_tensor(acc3[:, m:m + 1], acc3[:, m:m + 1], cb(8 + m), ALU.add),
              r=[B_cb[8 + m], B_acc], w=[B_acc])
        E("t", lambda e: e.matmul(PSB[:4, 0:1], acc3[:], onescol[:], start=True, stop=True),
          r=[B_acc, B_const], w=[B_PSB])
        E("v", lambda e: e.tensor_copy(acc3[:4, 0:1], PSB[:4, 0:1]), r=[B_PSB], w=[B_acc])
        E("t", lambda e: e.transpose(PSB[0:1, 128:132], acc3[:4, 0:1], ident[0:4, 0:4]),
          r=[B_acc, B_const], w=[B_PSB])
        E("v", lambda e: e.tensor_copy(arb[:], PSB[0:1, 128:132]), r=[B_PSB], w=[B_arb])
        h0 = E("sy", lambda e: e.dma_start(out=ar_in, in_=arb[:]), r=[B_arb], dma=True)
        h = seq.cc(lambda e: e.collective_compute("AllReduce", ALU.add, replica_groups=RG8,
                                                  ins=[ar_in.opt()], outs=[ar_out.opt()]), [h0])
        E("sy", lambda e: e.dma_start(out=arb[:], in_=ar_out), w=[B_arb], x=[h], dma=True)
        E("t", lambda e: e.matmul(PSB[:, 128:132], onesrow[:], arb[:], start=True, stop=True),
          r=[B_arb, B_const], w=[B_PSB])
        E("v", lambda e: e.tensor_scalar(cbuf[:, 0:4], PSB[:, 128:132], 1.0 / (2048.0 * 8192.0),
                                         None, ALU.mult), r=[B_PSB], w=[B_cb[0]])
        E("v", lambda e: e.tensor_scalar(cbuf[:, 4:8], cbuf[:, 0:4], 1e-8, ATANH2, ALU.add, ALU.mult),
          r=[B_cb[0]], w=[B_cb[4]])
        E("v", lambda e: e.reciprocal(scw[:, 0:4], cbuf[:, 4:8]), r=[B_cb[4]], w=[B_scw])
        E("v", lambda e: e.tensor_scalar(cbuf[:, 8:12], cbuf[:, 0:4], -1.0, 1.0, ALU.mult, ALU.add),
          r=[B_cb[0]], w=[B_cb[8]])
        E("v", lambda e: e.reciprocal(cbuf[:, 12:16], cbuf[:, 8:12]), r=[B_cb[8]], w=[B_cb[12]])
        E("v", lambda e: e.tensor_scalar(cbuf[:, 16:20], cbuf[:, 0:4], 1.0, None, ALU.add),
          r=[B_cb[0]], w=[B_cb[16]])
        E("v", lambda e: e.tensor_tensor(cbuf[:, 20:24], cbuf[:, 16:20], cbuf[:, 12:16], ALU.mult),
          r=[B_cb[16], B_cb[12]], w=[B_cb[20]])
        E("sq", lambda e: e.activation(cbuf[:, 24:28], cbuf[:, 20:24], AF.Ln), r=[B_cb[20]], w=[B_cb[24]])
        E("v", lambda e: e.tensor_scalar(scw[:, 4:8], cbuf[:, 24:28], 0.5, None, ALU.mult),
          r=[B_cb[24], B_scw], w=[B_scw])

        prev_store = None
        for i, (src, entry, m) in enumerate(wtiles):
            b = wld[:, i % 2, :]
            E("sy", lambda e, s=src, b=b: e.dma_start(out=b, in_=s), w=[B_wld[i % 2]], dma=True)
            E("sq", lambda e, b=b, m=m: e.activation(wq1[:], b, AF.Copy, scale=scw[:, m:m + 1]),
              r=[B_wld[i % 2], B_scw], w=[B_tmp])
            E("v", lambda e: e.tensor_scalar(wq2[:], wq1[:], 1.4, -1.4, ALU.min, ALU.max),
              r=[B_tmp], w=[B_tmp])
            E("sq", lambda e: e.activation(wq1[:], wq2[:], AF.Identity, bias=betaP[:, 0:1]),
              r=[B_tmp, B_const], w=[B_tmp])
            E("v", lambda e: e.tensor_copy(wqi[:], wq1[:]), r=[B_tmp], w=[B_tmp])
            E("v", lambda e: e.tensor_scalar(wqb[:], wqi[:], betaN[:, 0:1], None, ALU.add),
              r=[B_tmp, B_const], w=[B_tmp])
            prev_store = E("sy", lambda e, entry=entry: e.dma_start(out=agt_in[entry], in_=wqb[:]),
                           r=[B_tmp], x=[prev_store], dma=True)
        g_agt = seq.cc(lambda e: e.collective_compute("AllGather", ALU.bypass, replica_groups=RG8,
                                                      ins=[agt_in.opt()], outs=[agt_out.opt()]), [prev_store])
        _ = g_agt
    _stk1.close()
    _stkP.close()
    seq.barrier()

    def w_up_tile(wi, ic, oc):
        return agt_out[ic // 2, wi * 8 + (ic % 2) * 4 + oc // 4, :, (oc % 4) * 512:(oc % 4) * 512 + 512]

    def w_dn_tile(icd, od):
        return agt_out[icd // 8, 16 + icd % 8, :, od * 512:od * 512 + 512]

    def ln_chain(src_ap, r_bufs, dst_ap, w_bufs, dmp_ap, dmp_buf, base, eps, x=()):
        E("v", lambda e: e.tensor_reduce(cb(base), src_ap, AX.X, ALU.add),
          r=r_bufs, w=[B_cb[base]], x=list(x))
        E("v", lambda e: e.tensor_scalar(cb(base + 1), cb(base), -1.0 / D, None, ALU.mult),
          r=[B_cb[base]], w=[B_cb[base + 1]])
        E("sq", lambda e: e.activation(dmp_ap, src_ap, AF.Square, bias=cb(base + 1),
                                       accum_out=cb(base + 2)),
          r=r_bufs + [B_cb[base + 1]], w=[dmp_buf, B_cb[base + 2]])
        E("v", lambda e: e.tensor_scalar(cb(base + 3), cb(base + 2), 1.0 / D, eps, ALU.mult, ALU.add),
          r=[B_cb[base + 2]], w=[B_cb[base + 3]])
        E("sq", lambda e: e.activation(cb(base + 4), cb(base + 3), AF.Ln),
          r=[B_cb[base + 3]], w=[B_cb[base + 4]])
        E("sq", lambda e: e.activation(cb(base + 5), cb(base + 4), AF.Exp, scale=-0.5),
          r=[B_cb[base + 4]], w=[B_cb[base + 5]])
        return E("v", lambda e: e.tensor_scalar(dst_ap, src_ap, cb(base + 1), cb(base + 5),
                                                ALU.add, ALU.mult),
                 r=r_bufs + [B_cb[base + 1], B_cb[base + 5]], w=w_bufs)

    last_sh = [None]
    last_so = [None]
    c1s = es.enter_context(nc.sbuf_tensor("c1s", [128, NT], F32))
    c2s = es.enter_context(nc.sbuf_tensor("c2s", [128, NT], F32))
    c3s = es.enter_context(nc.sbuf_tensor("c3s", [128, NT], F32))
    ssqr = es.enter_context(nc.sbuf_tensor("ssqr", [128, NT], F32))
    amaxr = es.enter_context(nc.sbuf_tensor("amaxr", [128, NT], F32))
    B_c1s = [Buf() for _ in range(NT)]
    B_c2s = [Buf() for _ in range(NT)]
    B_c3s = [Buf() for _ in range(NT)]
    B_ssqr = [Buf() for _ in range(NT)]
    B_amaxr = [Buf() for _ in range(NT)]

    def emit_ffn(call):
        x_src = x0 if call == 1 else x2sp
        ag1_last = [None]
        out_last = [None]
        hsp_done = {}
        _stk2 = ExitStack()
        gt3 = _stk2.enter_context(nc.sbuf_tensor(f"f{call}_gt3", [128, HID], BF16))
        g12t = _stk2.enter_context(nc.sbuf_tensor(f"f{call}_g12t", [128, 2, D], BF16))
        xin = _stk2.enter_context(nc.sbuf_tensor(f"f{call}_xin", [128, 2, D], F32))
        xln = _stk2.enter_context(nc.sbuf_tensor(f"f{call}_xln", [128, D], F32))
        ytmp = _stk2.enter_context(nc.sbuf_tensor(f"f{call}_ytmp", [128, D], F32))
        kint = _stk2.enter_context(nc.sbuf_tensor(f"f{call}_kint", [128, D], I32))
        kbf = _stk2.enter_context(nc.sbuf_tensor(f"f{call}_kbf", [128, 2, D], BF16))
        fdmp = _stk2.enter_context(nc.sbuf_tensor(f"f{call}_fdmp", [128, D], F32))
        k1T = _stk2.enter_context(nc.sbuf_tensor(f"f{call}_k1T", [128, 16, TBU], BF16))
        k2T = _stk2.enter_context(nc.sbuf_tensor(f"f{call}_k2T", [128, 16, TBU], BF16))
        wu0 = _stk2.enter_context(nc.sbuf_tensor(f"f{call}_wu0", [128, 8, 2, 512], BF16))
        wu1 = _stk2.enter_context(nc.sbuf_tensor(f"f{call}_wu1", [128, 8, 2, 512], BF16))
        wut = (wu0, wu1)
        silu1 = _stk2.enter_context(nc.sbuf_tensor(f"f{call}_silu1", [128, 512], F32))
        htile = _stk2.enter_context(nc.sbuf_tensor(f"f{call}_htile", [128, 512], BF16))
        hgt = _stk2.enter_context(nc.sbuf_tensor(f"f{call}_hgt", [128, 2, 512], BF16))
        ps1a = _stk2.enter_context(nc.psum_tensor(f"f{call}_ps1a", [128, 512], F32))
        ps1b = _stk2.enter_context(nc.psum_tensor(f"f{call}_ps1b", [128, 512], F32))
        ps2a = _stk2.enter_context(nc.psum_tensor(f"f{call}_ps2a", [128, 512], F32))
        ps2b = _stk2.enter_context(nc.psum_tensor(f"f{call}_ps2b", [128, 512], F32))
        psg = _stk2.enter_context(nc.psum_tensor(f"f{call}_psg", [128, 512], F32))
        if True:

            B_g = Buf()
            B_gs = Buf()
            B_psg = Buf()
            with nc.sbuf_tensor(f"f{call}_gstg", [1, 512], F32) as gstg:
                jobs = ([(g3in, gt3[:, j * 512:(j + 1) * 512], j) for j in range(16)] +
                        [(g1in, g12t[:, 0, j * 512:(j + 1) * 512], j) for j in range(4)] +
                        [(g2in, g12t[:, 1, j * 512:(j + 1) * 512], j) for j in range(4)])
                for srcT, dst, j in jobs:
                    E("sy", lambda e, srcT=srcT, j=j: e.dma_start(
                        out=gstg[:], in_=srcT[0:1, j * 512:(j + 1) * 512]), w=[B_gs], dma=True)
                    E("t", lambda e: e.matmul(psg[:, 0:512], onesrow[:], gstg[:],
                                              start=True, stop=True),
                      r=[B_const, B_gs], w=[B_psg])
                    E("v", lambda e, dst=dst: e.tensor_copy(dst, psg[:, 0:512]),
                      r=[B_psg], w=[B_g])

            B_xin = [Buf(), Buf()]
            B_xln = Buf()
            B_y = Buf()
            B_ki = Buf()
            B_kb = [Buf(), Buf()]
            B_kT = [Buf(), Buf()]
            last_tk = {0: None, 1: None}
            B_wu = [Buf(), Buf()]
            last_wu = {0: None, 1: None}
            B_psy = [[Buf(), Buf()], [Buf(), Buf()]]
            B_si = Buf()
            B_ht = Buf()
            B_hg = [Buf(), Buf()]
            B_fd = Buf()
            E("v", lambda e: e.memset(ssqr[:], 0.0), w=B_ssqr)
            E("g", lambda e: e.memset(amaxr[:], 0.0), w=B_amaxr)

            for tb in range(NBU):
                for tl in range(NTU):
                    tt = tb * NTU + tl
                    xb = xin[:, tt % 2, :]
                    E("sy", lambda e, tt=tt, xb=xb: e.dma_start(
                        out=xb, in_=x_src[tt * 128:(tt + 1) * 128, :]), w=[B_xin[tt % 2]], dma=True,
                      chan="x" + str(tt % 2))
                    ln_chain(xb, [B_xin[tt % 2]], xln[:], [B_xln], fdmp[:], B_fd, 0, 1e-5)
                    E("sq", lambda e: e.activation(fdmp[:], xln[:], AF.Square, accum_out=cb(6)),
                      r=[B_xln], w=[B_fd, B_cb[6]])
                    E("v", lambda e: e.tensor_scalar(cb(7), cb(6), 1.0 / D, 1e-8, ALU.mult, ALU.add),
                      r=[B_cb[6]], w=[B_cb[7]])
                    E("sq", lambda e: e.activation(cb(8), cb(7), AF.Ln), r=[B_cb[7]], w=[B_cb[8]])
                    E("sq", lambda e: e.activation(cb(9), cb(8), AF.Exp, scale=-0.5),
                      r=[B_cb[8]], w=[B_cb[9]])
                    for gi in range(2):
                        kb = kbf[:, gi, :]
                        kT = (k1T, k2T)[gi]
                        cs = (c1s, c2s)[gi]
                        Bcs = (B_c1s, B_c2s)[gi][tt]
                        E("v", lambda e, gi=gi: e.scalar_tensor_tensor(
                            ytmp[:], xln[:], cb(9), g12t[:, gi, :], ALU.mult, ALU.mult),
                          r=[B_xln, B_cb[9], B_g], w=[B_y])
                        E("v", lambda e: e.tensor_reduce(cb(10), ytmp[:], AX.X, ALU.max,
                                                         apply_absolute_value=True),
                          r=[B_y], w=[B_cb[10]])
                        E("v", lambda e: e.tensor_scalar(cb(11), cb(10), 1e-4, None, ALU.max),
                          r=[B_cb[10]], w=[B_cb[11]])
                        E("v", lambda e: e.reciprocal(cb(12), cb(11)), r=[B_cb[11]], w=[B_cb[12]])
                        E("v", lambda e: e.tensor_scalar(cb(13), cb(12), 127.0, None, ALU.mult),
                          r=[B_cb[12]], w=[B_cb[13]])
                        E("v", lambda e, cs=cs, tt=tt, gi=gi: e.scalar_tensor_tensor(
                            cs[:, tt:tt + 1], cb(11), 1.0 / 127.0, scw[:, 4 + gi:5 + gi],
                            ALU.mult, ALU.mult), r=[B_cb[11], B_scw], w=[Bcs])
                        E("sq", lambda e: e.activation(ytmp[:], ytmp[:], AF.Identity, scale=cb(13),
                                                       bias=betaP[:, 0:1]),
                          r=[B_y, B_cb[13], B_const], w=[B_y])
                        E("v", lambda e: e.tensor_copy(kint[:], ytmp[:]), r=[B_y], w=[B_ki])
                        E("v", lambda e, kb=kb: e.tensor_scalar(kb, kint[:], betaN[:, 0:1], None, ALU.add),
                          r=[B_ki, B_const], w=[B_kb[gi]])
                        for ic in range(16):
                            h_tk = E("sy", lambda e, kb=kb, kT=kT, ic=ic, tl=tl: e.dma_start(
                                out=kT[:, ic, tl * 128:(tl + 1) * 128],
                                in_=kb[:, ic * 128:(ic + 1) * 128], transpose=True),
                              r=[B_kb[gi]] if ic == 15 else [], w=[B_kT[gi]],
                              x=([last_tk[gi]] if ic == 0 else []) +
                                ([B_kb[gi].writer] if ic < 15 else []),
                              dma=True, chan="tk" + str(gi), serial=False)
                        last_tk[gi] = h_tk
                for oc in range(16):
                    for wi in range(2):
                        for ch in range(2):
                            h_wu = E("sq", lambda e, wi=wi, oc=oc, ch=ch: e.dma_start(
                                out=wut[wi][:, :, ch, :],
                                in_=agt_out[:, wi * 8 + ch * 4 + oc // 4, :,
                                            (oc % 4) * 512:(oc % 4) * 512 + 512]
                                    .rearrange("r p n -> p r n")),
                              w=[B_wu[wi]],
                              x=[g_agt] + ([last_wu[wi]] if ch == 0 else []),
                              dma=True, chan="wu" + str(wi), serial=False)
                            last_wu[wi] = h_wu
                    for tl in range(NTU):
                        tt = tb * NTU + tl
                        psy1 = (ps1a, ps1b)[tl % 2]
                        psy2 = (ps2a, ps2b)[tl % 2]

                        def mk_mm(tl=tl, psy1=psy1, psy2=psy2):
                            def f(e):
                                ins = []
                                for ic in range(16):
                                    ins.append(e.matmul(psy1[:], k1T[:, ic, tl * 128:(tl + 1) * 128],
                                                        wu0[:, ic // 2, ic % 2, :],
                                                        start=(ic == 0), stop=(ic == 15)))
                                for ic in range(16):
                                    ins.append(e.matmul(psy2[:], k2T[:, ic, tl * 128:(tl + 1) * 128],
                                                        wu1[:, ic // 2, ic % 2, :],
                                                        start=(ic == 0), stop=(ic == 15)))
                                return ins
                            return f
                        E("t", mk_mm(), r=[B_kT[0], B_kT[1], B_wu[0], B_wu[1]],
                          w=[B_psy[0][tl % 2], B_psy[1][tl % 2]])
                        E("sq", lambda e, psy1=psy1, tt=tt: e.activation(
                            silu1[:], psy1[:], AF.Silu, scale=c1s[:, tt:tt + 1]),
                          r=[B_psy[0][tl % 2], B_c1s[tt]], w=[B_si])
                        E("v", lambda e, psy2=psy2, tt=tt: e.scalar_tensor_tensor(
                            htile[:], psy2[:], c2s[:, tt:tt + 1], silu1[:], ALU.mult, ALU.mult),
                          r=[B_psy[1][tl % 2], B_c2s[tt], B_si], w=[B_ht])
                        E("sq", lambda e: e.activation(fdmp[:, 0:512], htile[:], AF.Square,
                                                       accum_out=cb(14)),
                          r=[B_ht], w=[B_fd, B_cb[14]])
                        E("v", lambda e, tt=tt: e.tensor_tensor(
                            ssqr[:, tt:tt + 1], ssqr[:, tt:tt + 1], cb(14), ALU.add),
                          r=[B_cb[14], B_ssqr[tt]], w=[B_ssqr[tt]])
                        hb = hgt[:, tl % 2, :]
                        E("v", lambda e, oc=oc, hb=hb: e.tensor_tensor(
                            hb, htile[:], gt3[:, oc * 512:(oc + 1) * 512], ALU.mult),
                          r=[B_ht, B_g], w=[B_hg[tl % 2]])
                        E("v", lambda e, hb=hb: e.tensor_reduce(cb(15), hb, AX.X, ALU.max,
                                                                apply_absolute_value=True),
                          r=[B_hg[tl % 2]], w=[B_cb[15]])
                        E("v", lambda e, tt=tt: e.tensor_tensor(
                            amaxr[:, tt:tt + 1], amaxr[:, tt:tt + 1], cb(15), ALU.max),
                          r=[B_cb[15], B_amaxr[tt]], w=[B_amaxr[tt]])
                        hsp_done[(oc, tt)] = E("sy", lambda e, oc=oc, tt=tt, hb=hb: e.dma_start(
                            out=hgsp[tt, :, oc * 512:(oc + 1) * 512], in_=hb),
                          r=[B_hg[tl % 2]],
                          x=[last_sh[0]] if (oc == 0 and tt == tb * NTU) else [],
                          dma=True, chan="sh")
                        last_sh[0] = hsp_done[(oc, tt)]

        _stk2.close()
        seq.barrier()
        _stk3 = ExitStack()
        khT = _stk3.enter_context(nc.sbuf_tensor(f"f{call}_khT", [128, 64, 512], BF16))
        wdt = _stk3.enter_context(nc.sbuf_tensor(f"f{call}_wdt", [128, 8, 8, 512], BF16))
        hgl = _stk3.enter_context(nc.sbuf_tensor(f"f{call}_hgl", [128, 2048], BF16))
        uh = _stk3.enter_context(nc.sbuf_tensor(f"f{call}_uh", [128, 2048], F32))
        kih = _stk3.enter_context(nc.sbuf_tensor(f"f{call}_kih", [128, 2048], I32))
        khb = _stk3.enter_context(nc.sbuf_tensor(f"f{call}_khb", [128, 2048], BF16))
        oacc = _stk3.enter_context(nc.sbuf_tensor(f"f{call}_oacc", [128, 512], F32))
        xres = _stk3.enter_context(nc.sbuf_tensor(f"f{call}_xres", [128, D], F32))
        x1t = _stk3.enter_context(nc.sbuf_tensor(f"f{call}_x1t", [128, D], F32))
        cdmp = _stk3.enter_context(nc.sbuf_tensor(f"f{call}_cdmp", [128, D], F32))
        x2nb = _stk3.enter_context(nc.sbuf_tensor(f"f{call}_x2nb", [128, D], BF16))
        x2nT = _stk3.enter_context(nc.sbuf_tensor(f"f{call}_x2nT", [128, 16, 128], BF16))
        psoa = _stk3.enter_context(nc.psum_tensor(f"f{call}_psoa", [128, 512], F32))
        psob = _stk3.enter_context(nc.psum_tensor(f"f{call}_psob", [128, 512], F32))
        if True:
            B_khT = Buf()
            B_wd = Buf()
            last_wd = [None]
            last_tkh = [None]
            last_tx = [None]
            B_hgl = Buf()
            B_u = Buf()
            B_k2 = Buf()
            B_k3 = Buf()
            B_oa = Buf()
            B_xr = Buf()
            B_pso = [Buf(), Buf()]
            B_x1t = Buf()
            B_x2n = Buf()
            B_x2T = Buf()
            B_cd = Buf()
            for bd in range(LT // 512):
                for tl in range(4):
                    tt = bd * 4 + tl
                    E("v", lambda e, tt=tt: e.tensor_scalar(cb(16), ssqr[:, tt:tt + 1], 1.0 / HID,
                                                            1e-8, ALU.mult, ALU.add),
                      r=[B_ssqr[tt]], w=[B_cb[16]])
                    E("sq", lambda e: e.activation(cb(17), cb(16), AF.Ln), r=[B_cb[16]], w=[B_cb[17]])
                    E("sq", lambda e: e.activation(cb(18), cb(17), AF.Exp, scale=-0.5),
                      r=[B_cb[17]], w=[B_cb[18]])
                    E("v", lambda e, tt=tt: e.tensor_tensor(cb(19), amaxr[:, tt:tt + 1], cb(18), ALU.mult),
                      r=[B_amaxr[tt], B_cb[18]], w=[B_cb[19]])
                    E("v", lambda e: e.tensor_scalar(cb(20), cb(19), 1e-4, None, ALU.max),
                      r=[B_cb[19]], w=[B_cb[20]])
                    E("v", lambda e: e.reciprocal(cb(21), cb(20)), r=[B_cb[20]], w=[B_cb[21]])
                    E("v", lambda e: e.tensor_scalar(cb(22), cb(21), 127.0, None, ALU.mult),
                      r=[B_cb[21]], w=[B_cb[22]])
                    E("v", lambda e: e.tensor_tensor(cb(23), cb(22), cb(18), ALU.mult),
                      r=[B_cb[22], B_cb[18]], w=[B_cb[23]])
                    E("v", lambda e, tt=tt: e.scalar_tensor_tensor(
                        c3s[:, tt:tt + 1], cb(20), 0.5 / 127.0, scw[:, 6:7], ALU.mult, ALU.mult),
                      r=[B_cb[20], B_scw], w=[B_c3s[tt]])
                    for q4 in range(4):
                        E("sy", lambda e, tt=tt, q4=q4: e.dma_start(
                            out=hgl[:], in_=hgsp[tt, :, q4 * 2048:(q4 + 1) * 2048]),
                          w=[B_hgl], x=[hsp_done[(15, NT - 1)]], dma=True, chan="hl")
                        E("sq", lambda e: e.activation(uh[:], hgl[:], AF.Identity, scale=cb(23),
                                                       bias=betaP[:, 0:1]),
                          r=[B_hgl, B_cb[23], B_const], w=[B_u])
                        E("v", lambda e: e.tensor_copy(kih[:], uh[:]), r=[B_u], w=[B_k2])
                        E("v", lambda e: e.tensor_scalar(khb[:], kih[:], betaN[:, 0:1], None, ALU.add),
                          r=[B_k2, B_const], w=[B_k3])
                        for ic in range(16):
                            icd = q4 * 16 + ic
                            h_kh = E("sy", lambda e, icd=icd, ic=ic, tl=tl: e.dma_start(
                                out=khT[:, icd, tl * 128:(tl + 1) * 128],
                                in_=khb[:, ic * 128:(ic + 1) * 128], transpose=True),
                              r=[B_k3] if ic == 15 else [], w=[B_khT],
                              x=([last_tkh[0]] if (tl == 0 and q4 == 0 and ic == 0) else []) +
                                ([B_k3.writer] if ic < 15 else []),
                              dma=True, chan="tkh", serial=False)

                last_tkh[0] = B_khT.writer
                ostores = {}
                for od in range(4):
                    wls = []
                    for rc in range(8):
                        h_wd = E("sq", lambda e, od=od, rc=rc: e.dma_start(
                            out=wdt[:, :, rc, :],
                            in_=agt_out[:, 16 + rc, :, od * 512:od * 512 + 512]
                                .rearrange("r p n -> p r n")),
                          w=[B_wd], x=[g_agt] + ([last_wd[0]] if rc == 0 else []),
                          dma=True, chan="wd", serial=False)
                        wls.append(h_wd)
                    last_wd[0] = wls[-1]
                    for tl in range(4):
                        tt = bd * 4 + tl
                        pso = (psoa, psob)[tl % 2]

                        def mk_dn(tl=tl, pso=pso):
                            def f(e):
                                ins = []
                                for icd in range(64):
                                    ins.append(e.matmul(pso[:], khT[:, icd, tl * 128:(tl + 1) * 128],
                                                        wdt[:, icd // 8, icd % 8, :],
                                                        start=(icd == 0), stop=(icd == 63)))
                                return ins
                            return f
                        E("t", mk_dn(), r=[B_khT, B_wd], w=[B_pso[tl % 2]], x=wls)
                        E("sy", lambda e, tt=tt, od=od: e.dma_start(
                            out=xres[:, od * 512:(od + 1) * 512],
                            in_=x_src[tt * 128:(tt + 1) * 128, od * 512:(od + 1) * 512]),
                          w=[B_xr], dma=True, chan="xr")
                        E("v", lambda e, tt=tt, od=od, pso=pso: e.scalar_tensor_tensor(
                            oacc[:], pso[:], c3s[:, tt:tt + 1], xres[:, od * 512:(od + 1) * 512],
                            ALU.mult, ALU.add), r=[B_pso[tl % 2], B_c3s[tt], B_xr], w=[B_oa])
                        dst = x1sp if call == 1 else out
                        st = E("sy", lambda e, tt=tt, od=od, dst=dst: e.dma_start(
                            out=dst[tt * 128:(tt + 1) * 128, od * 512:(od + 1) * 512], in_=oacc[:]),
                          r=[B_oa], x=[last_so[0]] if (od == 0 and tl == 0) else [],
                          dma=True, chan="so", serial=False)
                        last_so[0] = st
                        ostores[(od, tl)] = st
                        out_last[0] = st
                if call == 1:
                    for tl in range(4):
                        tt = bd * 4 + tl
                        E("sy", lambda e, tt=tt: e.dma_start(
                            out=x1t[:], in_=x1sp[tt * 128:(tt + 1) * 128, :]),
                          w=[B_x1t], x=[ostores[(3, 3)]], dma=True, chan="x1")
                        ln_chain(x1t[:], [B_x1t], x2nb[:], [B_x2n], cdmp[:], B_cd, 24, 1e-5)
                        for ic in range(16):
                            h_tx = E("sy", lambda e, ic=ic: e.dma_start(
                                out=x2nT[:, ic, :], in_=x2nb[:, ic * 128:(ic + 1) * 128], transpose=True),
                              r=[B_x2n] if ic == 15 else [], w=[B_x2T],
                              x=([last_tx[0]] if ic == 0 else []) +
                                ([B_x2n.writer] if ic < 15 else []),
                              dma=True, chan="tx", serial=False)
                        last_tx[0] = h_tx
                        ag1_last[0] = E("sy", lambda e, tt=tt: e.dma_start(
                            out=ag1_in[:, :, tt * 128:(tt + 1) * 128].rearrange("a p b -> p a b"),
                            in_=x2nT[:]), r=[B_x2T], dma=True, chan="sa", serial=False)
        _stk3.close()
        seq.barrier()
        return ag1_last[0] if call == 1 else out_last[0]

    def emit_zsweep():
        zstores = {}
        _stk5 = ExitStack()
        wzsb = _stk5.enter_context(nc.sbuf_tensor("wzsb", [128, 16, 1024], BF16))
        wbasb = _stk5.enter_context(nc.sbuf_tensor("wbasb", [128, 16, 16], BF16))
        batile = _stk5.enter_context(nc.sbuf_tensor("batile", [128, 16], F32))
        zt2 = _stk5.enter_context(nc.sbuf_tensor("zt2", [128, 1024], BF16))
        xTz = _stk5.enter_context(nc.sbuf_tensor("xTz", [128, 16, 128], BF16))
        bafm = _stk5.enter_context(nc.sbuf_tensor("bafm", [8, 2, T], F32))
        bg2 = _stk5.enter_context(nc.sbuf_tensor("bg2", [8, 2, T], F32))
        zcol = _stk5.enter_context(nc.sbuf_tensor("zcol", [8, 1], F32))
        psZ1 = _stk5.enter_context(nc.psum_tensor("psZ1", [128, 512], F32))
        psZ2 = _stk5.enter_context(nc.psum_tensor("psZ2", [128, 512], F32))
        psZ3 = _stk5.enter_context(nc.psum_tensor("psZ3", [128, 512], F32))
        B_zt2 = Buf(); B_bat = Buf(); B_xTz = Buf(); B_ba = Buf(); B_bg = Buf(); B_zc = Buf()
        B_pz1 = Buf(); B_pz2 = Buf(); B_pz3 = Buf()
        wz_l = []
        for ic in range(16):
            wz_l.append(E("g", lambda e, ic=ic: e.dma_start(
                out=wzsb[:, ic, :], in_=wzin[ic * 128:(ic + 1) * 128, :]), dma=True))
        wba_l = E("g", lambda e: e.dma_start(
            out=wbasb[:], in_=wbain), dma=True)
        for tt2 in range(T // 128):
            E("sy", lambda e, tt2=tt2: e.dma_start(
                out=xTz[:],
                in_=ag1_out[(tt2 * 128) // LT, :, :,
                            (tt2 * 128) % LT:(tt2 * 128) % LT + 128].rearrange("a p b -> p a b")),
              w=[B_xTz], x=[g_ag1], dma=True, chan="zx")

            def mk_zh(half):
                def f(e):
                    ins = []
                    for ic in range(16):
                        ins.append(e.matmul((psZ1 if half == 0 else psZ2)[:],
                                            xTz[:, ic, :],
                                            wzsb[:, ic, half * 512:half * 512 + 512],
                                            start=(ic == 0), stop=(ic == 15)))
                    return ins
                return f
            E("t", mk_zh(0), r=[B_xTz], w=[B_pz1], x=wz_l)
            E("t", mk_zh(1), r=[B_xTz], w=[B_pz2], x=wz_l)

            def mk_ba(e):
                ins = []
                for ic in range(16):
                    ins.append(e.matmul(psZ3[:, 0:16], xTz[:, ic, :],
                                        wbasb[:, ic, :], start=(ic == 0), stop=(ic == 15)))
                return ins
            E("t", mk_ba, r=[B_xTz], w=[B_pz3], x=[wba_l])
            E("sq", lambda e: e.activation(zt2[:, 0:512], psZ1[:], AF.Silu), r=[B_pz1], w=[B_zt2])
            E("sq", lambda e: e.activation(zt2[:, 512:1024], psZ2[:], AF.Silu), r=[B_pz2], w=[B_zt2])
            zstores[tt2] = E("sy", lambda e, tt2=tt2: e.dma_start(out=zsp[tt2], in_=zt2[:]),
                             r=[B_zt2], dma=True, chan="sz", serial=False)
            E("v", lambda e: e.tensor_copy(batile[:], psZ3[:, 0:16]), r=[B_pz3], w=[B_bat])
            for half in range(2):
                E("t", lambda e, half=half: e.transpose(psZ3[0:8, 128 + half * 128:256 + half * 128],
                                                        batile[:, half * 8:half * 8 + 8], ident[:]),
                  r=[B_bat, B_const], w=[B_pz3])
                E("v", lambda e, tt2=tt2, half=half: e.tensor_copy(
                    bafm[:, half, tt2 * 128:(tt2 + 1) * 128],
                    psZ3[0:8, 128 + half * 128:256 + half * 128]),
                  r=[B_pz3], w=[B_ba])
        l_dtb = E("sy", lambda e: e.dma_start(out=cbuf[0:8, 28:29], in_=dtbin), dma=True)
        l_alog = E("sy", lambda e: e.dma_start(out=cbuf[0:8, 29:30], in_=alogin), dma=True)
        E("sq", lambda e: e.activation(bg2[:, 0, :], bafm[:, 0, :], AF.Sigmoid), r=[B_ba], w=[B_bg])
        E("v", lambda e: e.tensor_scalar(bafm[:, 1, :], bafm[:, 1, :], cbuf[0:8, 28:29],
                                         None, ALU.add), r=[B_ba], w=[B_ba], x=[l_dtb])
        E("sq", lambda e: e.activation(bg2[:, 1, :], bafm[:, 1, :], AF.Abs), r=[B_ba], w=[B_bg])
        E("sq", lambda e: e.activation(bg2[:, 1, :], bg2[:, 1, :], AF.Exp, scale=-1.0),
          r=[B_bg], w=[B_bg])
        E("sq", lambda e: e.activation(bg2[:, 1, :], bg2[:, 1, :], AF.Ln, bias=1.0),
          r=[B_bg], w=[B_bg])
        E("v", lambda e: e.tensor_scalar(bafm[:, 1, :], bafm[:, 1, :], 0.0, None, ALU.max),
          r=[B_ba], w=[B_ba])
        E("v", lambda e: e.tensor_tensor(bg2[:, 1, :], bg2[:, 1, :], bafm[:, 1, :], ALU.add),
          r=[B_bg, B_ba], w=[B_bg])
        E("sq", lambda e: e.activation(cbuf[0:8, 30:31], cbuf[0:8, 29:30], AF.Exp),
          x=[l_alog], w=[B_cb[30]])
        E("v", lambda e: e.tensor_scalar(cbuf[0:8, 31:32], cbuf[0:8, 30:31], -1.0, None,
                                         ALU.mult), r=[B_cb[30]], w=[B_cb[31]])
        E("v", lambda e: e.tensor_scalar(bg2[:, 1, :], bg2[:, 1, :], cbuf[0:8, 31:32], None,
                                         ALU.mult), r=[B_bg, B_cb[31]], w=[B_bg])
        E("v", lambda e: e.tensor_tensor_scan(bg2[:, 1, :], bg2[:, 1, :], bafm[:, 1, :], 0.0,
                                              ALU.add, ALU.bypass), r=[B_bg, B_ba], w=[B_bg])
        E("v", lambda e: e.memset(zcol[:], 0.0), w=[B_zc])
        E("sy", lambda e: e.dma_start(out=betagsp[0, :, 0:1], in_=zcol[:], allow_slow_non_contiguous=True), r=[B_zc], dma=True)
        E("sy", lambda e: e.dma_start(out=betagsp[1, :, 0:1], in_=zcol[:], allow_slow_non_contiguous=True), r=[B_zc], dma=True)
        E("sy", lambda e: e.dma_start(out=betagsp[0, :, 1:T + 1], in_=bg2[:, 0, :]),
          r=[B_bg], dma=True)
        bgst = E("sy", lambda e: e.dma_start(out=betagsp[1, :, 1:T + 1], in_=bg2[:, 1, :]),
                 r=[B_bg], dma=True)
        _stk5.close()
        seq.barrier()
        return zstores, bgst

    def emit_attention(zstores, bgst):
        _stk4 = ExitStack()
        wq = _stk4.enter_context(nc.sbuf_tensor("wq", [128, 16, 2048], BF16))
        wo = _stk4.enter_context(nc.sbuf_tensor("wo", [128, 8, 2048], BF16))
        cdg = _stk4.enter_context(nc.sbuf_tensor("cdg", [128, 16, 4, 128], BF16))
        xT = _stk4.enter_context(nc.sbuf_tensor("xT", [128, 1, 16, 512], BF16))
        mxs = _stk4.enter_context(nc.sbuf_tensor("mxs", [128, 515], BF16))
        tails = _stk4.enter_context(nc.sbuf_tensor("tails", [128, 16, 4], BF16))
        qkv = _stk4.enter_context(nc.sbuf_tensor("qkv", [128, 2, 16, 512], BF16))
        bg8 = _stk4.enter_context(nc.sbuf_tensor("bg8", [8, 2, 129], F32))
        hrow = _stk4.enter_context(nc.sbuf_tensor("hrow", [8, 2, 128], F32))
        hcolT = _stk4.enter_context(nc.sbuf_tensor("hcolT", [128, 4, 8], F32))
        grow = _stk4.enter_context(nc.sbuf_tensor("grow", [1, 128], F32))
        gamr = _stk4.enter_context(nc.sbuf_tensor("gamr", [8, 1], F32))
        gamd = _stk4.enter_context(nc.sbuf_tensor("gamd", [8, 8], F32))
        gamc = _stk4.enter_context(nc.sbuf_tensor("gamc", [128, 8], F32))
        lrow = _stk4.enter_context(nc.sbuf_tensor("lrow", [1, 1024], F32))
        rcolk = _stk4.enter_context(nc.sbuf_tensor("rcolk", [128, 4], F32))
        qn = _stk4.enter_context(nc.sbuf_tensor("qn", [128, 4, 128], BF16))
        kn = _stk4.enter_context(nc.sbuf_tensor("kn", [128, 4, 128], BF16))
        sq2t = _stk4.enter_context(nc.sbuf_tensor("sq2t", [128, 128], F32))
        ktok = _stk4.enter_context(nc.sbuf_tensor("ktok", [128, 4, 128], BF16))
        vtok = _stk4.enter_context(nc.sbuf_tensor("vtok", [128, 8, 128], BF16))
        kbgt = _stk4.enter_context(nc.sbuf_tensor("kbgt", [128, 128], BF16))
        vbt = _stk4.enter_context(nc.sbuf_tensor("vbt", [128, 128], BF16))
        kdect = _stk4.enter_context(nc.sbuf_tensor("kdect", [128, 128], BF16))
        dect = _stk4.enter_context(nc.sbuf_tensor("dect", [128, 2, 128], F32))
        dmneg = _stk4.enter_context(nc.sbuf_tensor("dmneg", [128, 128], F32))
        dmTin = _stk4.enter_context(nc.sbuf_tensor("dmTin", [128, 128], F32))
        Amat = _stk4.enter_context(nc.sbuf_tensor("Amat", [128, 128], BF16))
        Bmat = _stk4.enter_context(nc.sbuf_tensor("Bmat", [128, 128], BF16))
        Apw = _stk4.enter_context(nc.sbuf_tensor("Apw", [128, 2, 128], BF16))
        Bpw = _stk4.enter_context(nc.sbuf_tensor("Bpw", [128, 2, 128], BF16))
        Prod = _stk4.enter_context(nc.sbuf_tensor("Prod", [128, 2, 128], BF16))
        vcorr = _stk4.enter_context(nc.sbuf_tensor("vcorr", [128, 128], BF16))
        kcdf = _stk4.enter_context(nc.sbuf_tensor("kcdf", [128, 128], BF16))
        vnew = _stk4.enter_context(nc.sbuf_tensor("vnew", [128, 128], BF16))
        intraT = _stk4.enter_context(nc.sbuf_tensor("intraT", [128, 128], BF16))
        o1s = _stk4.enter_context(nc.sbuf_tensor("o1s", [128, 128], F32))
        oc2 = _stk4.enter_context(nc.sbuf_tensor("oc2", [128, 128], F32))
        xg = _stk4.enter_context(nc.sbuf_tensor("xg", [128, 128], F32))
        nrm = _stk4.enter_context(nc.sbuf_tensor("nrm", [128, 128], BF16))
        nrmT = _stk4.enter_context(nc.sbuf_tensor("nrmT", [128, 8, 128], BF16))
        stm = _stk4.enter_context(nc.sbuf_tensor("stm", [128, 2, 8, 128], F32))
        stb = _stk4.enter_context(nc.sbuf_tensor("stb", [128, 128], BF16))
        ztile = _stk4.enter_context(nc.sbuf_tensor("ztile", [128, 1024], BF16))
        rst = _stk4.enter_context(nc.sbuf_tensor("rst", [128, 512], BF16))
        grow1 = _stk4.enter_context(nc.sbuf_tensor("grow1", [1, 128], F32))
        sq2t1 = _stk4.enter_context(nc.sbuf_tensor("sq2t1", [128, 128], F32))
        kbgt1 = _stk4.enter_context(nc.sbuf_tensor("kbgt1", [128, 128], BF16))
        vbt1 = _stk4.enter_context(nc.sbuf_tensor("vbt1", [128, 128], BF16))
        kdect1 = _stk4.enter_context(nc.sbuf_tensor("kdect1", [128, 128], BF16))
        dect1 = _stk4.enter_context(nc.sbuf_tensor("dect1", [128, 2, 128], F32))
        dmneg1 = _stk4.enter_context(nc.sbuf_tensor("dmneg1", [128, 128], F32))
        dmTin1 = _stk4.enter_context(nc.sbuf_tensor("dmTin1", [128, 128], F32))
        Amat1 = _stk4.enter_context(nc.sbuf_tensor("Amat1", [128, 128], BF16))
        Bmat1 = _stk4.enter_context(nc.sbuf_tensor("Bmat1", [128, 128], BF16))
        Apw1 = _stk4.enter_context(nc.sbuf_tensor("Apw1", [128, 2, 128], BF16))
        Bpw1 = _stk4.enter_context(nc.sbuf_tensor("Bpw1", [128, 2, 128], BF16))
        Prod1 = _stk4.enter_context(nc.sbuf_tensor("Prod1", [128, 2, 128], BF16))
        vcorr1 = _stk4.enter_context(nc.sbuf_tensor("vcorr1", [128, 128], BF16))
        kcdf1 = _stk4.enter_context(nc.sbuf_tensor("kcdf1", [128, 128], BF16))
        vnew1 = _stk4.enter_context(nc.sbuf_tensor("vnew1", [128, 128], BF16))
        intraT1 = _stk4.enter_context(nc.sbuf_tensor("intraT1", [128, 128], BF16))
        o1s1 = _stk4.enter_context(nc.sbuf_tensor("o1s1", [128, 128], F32))
        oc21 = _stk4.enter_context(nc.sbuf_tensor("oc21", [128, 128], F32))
        xg1 = _stk4.enter_context(nc.sbuf_tensor("xg1", [128, 128], F32))
        nrm1 = _stk4.enter_context(nc.sbuf_tensor("nrm1", [128, 128], BF16))
        stb1 = _stk4.enter_context(nc.sbuf_tensor("stb1", [128, 128], BF16))
        psP = _stk4.enter_context(nc.psum_tensor("psP", [128, 512], F32))
        psC = _stk4.enter_context(nc.psum_tensor("psC", [128, 512], F32))
        psK1 = _stk4.enter_context(nc.psum_tensor("psK1", [128, 512], F32))
        psK2 = _stk4.enter_context(nc.psum_tensor("psK2", [128, 512], F32))
        psE = _stk4.enter_context(nc.psum_tensor("psE", [128, 512], F32))
        psF = _stk4.enter_context(nc.psum_tensor("psF", [128, 512], F32))
        psE1 = _stk4.enter_context(nc.psum_tensor("psE1", [128, 512], F32))
        psF1 = _stk4.enter_context(nc.psum_tensor("psF1", [128, 512], F32))
        if True:

            BN = {}
            for nm in ("psP", "psC", "psK1", "psK2", "hrow", "hcolT", "grow", "gamr", "gamd",
                       "gamc", "lrow", "rcolk", "sq2t", "kbgt", "vbt", "kdect", "dect", "dmneg",
                       "dmTin", "Amat", "Bmat", "vcorr", "kcdf", "vnew", "intraT", "o1s", "oc2",
                       "xg", "nrm", "stb", "ztile", "mxs", "tails", "rst", "wcst"):
                BN[nm] = Buf()
            _bE = Buf(); _bF = Buf()
            _bE1 = Buf(); _bF1 = Buf()
            B_psE = [_bE, _bE, _bE, _bE]
            B_psF = [_bF, _bF, _bF, _bF]
            B_xT = [Buf()]
            B_qkv = [Buf(), Buf()]
            B_qn = [Buf() for _ in range(4)]
            last_tkv = [None]
            last_tb = [None]
            last_tn = [None]
            B_kn = [Buf() for _ in range(4)]
            B_kvtok = Buf()
            B_ktok = [B_kvtok] * 4
            B_vtok = [B_kvtok] * 8
            B_Apw = [Buf(), Buf()]
            B_Bpw = [Buf(), Buf()]
            B_Prod = [Buf(), Buf()]
            B_Apw1 = [Buf(), Buf()]
            B_Bpw1 = [Buf(), Buf()]
            B_Prod1 = [Buf(), Buf()]
            B_st = [[Buf() for _ in range(8)], [Buf() for _ in range(8)]]
            B_nrmT = [Buf(), Buf()]
            B_bg8 = Buf()
            VRS = [dict(grow=grow, dect=dect, dmneg=dmneg, dmTin=dmTin, Amat=Amat, Bmat=Bmat,
                        Apw=Apw, Bpw=Bpw, Prod=Prod, kbgt=kbgt, vbt=vbt, kdect=kdect,
                        vcorr=vcorr, kcdf=kcdf, stb=stb, vnew=vnew, o1s=o1s, oc2=oc2,
                        xg=xg, nrm=nrm, sq2t=sq2t, intraT=intraT),
                   dict(grow=grow1, dect=dect1, dmneg=dmneg1, dmTin=dmTin1, Amat=Amat1, Bmat=Bmat1,
                        Apw=Apw1, Bpw=Bpw1, Prod=Prod1, kbgt=kbgt1, vbt=vbt1, kdect=kdect1,
                        vcorr=vcorr1, kcdf=kcdf1, stb=stb1, vnew=vnew1, o1s=o1s1, oc2=oc21,
                        xg=xg1, nrm=nrm1, sq2t=sq2t1, intraT=intraT1)]
            VRB = [{k: BN[k] for k in VRS[0] if k in BN},
                   {k: Buf() for k in VRS[1] if k in BN}]

            pe = lambda q, w=128: psE[:, q * 128:q * 128 + w]
            pf = lambda q, w=128: psF[:, q * 128:q * 128 + w]

            B_wo = Buf()
            for ic in range(16):
                E("g", lambda e, ic=ic: e.dma_start(out=wq[:, ic, :],
                                                    in_=wqkvin[ic * 128:(ic + 1) * 128, :]),
                  w=[BN["wcst"]], dma=True)
            for kc in range(8):
                E("g", lambda e, kc=kc: e.dma_start(out=wo[:, kc, :],
                                                    in_=woutin[kc * 128:(kc + 1) * 128, :]),
                  w=[B_wo], dma=True)
            cd_l = E("g", lambda e: e.dma_start(
                out=cdg[:], in_=convdin), dma=True)
            E("v", lambda e: e.memset(tails[:], 0.0), w=[BN["tails"]])

            for vh in range(8):
                E("v", lambda e, vh=vh: e.memset(stm[:, 0, vh, :], 0.0), w=[B_st[0][vh]])

            rs_done = []
            NGx = ATTN_CFG["groups"] if ATTN_CFG["groups"] is not None else NG
            for gidx in range(NGx):
                gb = gidx % 2
                E("sy", lambda e, gidx=gidx, gb=gb: e.dma_start(
                    out=xT[:, 0].rearrange("p a b -> p a b"),
                    in_=ag1_out[(gidx * 512) // LT, :, :,
                                (gidx * 512) % LT:(gidx * 512) % LT + 512].rearrange("a p b -> p a b")),
                  w=[B_xT[0]], x=[g_ag1], dma=True, chan="zx")
                for oc in range(16):
                    def mk_p(oc=oc, gb=gb):
                        def f(e):
                            ins = []
                            for ic in range(16):
                                ins.append(e.matmul(psP[:], wq[:, ic, oc * 128:(oc + 1) * 128],
                                                    xT[:, 0, ic, :], start=(ic == 0), stop=(ic == 15)))
                            return ins
                        return f
                    E("t", mk_p(), r=[B_xT[0], BN["wcst"]], w=[BN["psP"]])
                    E("v", lambda e: e.tensor_copy(mxs[:, 3:515], psP[:]), r=[BN["psP"]], w=[BN["mxs"]])
                    E("v", lambda e, oc=oc: e.tensor_copy(mxs[:, 0:3], tails[:, oc, 0:3]),
                      r=[BN["tails"]], w=[BN["mxs"]])
                    E("v", lambda e, oc=oc: e.tensor_copy(tails[:, oc, 0:3], mxs[:, 512:515]),
                      r=[BN["mxs"]], w=[BN["tails"]])

                    def mk_c(oc=oc):
                        def f(e):
                            ins = []
                            for k in range(4):
                                ins.append(e.matmul(psC[:], cdg[:, oc, k, :], mxs[:, k:k + 512],
                                                    start=(k == 0), stop=(k == 3)))
                            return ins
                        return f
                    E("t", mk_c(), r=[BN["mxs"]], w=[BN["psC"]], x=[cd_l])
                    E("sq", lambda e, oc=oc, gb=gb: e.activation(qkv[:, gb, oc, :], psC[:], AF.Silu),
                      r=[BN["psC"]], w=[B_qkv[gb]])

                for ch in range(ATTN_CFG["chunks"] if ATTN_CFG["chunks"] is not None else 4):
                    j = gidx * 4 + ch
                    t0 = j * 128
                    sl = slice(ch * 128, ch * 128 + 128)
                    pm = j % 2
                    pn = 1 - pm
                    Bq = B_qkv[gb]
                    E("sy", lambda e, t0=t0: e.dma_start(out=bg8[:, 0, 0:129],
                                                         in_=betagsp[1, :, t0:t0 + 129]),
                      w=[B_bg8], x=[bgst], dma=True, chan="bg")
                    E("sy", lambda e, t0=t0: e.dma_start(out=bg8[:, 1, 0:128],
                                                         in_=betagsp[0, :, t0 + 1:t0 + 129]),
                      w=[B_bg8], x=[bgst], dma=True, chan="bg")
                    E("v", lambda e: e.tensor_scalar(
                        hrow[:, 0, :], bg8[:, 0, 1:129], bg8[:, 0, 0:1], None, ALU.subtract),
                      r=[B_bg8], w=[BN["hrow"]])
                    E("v", lambda e: e.tensor_scalar(hrow[:, 1, :], hrow[:, 0, :],
                                                     hrow[:, 0, 127:128], None, ALU.subtract),
                      r=[BN["hrow"]], w=[BN["hrow"]])
                    E("sq", lambda e: e.activation(gamr[:], hrow[:, 0, 127:128], AF.Exp),
                      r=[BN["hrow"]], w=[BN["gamr"]])
                    E("t", lambda e: e.transpose(pe(0, 8), hrow[:, 0, :], ident[0:8, 0:8]),
                      r=[BN["hrow"], B_const], w=[B_psE[0]])
                    E("v", lambda e: e.tensor_copy(hcolT[:, 0, :], psE[0:128, 0:8]),
                      r=[B_psE[0]], w=[BN["hcolT"]])
                    E("t", lambda e: e.transpose(pe(0, 8), hrow[:, 1, :], ident[0:8, 0:8]),
                      r=[BN["hrow"], B_const], w=[B_psE[0]])
                    E("v", lambda e: e.tensor_copy(hcolT[:, 1, :], psE[0:128, 0:8]),
                      r=[B_psE[0]], w=[BN["hcolT"]])
                    E("t", lambda e: e.transpose(pe(0, 8), bg8[:, 1, 0:128], ident[0:8, 0:8]),
                      r=[B_bg8, B_const], w=[B_psE[0]])
                    E("v", lambda e: e.tensor_copy(hcolT[:, 2, :], psE[0:128, 0:8]),
                      r=[B_psE[0]], w=[BN["hcolT"]])
                    E("sq", lambda e: e.activation(hcolT[:, 3, :], hcolT[:, 0, :], AF.Exp),
                      r=[BN["hcolT"]], w=[BN["hcolT"]])
                    E("sq", lambda e: e.activation(hcolT[:, 1, :], hcolT[:, 1, :], AF.Exp, scale=-1.0),
                      r=[BN["hcolT"]], w=[BN["hcolT"]])
                    E("v", lambda e: e.tensor_scalar(gamd[:], identb[0:8, 0:8], gamr[:], None, ALU.mult),
                      r=[BN["gamr"], B_const], w=[BN["gamd"]])
                    E("t", lambda e: e.matmul(pe(0, 8), sel8[:], gamd[:], start=True, stop=True),
                      r=[BN["gamd"], B_const], w=[B_psE[0]])
                    E("v", lambda e: e.tensor_copy(gamc[:], psE[0:128, 0:8]), r=[B_psE[0]], w=[BN["gamc"]])
                    for kh in range(4):
                        for qk in range(2):
                            src_oc = kh if qk == 0 else 4 + kh
                            ps_t = psE if qk == 0 else psF
                            Bp = B_psE[0] if qk == 0 else B_psF[0]
                            E("sq", lambda e, src_oc=src_oc, gb=gb, sl=sl: e.activation(
                                sq2t[:], qkv[:, gb, src_oc, sl], AF.Square), r=[Bq], w=[BN["sq2t"]])
                            E("t", lambda e, kh=kh, ps_t=ps_t: e.matmul(
                                ps_t[0:1, kh * 128:kh * 128 + 128], onescol[:], sq2t[:],
                                start=True, stop=True),
                              r=[BN["sq2t"], B_const], w=[Bp])
                    E("v", lambda e: e.tensor_copy(lrow[0:1, 0:512], psE[0:1, 0:512]),
                      r=[B_psE[0]], w=[BN["lrow"]])
                    E("v", lambda e: e.tensor_copy(lrow[0:1, 512:1024], psF[0:1, 0:512]),
                      r=[B_psF[0]], w=[BN["lrow"]])
                    E("v", lambda e: e.tensor_scalar(lrow[:], lrow[:], 1e-6, None, ALU.add),
                      r=[BN["lrow"]], w=[BN["lrow"]])
                    E("sq", lambda e: e.activation(lrow[:], lrow[:], AF.Ln),
                      r=[BN["lrow"]], w=[BN["lrow"]])
                    E("sq", lambda e: e.activation(lrow[:], lrow[:], AF.Exp, scale=-0.5),
                      r=[BN["lrow"]], w=[BN["lrow"]])
                    for kh in range(4):
                        E("t", lambda e, kh=kh: e.matmul(
                            pf(1)[:, kh:kh + 1], lrow[0:1, 512 + kh * 128:512 + (kh + 1) * 128],
                            onescol[0:1, 0:1], start=True, stop=True),
                          r=[BN["lrow"], B_const], w=[B_psF[1]])
                    E("v", lambda e: e.tensor_copy(rcolk[:], psF[:, 128:132]),
                      r=[B_psF[1]], w=[BN["rcolk"]])
                    for kh in range(4):
                        for qk in range(2):
                            src_oc = kh if qk == 0 else 4 + kh
                            dstt = (qn if qk == 0 else kn)
                            Bd = (B_qn if qk == 0 else B_kn)[kh]
                            E("t", lambda e, kh=kh, qk=qk: e.matmul(
                                pf(2), onesrow[:], lrow[0:1, qk * 512 + kh * 128:qk * 512 + (kh + 1) * 128],
                                start=True, stop=True), r=[BN["lrow"], B_const], w=[B_psF[2]])
                            scl = RSQ2 if qk == 0 else 1.0
                            E("v", lambda e, src_oc=src_oc, dstt=dstt, kh=kh, gb=gb, sl=sl, scl=scl:
                              e.scalar_tensor_tensor(dstt[:, kh, :], qkv[:, gb, src_oc, sl], scl,
                                                     pf(2), ALU.mult, ALU.mult),
                              r=[Bq, B_psF[2]], w=[Bd])
                    for kh in range(4):
                        E("sy", lambda e, kh=kh, gb=gb, sl=sl: e.dma_start(
                            out=ktok[:, kh, :], in_=qkv[:, gb, 4 + kh, sl], transpose=True),
                          r=[], w=[B_kvtok],
                          x=([last_tkv[0]] if kh == 0 else []) + [Bq.writer],
                          dma=True, chan="tkv", serial=False)
                    tpkv = None
                    for vh in range(8):
                        tpkv = E("sy", lambda e, vh=vh, gb=gb, sl=sl: e.dma_start(
                            out=vtok[:, vh, :], in_=qkv[:, gb, 8 + vh, sl], transpose=True),
                          r=[Bq] if vh == 7 else [], w=[B_kvtok],
                          x=[Bq.writer] if vh < 7 else [], dma=True, chan="tkv", serial=False)
                    last_tkv[0] = tpkv
                    for kh in range(4):
                        E("t", lambda e, kh=kh: e.matmul(
                            psK1[:, kh * 128:(kh + 1) * 128], kn[:, kh, :], kn[:, kh, :],
                            start=True, stop=True), r=[B_kn[kh]], w=[BN["psK1"]])
                        E("t", lambda e, kh=kh: e.matmul(
                            psK2[:, kh * 128:(kh + 1) * 128], kn[:, kh, :], qn[:, kh, :],
                            start=True, stop=True), r=[B_kn[kh], B_qn[kh]], w=[BN["psK2"]])
                    E("sy", lambda e, j=j: e.dma_start(out=ztile[:], in_=zsp[j]),
                      w=[BN["ztile"]], dma=True, chan="zt")

                    _ven = [0]

                    def _VE(*aa, **kk):
                        _ven[0] += 1
                        if ATTN_CFG.get("vhsteps") is not None and _ven[0] > ATTN_CFG["vhsteps"]:
                            return None
                        return E(*aa, **kk)
                    def vh_chain(vh):
                        s = vh % 2
                        kh = vh // 2
                        R = VRS[s]
                        RB = VRB[s]
                        bE = (_bE, _bE1)[s]
                        bF = (_bF, _bF1)[s]
                        psE_s = (psE, psE1)[s]
                        psF_s = (psF, psF1)[s]
                        BPw = (B_Apw, B_Apw1)[s]
                        BBw = (B_Bpw, B_Bpw1)[s]
                        BPr = (B_Prod, B_Prod1)[s]
                        peS = lambda q, w=128: psE_s[:, q * 128:q * 128 + w]
                        pfS = lambda q, w=128: psF_s[:, q * 128:q * 128 + w]
                        co = 32 if s == 0 else 40
                        cbS = lambda j: cb(co + j)
                        grow_ = R["grow"]; dect_ = R["dect"]; dmneg_ = R["dmneg"]; dmTin_ = R["dmTin"]
                        Amat_ = R["Amat"]; Bmat_ = R["Bmat"]; Apw_ = R["Apw"]; Bpw_ = R["Bpw"]
                        Prod_ = R["Prod"]; kbgt_ = R["kbgt"]; vbt_ = R["vbt"]; kdect_ = R["kdect"]
                        vcorr_ = R["vcorr"]; kcdf_ = R["kcdf"]; stb_ = R["stb"]; vnew_ = R["vnew"]
                        o1s_ = R["o1s"]; oc2_ = R["oc2"]; xg_ = R["xg"]; nrm_ = R["nrm"]
                        sq2t_ = R["sq2t"]; intraT_ = R["intraT"]
                        _VE("t", lambda e, vh=vh: e.transpose(psE_s[0:1, 384:512],
                                                            hcolT[:, 0, vh:vh + 1], ident[:]),
                          r=[BN["hcolT"], B_const], w=[bE]); yield
                        _VE("v", lambda e: e.tensor_copy(grow_[:], psE_s[0:1, 384:512]),
                          r=[bE], w=[RB["grow"]]); yield
                        _VE("t", lambda e: e.matmul(psE_s[:, 384:512], onesrow[:], grow_[:],
                                                  start=True, stop=True),
                          r=[RB["grow"], B_const], w=[bE]); yield
                        _VE("v", lambda e, vh=vh: e.tensor_scalar(
                            dect_[:, 0, :], psE_s[:, 384:512], hcolT[:, 0, vh:vh + 1], 0.0,
                            ALU.subtract, ALU.max), r=[bE, BN["hcolT"]], w=[RB["dect"]]); yield
                        _VE("v", lambda e, vh=vh: e.tensor_scalar(
                            dect_[:, 1, :], psE_s[:, 384:512], hcolT[:, 0, vh:vh + 1], 0.0,
                            ALU.subtract, ALU.min), r=[bE, BN["hcolT"]], w=[RB["dect"]]); yield
                        _VE("sq", lambda e: e.activation(dect_[:, 0, :], dect_[:, 0, :], AF.Exp, scale=-1.0),
                          r=[RB["dect"]], w=[RB["dect"]]); yield
                        _VE("sq", lambda e: e.activation(dect_[:, 1, :], dect_[:, 1, :], AF.Exp),
                          r=[RB["dect"]], w=[RB["dect"]]); yield
                        _VE("v", lambda e: e.tensor_tensor(dmneg_[:], dect_[:, 0, :], mnegst[:], ALU.mult),
                          r=[RB["dect"], B_const], w=[RB["dmneg"]]); yield
                        _VE("v", lambda e: e.tensor_tensor(dmTin_[:], dect_[:, 1, :], mtriu[:], ALU.mult),
                          r=[RB["dect"], B_const], w=[RB["dmTin"]]); yield
                        _VE("v", lambda e, kh=kh, vh=vh: e.scalar_tensor_tensor(
                            Amat_[:], psK1[:, kh * 128:(kh + 1) * 128], hcolT[:, 2, vh:vh + 1],
                            dmneg_[:], ALU.mult, ALU.mult),
                          r=[BN["psK1"], BN["hcolT"], RB["dmneg"]], w=[RB["Amat"]]); yield
                        _VE("sy", lambda e: e.dma_start(out=Bmat_[:], in_=Amat_[:], transpose=True),
                          r=[RB["Amat"]], w=[RB["Bmat"]], dma=True, chan="tb" + str(s)); yield
                        _VE("v", lambda e: e.tensor_tensor(Prod_[:, 0, :], identb[:], Bmat_[:], ALU.add),
                          r=[RB["Bmat"], B_const], w=[BPr[0]]); yield
                        lastP = 0
                        AsrcAP, BsrcAP = Amat_[:], Bmat_[:]
                        AsrcB, BsrcB = RB["Amat"], RB["Bmat"]
                        for lev in range(1, 7):
                            pA = lev % 2
                            _VE("t", lambda e, B_=BsrcAP, A_=AsrcAP: e.matmul(peS(1), B_, A_,
                                                                            start=True, stop=True),
                              r=[AsrcB, BsrcB], w=[bE]); yield
                            _VE("sq", lambda e, pA=pA: e.activation(Apw_[:, pA, :], peS(1), AF.Copy),
                              r=[bE], w=[BPw[pA]]); yield
                            if lev < 6:
                                _VE("t", lambda e, B_=BsrcAP, A_=AsrcAP: e.matmul(peS(2), A_, B_,
                                                                                start=True, stop=True),
                                  r=[AsrcB, BsrcB], w=[bE]); yield
                                _VE("sq", lambda e, pA=pA: e.activation(Bpw_[:, pA, :], peS(2), AF.Copy),
                                  r=[bE], w=[BBw[pA]]); yield
                            _VE("t", lambda e, pA=pA, lastP=lastP: e.matmul(
                                peS(3), Apw_[:, pA, :], Prod_[:, lastP, :], start=True, stop=True),
                              r=[BPw[pA], BPr[lastP]], w=[bE]); yield
                            _VE("v", lambda e, lastP=lastP: e.tensor_tensor(
                                Prod_[:, 1 - lastP, :], Prod_[:, lastP, :], peS(3), ALU.add),
                              r=[bE, BPr[lastP]], w=[BPr[1 - lastP]]); yield
                            lastP = 1 - lastP
                            AsrcAP, BsrcAP = Apw_[:, pA, :], Bpw_[:, pA, :]
                            AsrcB, BsrcB = BPw[pA], BBw[pA]
                        _VE("v", lambda e, kh=kh, vh=vh: e.tensor_tensor(
                            cbS(0), rcolk[:, kh:kh + 1], hcolT[:, 2, vh:vh + 1], ALU.mult),
                          r=[BN["rcolk"], BN["hcolT"]], w=[B_cb[co + 0]]); yield
                        _VE("v", lambda e, vh=vh: e.tensor_tensor(
                            cbS(1), cbS(0), hcolT[:, 3, vh:vh + 1], ALU.mult),
                          r=[B_cb[co + 0], BN["hcolT"]], w=[B_cb[co + 1]]); yield
                        _VE("v", lambda e, kh=kh, vh=vh: e.tensor_tensor(
                            cbS(2), rcolk[:, kh:kh + 1], hcolT[:, 1, vh:vh + 1], ALU.mult),
                          r=[BN["rcolk"], BN["hcolT"]], w=[B_cb[co + 2]]); yield
                        _VE("g", lambda e, kh=kh: e.tensor_scalar(kbgt_[:], ktok[:, kh, :], cbS(1),
                                                                None, ALU.mult),
                          r=[B_ktok[kh], B_cb[co + 1]], w=[RB["kbgt"]], x=[tpkv]); yield
                        _VE("g", lambda e, vh=vh: e.tensor_scalar(vbt_[:], vtok[:, vh, :],
                                                                hcolT[:, 2, vh:vh + 1], None, ALU.mult),
                          r=[B_vtok[vh], BN["hcolT"]], w=[RB["vbt"]], x=[tpkv]); yield
                        _VE("g", lambda e, kh=kh: e.tensor_scalar(kdect_[:], ktok[:, kh, :], cbS(2),
                                                                None, ALU.mult),
                          r=[B_ktok[kh], B_cb[co + 2]], w=[RB["kdect"]], x=[tpkv]); yield
                        _VE("t", lambda e, lastP=lastP: e.matmul(pfS(0), Prod_[:, lastP, :], vbt_[:],
                                                               start=True, stop=True),
                          r=[BPr[lastP], RB["vbt"]], w=[bF]); yield
                        _VE("v", lambda e: e.tensor_copy(vcorr_[:], pfS(0)), r=[bF], w=[RB["vcorr"]]); yield
                        _VE("t", lambda e, lastP=lastP: e.matmul(pfS(1), kbgt_[:], Prod_[:, lastP, :],
                                                               start=True, stop=True),
                          r=[BPr[lastP], RB["kbgt"]], w=[bF]); yield
                        _VE("sq", lambda e: e.activation(kcdf_[:], pfS(1), AF.Copy),
                          r=[bF], w=[RB["kcdf"]]); yield
                        _VE("sq", lambda e, pm=pm, vh=vh: e.activation(stb_[:], stm[:, pm, vh, :], AF.Copy),
                          r=[B_st[pm][vh]], w=[RB["stb"]]); yield
                        _VE("t", lambda e: e.matmul(pfS(2), kcdf_[:], stb_[:], start=True, stop=True),
                          r=[RB["kcdf"], RB["stb"]], w=[bF]); yield
                        _VE("v", lambda e: e.tensor_tensor(vnew_[:], vcorr_[:], pfS(2), ALU.subtract),
                          r=[RB["vcorr"], bF], w=[RB["vnew"]]); yield
                        _VE("t", lambda e, kh=kh: e.matmul(pfS(3), qn[:, kh, :], stb_[:],
                                                         start=True, stop=True),
                          r=[B_qn[kh], RB["stb"]], w=[bF]); yield
                        _VE("sq", lambda e, vh=vh: e.activation(o1s_[:], pfS(3), AF.Copy,
                                                              scale=hcolT[:, 3, vh:vh + 1]),
                          r=[bF, BN["hcolT"]], w=[RB["o1s"]]); yield
                        _VE("v", lambda e, kh=kh: e.tensor_tensor(
                            intraT_[:], psK2[:, kh * 128:(kh + 1) * 128], dmTin_[:], ALU.mult),
                          r=[BN["psK2"], RB["dmTin"]], w=[RB["intraT"]]); yield
                        _VE("t", lambda e: e.matmul(pfS(0), intraT_[:], vnew_[:], start=True, stop=True),
                          r=[RB["intraT"], RB["vnew"]], w=[bF]); yield
                        _VE("v", lambda e: e.tensor_tensor(oc2_[:], o1s_[:], pfS(0), ALU.add),
                          r=[RB["o1s"], bF], w=[RB["oc2"]]); yield
                        _VE("v", lambda e, vh=vh: e.tensor_tensor(
                            xg_[:], oc2_[:], ztile[:, vh * 128:(vh + 1) * 128], ALU.mult),
                          r=[RB["oc2"], BN["ztile"]], w=[RB["xg"]]); yield
                        _VE("sq", lambda e: e.activation(sq2t_[:], xg_[:], AF.Square, accum_out=cbS(3)),
                          r=[RB["xg"]], w=[RB["sq2t"], B_cb[co + 3]]); yield
                        _VE("v", lambda e: e.tensor_scalar(cbS(4), cbS(3), 1.0 / 128.0, 1e-6,
                                                         ALU.mult, ALU.add),
                          r=[B_cb[co + 3]], w=[B_cb[co + 4]]); yield
                        _VE("sq", lambda e: e.activation(cbS(5), cbS(4), AF.Ln),
                          r=[B_cb[co + 4]], w=[B_cb[co + 5]]); yield
                        _VE("sq", lambda e: e.activation(cbS(6), cbS(5), AF.Exp, scale=-0.5),
                          r=[B_cb[co + 5]], w=[B_cb[co + 6]]); yield
                        _VE("v", lambda e: e.scalar_tensor_tensor(nrm_[:], xg_[:], cbS(6), normw_t[:],
                                                                ALU.mult, ALU.mult),
                          r=[RB["xg"], B_cb[co + 6], B_const], w=[RB["nrm"]]); yield
                        _VE("sy", lambda e, vh=vh: e.dma_start(out=nrmT[:, vh, :], in_=nrm_[:],
                                                             transpose=True),
                          r=[RB["nrm"]], w=[B_nrmT[s]], dma=True, chan="tn" + str(s)); yield
                        _VE("t", lambda e: e.matmul(pfS(1), kdect_[:], vnew_[:], start=True, stop=True),
                          r=[RB["kdect"], RB["vnew"]], w=[bF]); yield
                        _VE("v", lambda e, pm=pm, pn=pn, vh=vh: e.scalar_tensor_tensor(
                            stm[:, pn, vh, :], stm[:, pm, vh, :], gamc[:, vh:vh + 1], pfS(1),
                            ALU.mult, ALU.add),
                          r=[bF, B_st[pm][vh], BN["gamc"]], w=[B_st[pn][vh]]); yield

                    VHN = ATTN_CFG["vh"] if ATTN_CFG["vh"] is not None else 8

                    def stream_gen(s):
                        for vh in range(s, VHN, 2):
                            for _ in vh_chain(vh):
                                yield

                    alive = [stream_gen(0), stream_gen(1)]
                    while alive:
                        for g_ in list(alive):
                            try:
                                next(g_)
                            except StopIteration:
                                alive.remove(g_)
                    for od in range(4 if (ATTN_CFG["rs"] and (ATTN_CFG["vh"] is None or ATTN_CFG["vh"] == 8)) else 0):
                        ps_t = psK1 if od % 2 == 0 else psK2
                        Bp = BN["psK1"] if od % 2 == 0 else BN["psK2"]

                        def mk_wo(od=od, ps_t=ps_t):
                            def f(e):
                                ins = []
                                for vh in range(8):
                                    ins.append(e.matmul(ps_t[:], nrmT[:, vh, :],
                                                        wo[:, vh, od * 512:(od + 1) * 512],
                                                        start=(vh == 0), stop=(vh == 7)))
                                return ins
                            return f
                        E("t", mk_wo(), r=[B_nrmT[0], B_nrmT[1], B_wo], w=[Bp])
                        E("v", lambda e, od=od, ps_t=ps_t: e.tensor_copy(rst[:, 0:512], ps_t[:]),
                          r=[Bp], w=[BN["rst"]])
                        rs_done.append(E("sy", lambda e, t0=t0, od=od: e.dma_start(
                            out=rs_in[t0:t0 + 128, od * 512:(od + 1) * 512], in_=rst[:, 0:512]),
                          r=[BN["rst"]], dma=True, chan="sr", serial=False))
            if ATTN_CFG["rs"] and rs_done:
                g_rs = seq.cc(lambda e: e.collective_compute("ReduceScatter", ALU.add, replica_groups=RG4,
                                                             ins=[rs_in.opt()], outs=[rs_out.opt()]),
                              [rs_done[-1]])
            else:
                g_rs = E("g", lambda e: e.nop())
            _stk4.close()
            seq.barrier()
            return g_rs

    def emit_x2():
        _stk6 = ExitStack()
        xa = _stk6.enter_context(nc.sbuf_tensor("xa", [128, D], F32))
        xb2 = _stk6.enter_context(nc.sbuf_tensor("xb2", [128, D], BF16))
        xc = _stk6.enter_context(nc.sbuf_tensor("xc", [128, D], F32))
        if True:
            B_a, B_b, B_c2 = Buf(), Buf(), Buf()
            for tt in range(NT):
                E("sy", lambda e, tt=tt: e.dma_start(out=xa[:], in_=x1sp[tt * 128:(tt + 1) * 128, :]),
                  w=[B_a], dma=True, chan="xa")
                E("sy", lambda e, tt=tt: e.dma_start(out=xb2[:], in_=rs_out[tt * 128:(tt + 1) * 128, :]),
                  w=[B_b], dma=True, chan="xb")
                E("v", lambda e: e.tensor_tensor(xc[:], xa[:], xb2[:], ALU.add),
                  r=[B_a, B_b], w=[B_c2])
                E("sy", lambda e, tt=tt: e.dma_start(out=x2sp[tt * 128:(tt + 1) * 128, :], in_=xc[:]),
                  r=[B_c2], dma=True)
        _stk6.close()
        seq.barrier()

    def finish_dummy():
        with nc.sbuf_tensor("dz", [128, D], F32) as dz:
            B_dz = Buf()
            E("v", lambda e: e.memset(dz[:], 0.0), w=[B_dz])
            for tt in range(NT):
                E("sy", lambda e, tt=tt: e.dma_start(out=out[tt * 128:(tt + 1) * 128, :], in_=dz[:]),
                  r=[B_dz], dma=True)
        seq.barrier()
        with nc.Block() as block:
            seq.emit(block)
        es.close()
        return nc

    if upto == 0:
        return finish_dummy()
    ag1_dep = emit_ffn(1)
    if upto == 1:
        return finish_dummy()
    g_ag1 = seq.cc(lambda e: e.collective_compute("AllGather", ALU.bypass, replica_groups=RG4,
                                                  ins=[ag1_in.opt()], outs=[ag1_out.opt()]), [ag1_dep])
    seq.barrier()
    if upto == 2:
        return finish_dummy()
    zstores, bgst = emit_zsweep()
    if upto == 3:
        return finish_dummy()
    emit_attention(zstores, bgst)
    if upto == 4:
        return finish_dummy()
    emit_x2()
    if upto == 5:
        return finish_dummy()
    emit_ffn(2)
    seq.barrier()

    with nc.Block() as block:
        seq.emit(block)
    es.close()
    return nc


_prog_cache = {}


def kernel(L0_attn_hidden, L0_attn_W_qkv, L0_attn_W_z, L0_attn_W_b, L0_attn_W_a,
           L0_attn_conv_weight, L0_attn_dt_bias, L0_attn_A_log, L0_attn_norm_w,
           L0_attn_W_out, L0_ffn_w1, L0_ffn_g1, L0_ffn_w2, L0_ffn_g2, L0_ffn_w3, L0_ffn_g3):
    hid = np.ascontiguousarray(np.asarray(L0_attn_hidden, np.float32))
    B, T, _ = hid.shape
    LT = T // 4
    w1T = np.ascontiguousarray(np.asarray(L0_ffn_w1, np.float32).T)
    w2T = np.ascontiguousarray(np.asarray(L0_ffn_w2, np.float32).T)
    w3T = np.ascontiguousarray(np.asarray(L0_ffn_w3, np.float32).T)
    Wqkv = np.asarray(L0_attn_W_qkv, np.float32)
    Wz = np.asarray(L0_attn_W_z, np.float32)
    Wb = np.asarray(L0_attn_W_b, np.float32)
    Wa = np.asarray(L0_attn_W_a, np.float32)
    convw = np.asarray(L0_attn_conv_weight, np.float32)
    Wout = np.asarray(L0_attn_W_out, np.float32)
    ident = np.eye(128, dtype=np.float32)
    ones = np.ones((128, 128), np.float32)
    masks = np.stack([-np.tril(ones, -1), np.triu(ones)]).astype(np.float32)
    sel8 = np.ones((8, 128), np.float32)

    in_maps = []
    for c in range(NCORE):
        bb, hg = c // 4, c % 4
        kh0, vh0 = 4 * hg, 8 * hg
        qcols = slice(kh0 * 128, kh0 * 128 + 512)
        kcols = slice(2048 + kh0 * 128, 2048 + kh0 * 128 + 512)
        vcols = slice(4096 + vh0 * 128, 4096 + vh0 * 128 + 1024)
        wqkv_sl = np.concatenate([Wqkv[:, qcols], Wqkv[:, kcols], Wqkv[:, vcols]], axis=1)
        conv_sl = np.concatenate([convw[kh0 * 128:kh0 * 128 + 512],
                                  convw[2048 + kh0 * 128:2048 + kh0 * 128 + 512],
                                  convw[4096 + vh0 * 128:4096 + vh0 * 128 + 1024]], axis=0)
        convd = np.zeros((128, 16, 4, 128), np.float32)
        for oc in range(16):
            for k in range(4):
                np.fill_diagonal(convd[:, oc, k, :], conv_sl[oc * 128:(oc + 1) * 128, k])
        wba_full = np.concatenate([Wb[:, vh0:vh0 + 8], Wa[:, vh0:vh0 + 8]], axis=1)
        wba = np.ascontiguousarray(wba_full.reshape(16, 128, 16).transpose(1, 0, 2))
        in_maps.append({
            "x0": np.ascontiguousarray(hid[bb, (c % 4) * LT:(c % 4) * LT + LT, :]),
            "w1Ts": np.ascontiguousarray(w1T[c * 256:(c + 1) * 256].reshape(2, 128, 8192)),
            "w2Ts": np.ascontiguousarray(w2T[c * 256:(c + 1) * 256].reshape(2, 128, 8192)),
            "w3Ts": np.ascontiguousarray(w3T[c * 1024:(c + 1) * 1024].reshape(8, 128, 2048)),
            "g1in": np.asarray(L0_ffn_g1, np.float32).reshape(1, -1),
            "g2in": np.asarray(L0_ffn_g2, np.float32).reshape(1, -1),
            "g3in": np.asarray(L0_ffn_g3, np.float32).reshape(1, -1),
            "wqkvin": np.ascontiguousarray(wqkv_sl),
            "wzin": np.ascontiguousarray(Wz[:, vh0 * 128:vh0 * 128 + 1024]),
            "wbain": wba,
            "convdin": convd,
            "woutin": np.ascontiguousarray(Wout[vh0 * 128:vh0 * 128 + 1024, :]),
            "alogin": np.ascontiguousarray(np.asarray(L0_attn_A_log, np.float32)[vh0:vh0 + 8]
                                           .reshape(8, 1)),
            "dtbin": np.ascontiguousarray(np.asarray(L0_attn_dt_bias, np.float32)[vh0:vh0 + 8]
                                          .reshape(8, 1)),
            "normwin": np.asarray(L0_attn_norm_w, np.float32).reshape(1, 128),
            "identin": ident,
            "masksin": masks,
            "sel8in": sel8,
        })

    key = T
    if key not in _prog_cache:
        _prog_cache[key] = build_program(T)
    nc = _prog_cache[key]
    res = run_bass_kernel_spmd(nc, in_maps, core_ids=list(range(NCORE)))
    outp = np.empty((B, T, D), np.float32)
    for c in range(NCORE):
        outp[c // 4, (c % 4) * LT:(c % 4) * LT + LT, :] = res.results[c]["out"]
    return outp


# version 09
# speedup vs baseline: 1.4589x; 1.0498x over previous
import numpy as np
import concourse.bass as bass
import concourse.mybir as mybir
from concourse.bass_utils import run_bass_kernel_spmd
from contextlib import ExitStack

F32 = mybir.dt.float32
BF16 = mybir.dt.bfloat16
I32 = mybir.dt.int32
AF = mybir.ActivationFunctionType
ALU = mybir.AluOpType
AX = mybir.AxisListType

D = 2048
HID = 8192
NCORE = 8
ATANH2 = float(2.0 * np.arctanh(0.5))
RG8 = [list(range(NCORE))]
RG4 = [[0, 1, 2, 3], [4, 5, 6, 7]]
ENGS = ("sy", "sq", "v", "t", "g")
RSQ2 = float(128.0 ** -0.5)


class Seq:
    def __init__(self, nc, es):
        self.nc = nc
        self.es = es
        self.sems = {e: es.enter_context(nc.semaphore(f"sq_{e}")) for e in ENGS}
        self.ops = {e: [] for e in ENGS}
        self.count = {e: 0 for e in ENGS}
        self.ccount = {}

    def _chan_sem(self, key):
        if key not in self.ccount:
            self.sems[key] = self.es.enter_context(self.nc.semaphore(f"sq_{key}"))
            self.ccount[key] = 0
        return key

    def raw(self, eng, fn, deps=(), dma=False, chan=None, serial=True):
        deps = [d for d in deps if d is not None]
        if dma:
            key = self._chan_sem(f"{eng}.{chan or 'd'}")
            if serial and self.ccount[key] > 0:
                deps.append((key, self.ccount[key]))
            self.ccount[key] += 16
            self.ops[eng].append((fn, deps, key, 16))
            return (key, self.ccount[key])
        if self.count[eng] > 0:
            deps.append((eng, self.count[eng]))
        self.count[eng] += 1
        self.ops[eng].append((fn, deps, eng, 1))
        return (eng, self.count[eng])

    def cc(self, build, deps):
        key = self._chan_sem("g.cc")
        deps = list(deps) + ([(key, self.ccount[key])] if self.ccount[key] else [])
        self.ccount[key] += 1
        val = self.ccount[key]

        def f(e):
            build(e).then_inc(self.sems[key], 1)
            return e.nop()
        self.raw("g", f, [d for d in deps if d is not None])
        return (key, val)

    def barrier(self):
        marks = {e: self.count[e] for e in ENGS}
        cmarks = dict(self.ccount)
        out = {}
        for e in ENGS:
            deps = [(o, marks[o]) for o in ENGS if o != e and marks[o] > 0]
            deps += [(k, v) for k, v in cmarks.items() if v > 0]
            out[e] = self.raw(e, lambda en: en.nop(), deps=deps)
        return out

    def emit(self, block):
        sections = {"sy": block.sync, "sq": block.scalar, "v": block.vector,
                    "t": block.tensor, "g": block.gpsimd}
        for eng_name in ENGS:
            def make(eng_name=eng_name):
                def body(en):
                    last = {}
                    done = 0
                    for fn, waits, skey, inc in self.ops[eng_name]:
                        wmax = {}
                        for (oe, val) in waits:
                            if wmax.get(oe, -1) < val:
                                wmax[oe] = val
                        for (oe, val) in wmax.items():
                            if last.get(oe, 0) >= val:
                                continue
                            if oe == eng_name and val > done:
                                raise RuntimeError("forward same-engine dep")
                            en.wait_ge(self.sems[oe], val)
                            last[oe] = val
                        ins = fn(en)
                        if isinstance(ins, (list, tuple)):
                            ins = ins[-1]
                        ins.then_inc(self.sems[skey], inc)
                        if skey == eng_name:
                            done += inc
                return body
            sections[eng_name](make())


class Buf:
    def __init__(self):
        self.writer = None
        self.readers = []


ATTN_CFG = {"groups": None, "chunks": None, "vh": None, "rs": True, "proj": True}


def build_program(T, upto=9):
    LT = T // 4
    NT = LT // 128
    NG = T // 512
    assert T % 2048 == 0
    TBU = min(LT, 1024)
    NBU = LT // TBU
    NTU = TBU // 128

    nc = bass.Bass("TRN2", target_bir_lowering=False)

    x0 = nc.dram_tensor("x0", [LT, D], F32, kind="ExternalInput").ap()
    w1Ts = nc.dram_tensor("w1Ts", [2, 128, HID], F32, kind="ExternalInput").ap()
    w2Ts = nc.dram_tensor("w2Ts", [2, 128, HID], F32, kind="ExternalInput").ap()
    w3Ts = nc.dram_tensor("w3Ts", [8, 128, D], F32, kind="ExternalInput").ap()
    g1in = nc.dram_tensor("g1in", [1, D], F32, kind="ExternalInput").ap()
    g2in = nc.dram_tensor("g2in", [1, D], F32, kind="ExternalInput").ap()
    g3in = nc.dram_tensor("g3in", [1, HID], F32, kind="ExternalInput").ap()
    wqkvin = nc.dram_tensor("wqkvin", [D, 2048], F32, kind="ExternalInput").ap()
    wzin = nc.dram_tensor("wzin", [D, 1024], F32, kind="ExternalInput").ap()
    wbain = nc.dram_tensor("wbain", [128, 16, 16], F32, kind="ExternalInput").ap()
    convdin = nc.dram_tensor("convdin", [128, 16, 4, 128], F32, kind="ExternalInput").ap()
    woutin = nc.dram_tensor("woutin", [1024, D], F32, kind="ExternalInput").ap()
    alogin = nc.dram_tensor("alogin", [8, 1], F32, kind="ExternalInput").ap()
    dtbin = nc.dram_tensor("dtbin", [8, 1], F32, kind="ExternalInput").ap()
    normwin = nc.dram_tensor("normwin", [1, 128], F32, kind="ExternalInput").ap()
    identin = nc.dram_tensor("identin", [128, 128], F32, kind="ExternalInput").ap()
    masksin = nc.dram_tensor("masksin", [2, 128, 128], F32, kind="ExternalInput").ap()
    sel8in = nc.dram_tensor("sel8in", [8, 128], F32, kind="ExternalInput").ap()
    out = nc.dram_tensor("out", [LT, D], F32, kind="ExternalOutput").ap()

    agt_in = nc.dram_tensor("agt_in", [24, 128, 2048], BF16).ap()
    agt_out = nc.dram_tensor("agt_out", [NCORE, 24, 128, 2048], BF16).ap()
    ar_in = nc.dram_tensor("ar_in", [1, 4], F32).ap()
    ar_out = nc.dram_tensor("ar_out", [1, 4], F32).ap()
    ag1_in = nc.dram_tensor("ag1_in", [16, 128, LT], BF16).ap()
    ag1_out = nc.dram_tensor("ag1_out", [4, 16, 128, LT], BF16).ap()
    rs_in = nc.dram_tensor("rs_in", [T, D], BF16).ap()
    rs_out = nc.dram_tensor("rs_out", [LT, D], BF16).ap()
    zsp = nc.dram_tensor("zsp", [T // 128, 128, 1024], BF16).ap()
    betagsp = nc.dram_tensor("betagsp", [2, 8, T + 1], F32).ap()
    hgsp = nc.dram_tensor("hgsp", [NT, 128, HID], BF16).ap()
    x1sp = nc.dram_tensor("x1sp", [LT, D], F32).ap()
    x2sp = nc.dram_tensor("x2sp", [LT, D], F32).ap()

    es = ExitStack()
    seq = Seq(nc, es)

    def E(eng, fn, r=(), w=(), x=(), dma=False, chan=None, serial=True):
        deps = []
        for b in r:
            if b.writer is not None:
                deps.append(b.writer)
        for b in w:
            deps.extend(b.readers if b.readers else ([b.writer] if b.writer else []))
        deps.extend(x)
        h = seq.raw(eng, fn, deps, dma=dma, chan=chan, serial=serial)
        for b in r:
            b.readers.append(h)
        for b in w:
            b.writer = h
            b.readers = []
        return h

    ident = es.enter_context(nc.sbuf_tensor("ident", [128, 128], F32))
    identb = es.enter_context(nc.sbuf_tensor("identb", [128, 128], BF16))
    mnegst = es.enter_context(nc.sbuf_tensor("mnegst", [128, 128], F32))
    mtriu = es.enter_context(nc.sbuf_tensor("mtriu", [128, 128], F32))
    sel8 = es.enter_context(nc.sbuf_tensor("sel8", [8, 128], F32))
    onesrow = es.enter_context(nc.sbuf_tensor("onesrow", [1, 128], F32))
    onescol = es.enter_context(nc.sbuf_tensor("onescol", [128, 1], F32))
    betaP = es.enter_context(nc.sbuf_tensor("betaP", [128, 1], F32))
    betaN = es.enter_context(nc.sbuf_tensor("betaN", [128, 1], F32))
    probef = es.enter_context(nc.sbuf_tensor("probef", [128, 1], F32))
    probei = es.enter_context(nc.sbuf_tensor("probei", [128, 1], I32))
    normw_t = es.enter_context(nc.sbuf_tensor("normw_t", [128, 128], F32))
    scw = es.enter_context(nc.sbuf_tensor("scw", [128, 8], F32))
    cbuf = es.enter_context(nc.sbuf_tensor("cbuf", [128, 48], F32))
    _stkP = ExitStack()
    PSB = _stkP.enter_context(nc.psum_tensor("PSB", [128, 512], F32))
    B_PSB = Buf()
    B_cb = [Buf() for _ in range(48)]
    B_const = Buf()

    def cb(j):
        return cbuf[:, j:j + 1]

    E("sy", lambda e: e.dma_start(out=ident[:], in_=identin), w=[B_const], dma=True)
    E("sy", lambda e: e.dma_start(out=mnegst[:], in_=masksin[0]), w=[B_const], dma=True)
    E("sy", lambda e: e.dma_start(out=mtriu[:], in_=masksin[1]), w=[B_const], dma=True)
    E("sy", lambda e: e.dma_start(out=sel8[:], in_=sel8in), w=[B_const], dma=True)
    E("sy", lambda e: e.dma_start(out=normw_t[0:1, :], in_=normwin), w=[B_const], dma=True)
    E("v", lambda e: e.memset(onesrow[:], 1.0), w=[B_const])
    E("v", lambda e: e.memset(onescol[:], 1.0), w=[B_const])
    E("v", lambda e: e.tensor_copy(identb[:], ident[:]), r=[B_const], w=[B_const])
    E("v", lambda e: e.memset(probef[:], 1.5), w=[B_const])
    E("v", lambda e: e.tensor_copy(probei[:], probef[:]), r=[B_const], w=[B_const])
    E("v", lambda e: e.tensor_scalar(betaP[:], probei[:], -2.0, -256.5, ALU.add, ALU.mult),
      r=[B_const], w=[B_const])
    E("v", lambda e: e.tensor_scalar(betaN[:], probei[:], -2.0, 256.0, ALU.add, ALU.mult),
      r=[B_const], w=[B_const])
    E("t", lambda e: e.matmul(PSB[:, 0:128], onesrow[:], normw_t[0:1, :], start=True, stop=True),
      r=[B_const], w=[B_PSB])
    cdep = E("v", lambda e: e.tensor_copy(normw_t[:], PSB[:, 0:128]), r=[B_PSB], w=[B_const])

    wtiles = []
    for wi, src in ((0, w1Ts), (1, w2Ts)):
        for ch in range(2):
            for c4 in range(4):
                wtiles.append((src[ch, :, c4 * 2048:(c4 + 1) * 2048], wi * 8 + ch * 4 + c4, wi))
    for rc in range(8):
        wtiles.append((w3Ts[rc, :, :], 16 + rc, 2))

    B_scw = Buf()
    _stk1 = ExitStack()
    wld = _stk1.enter_context(nc.sbuf_tensor("wld", [128, 2, 2048], F32))
    wq1 = _stk1.enter_context(nc.sbuf_tensor("wq1", [128, 2048], F32))
    wq2 = _stk1.enter_context(nc.sbuf_tensor("wq2", [128, 2048], F32))
    wqi = _stk1.enter_context(nc.sbuf_tensor("wqi", [128, 2048], I32))
    wqb = _stk1.enter_context(nc.sbuf_tensor("wqb", [128, 2048], BF16))
    wdmp = _stk1.enter_context(nc.sbuf_tensor("wdmp", [128, 2048], F32))
    acc3 = _stk1.enter_context(nc.sbuf_tensor("acc3", [128, 4], F32))
    arb = _stk1.enter_context(nc.sbuf_tensor("arb", [1, 4], F32))
    if True:
        B_wld = [Buf(), Buf()]
        B_acc = Buf()
        B_tmp = Buf()
        B_arb = Buf()
        E("v", lambda e: e.memset(acc3[:], 0.0), w=[B_acc])
        for i, (src, entry, m) in enumerate(wtiles):
            b = wld[:, i % 2, :]
            E("sy", lambda e, s=src, b=b: e.dma_start(out=b, in_=s), w=[B_wld[i % 2]], dma=True)
            E("sq", lambda e, b=b, m=m: e.activation(wdmp[:], b, AF.Abs, accum_out=cb(8 + m)),
              r=[B_wld[i % 2]], w=[B_tmp, B_cb[8 + m]])
            E("v", lambda e, m=m: e.tensor_tensor(acc3[:, m:m + 1], acc3[:, m:m + 1], cb(8 + m), ALU.add),
              r=[B_cb[8 + m], B_acc], w=[B_acc])
        E("t", lambda e: e.matmul(PSB[:4, 0:1], acc3[:], onescol[:], start=True, stop=True),
          r=[B_acc, B_const], w=[B_PSB])
        E("v", lambda e: e.tensor_copy(acc3[:4, 0:1], PSB[:4, 0:1]), r=[B_PSB], w=[B_acc])
        E("t", lambda e: e.transpose(PSB[0:1, 128:132], acc3[:4, 0:1], ident[0:4, 0:4]),
          r=[B_acc, B_const], w=[B_PSB])
        E("v", lambda e: e.tensor_copy(arb[:], PSB[0:1, 128:132]), r=[B_PSB], w=[B_arb])
        h0 = E("sy", lambda e: e.dma_start(out=ar_in, in_=arb[:]), r=[B_arb], dma=True)
        h = seq.cc(lambda e: e.collective_compute("AllReduce", ALU.add, replica_groups=RG8,
                                                  ins=[ar_in.opt()], outs=[ar_out.opt()]), [h0])
        E("sy", lambda e: e.dma_start(out=arb[:], in_=ar_out), w=[B_arb], x=[h], dma=True)
        E("t", lambda e: e.matmul(PSB[:, 128:132], onesrow[:], arb[:], start=True, stop=True),
          r=[B_arb, B_const], w=[B_PSB])
        E("v", lambda e: e.tensor_scalar(cbuf[:, 0:4], PSB[:, 128:132], 1.0 / (2048.0 * 8192.0),
                                         None, ALU.mult), r=[B_PSB], w=[B_cb[0]])
        E("v", lambda e: e.tensor_scalar(cbuf[:, 4:8], cbuf[:, 0:4], 1e-8, ATANH2, ALU.add, ALU.mult),
          r=[B_cb[0]], w=[B_cb[4]])
        E("v", lambda e: e.reciprocal(scw[:, 0:4], cbuf[:, 4:8]), r=[B_cb[4]], w=[B_scw])
        E("v", lambda e: e.tensor_scalar(cbuf[:, 8:12], cbuf[:, 0:4], -1.0, 1.0, ALU.mult, ALU.add),
          r=[B_cb[0]], w=[B_cb[8]])
        E("v", lambda e: e.reciprocal(cbuf[:, 12:16], cbuf[:, 8:12]), r=[B_cb[8]], w=[B_cb[12]])
        E("v", lambda e: e.tensor_scalar(cbuf[:, 16:20], cbuf[:, 0:4], 1.0, None, ALU.add),
          r=[B_cb[0]], w=[B_cb[16]])
        E("v", lambda e: e.tensor_tensor(cbuf[:, 20:24], cbuf[:, 16:20], cbuf[:, 12:16], ALU.mult),
          r=[B_cb[16], B_cb[12]], w=[B_cb[20]])
        E("sq", lambda e: e.activation(cbuf[:, 24:28], cbuf[:, 20:24], AF.Ln), r=[B_cb[20]], w=[B_cb[24]])
        E("v", lambda e: e.tensor_scalar(scw[:, 4:8], cbuf[:, 24:28], 0.5, None, ALU.mult),
          r=[B_cb[24], B_scw], w=[B_scw])

        prev_store = None
        for i, (src, entry, m) in enumerate(wtiles):
            b = wld[:, i % 2, :]
            E("sy", lambda e, s=src, b=b: e.dma_start(out=b, in_=s), w=[B_wld[i % 2]], dma=True)
            E("sq", lambda e, b=b, m=m: e.activation(wq1[:], b, AF.Copy, scale=scw[:, m:m + 1]),
              r=[B_wld[i % 2], B_scw], w=[B_tmp])
            E("v", lambda e: e.tensor_scalar(wq2[:], wq1[:], 1.4, -1.4, ALU.min, ALU.max),
              r=[B_tmp], w=[B_tmp])
            E("sq", lambda e: e.activation(wq1[:], wq2[:], AF.Identity, bias=betaP[:, 0:1]),
              r=[B_tmp, B_const], w=[B_tmp])
            E("v", lambda e: e.tensor_copy(wqi[:], wq1[:]), r=[B_tmp], w=[B_tmp])
            E("v", lambda e: e.tensor_scalar(wqb[:], wqi[:], betaN[:, 0:1], None, ALU.add),
              r=[B_tmp, B_const], w=[B_tmp])
            prev_store = E("sy", lambda e, entry=entry: e.dma_start(out=agt_in[entry], in_=wqb[:]),
                           r=[B_tmp], x=[prev_store], dma=True)
        g_agt = seq.cc(lambda e: e.collective_compute("AllGather", ALU.bypass, replica_groups=RG8,
                                                      ins=[agt_in.opt()], outs=[agt_out.opt()]), [prev_store])
        _ = g_agt
    _stk1.close()
    _stkP.close()
    seq.barrier()

    def w_up_tile(wi, ic, oc):
        return agt_out[ic // 2, wi * 8 + (ic % 2) * 4 + oc // 4, :, (oc % 4) * 512:(oc % 4) * 512 + 512]

    def w_dn_tile(icd, od):
        return agt_out[icd // 8, 16 + icd % 8, :, od * 512:od * 512 + 512]

    def ln_chain(src_ap, r_bufs, dst_ap, w_bufs, dmp_ap, dmp_buf, base, eps, x=()):
        E("v", lambda e: e.tensor_reduce(cb(base), src_ap, AX.X, ALU.add),
          r=r_bufs, w=[B_cb[base]], x=list(x))
        E("v", lambda e: e.tensor_scalar(cb(base + 1), cb(base), -1.0 / D, None, ALU.mult),
          r=[B_cb[base]], w=[B_cb[base + 1]])
        E("sq", lambda e: e.activation(dmp_ap, src_ap, AF.Square, bias=cb(base + 1),
                                       accum_out=cb(base + 2)),
          r=r_bufs + [B_cb[base + 1]], w=[dmp_buf, B_cb[base + 2]])
        E("v", lambda e: e.tensor_scalar(cb(base + 3), cb(base + 2), 1.0 / D, eps, ALU.mult, ALU.add),
          r=[B_cb[base + 2]], w=[B_cb[base + 3]])
        E("sq", lambda e: e.activation(cb(base + 4), cb(base + 3), AF.Ln),
          r=[B_cb[base + 3]], w=[B_cb[base + 4]])
        E("sq", lambda e: e.activation(cb(base + 5), cb(base + 4), AF.Exp, scale=-0.5),
          r=[B_cb[base + 4]], w=[B_cb[base + 5]])
        return E("v", lambda e: e.tensor_scalar(dst_ap, src_ap, cb(base + 1), cb(base + 5),
                                                ALU.add, ALU.mult),
                 r=r_bufs + [B_cb[base + 1], B_cb[base + 5]], w=w_bufs)

    last_sh = [None]
    last_so = [None]
    c1s = es.enter_context(nc.sbuf_tensor("c1s", [128, NT], F32))
    c2s = es.enter_context(nc.sbuf_tensor("c2s", [128, NT], F32))
    c3s = es.enter_context(nc.sbuf_tensor("c3s", [128, NT], F32))
    ssqr = es.enter_context(nc.sbuf_tensor("ssqr", [128, NT], F32))
    amaxr = es.enter_context(nc.sbuf_tensor("amaxr", [128, NT], F32))
    B_c1s = [Buf() for _ in range(NT)]
    B_c2s = [Buf() for _ in range(NT)]
    B_c3s = [Buf() for _ in range(NT)]
    B_ssqr = [Buf() for _ in range(NT)]
    B_amaxr = [Buf() for _ in range(NT)]

    def emit_ffn(call):
        x_src = x0 if call == 1 else x2sp
        ag1_last = [None]
        out_last = [None]
        hsp_done = {}
        _stk2 = ExitStack()
        gt3 = _stk2.enter_context(nc.sbuf_tensor(f"f{call}_gt3", [128, HID], BF16))
        g12t = _stk2.enter_context(nc.sbuf_tensor(f"f{call}_g12t", [128, 2, D], BF16))
        xin = _stk2.enter_context(nc.sbuf_tensor(f"f{call}_xin", [128, 2, D], F32))
        xln = _stk2.enter_context(nc.sbuf_tensor(f"f{call}_xln", [128, D], F32))
        ytmp = _stk2.enter_context(nc.sbuf_tensor(f"f{call}_ytmp", [128, D], F32))
        kint = _stk2.enter_context(nc.sbuf_tensor(f"f{call}_kint", [128, D], I32))
        kbf = _stk2.enter_context(nc.sbuf_tensor(f"f{call}_kbf", [128, 2, D], BF16))
        fdmp = _stk2.enter_context(nc.sbuf_tensor(f"f{call}_fdmp", [128, D], F32))
        k1T = _stk2.enter_context(nc.sbuf_tensor(f"f{call}_k1T", [128, 16, TBU], BF16))
        k2T = _stk2.enter_context(nc.sbuf_tensor(f"f{call}_k2T", [128, 16, TBU], BF16))
        wu0 = _stk2.enter_context(nc.sbuf_tensor(f"f{call}_wu0", [128, 8, 2, 512], BF16))
        wu1 = _stk2.enter_context(nc.sbuf_tensor(f"f{call}_wu1", [128, 8, 2, 512], BF16))
        wut = (wu0, wu1)
        silu1 = _stk2.enter_context(nc.sbuf_tensor(f"f{call}_silu1", [128, 512], F32))
        htile = _stk2.enter_context(nc.sbuf_tensor(f"f{call}_htile", [128, 512], BF16))
        hgt = _stk2.enter_context(nc.sbuf_tensor(f"f{call}_hgt", [128, 2, 512], BF16))
        ps1a = _stk2.enter_context(nc.psum_tensor(f"f{call}_ps1a", [128, 512], F32))
        ps1b = _stk2.enter_context(nc.psum_tensor(f"f{call}_ps1b", [128, 512], F32))
        ps2a = _stk2.enter_context(nc.psum_tensor(f"f{call}_ps2a", [128, 512], F32))
        ps2b = _stk2.enter_context(nc.psum_tensor(f"f{call}_ps2b", [128, 512], F32))
        psg = _stk2.enter_context(nc.psum_tensor(f"f{call}_psg", [128, 512], F32))
        if True:

            B_g = Buf()
            B_gs = Buf()
            B_psg = Buf()
            with nc.sbuf_tensor(f"f{call}_gstg", [1, 512], F32) as gstg:
                jobs = ([(g3in, gt3[:, j * 512:(j + 1) * 512], j) for j in range(16)] +
                        [(g1in, g12t[:, 0, j * 512:(j + 1) * 512], j) for j in range(4)] +
                        [(g2in, g12t[:, 1, j * 512:(j + 1) * 512], j) for j in range(4)])
                for srcT, dst, j in jobs:
                    E("sy", lambda e, srcT=srcT, j=j: e.dma_start(
                        out=gstg[:], in_=srcT[0:1, j * 512:(j + 1) * 512]), w=[B_gs], dma=True)
                    E("t", lambda e: e.matmul(psg[:, 0:512], onesrow[:], gstg[:],
                                              start=True, stop=True),
                      r=[B_const, B_gs], w=[B_psg])
                    E("v", lambda e, dst=dst: e.tensor_copy(dst, psg[:, 0:512]),
                      r=[B_psg], w=[B_g])

            B_xin = [Buf(), Buf()]
            B_xln = Buf()
            B_y = Buf()
            B_ki = Buf()
            B_kb = [Buf(), Buf()]
            B_kT = [[Buf() for _ in range(NTU)], [Buf() for _ in range(NTU)]]
            last_tk = {0: None, 1: None}
            B_wu = [Buf(), Buf()]
            last_wu = {0: None, 1: None}
            B_psy = [[Buf(), Buf()], [Buf(), Buf()]]
            B_si = Buf()
            B_ht = Buf()
            B_hg = [Buf(), Buf()]
            B_fd = Buf()
            E("v", lambda e: e.memset(ssqr[:], 0.0), w=B_ssqr)
            E("g", lambda e: e.memset(amaxr[:], 0.0), w=B_amaxr)

            for tb in range(NBU):
                for tl in range(NTU):
                    tt = tb * NTU + tl
                    xb = xin[:, tt % 2, :]
                    E("sy", lambda e, tt=tt, xb=xb: e.dma_start(
                        out=xb, in_=x_src[tt * 128:(tt + 1) * 128, :]), w=[B_xin[tt % 2]], dma=True,
                      chan="x" + str(tt % 2))
                    ln_chain(xb, [B_xin[tt % 2]], xln[:], [B_xln], fdmp[:], B_fd, 0, 1e-5)
                    E("sq", lambda e: e.activation(fdmp[:], xln[:], AF.Square, accum_out=cb(6)),
                      r=[B_xln], w=[B_fd, B_cb[6]])
                    E("v", lambda e: e.tensor_scalar(cb(7), cb(6), 1.0 / D, 1e-8, ALU.mult, ALU.add),
                      r=[B_cb[6]], w=[B_cb[7]])
                    E("sq", lambda e: e.activation(cb(8), cb(7), AF.Ln), r=[B_cb[7]], w=[B_cb[8]])
                    E("sq", lambda e: e.activation(cb(9), cb(8), AF.Exp, scale=-0.5),
                      r=[B_cb[8]], w=[B_cb[9]])
                    for gi in range(2):
                        kb = kbf[:, gi, :]
                        kT = (k1T, k2T)[gi]
                        cs = (c1s, c2s)[gi]
                        Bcs = (B_c1s, B_c2s)[gi][tt]
                        E("v", lambda e, gi=gi: e.scalar_tensor_tensor(
                            ytmp[:], xln[:], cb(9), g12t[:, gi, :], ALU.mult, ALU.mult),
                          r=[B_xln, B_cb[9], B_g], w=[B_y])
                        E("v", lambda e: e.tensor_reduce(cb(10), ytmp[:], AX.X, ALU.max,
                                                         apply_absolute_value=True),
                          r=[B_y], w=[B_cb[10]])
                        E("v", lambda e: e.tensor_scalar(cb(11), cb(10), 1e-4, None, ALU.max),
                          r=[B_cb[10]], w=[B_cb[11]])
                        E("v", lambda e: e.reciprocal(cb(12), cb(11)), r=[B_cb[11]], w=[B_cb[12]])
                        E("v", lambda e: e.tensor_scalar(cb(13), cb(12), 127.0, None, ALU.mult),
                          r=[B_cb[12]], w=[B_cb[13]])
                        E("v", lambda e, cs=cs, tt=tt, gi=gi: e.scalar_tensor_tensor(
                            cs[:, tt:tt + 1], cb(11), 1.0 / 127.0, scw[:, 4 + gi:5 + gi],
                            ALU.mult, ALU.mult), r=[B_cb[11], B_scw], w=[Bcs])
                        E("sq", lambda e: e.activation(ytmp[:], ytmp[:], AF.Identity, scale=cb(13),
                                                       bias=betaP[:, 0:1]),
                          r=[B_y, B_cb[13], B_const], w=[B_y])
                        E("v", lambda e: e.tensor_copy(kint[:], ytmp[:]), r=[B_y], w=[B_ki])
                        E("v", lambda e, kb=kb: e.tensor_scalar(kb, kint[:], betaN[:, 0:1], None, ALU.add),
                          r=[B_ki, B_const], w=[B_kb[gi]])
                        for ic in range(16):
                            h_tk = E("sy",
                              lambda e, kb=kb, kT=kT, ic=ic, tl=tl: e.dma_start(
                                out=kT[:, ic, tl * 128:(tl + 1) * 128],
                                in_=kb[:, ic * 128:(ic + 1) * 128], transpose=True),
                              r=[B_kb[gi]] if ic == 15 else [], w=[B_kT[gi][tl]],
                              x=([last_tk[gi]] if ic == 0 else []) +
                                ([B_kb[gi].writer] if ic < 15 else []),
                              dma=True, chan="tk" + str(gi), serial=False)
                        last_tk[gi] = h_tk
                for oc in range(16):
                    for wi in range(2):
                        for ch in range(2):
                            h_wu = E("sq", lambda e, wi=wi, oc=oc, ch=ch: e.dma_start(
                                out=wut[wi][:, :, ch, :],
                                in_=agt_out[:, wi * 8 + ch * 4 + oc // 4, :,
                                            (oc % 4) * 512:(oc % 4) * 512 + 512]
                                    .rearrange("r p n -> p r n")),
                              w=[B_wu[wi]],
                              x=[g_agt] + ([last_wu[wi]] if ch == 0 else []),
                              dma=True, chan="wu" + str(wi), serial=False)
                            last_wu[wi] = h_wu
                    for tl in range(NTU):
                        tt = tb * NTU + tl
                        psy1 = (ps1a, ps1b)[tl % 2]
                        psy2 = (ps2a, ps2b)[tl % 2]

                        def mk_mm(tl=tl, psy1=psy1, psy2=psy2):
                            def f(e):
                                ins = []
                                for ic in range(16):
                                    ins.append(e.matmul(psy1[:], k1T[:, ic, tl * 128:(tl + 1) * 128],
                                                        wu0[:, ic // 2, ic % 2, :],
                                                        start=(ic == 0), stop=(ic == 15)))
                                for ic in range(16):
                                    ins.append(e.matmul(psy2[:], k2T[:, ic, tl * 128:(tl + 1) * 128],
                                                        wu1[:, ic // 2, ic % 2, :],
                                                        start=(ic == 0), stop=(ic == 15)))
                                return ins
                            return f
                        E("t", mk_mm(), r=[B_kT[0][tl], B_kT[1][tl], B_wu[0], B_wu[1]],
                          w=[B_psy[0][tl % 2], B_psy[1][tl % 2]])
                        E("sq", lambda e, psy1=psy1, tt=tt: e.activation(
                            silu1[:], psy1[:], AF.Silu, scale=c1s[:, tt:tt + 1]),
                          r=[B_psy[0][tl % 2], B_c1s[tt]], w=[B_si])
                        E("v", lambda e, psy2=psy2, tt=tt: e.scalar_tensor_tensor(
                            htile[:], psy2[:], c2s[:, tt:tt + 1], silu1[:], ALU.mult, ALU.mult),
                          r=[B_psy[1][tl % 2], B_c2s[tt], B_si], w=[B_ht])
                        E("sq", lambda e: e.activation(fdmp[:, 0:512], htile[:], AF.Square,
                                                       accum_out=cb(14)),
                          r=[B_ht], w=[B_fd, B_cb[14]])
                        E("v", lambda e, tt=tt: e.tensor_tensor(
                            ssqr[:, tt:tt + 1], ssqr[:, tt:tt + 1], cb(14), ALU.add),
                          r=[B_cb[14], B_ssqr[tt]], w=[B_ssqr[tt]])
                        hb = hgt[:, tl % 2, :]
                        E("v", lambda e, oc=oc, hb=hb: e.tensor_tensor(
                            hb, htile[:], gt3[:, oc * 512:(oc + 1) * 512], ALU.mult),
                          r=[B_ht, B_g], w=[B_hg[tl % 2]])
                        E("v", lambda e, hb=hb: e.tensor_reduce(cb(15), hb, AX.X, ALU.max,
                                                                apply_absolute_value=True),
                          r=[B_hg[tl % 2]], w=[B_cb[15]])
                        E("v", lambda e, tt=tt: e.tensor_tensor(
                            amaxr[:, tt:tt + 1], amaxr[:, tt:tt + 1], cb(15), ALU.max),
                          r=[B_cb[15], B_amaxr[tt]], w=[B_amaxr[tt]])
                        hsp_done[(oc, tt)] = E("sy", lambda e, oc=oc, tt=tt, hb=hb: e.dma_start(
                            out=hgsp[tt, :, oc * 512:(oc + 1) * 512], in_=hb),
                          r=[B_hg[tl % 2]],
                          x=[last_sh[0]] if (oc == 0 and tt == tb * NTU) else [],
                          dma=True, chan="sh")
                        last_sh[0] = hsp_done[(oc, tt)]

        _stk2.close()
        seq.barrier()
        _stk3 = ExitStack()
        khT = _stk3.enter_context(nc.sbuf_tensor(f"f{call}_khT", [128, 64, 512], BF16))
        wdt = _stk3.enter_context(nc.sbuf_tensor(f"f{call}_wdt", [128, 8, 8, 512], BF16))
        hgl = _stk3.enter_context(nc.sbuf_tensor(f"f{call}_hgl", [128, 2048], BF16))
        uh = _stk3.enter_context(nc.sbuf_tensor(f"f{call}_uh", [128, 2048], F32))
        kih = _stk3.enter_context(nc.sbuf_tensor(f"f{call}_kih", [128, 2048], I32))
        khb = _stk3.enter_context(nc.sbuf_tensor(f"f{call}_khb", [128, 2048], BF16))
        oacc = _stk3.enter_context(nc.sbuf_tensor(f"f{call}_oacc", [128, 512], F32))
        xres = _stk3.enter_context(nc.sbuf_tensor(f"f{call}_xres", [128, D], F32))
        x1t = _stk3.enter_context(nc.sbuf_tensor(f"f{call}_x1t", [128, D], F32))
        cdmp = _stk3.enter_context(nc.sbuf_tensor(f"f{call}_cdmp", [128, D], F32))
        x2nb = _stk3.enter_context(nc.sbuf_tensor(f"f{call}_x2nb", [128, D], BF16))
        x2nT = _stk3.enter_context(nc.sbuf_tensor(f"f{call}_x2nT", [128, 16, 128], BF16))
        psoa = _stk3.enter_context(nc.psum_tensor(f"f{call}_psoa", [128, 512], F32))
        psob = _stk3.enter_context(nc.psum_tensor(f"f{call}_psob", [128, 512], F32))
        if True:
            B_khT = [Buf() for _ in range(4)]
            B_khTb = [Buf() for _ in range(4)]
            last_tkhb = [None]
            B_wd = Buf()
            last_wd = [None]
            last_tkh = [None]
            last_tx = [None]
            B_hgl = Buf()
            B_u = Buf()
            B_k2 = Buf()
            B_k3 = Buf()
            B_oa = Buf()
            B_xr = Buf()
            B_pso = [Buf(), Buf()]
            B_x1t = Buf()
            B_x2n = Buf()
            B_x2T = Buf()
            B_cd = Buf()
            for bd in range(LT // 512):
                for tl in range(4):
                    tt = bd * 4 + tl
                    E("v", lambda e, tt=tt: e.tensor_scalar(cb(16), ssqr[:, tt:tt + 1], 1.0 / HID,
                                                            1e-8, ALU.mult, ALU.add),
                      r=[B_ssqr[tt]], w=[B_cb[16]])
                    E("sq", lambda e: e.activation(cb(17), cb(16), AF.Ln), r=[B_cb[16]], w=[B_cb[17]])
                    E("sq", lambda e: e.activation(cb(18), cb(17), AF.Exp, scale=-0.5),
                      r=[B_cb[17]], w=[B_cb[18]])
                    E("v", lambda e, tt=tt: e.tensor_tensor(cb(19), amaxr[:, tt:tt + 1], cb(18), ALU.mult),
                      r=[B_amaxr[tt], B_cb[18]], w=[B_cb[19]])
                    E("v", lambda e: e.tensor_scalar(cb(20), cb(19), 1e-4, None, ALU.max),
                      r=[B_cb[19]], w=[B_cb[20]])
                    E("v", lambda e: e.reciprocal(cb(21), cb(20)), r=[B_cb[20]], w=[B_cb[21]])
                    E("v", lambda e: e.tensor_scalar(cb(22), cb(21), 127.0, None, ALU.mult),
                      r=[B_cb[21]], w=[B_cb[22]])
                    E("v", lambda e: e.tensor_tensor(cb(23), cb(22), cb(18), ALU.mult),
                      r=[B_cb[22], B_cb[18]], w=[B_cb[23]])
                    E("v", lambda e, tt=tt: e.scalar_tensor_tensor(
                        c3s[:, tt:tt + 1], cb(20), 0.5 / 127.0, scw[:, 6:7], ALU.mult, ALU.mult),
                      r=[B_cb[20], B_scw], w=[B_c3s[tt]])
                    for q4 in range(4):
                        E("sy", lambda e, tt=tt, q4=q4: e.dma_start(
                            out=hgl[:], in_=hgsp[tt, :, q4 * 2048:(q4 + 1) * 2048]),
                          w=[B_hgl], x=[hsp_done[(15, NT - 1)]], dma=True, chan="hl")
                        E("sq", lambda e: e.activation(uh[:], hgl[:], AF.Identity, scale=cb(23),
                                                       bias=betaP[:, 0:1]),
                          r=[B_hgl, B_cb[23], B_const], w=[B_u])
                        E("v", lambda e: e.tensor_copy(kih[:], uh[:]), r=[B_u], w=[B_k2])
                        E("v", lambda e: e.tensor_scalar(khb[:], kih[:], betaN[:, 0:1], None, ALU.add),
                          r=[B_k2, B_const], w=[B_k3])
                        for ic in range(16):
                            icd = q4 * 16 + ic
                            h_kh = E("sy",
                              lambda e, icd=icd, ic=ic, tl=tl: e.dma_start(
                                out=khT[:, icd, tl * 128:(tl + 1) * 128],
                                in_=khb[:, ic * 128:(ic + 1) * 128], transpose=True),
                              r=[B_k3] if ic >= 14 else [],
                              w=[(B_khT if ic % 2 == 0 else B_khTb)[tl]],
                              x=(([last_tkh[0]] if (tl == 0 and q4 == 0 and ic == 0) else []) +
                                 ([last_tkhb[0]] if (tl == 0 and q4 == 0 and ic == 1) else []) +
                                 ([B_k3.writer] if ic < 14 else [])),
                              dma=True, chan="tkh", serial=False)

                last_tkh[0] = B_khT[3].writer
                last_tkhb[0] = B_khTb[3].writer
                ostores = {}
                for od in range(4):
                    wls = []
                    for rc in range(8):
                        h_wd = E("sq", lambda e, od=od, rc=rc: e.dma_start(
                            out=wdt[:, :, rc, :],
                            in_=agt_out[:, 16 + rc, :, od * 512:od * 512 + 512]
                                .rearrange("r p n -> p r n")),
                          w=[B_wd], x=[g_agt] + ([last_wd[0]] if rc == 0 else []),
                          dma=True, chan="wd", serial=False)
                        wls.append(h_wd)
                    last_wd[0] = wls[-1]
                    for tl in range(4):
                        tt = bd * 4 + tl
                        pso = (psoa, psob)[tl % 2]

                        def mk_dn(tl=tl, pso=pso):
                            def f(e):
                                ins = []
                                for icd in range(64):
                                    ins.append(e.matmul(pso[:], khT[:, icd, tl * 128:(tl + 1) * 128],
                                                        wdt[:, icd // 8, icd % 8, :],
                                                        start=(icd == 0), stop=(icd == 63)))
                                return ins
                            return f
                        E("t", mk_dn(), r=[B_khT[tl], B_khTb[tl], B_wd], w=[B_pso[tl % 2]], x=wls)
                        E("sy", lambda e, tt=tt, od=od: e.dma_start(
                            out=xres[:, od * 512:(od + 1) * 512],
                            in_=x_src[tt * 128:(tt + 1) * 128, od * 512:(od + 1) * 512]),
                          w=[B_xr], dma=True, chan="xr")
                        E("v", lambda e, tt=tt, od=od, pso=pso: e.scalar_tensor_tensor(
                            oacc[:], pso[:], c3s[:, tt:tt + 1], xres[:, od * 512:(od + 1) * 512],
                            ALU.mult, ALU.add), r=[B_pso[tl % 2], B_c3s[tt], B_xr], w=[B_oa])
                        dst = x1sp if call == 1 else out
                        st = E("sy", lambda e, tt=tt, od=od, dst=dst: e.dma_start(
                            out=dst[tt * 128:(tt + 1) * 128, od * 512:(od + 1) * 512], in_=oacc[:]),
                          r=[B_oa], x=[last_so[0]] if (od == 0 and tl == 0) else [],
                          dma=True, chan="so", serial=False)
                        last_so[0] = st
                        ostores[(od, tl)] = st
                        out_last[0] = st
                if call == 1:
                    for tl in range(4):
                        tt = bd * 4 + tl
                        E("sy", lambda e, tt=tt: e.dma_start(
                            out=x1t[:], in_=x1sp[tt * 128:(tt + 1) * 128, :]),
                          w=[B_x1t], x=[ostores[(3, 3)]], dma=True, chan="x1")
                        ln_chain(x1t[:], [B_x1t], x2nb[:], [B_x2n], cdmp[:], B_cd, 24, 1e-5)
                        for ic in range(16):
                            h_tx = E("sy", lambda e, ic=ic: e.dma_start(
                                out=x2nT[:, ic, :], in_=x2nb[:, ic * 128:(ic + 1) * 128], transpose=True),
                              r=[B_x2n] if ic == 15 else [], w=[B_x2T],
                              x=([last_tx[0]] if ic == 0 else []) +
                                ([B_x2n.writer] if ic < 15 else []),
                              dma=True, chan="tx", serial=False)
                        last_tx[0] = h_tx
                        ag1_last[0] = E("sy", lambda e, tt=tt: e.dma_start(
                            out=ag1_in[:, :, tt * 128:(tt + 1) * 128].rearrange("a p b -> p a b"),
                            in_=x2nT[:]), r=[B_x2T], dma=True, chan="sa", serial=False)
        _stk3.close()
        seq.barrier()
        return ag1_last[0] if call == 1 else out_last[0]

    def emit_zsweep():
        zstores = {}
        _stk5 = ExitStack()
        wzsb = _stk5.enter_context(nc.sbuf_tensor("wzsb", [128, 16, 1024], BF16))
        wbasb = _stk5.enter_context(nc.sbuf_tensor("wbasb", [128, 16, 16], BF16))
        batile = _stk5.enter_context(nc.sbuf_tensor("batile", [128, 16], F32))
        zt2 = _stk5.enter_context(nc.sbuf_tensor("zt2", [128, 1024], BF16))
        xTz = _stk5.enter_context(nc.sbuf_tensor("xTz", [128, 16, 128], BF16))
        bafm = _stk5.enter_context(nc.sbuf_tensor("bafm", [8, 2, T], F32))
        bg2 = _stk5.enter_context(nc.sbuf_tensor("bg2", [8, 2, T], F32))
        zcol = _stk5.enter_context(nc.sbuf_tensor("zcol", [8, 1], F32))
        psZ1 = _stk5.enter_context(nc.psum_tensor("psZ1", [128, 512], F32))
        psZ2 = _stk5.enter_context(nc.psum_tensor("psZ2", [128, 512], F32))
        psZ3 = _stk5.enter_context(nc.psum_tensor("psZ3", [128, 512], F32))
        B_zt2 = Buf(); B_bat = Buf(); B_xTz = Buf(); B_ba = Buf(); B_bg = Buf(); B_zc = Buf()
        B_pz1 = Buf(); B_pz2 = Buf(); B_pz3 = Buf()
        wz_l = []
        for ic in range(16):
            wz_l.append(E("g", lambda e, ic=ic: e.dma_start(
                out=wzsb[:, ic, :], in_=wzin[ic * 128:(ic + 1) * 128, :]), dma=True))
        wba_l = E("g", lambda e: e.dma_start(
            out=wbasb[:], in_=wbain), dma=True)
        for tt2 in range(T // 128):
            E("sy", lambda e, tt2=tt2: e.dma_start(
                out=xTz[:],
                in_=ag1_out[(tt2 * 128) // LT, :, :,
                            (tt2 * 128) % LT:(tt2 * 128) % LT + 128].rearrange("a p b -> p a b")),
              w=[B_xTz], x=[g_ag1], dma=True, chan="zx")

            def mk_zh(half):
                def f(e):
                    ins = []
                    for ic in range(16):
                        ins.append(e.matmul((psZ1 if half == 0 else psZ2)[:],
                                            xTz[:, ic, :],
                                            wzsb[:, ic, half * 512:half * 512 + 512],
                                            start=(ic == 0), stop=(ic == 15)))
                    return ins
                return f
            E("t", mk_zh(0), r=[B_xTz], w=[B_pz1], x=wz_l)
            E("t", mk_zh(1), r=[B_xTz], w=[B_pz2], x=wz_l)

            def mk_ba(e):
                ins = []
                for ic in range(16):
                    ins.append(e.matmul(psZ3[:, 0:16], xTz[:, ic, :],
                                        wbasb[:, ic, :], start=(ic == 0), stop=(ic == 15)))
                return ins
            E("t", mk_ba, r=[B_xTz], w=[B_pz3], x=[wba_l])
            E("sq", lambda e: e.activation(zt2[:, 0:512], psZ1[:], AF.Silu), r=[B_pz1], w=[B_zt2])
            E("sq", lambda e: e.activation(zt2[:, 512:1024], psZ2[:], AF.Silu), r=[B_pz2], w=[B_zt2])
            zstores[tt2] = E("sy", lambda e, tt2=tt2: e.dma_start(out=zsp[tt2], in_=zt2[:]),
                             r=[B_zt2], dma=True, chan="sz", serial=False)
            E("v", lambda e: e.tensor_copy(batile[:], psZ3[:, 0:16]), r=[B_pz3], w=[B_bat])
            for half in range(2):
                E("t", lambda e, half=half: e.transpose(psZ3[0:8, 128 + half * 128:256 + half * 128],
                                                        batile[:, half * 8:half * 8 + 8], ident[:]),
                  r=[B_bat, B_const], w=[B_pz3])
                E("v", lambda e, tt2=tt2, half=half: e.tensor_copy(
                    bafm[:, half, tt2 * 128:(tt2 + 1) * 128],
                    psZ3[0:8, 128 + half * 128:256 + half * 128]),
                  r=[B_pz3], w=[B_ba])
        l_dtb = E("sy", lambda e: e.dma_start(out=cbuf[0:8, 28:29], in_=dtbin), dma=True)
        l_alog = E("sy", lambda e: e.dma_start(out=cbuf[0:8, 29:30], in_=alogin), dma=True)
        E("sq", lambda e: e.activation(bg2[:, 0, :], bafm[:, 0, :], AF.Sigmoid), r=[B_ba], w=[B_bg])
        E("v", lambda e: e.tensor_scalar(bafm[:, 1, :], bafm[:, 1, :], cbuf[0:8, 28:29],
                                         None, ALU.add), r=[B_ba], w=[B_ba], x=[l_dtb])
        E("sq", lambda e: e.activation(bg2[:, 1, :], bafm[:, 1, :], AF.Abs), r=[B_ba], w=[B_bg])
        E("sq", lambda e: e.activation(bg2[:, 1, :], bg2[:, 1, :], AF.Exp, scale=-1.0),
          r=[B_bg], w=[B_bg])
        E("sq", lambda e: e.activation(bg2[:, 1, :], bg2[:, 1, :], AF.Ln, bias=1.0),
          r=[B_bg], w=[B_bg])
        E("v", lambda e: e.tensor_scalar(bafm[:, 1, :], bafm[:, 1, :], 0.0, None, ALU.max),
          r=[B_ba], w=[B_ba])
        E("v", lambda e: e.tensor_tensor(bg2[:, 1, :], bg2[:, 1, :], bafm[:, 1, :], ALU.add),
          r=[B_bg, B_ba], w=[B_bg])
        E("sq", lambda e: e.activation(cbuf[0:8, 30:31], cbuf[0:8, 29:30], AF.Exp),
          x=[l_alog], w=[B_cb[30]])
        E("v", lambda e: e.tensor_scalar(cbuf[0:8, 31:32], cbuf[0:8, 30:31], -1.0, None,
                                         ALU.mult), r=[B_cb[30]], w=[B_cb[31]])
        E("v", lambda e: e.tensor_scalar(bg2[:, 1, :], bg2[:, 1, :], cbuf[0:8, 31:32], None,
                                         ALU.mult), r=[B_bg, B_cb[31]], w=[B_bg])
        E("v", lambda e: e.tensor_tensor_scan(bg2[:, 1, :], bg2[:, 1, :], bafm[:, 1, :], 0.0,
                                              ALU.add, ALU.bypass), r=[B_bg, B_ba], w=[B_bg])
        E("v", lambda e: e.memset(zcol[:], 0.0), w=[B_zc])
        E("sy", lambda e: e.dma_start(out=betagsp[0, :, 0:1], in_=zcol[:], allow_slow_non_contiguous=True), r=[B_zc], dma=True)
        E("sy", lambda e: e.dma_start(out=betagsp[1, :, 0:1], in_=zcol[:], allow_slow_non_contiguous=True), r=[B_zc], dma=True)
        E("sy", lambda e: e.dma_start(out=betagsp[0, :, 1:T + 1], in_=bg2[:, 0, :]),
          r=[B_bg], dma=True)
        bgst = E("sy", lambda e: e.dma_start(out=betagsp[1, :, 1:T + 1], in_=bg2[:, 1, :]),
                 r=[B_bg], dma=True)
        _stk5.close()
        seq.barrier()
        return zstores, bgst

    def emit_attention(zstores, bgst):
        _stk4 = ExitStack()
        wq = _stk4.enter_context(nc.sbuf_tensor("wq", [128, 16, 2048], BF16))
        wo = _stk4.enter_context(nc.sbuf_tensor("wo", [128, 8, 2048], BF16))
        cdg = _stk4.enter_context(nc.sbuf_tensor("cdg", [128, 16, 4, 128], BF16))
        xT = _stk4.enter_context(nc.sbuf_tensor("xT", [128, 1, 16, 512], BF16))
        mxs = _stk4.enter_context(nc.sbuf_tensor("mxs", [128, 515], BF16))
        tails = _stk4.enter_context(nc.sbuf_tensor("tails", [128, 16, 4], BF16))
        qkv = _stk4.enter_context(nc.sbuf_tensor("qkv", [128, 2, 16, 512], BF16))
        bg8 = _stk4.enter_context(nc.sbuf_tensor("bg8", [8, 2, 129], F32))
        hrow = _stk4.enter_context(nc.sbuf_tensor("hrow", [8, 2, 128], F32))
        hcolT = _stk4.enter_context(nc.sbuf_tensor("hcolT", [128, 4, 8], F32))
        grow = _stk4.enter_context(nc.sbuf_tensor("grow", [1, 128], F32))
        gamr = _stk4.enter_context(nc.sbuf_tensor("gamr", [8, 1], F32))
        gamd = _stk4.enter_context(nc.sbuf_tensor("gamd", [8, 8], F32))
        gamc = _stk4.enter_context(nc.sbuf_tensor("gamc", [128, 8], F32))
        lrow = _stk4.enter_context(nc.sbuf_tensor("lrow", [1, 1024], F32))
        rcolk = _stk4.enter_context(nc.sbuf_tensor("rcolk", [128, 4], F32))
        qn = _stk4.enter_context(nc.sbuf_tensor("qn", [128, 4, 128], BF16))
        kn = _stk4.enter_context(nc.sbuf_tensor("kn", [128, 4, 128], BF16))
        sq2t = _stk4.enter_context(nc.sbuf_tensor("sq2t", [128, 128], F32))
        ktok = _stk4.enter_context(nc.sbuf_tensor("ktok", [128, 4, 128], BF16))
        vtok = _stk4.enter_context(nc.sbuf_tensor("vtok", [128, 8, 128], BF16))
        kbgt = _stk4.enter_context(nc.sbuf_tensor("kbgt", [128, 128], BF16))
        vbt = _stk4.enter_context(nc.sbuf_tensor("vbt", [128, 128], BF16))
        kdect = _stk4.enter_context(nc.sbuf_tensor("kdect", [128, 128], BF16))
        dect = _stk4.enter_context(nc.sbuf_tensor("dect", [128, 2, 128], F32))
        dmneg = _stk4.enter_context(nc.sbuf_tensor("dmneg", [128, 128], F32))
        dmTin = _stk4.enter_context(nc.sbuf_tensor("dmTin", [128, 128], F32))
        Amat = _stk4.enter_context(nc.sbuf_tensor("Amat", [128, 128], BF16))
        Bmat = _stk4.enter_context(nc.sbuf_tensor("Bmat", [128, 128], BF16))
        Apw = _stk4.enter_context(nc.sbuf_tensor("Apw", [128, 2, 128], BF16))
        Bpw = _stk4.enter_context(nc.sbuf_tensor("Bpw", [128, 2, 128], BF16))
        Prod = _stk4.enter_context(nc.sbuf_tensor("Prod", [128, 2, 128], BF16))
        vcorr = _stk4.enter_context(nc.sbuf_tensor("vcorr", [128, 128], BF16))
        kcdf = _stk4.enter_context(nc.sbuf_tensor("kcdf", [128, 128], BF16))
        vnew = _stk4.enter_context(nc.sbuf_tensor("vnew", [128, 128], BF16))
        intraT = _stk4.enter_context(nc.sbuf_tensor("intraT", [128, 128], BF16))
        o1s = _stk4.enter_context(nc.sbuf_tensor("o1s", [128, 128], F32))
        oc2 = _stk4.enter_context(nc.sbuf_tensor("oc2", [128, 128], F32))
        xg = _stk4.enter_context(nc.sbuf_tensor("xg", [128, 128], F32))
        nrm = _stk4.enter_context(nc.sbuf_tensor("nrm", [128, 128], BF16))
        nrmT = _stk4.enter_context(nc.sbuf_tensor("nrmT", [128, 8, 128], BF16))
        stm = _stk4.enter_context(nc.sbuf_tensor("stm", [128, 2, 8, 128], F32))
        stb = _stk4.enter_context(nc.sbuf_tensor("stb", [128, 128], BF16))
        ztile = _stk4.enter_context(nc.sbuf_tensor("ztile", [128, 1024], BF16))
        rst = _stk4.enter_context(nc.sbuf_tensor("rst", [128, 512], BF16))
        grow1 = _stk4.enter_context(nc.sbuf_tensor("grow1", [1, 128], F32))
        sq2t1 = _stk4.enter_context(nc.sbuf_tensor("sq2t1", [128, 128], F32))
        kbgt1 = _stk4.enter_context(nc.sbuf_tensor("kbgt1", [128, 128], BF16))
        vbt1 = _stk4.enter_context(nc.sbuf_tensor("vbt1", [128, 128], BF16))
        kdect1 = _stk4.enter_context(nc.sbuf_tensor("kdect1", [128, 128], BF16))
        dect1 = _stk4.enter_context(nc.sbuf_tensor("dect1", [128, 2, 128], F32))
        dmneg1 = _stk4.enter_context(nc.sbuf_tensor("dmneg1", [128, 128], F32))
        dmTin1 = _stk4.enter_context(nc.sbuf_tensor("dmTin1", [128, 128], F32))
        Amat1 = _stk4.enter_context(nc.sbuf_tensor("Amat1", [128, 128], BF16))
        Bmat1 = _stk4.enter_context(nc.sbuf_tensor("Bmat1", [128, 128], BF16))
        Apw1 = _stk4.enter_context(nc.sbuf_tensor("Apw1", [128, 2, 128], BF16))
        Bpw1 = _stk4.enter_context(nc.sbuf_tensor("Bpw1", [128, 2, 128], BF16))
        Prod1 = _stk4.enter_context(nc.sbuf_tensor("Prod1", [128, 2, 128], BF16))
        vcorr1 = _stk4.enter_context(nc.sbuf_tensor("vcorr1", [128, 128], BF16))
        kcdf1 = _stk4.enter_context(nc.sbuf_tensor("kcdf1", [128, 128], BF16))
        vnew1 = _stk4.enter_context(nc.sbuf_tensor("vnew1", [128, 128], BF16))
        intraT1 = _stk4.enter_context(nc.sbuf_tensor("intraT1", [128, 128], BF16))
        o1s1 = _stk4.enter_context(nc.sbuf_tensor("o1s1", [128, 128], F32))
        oc21 = _stk4.enter_context(nc.sbuf_tensor("oc21", [128, 128], F32))
        xg1 = _stk4.enter_context(nc.sbuf_tensor("xg1", [128, 128], F32))
        nrm1 = _stk4.enter_context(nc.sbuf_tensor("nrm1", [128, 128], BF16))
        stb1 = _stk4.enter_context(nc.sbuf_tensor("stb1", [128, 128], BF16))
        psP = _stk4.enter_context(nc.psum_tensor("psP", [128, 512], F32))
        psC = _stk4.enter_context(nc.psum_tensor("psC", [128, 512], F32))
        psK1 = _stk4.enter_context(nc.psum_tensor("psK1", [128, 512], F32))
        psK2 = _stk4.enter_context(nc.psum_tensor("psK2", [128, 512], F32))
        psE = _stk4.enter_context(nc.psum_tensor("psE", [128, 512], F32))
        psF = _stk4.enter_context(nc.psum_tensor("psF", [128, 512], F32))
        psE1 = _stk4.enter_context(nc.psum_tensor("psE1", [128, 512], F32))
        psF1 = _stk4.enter_context(nc.psum_tensor("psF1", [128, 512], F32))
        if True:

            BN = {}
            for nm in ("psP", "psC", "psK1", "psK2", "hrow", "hcolT", "grow", "gamr", "gamd",
                       "gamc", "lrow", "rcolk", "sq2t", "kbgt", "vbt", "kdect", "dect", "dmneg",
                       "dmTin", "Amat", "Bmat", "vcorr", "kcdf", "vnew", "intraT", "o1s", "oc2",
                       "xg", "nrm", "stb", "ztile", "mxs", "tails", "rst", "wcst"):
                BN[nm] = Buf()
            _bE = Buf(); _bF = Buf()
            _bE1 = Buf(); _bF1 = Buf()
            B_psE = [_bE, _bE, _bE, _bE]
            B_psF = [_bF, _bF, _bF, _bF]
            B_xT = [Buf()]
            B_qkv = [Buf(), Buf()]
            B_qn = [Buf() for _ in range(4)]
            last_tkv = [None]
            last_tb = [None]
            last_tn = [None]
            B_kn = [Buf() for _ in range(4)]
            B_kvtok = Buf()
            B_ktok = [B_kvtok] * 4
            B_vtok = [B_kvtok] * 8
            B_Apw = [Buf(), Buf()]
            B_Bpw = [Buf(), Buf()]
            B_Prod = [Buf(), Buf()]
            B_Apw1 = [Buf(), Buf()]
            B_Bpw1 = [Buf(), Buf()]
            B_Prod1 = [Buf(), Buf()]
            B_st = [[Buf() for _ in range(8)], [Buf() for _ in range(8)]]
            B_nrmT = [Buf(), Buf()]
            B_bg8 = Buf()
            VRS = [dict(grow=grow, dect=dect, dmneg=dmneg, dmTin=dmTin, Amat=Amat, Bmat=Bmat,
                        Apw=Apw, Bpw=Bpw, Prod=Prod, kbgt=kbgt, vbt=vbt, kdect=kdect,
                        vcorr=vcorr, kcdf=kcdf, stb=stb, vnew=vnew, o1s=o1s, oc2=oc2,
                        xg=xg, nrm=nrm, sq2t=sq2t, intraT=intraT),
                   dict(grow=grow1, dect=dect1, dmneg=dmneg1, dmTin=dmTin1, Amat=Amat1, Bmat=Bmat1,
                        Apw=Apw1, Bpw=Bpw1, Prod=Prod1, kbgt=kbgt1, vbt=vbt1, kdect=kdect1,
                        vcorr=vcorr1, kcdf=kcdf1, stb=stb1, vnew=vnew1, o1s=o1s1, oc2=oc21,
                        xg=xg1, nrm=nrm1, sq2t=sq2t1, intraT=intraT1)]
            VRB = [{k: BN[k] for k in VRS[0] if k in BN},
                   {k: Buf() for k in VRS[1] if k in BN}]

            pe = lambda q, w=128: psE[:, q * 128:q * 128 + w]
            pf = lambda q, w=128: psF[:, q * 128:q * 128 + w]

            B_wo = Buf()
            for ic in range(16):
                E("g", lambda e, ic=ic: e.dma_start(out=wq[:, ic, :],
                                                    in_=wqkvin[ic * 128:(ic + 1) * 128, :]),
                  w=[BN["wcst"]], dma=True)
            for kc in range(8):
                E("g", lambda e, kc=kc: e.dma_start(out=wo[:, kc, :],
                                                    in_=woutin[kc * 128:(kc + 1) * 128, :]),
                  w=[B_wo], dma=True)
            cd_l = E("g", lambda e: e.dma_start(
                out=cdg[:], in_=convdin), dma=True)
            E("v", lambda e: e.memset(tails[:], 0.0), w=[BN["tails"]])

            for vh in range(8):
                E("v", lambda e, vh=vh: e.memset(stm[:, 0, vh, :], 0.0), w=[B_st[0][vh]])

            rs_done = []
            NGx = ATTN_CFG["groups"] if ATTN_CFG["groups"] is not None else NG
            for gidx in range(NGx):
                gb = gidx % 2
                E("sy", lambda e, gidx=gidx, gb=gb: e.dma_start(
                    out=xT[:, 0].rearrange("p a b -> p a b"),
                    in_=ag1_out[(gidx * 512) // LT, :, :,
                                (gidx * 512) % LT:(gidx * 512) % LT + 512].rearrange("a p b -> p a b")),
                  w=[B_xT[0]], x=[g_ag1], dma=True, chan="zx")
                for oc in range(16):
                    def mk_p(oc=oc, gb=gb):
                        def f(e):
                            ins = []
                            for ic in range(16):
                                ins.append(e.matmul(psP[:], wq[:, ic, oc * 128:(oc + 1) * 128],
                                                    xT[:, 0, ic, :], start=(ic == 0), stop=(ic == 15)))
                            return ins
                        return f
                    E("t", mk_p(), r=[B_xT[0], BN["wcst"]], w=[BN["psP"]])
                    E("v", lambda e: e.tensor_copy(mxs[:, 3:515], psP[:]), r=[BN["psP"]], w=[BN["mxs"]])
                    E("v", lambda e, oc=oc: e.tensor_copy(mxs[:, 0:3], tails[:, oc, 0:3]),
                      r=[BN["tails"]], w=[BN["mxs"]])
                    E("v", lambda e, oc=oc: e.tensor_copy(tails[:, oc, 0:3], mxs[:, 512:515]),
                      r=[BN["mxs"]], w=[BN["tails"]])

                    def mk_c(oc=oc):
                        def f(e):
                            ins = []
                            for k in range(4):
                                ins.append(e.matmul(psC[:], cdg[:, oc, k, :], mxs[:, k:k + 512],
                                                    start=(k == 0), stop=(k == 3)))
                            return ins
                        return f
                    E("t", mk_c(), r=[BN["mxs"]], w=[BN["psC"]], x=[cd_l])
                    E("sq", lambda e, oc=oc, gb=gb: e.activation(qkv[:, gb, oc, :], psC[:], AF.Silu),
                      r=[BN["psC"]], w=[B_qkv[gb]])

                for ch in range(ATTN_CFG["chunks"] if ATTN_CFG["chunks"] is not None else 4):
                    j = gidx * 4 + ch
                    t0 = j * 128
                    sl = slice(ch * 128, ch * 128 + 128)
                    pm = j % 2
                    pn = 1 - pm
                    Bq = B_qkv[gb]
                    E("sy", lambda e, t0=t0: e.dma_start(out=bg8[:, 0, 0:129],
                                                         in_=betagsp[1, :, t0:t0 + 129]),
                      w=[B_bg8], x=[bgst], dma=True, chan="bg")
                    E("sy", lambda e, t0=t0: e.dma_start(out=bg8[:, 1, 0:128],
                                                         in_=betagsp[0, :, t0 + 1:t0 + 129]),
                      w=[B_bg8], x=[bgst], dma=True, chan="bg")
                    E("v", lambda e: e.tensor_scalar(
                        hrow[:, 0, :], bg8[:, 0, 1:129], bg8[:, 0, 0:1], None, ALU.subtract),
                      r=[B_bg8], w=[BN["hrow"]])
                    E("v", lambda e: e.tensor_scalar(hrow[:, 1, :], hrow[:, 0, :],
                                                     hrow[:, 0, 127:128], None, ALU.subtract),
                      r=[BN["hrow"]], w=[BN["hrow"]])
                    E("sq", lambda e: e.activation(gamr[:], hrow[:, 0, 127:128], AF.Exp),
                      r=[BN["hrow"]], w=[BN["gamr"]])
                    E("t", lambda e: e.transpose(pe(0, 8), hrow[:, 0, :], ident[0:8, 0:8]),
                      r=[BN["hrow"], B_const], w=[B_psE[0]])
                    E("v", lambda e: e.tensor_copy(hcolT[:, 0, :], psE[0:128, 0:8]),
                      r=[B_psE[0]], w=[BN["hcolT"]])
                    E("t", lambda e: e.transpose(pe(0, 8), hrow[:, 1, :], ident[0:8, 0:8]),
                      r=[BN["hrow"], B_const], w=[B_psE[0]])
                    E("v", lambda e: e.tensor_copy(hcolT[:, 1, :], psE[0:128, 0:8]),
                      r=[B_psE[0]], w=[BN["hcolT"]])
                    E("t", lambda e: e.transpose(pe(0, 8), bg8[:, 1, 0:128], ident[0:8, 0:8]),
                      r=[B_bg8, B_const], w=[B_psE[0]])
                    E("v", lambda e: e.tensor_copy(hcolT[:, 2, :], psE[0:128, 0:8]),
                      r=[B_psE[0]], w=[BN["hcolT"]])
                    E("sq", lambda e: e.activation(hcolT[:, 3, :], hcolT[:, 0, :], AF.Exp),
                      r=[BN["hcolT"]], w=[BN["hcolT"]])
                    E("sq", lambda e: e.activation(hcolT[:, 1, :], hcolT[:, 1, :], AF.Exp, scale=-1.0),
                      r=[BN["hcolT"]], w=[BN["hcolT"]])
                    E("v", lambda e: e.tensor_scalar(gamd[:], identb[0:8, 0:8], gamr[:], None, ALU.mult),
                      r=[BN["gamr"], B_const], w=[BN["gamd"]])
                    E("t", lambda e: e.matmul(pe(0, 8), sel8[:], gamd[:], start=True, stop=True),
                      r=[BN["gamd"], B_const], w=[B_psE[0]])
                    E("v", lambda e: e.tensor_copy(gamc[:], psE[0:128, 0:8]), r=[B_psE[0]], w=[BN["gamc"]])
                    for kh in range(4):
                        for qk in range(2):
                            src_oc = kh if qk == 0 else 4 + kh
                            ps_t = psE if qk == 0 else psF
                            Bp = B_psE[0] if qk == 0 else B_psF[0]
                            E("sq", lambda e, src_oc=src_oc, gb=gb, sl=sl: e.activation(
                                sq2t[:], qkv[:, gb, src_oc, sl], AF.Square), r=[Bq], w=[BN["sq2t"]])
                            E("t", lambda e, kh=kh, ps_t=ps_t: e.matmul(
                                ps_t[0:1, kh * 128:kh * 128 + 128], onescol[:], sq2t[:],
                                start=True, stop=True),
                              r=[BN["sq2t"], B_const], w=[Bp])
                    E("v", lambda e: e.tensor_copy(lrow[0:1, 0:512], psE[0:1, 0:512]),
                      r=[B_psE[0]], w=[BN["lrow"]])
                    E("v", lambda e: e.tensor_copy(lrow[0:1, 512:1024], psF[0:1, 0:512]),
                      r=[B_psF[0]], w=[BN["lrow"]])
                    E("v", lambda e: e.tensor_scalar(lrow[:], lrow[:], 1e-6, None, ALU.add),
                      r=[BN["lrow"]], w=[BN["lrow"]])
                    E("sq", lambda e: e.activation(lrow[:], lrow[:], AF.Ln),
                      r=[BN["lrow"]], w=[BN["lrow"]])
                    E("sq", lambda e: e.activation(lrow[:], lrow[:], AF.Exp, scale=-0.5),
                      r=[BN["lrow"]], w=[BN["lrow"]])
                    for kh in range(4):
                        E("t", lambda e, kh=kh: e.matmul(
                            pf(1)[:, kh:kh + 1], lrow[0:1, 512 + kh * 128:512 + (kh + 1) * 128],
                            onescol[0:1, 0:1], start=True, stop=True),
                          r=[BN["lrow"], B_const], w=[B_psF[1]])
                    E("v", lambda e: e.tensor_copy(rcolk[:], psF[:, 128:132]),
                      r=[B_psF[1]], w=[BN["rcolk"]])
                    for kh in range(4):
                        for qk in range(2):
                            src_oc = kh if qk == 0 else 4 + kh
                            dstt = (qn if qk == 0 else kn)
                            Bd = (B_qn if qk == 0 else B_kn)[kh]
                            E("t", lambda e, kh=kh, qk=qk: e.matmul(
                                pf(2), onesrow[:], lrow[0:1, qk * 512 + kh * 128:qk * 512 + (kh + 1) * 128],
                                start=True, stop=True), r=[BN["lrow"], B_const], w=[B_psF[2]])
                            scl = RSQ2 if qk == 0 else 1.0
                            E("v", lambda e, src_oc=src_oc, dstt=dstt, kh=kh, gb=gb, sl=sl, scl=scl:
                              e.scalar_tensor_tensor(dstt[:, kh, :], qkv[:, gb, src_oc, sl], scl,
                                                     pf(2), ALU.mult, ALU.mult),
                              r=[Bq, B_psF[2]], w=[Bd])
                    for kh in range(4):
                        E("sy", lambda e, kh=kh, gb=gb, sl=sl: e.dma_start(
                            out=ktok[:, kh, :], in_=qkv[:, gb, 4 + kh, sl], transpose=True),
                          r=[], w=[B_kvtok],
                          x=([last_tkv[0]] if kh == 0 else []) + [Bq.writer],
                          dma=True, chan="tkv", serial=False)
                    tpkv = None
                    for vh in range(8):
                        tpkv = E("sy", lambda e, vh=vh, gb=gb, sl=sl: e.dma_start(
                            out=vtok[:, vh, :], in_=qkv[:, gb, 8 + vh, sl], transpose=True),
                          r=[Bq] if vh == 7 else [], w=[B_kvtok],
                          x=[Bq.writer] if vh < 7 else [], dma=True, chan="tkv", serial=False)
                    last_tkv[0] = tpkv
                    for kh in range(4):
                        E("t", lambda e, kh=kh: e.matmul(
                            psK1[:, kh * 128:(kh + 1) * 128], kn[:, kh, :], kn[:, kh, :],
                            start=True, stop=True), r=[B_kn[kh]], w=[BN["psK1"]])
                        E("t", lambda e, kh=kh: e.matmul(
                            psK2[:, kh * 128:(kh + 1) * 128], kn[:, kh, :], qn[:, kh, :],
                            start=True, stop=True), r=[B_kn[kh], B_qn[kh]], w=[BN["psK2"]])
                    E("sy", lambda e, j=j: e.dma_start(out=ztile[:], in_=zsp[j]),
                      w=[BN["ztile"]], dma=True, chan="zt")

                    _ven = [0]

                    def _VE(*aa, **kk):
                        _ven[0] += 1
                        if ATTN_CFG.get("vhsteps") is not None and _ven[0] > ATTN_CFG["vhsteps"]:
                            return None
                        return E(*aa, **kk)
                    def vh_chain(vh):
                        s = vh % 2
                        kh = vh // 2
                        R = VRS[s]
                        RB = VRB[s]
                        bE = (_bE, _bE1)[s]
                        bF = (_bF, _bF1)[s]
                        psE_s = (psE, psE1)[s]
                        psF_s = (psF, psF1)[s]
                        BPw = (B_Apw, B_Apw1)[s]
                        BBw = (B_Bpw, B_Bpw1)[s]
                        BPr = (B_Prod, B_Prod1)[s]
                        peS = lambda q, w=128: psE_s[:, q * 128:q * 128 + w]
                        pfS = lambda q, w=128: psF_s[:, q * 128:q * 128 + w]
                        co = 32 if s == 0 else 40
                        cbS = lambda j: cb(co + j)
                        grow_ = R["grow"]; dect_ = R["dect"]; dmneg_ = R["dmneg"]; dmTin_ = R["dmTin"]
                        Amat_ = R["Amat"]; Bmat_ = R["Bmat"]; Apw_ = R["Apw"]; Bpw_ = R["Bpw"]
                        Prod_ = R["Prod"]; kbgt_ = R["kbgt"]; vbt_ = R["vbt"]; kdect_ = R["kdect"]
                        vcorr_ = R["vcorr"]; kcdf_ = R["kcdf"]; stb_ = R["stb"]; vnew_ = R["vnew"]
                        o1s_ = R["o1s"]; oc2_ = R["oc2"]; xg_ = R["xg"]; nrm_ = R["nrm"]
                        sq2t_ = R["sq2t"]; intraT_ = R["intraT"]
                        _VE("t", lambda e, vh=vh: e.transpose(psE_s[0:1, 384:512],
                                                            hcolT[:, 0, vh:vh + 1], ident[:]),
                          r=[BN["hcolT"], B_const], w=[bE]); yield
                        _VE("v", lambda e: e.tensor_copy(grow_[:], psE_s[0:1, 384:512]),
                          r=[bE], w=[RB["grow"]]); yield
                        _VE("t", lambda e: e.matmul(psE_s[:, 384:512], onesrow[:], grow_[:],
                                                  start=True, stop=True),
                          r=[RB["grow"], B_const], w=[bE]); yield
                        _VE("v", lambda e, vh=vh: e.tensor_scalar(
                            dect_[:, 0, :], psE_s[:, 384:512], hcolT[:, 0, vh:vh + 1], 0.0,
                            ALU.subtract, ALU.max), r=[bE, BN["hcolT"]], w=[RB["dect"]]); yield
                        _VE("v", lambda e, vh=vh: e.tensor_scalar(
                            dect_[:, 1, :], psE_s[:, 384:512], hcolT[:, 0, vh:vh + 1], 0.0,
                            ALU.subtract, ALU.min), r=[bE, BN["hcolT"]], w=[RB["dect"]]); yield
                        _VE("sq", lambda e: e.activation(dect_[:, 0, :], dect_[:, 0, :], AF.Exp, scale=-1.0),
                          r=[RB["dect"]], w=[RB["dect"]]); yield
                        _VE("sq", lambda e: e.activation(dect_[:, 1, :], dect_[:, 1, :], AF.Exp),
                          r=[RB["dect"]], w=[RB["dect"]]); yield
                        _VE("v", lambda e: e.tensor_tensor(dmneg_[:], dect_[:, 0, :], mnegst[:], ALU.mult),
                          r=[RB["dect"], B_const], w=[RB["dmneg"]]); yield
                        _VE("v", lambda e: e.tensor_tensor(dmTin_[:], dect_[:, 1, :], mtriu[:], ALU.mult),
                          r=[RB["dect"], B_const], w=[RB["dmTin"]]); yield
                        _VE("v", lambda e, kh=kh, vh=vh: e.scalar_tensor_tensor(
                            Amat_[:], psK1[:, kh * 128:(kh + 1) * 128], hcolT[:, 2, vh:vh + 1],
                            dmneg_[:], ALU.mult, ALU.mult),
                          r=[BN["psK1"], BN["hcolT"], RB["dmneg"]], w=[RB["Amat"]]); yield
                        _VE("sy", lambda e: e.dma_start(out=Bmat_[:], in_=Amat_[:], transpose=True),
                          r=[RB["Amat"]], w=[RB["Bmat"]], dma=True, chan="tb" + str(s)); yield
                        _VE("v", lambda e: e.tensor_tensor(Prod_[:, 0, :], identb[:], Bmat_[:], ALU.add),
                          r=[RB["Bmat"], B_const], w=[BPr[0]]); yield
                        lastP = 0
                        AsrcAP, BsrcAP = Amat_[:], Bmat_[:]
                        AsrcB, BsrcB = RB["Amat"], RB["Bmat"]
                        for lev in range(1, 7):
                            pA = lev % 2
                            _VE("t", lambda e, B_=BsrcAP, A_=AsrcAP: e.matmul(peS(1), B_, A_,
                                                                            start=True, stop=True),
                              r=[AsrcB, BsrcB], w=[bE]); yield
                            _VE("sq", lambda e, pA=pA: e.activation(Apw_[:, pA, :], peS(1), AF.Copy),
                              r=[bE], w=[BPw[pA]]); yield
                            if lev < 6:
                                _VE("t", lambda e, B_=BsrcAP, A_=AsrcAP: e.matmul(peS(2), A_, B_,
                                                                                start=True, stop=True),
                                  r=[AsrcB, BsrcB], w=[bE]); yield
                                _VE("sq", lambda e, pA=pA: e.activation(Bpw_[:, pA, :], peS(2), AF.Copy),
                                  r=[bE], w=[BBw[pA]]); yield
                            _VE("t", lambda e, pA=pA, lastP=lastP: e.matmul(
                                peS(3), Apw_[:, pA, :], Prod_[:, lastP, :], start=True, stop=True),
                              r=[BPw[pA], BPr[lastP]], w=[bE]); yield
                            _VE("v", lambda e, lastP=lastP: e.tensor_tensor(
                                Prod_[:, 1 - lastP, :], Prod_[:, lastP, :], peS(3), ALU.add),
                              r=[bE, BPr[lastP]], w=[BPr[1 - lastP]]); yield
                            lastP = 1 - lastP
                            AsrcAP, BsrcAP = Apw_[:, pA, :], Bpw_[:, pA, :]
                            AsrcB, BsrcB = BPw[pA], BBw[pA]
                        _VE("v", lambda e, kh=kh, vh=vh: e.tensor_tensor(
                            cbS(0), rcolk[:, kh:kh + 1], hcolT[:, 2, vh:vh + 1], ALU.mult),
                          r=[BN["rcolk"], BN["hcolT"]], w=[B_cb[co + 0]]); yield
                        _VE("v", lambda e, vh=vh: e.tensor_tensor(
                            cbS(1), cbS(0), hcolT[:, 3, vh:vh + 1], ALU.mult),
                          r=[B_cb[co + 0], BN["hcolT"]], w=[B_cb[co + 1]]); yield
                        _VE("v", lambda e, kh=kh, vh=vh: e.tensor_tensor(
                            cbS(2), rcolk[:, kh:kh + 1], hcolT[:, 1, vh:vh + 1], ALU.mult),
                          r=[BN["rcolk"], BN["hcolT"]], w=[B_cb[co + 2]]); yield
                        _VE("g", lambda e, kh=kh: e.tensor_scalar(kbgt_[:], ktok[:, kh, :], cbS(1),
                                                                None, ALU.mult),
                          r=[B_ktok[kh], B_cb[co + 1]], w=[RB["kbgt"]], x=[tpkv]); yield
                        _VE("g", lambda e, vh=vh: e.tensor_scalar(vbt_[:], vtok[:, vh, :],
                                                                hcolT[:, 2, vh:vh + 1], None, ALU.mult),
                          r=[B_vtok[vh], BN["hcolT"]], w=[RB["vbt"]], x=[tpkv]); yield
                        _VE("g", lambda e, kh=kh: e.tensor_scalar(kdect_[:], ktok[:, kh, :], cbS(2),
                                                                None, ALU.mult),
                          r=[B_ktok[kh], B_cb[co + 2]], w=[RB["kdect"]], x=[tpkv]); yield
                        _VE("t", lambda e, lastP=lastP: e.matmul(pfS(0), Prod_[:, lastP, :], vbt_[:],
                                                               start=True, stop=True),
                          r=[BPr[lastP], RB["vbt"]], w=[bF]); yield
                        _VE("v", lambda e: e.tensor_copy(vcorr_[:], pfS(0)), r=[bF], w=[RB["vcorr"]]); yield
                        _VE("t", lambda e, lastP=lastP: e.matmul(pfS(1), kbgt_[:], Prod_[:, lastP, :],
                                                               start=True, stop=True),
                          r=[BPr[lastP], RB["kbgt"]], w=[bF]); yield
                        _VE("sq", lambda e: e.activation(kcdf_[:], pfS(1), AF.Copy),
                          r=[bF], w=[RB["kcdf"]]); yield
                        _VE("sq", lambda e, pm=pm, vh=vh: e.activation(stb_[:], stm[:, pm, vh, :], AF.Copy),
                          r=[B_st[pm][vh]], w=[RB["stb"]]); yield
                        _VE("t", lambda e: e.matmul(pfS(2), kcdf_[:], stb_[:], start=True, stop=True),
                          r=[RB["kcdf"], RB["stb"]], w=[bF]); yield
                        _VE("v", lambda e: e.tensor_tensor(vnew_[:], vcorr_[:], pfS(2), ALU.subtract),
                          r=[RB["vcorr"], bF], w=[RB["vnew"]]); yield
                        _VE("t", lambda e, kh=kh: e.matmul(pfS(3), qn[:, kh, :], stb_[:],
                                                         start=True, stop=True),
                          r=[B_qn[kh], RB["stb"]], w=[bF]); yield
                        _VE("sq", lambda e, vh=vh: e.activation(o1s_[:], pfS(3), AF.Copy,
                                                              scale=hcolT[:, 3, vh:vh + 1]),
                          r=[bF, BN["hcolT"]], w=[RB["o1s"]]); yield
                        _VE("v", lambda e, kh=kh: e.tensor_tensor(
                            intraT_[:], psK2[:, kh * 128:(kh + 1) * 128], dmTin_[:], ALU.mult),
                          r=[BN["psK2"], RB["dmTin"]], w=[RB["intraT"]]); yield
                        _VE("t", lambda e: e.matmul(pfS(0), intraT_[:], vnew_[:], start=True, stop=True),
                          r=[RB["intraT"], RB["vnew"]], w=[bF]); yield
                        _VE("v", lambda e: e.tensor_tensor(oc2_[:], o1s_[:], pfS(0), ALU.add),
                          r=[RB["o1s"], bF], w=[RB["oc2"]]); yield
                        _VE("v", lambda e, vh=vh: e.tensor_tensor(
                            xg_[:], oc2_[:], ztile[:, vh * 128:(vh + 1) * 128], ALU.mult),
                          r=[RB["oc2"], BN["ztile"]], w=[RB["xg"]]); yield
                        _VE("sq", lambda e: e.activation(sq2t_[:], xg_[:], AF.Square, accum_out=cbS(3)),
                          r=[RB["xg"]], w=[RB["sq2t"], B_cb[co + 3]]); yield
                        _VE("v", lambda e: e.tensor_scalar(cbS(4), cbS(3), 1.0 / 128.0, 1e-6,
                                                         ALU.mult, ALU.add),
                          r=[B_cb[co + 3]], w=[B_cb[co + 4]]); yield
                        _VE("sq", lambda e: e.activation(cbS(5), cbS(4), AF.Ln),
                          r=[B_cb[co + 4]], w=[B_cb[co + 5]]); yield
                        _VE("sq", lambda e: e.activation(cbS(6), cbS(5), AF.Exp, scale=-0.5),
                          r=[B_cb[co + 5]], w=[B_cb[co + 6]]); yield
                        _VE("v", lambda e: e.scalar_tensor_tensor(nrm_[:], xg_[:], cbS(6), normw_t[:],
                                                                ALU.mult, ALU.mult),
                          r=[RB["xg"], B_cb[co + 6], B_const], w=[RB["nrm"]]); yield
                        _VE("sy", lambda e, vh=vh: e.dma_start(out=nrmT[:, vh, :], in_=nrm_[:],
                                                             transpose=True),
                          r=[RB["nrm"]], w=[B_nrmT[s]], dma=True, chan="tn" + str(s)); yield
                        _VE("t", lambda e: e.matmul(pfS(1), kdect_[:], vnew_[:], start=True, stop=True),
                          r=[RB["kdect"], RB["vnew"]], w=[bF]); yield
                        _VE("v", lambda e, pm=pm, pn=pn, vh=vh: e.scalar_tensor_tensor(
                            stm[:, pn, vh, :], stm[:, pm, vh, :], gamc[:, vh:vh + 1], pfS(1),
                            ALU.mult, ALU.add),
                          r=[bF, B_st[pm][vh], BN["gamc"]], w=[B_st[pn][vh]]); yield

                    VHN = ATTN_CFG["vh"] if ATTN_CFG["vh"] is not None else 8

                    def stream_gen(s):
                        for vh in range(s, VHN, 2):
                            for _ in vh_chain(vh):
                                yield

                    alive = [stream_gen(0), stream_gen(1)]
                    while alive:
                        for g_ in list(alive):
                            try:
                                next(g_)
                            except StopIteration:
                                alive.remove(g_)
                    for od in range(4 if (ATTN_CFG["rs"] and (ATTN_CFG["vh"] is None or ATTN_CFG["vh"] == 8)) else 0):
                        ps_t = psK1 if od % 2 == 0 else psK2
                        Bp = BN["psK1"] if od % 2 == 0 else BN["psK2"]

                        def mk_wo(od=od, ps_t=ps_t):
                            def f(e):
                                ins = []
                                for vh in range(8):
                                    ins.append(e.matmul(ps_t[:], nrmT[:, vh, :],
                                                        wo[:, vh, od * 512:(od + 1) * 512],
                                                        start=(vh == 0), stop=(vh == 7)))
                                return ins
                            return f
                        E("t", mk_wo(), r=[B_nrmT[0], B_nrmT[1], B_wo], w=[Bp])
                        E("v", lambda e, od=od, ps_t=ps_t: e.tensor_copy(rst[:, 0:512], ps_t[:]),
                          r=[Bp], w=[BN["rst"]])
                        rs_done.append(E("sy", lambda e, t0=t0, od=od: e.dma_start(
                            out=rs_in[t0:t0 + 128, od * 512:(od + 1) * 512], in_=rst[:, 0:512]),
                          r=[BN["rst"]], dma=True, chan="sr", serial=False))
            if ATTN_CFG["rs"] and rs_done:
                g_rs = seq.cc(lambda e: e.collective_compute("ReduceScatter", ALU.add, replica_groups=RG4,
                                                             ins=[rs_in.opt()], outs=[rs_out.opt()]),
                              [rs_done[-1]])
            else:
                g_rs = E("g", lambda e: e.nop())
            _stk4.close()
            seq.barrier()
            return g_rs

    def emit_x2():
        _stk6 = ExitStack()
        xa = _stk6.enter_context(nc.sbuf_tensor("xa", [128, D], F32))
        xb2 = _stk6.enter_context(nc.sbuf_tensor("xb2", [128, D], BF16))
        xc = _stk6.enter_context(nc.sbuf_tensor("xc", [128, D], F32))
        if True:
            B_a, B_b, B_c2 = Buf(), Buf(), Buf()
            for tt in range(NT):
                E("sy", lambda e, tt=tt: e.dma_start(out=xa[:], in_=x1sp[tt * 128:(tt + 1) * 128, :]),
                  w=[B_a], dma=True, chan="xa")
                E("sy", lambda e, tt=tt: e.dma_start(out=xb2[:], in_=rs_out[tt * 128:(tt + 1) * 128, :]),
                  w=[B_b], dma=True, chan="xb")
                E("v", lambda e: e.tensor_tensor(xc[:], xa[:], xb2[:], ALU.add),
                  r=[B_a, B_b], w=[B_c2])
                E("sy", lambda e, tt=tt: e.dma_start(out=x2sp[tt * 128:(tt + 1) * 128, :], in_=xc[:]),
                  r=[B_c2], dma=True)
        _stk6.close()
        seq.barrier()

    def finish_dummy():
        with nc.sbuf_tensor("dz", [128, D], F32) as dz:
            B_dz = Buf()
            E("v", lambda e: e.memset(dz[:], 0.0), w=[B_dz])
            for tt in range(NT):
                E("sy", lambda e, tt=tt: e.dma_start(out=out[tt * 128:(tt + 1) * 128, :], in_=dz[:]),
                  r=[B_dz], dma=True)
        seq.barrier()
        with nc.Block() as block:
            seq.emit(block)
        es.close()
        return nc

    if upto == 0:
        return finish_dummy()
    ag1_dep = emit_ffn(1)
    if upto == 1:
        return finish_dummy()
    g_ag1 = seq.cc(lambda e: e.collective_compute("AllGather", ALU.bypass, replica_groups=RG4,
                                                  ins=[ag1_in.opt()], outs=[ag1_out.opt()]), [ag1_dep])
    seq.barrier()
    if upto == 2:
        return finish_dummy()
    zstores, bgst = emit_zsweep()
    if upto == 3:
        return finish_dummy()
    emit_attention(zstores, bgst)
    if upto == 4:
        return finish_dummy()
    emit_x2()
    if upto == 5:
        return finish_dummy()
    emit_ffn(2)
    seq.barrier()

    with nc.Block() as block:
        seq.emit(block)
    es.close()
    return nc


_prog_cache = {}


def kernel(L0_attn_hidden, L0_attn_W_qkv, L0_attn_W_z, L0_attn_W_b, L0_attn_W_a,
           L0_attn_conv_weight, L0_attn_dt_bias, L0_attn_A_log, L0_attn_norm_w,
           L0_attn_W_out, L0_ffn_w1, L0_ffn_g1, L0_ffn_w2, L0_ffn_g2, L0_ffn_w3, L0_ffn_g3):
    hid = np.ascontiguousarray(np.asarray(L0_attn_hidden, np.float32))
    B, T, _ = hid.shape
    LT = T // 4
    w1T = np.ascontiguousarray(np.asarray(L0_ffn_w1, np.float32).T)
    w2T = np.ascontiguousarray(np.asarray(L0_ffn_w2, np.float32).T)
    w3T = np.ascontiguousarray(np.asarray(L0_ffn_w3, np.float32).T)
    Wqkv = np.asarray(L0_attn_W_qkv, np.float32)
    Wz = np.asarray(L0_attn_W_z, np.float32)
    Wb = np.asarray(L0_attn_W_b, np.float32)
    Wa = np.asarray(L0_attn_W_a, np.float32)
    convw = np.asarray(L0_attn_conv_weight, np.float32)
    Wout = np.asarray(L0_attn_W_out, np.float32)
    ident = np.eye(128, dtype=np.float32)
    ones = np.ones((128, 128), np.float32)
    masks = np.stack([-np.tril(ones, -1), np.triu(ones)]).astype(np.float32)
    sel8 = np.ones((8, 128), np.float32)

    in_maps = []
    for c in range(NCORE):
        bb, hg = c // 4, c % 4
        kh0, vh0 = 4 * hg, 8 * hg
        qcols = slice(kh0 * 128, kh0 * 128 + 512)
        kcols = slice(2048 + kh0 * 128, 2048 + kh0 * 128 + 512)
        vcols = slice(4096 + vh0 * 128, 4096 + vh0 * 128 + 1024)
        wqkv_sl = np.concatenate([Wqkv[:, qcols], Wqkv[:, kcols], Wqkv[:, vcols]], axis=1)
        conv_sl = np.concatenate([convw[kh0 * 128:kh0 * 128 + 512],
                                  convw[2048 + kh0 * 128:2048 + kh0 * 128 + 512],
                                  convw[4096 + vh0 * 128:4096 + vh0 * 128 + 1024]], axis=0)
        convd = np.zeros((128, 16, 4, 128), np.float32)
        for oc in range(16):
            for k in range(4):
                np.fill_diagonal(convd[:, oc, k, :], conv_sl[oc * 128:(oc + 1) * 128, k])
        wba_full = np.concatenate([Wb[:, vh0:vh0 + 8], Wa[:, vh0:vh0 + 8]], axis=1)
        wba = np.ascontiguousarray(wba_full.reshape(16, 128, 16).transpose(1, 0, 2))
        in_maps.append({
            "x0": np.ascontiguousarray(hid[bb, (c % 4) * LT:(c % 4) * LT + LT, :]),
            "w1Ts": np.ascontiguousarray(w1T[c * 256:(c + 1) * 256].reshape(2, 128, 8192)),
            "w2Ts": np.ascontiguousarray(w2T[c * 256:(c + 1) * 256].reshape(2, 128, 8192)),
            "w3Ts": np.ascontiguousarray(w3T[c * 1024:(c + 1) * 1024].reshape(8, 128, 2048)),
            "g1in": np.asarray(L0_ffn_g1, np.float32).reshape(1, -1),
            "g2in": np.asarray(L0_ffn_g2, np.float32).reshape(1, -1),
            "g3in": np.asarray(L0_ffn_g3, np.float32).reshape(1, -1),
            "wqkvin": np.ascontiguousarray(wqkv_sl),
            "wzin": np.ascontiguousarray(Wz[:, vh0 * 128:vh0 * 128 + 1024]),
            "wbain": wba,
            "convdin": convd,
            "woutin": np.ascontiguousarray(Wout[vh0 * 128:vh0 * 128 + 1024, :]),
            "alogin": np.ascontiguousarray(np.asarray(L0_attn_A_log, np.float32)[vh0:vh0 + 8]
                                           .reshape(8, 1)),
            "dtbin": np.ascontiguousarray(np.asarray(L0_attn_dt_bias, np.float32)[vh0:vh0 + 8]
                                          .reshape(8, 1)),
            "normwin": np.asarray(L0_attn_norm_w, np.float32).reshape(1, 128),
            "identin": ident,
            "masksin": masks,
            "sel8in": sel8,
        })

    key = T
    if key not in _prog_cache:
        _prog_cache[key] = build_program(T)
    nc = _prog_cache[key]
    res = run_bass_kernel_spmd(nc, in_maps, core_ids=list(range(NCORE)))
    outp = np.empty((B, T, D), np.float32)
    for c in range(NCORE):
        outp[c // 4, (c % 4) * LT:(c % 4) * LT + LT, :] = res.results[c]["out"]
    return outp


# version 11 (final)
# speedup vs baseline: 1.4636x; 1.0032x over previous
import numpy as np
import concourse.bass as bass
import concourse.mybir as mybir
from concourse.bass_utils import run_bass_kernel_spmd
from contextlib import ExitStack

F32 = mybir.dt.float32
BF16 = mybir.dt.bfloat16
I32 = mybir.dt.int32
AF = mybir.ActivationFunctionType
ALU = mybir.AluOpType
AX = mybir.AxisListType

D = 2048
HID = 8192
NCORE = 8
ATANH2 = float(2.0 * np.arctanh(0.5))
RG8 = [list(range(NCORE))]
RG4 = [[0, 1, 2, 3], [4, 5, 6, 7]]
ENGS = ("sy", "sq", "v", "t", "g")
RSQ2 = float(128.0 ** -0.5)


class Seq:
    def __init__(self, nc, es):
        self.nc = nc
        self.es = es
        self.sems = {e: es.enter_context(nc.semaphore(f"sq_{e}")) for e in ENGS}
        self.ops = {e: [] for e in ENGS}
        self.count = {e: 0 for e in ENGS}
        self.ccount = {}

    def _chan_sem(self, key):
        if key not in self.ccount:
            self.sems[key] = self.es.enter_context(self.nc.semaphore(f"sq_{key}"))
            self.ccount[key] = 0
        return key

    def raw(self, eng, fn, deps=(), dma=False, chan=None, serial=True):
        deps = [d for d in deps if d is not None]
        if dma:
            key = self._chan_sem(f"{eng}.{chan or 'd'}")
            if serial and self.ccount[key] > 0:
                deps.append((key, self.ccount[key]))
            self.ccount[key] += 16
            self.ops[eng].append((fn, deps, key, 16))
            return (key, self.ccount[key])
        if self.count[eng] > 0:
            deps.append((eng, self.count[eng]))
        self.count[eng] += 1
        self.ops[eng].append((fn, deps, eng, 1))
        return (eng, self.count[eng])

    def cc(self, build, deps):
        key = self._chan_sem("g.cc")
        deps = list(deps) + ([(key, self.ccount[key])] if self.ccount[key] else [])
        self.ccount[key] += 1
        val = self.ccount[key]

        def f(e):
            build(e).then_inc(self.sems[key], 1)
            return e.nop()
        self.raw("g", f, [d for d in deps if d is not None])
        return (key, val)

    def barrier(self):
        marks = {e: self.count[e] for e in ENGS}
        cmarks = dict(self.ccount)
        out = {}
        for e in ENGS:
            deps = [(o, marks[o]) for o in ENGS if o != e and marks[o] > 0]
            deps += [(k, v) for k, v in cmarks.items() if v > 0]
            out[e] = self.raw(e, lambda en: en.nop(), deps=deps)
        return out

    def emit(self, block):
        sections = {"sy": block.sync, "sq": block.scalar, "v": block.vector,
                    "t": block.tensor, "g": block.gpsimd}
        for eng_name in ENGS:
            def make(eng_name=eng_name):
                def body(en):
                    last = {}
                    done = 0
                    for fn, waits, skey, inc in self.ops[eng_name]:
                        wmax = {}
                        for (oe, val) in waits:
                            if wmax.get(oe, -1) < val:
                                wmax[oe] = val
                        for (oe, val) in wmax.items():
                            if last.get(oe, 0) >= val:
                                continue
                            if oe == eng_name and val > done:
                                raise RuntimeError("forward same-engine dep")
                            en.wait_ge(self.sems[oe], val)
                            last[oe] = val
                        ins = fn(en)
                        if isinstance(ins, (list, tuple)):
                            ins = ins[-1]
                        ins.then_inc(self.sems[skey], inc)
                        if skey == eng_name:
                            done += inc
                return body
            sections[eng_name](make())


class Buf:
    def __init__(self):
        self.writer = None
        self.readers = []


ATTN_CFG = {"groups": None, "chunks": None, "vh": None, "rs": True, "proj": True}


def build_program(T, upto=9):
    LT = T // 4
    NT = LT // 128
    NG = T // 512
    assert T % 2048 == 0
    TBU = min(LT, 1024)
    NBU = LT // TBU
    NTU = TBU // 128

    nc = bass.Bass("TRN2", target_bir_lowering=False)

    x0 = nc.dram_tensor("x0", [LT, D], F32, kind="ExternalInput").ap()
    w1Ts = nc.dram_tensor("w1Ts", [2, 128, HID], F32, kind="ExternalInput").ap()
    w2Ts = nc.dram_tensor("w2Ts", [2, 128, HID], F32, kind="ExternalInput").ap()
    w3Ts = nc.dram_tensor("w3Ts", [8, 128, D], F32, kind="ExternalInput").ap()
    g1in = nc.dram_tensor("g1in", [1, D], F32, kind="ExternalInput").ap()
    g2in = nc.dram_tensor("g2in", [1, D], F32, kind="ExternalInput").ap()
    g3in = nc.dram_tensor("g3in", [1, HID], F32, kind="ExternalInput").ap()
    wqkvin = nc.dram_tensor("wqkvin", [D, 2048], F32, kind="ExternalInput").ap()
    wzin = nc.dram_tensor("wzin", [D, 1024], F32, kind="ExternalInput").ap()
    wbain = nc.dram_tensor("wbain", [128, 16, 16], F32, kind="ExternalInput").ap()
    convdin = nc.dram_tensor("convdin", [128, 16, 4, 128], F32, kind="ExternalInput").ap()
    woutin = nc.dram_tensor("woutin", [1024, D], F32, kind="ExternalInput").ap()
    alogin = nc.dram_tensor("alogin", [8, 1], F32, kind="ExternalInput").ap()
    dtbin = nc.dram_tensor("dtbin", [8, 1], F32, kind="ExternalInput").ap()
    normwin = nc.dram_tensor("normwin", [1, 128], F32, kind="ExternalInput").ap()
    identin = nc.dram_tensor("identin", [128, 128], F32, kind="ExternalInput").ap()
    masksin = nc.dram_tensor("masksin", [2, 128, 128], F32, kind="ExternalInput").ap()
    sel8in = nc.dram_tensor("sel8in", [8, 128], F32, kind="ExternalInput").ap()
    out = nc.dram_tensor("out", [LT, D], F32, kind="ExternalOutput").ap()

    agt_in = nc.dram_tensor("agt_in", [24, 128, 2048], BF16).ap()
    agt_out = nc.dram_tensor("agt_out", [NCORE, 24, 128, 2048], BF16).ap()
    ar_in = nc.dram_tensor("ar_in", [1, 4], F32).ap()
    ar_out = nc.dram_tensor("ar_out", [1, 4], F32).ap()
    ag1_in = nc.dram_tensor("ag1_in", [16, 128, LT], BF16).ap()
    ag1_out = nc.dram_tensor("ag1_out", [4, 16, 128, LT], BF16).ap()
    rs_in = nc.dram_tensor("rs_in", [T, D], BF16).ap()
    rs_out = nc.dram_tensor("rs_out", [LT, D], BF16).ap()
    zsp = nc.dram_tensor("zsp", [T // 128, 128, 1024], BF16).ap()
    betagsp = nc.dram_tensor("betagsp", [2, 8, T + 1], F32).ap()
    hgsp = nc.dram_tensor("hgsp", [NT, 128, HID], BF16).ap()
    x1sp = nc.dram_tensor("x1sp", [LT, D], F32).ap()
    x2sp = nc.dram_tensor("x2sp", [LT, D], F32).ap()

    es = ExitStack()
    seq = Seq(nc, es)

    def E(eng, fn, r=(), w=(), x=(), dma=False, chan=None, serial=True):
        deps = []
        for b in r:
            if b.writer is not None:
                deps.append(b.writer)
        for b in w:
            deps.extend(b.readers if b.readers else ([b.writer] if b.writer else []))
        deps.extend(x)
        h = seq.raw(eng, fn, deps, dma=dma, chan=chan, serial=serial)
        for b in r:
            b.readers.append(h)
        for b in w:
            b.writer = h
            b.readers = []
        return h

    ident = es.enter_context(nc.sbuf_tensor("ident", [128, 128], F32))
    identb = es.enter_context(nc.sbuf_tensor("identb", [128, 128], BF16))
    mnegst = es.enter_context(nc.sbuf_tensor("mnegst", [128, 128], F32))
    mtriu = es.enter_context(nc.sbuf_tensor("mtriu", [128, 128], F32))
    sel8 = es.enter_context(nc.sbuf_tensor("sel8", [8, 128], F32))
    onesrow = es.enter_context(nc.sbuf_tensor("onesrow", [1, 128], F32))
    onescol = es.enter_context(nc.sbuf_tensor("onescol", [128, 1], F32))
    betaP = es.enter_context(nc.sbuf_tensor("betaP", [128, 1], F32))
    betaN = es.enter_context(nc.sbuf_tensor("betaN", [128, 1], F32))
    probef = es.enter_context(nc.sbuf_tensor("probef", [128, 1], F32))
    probei = es.enter_context(nc.sbuf_tensor("probei", [128, 1], I32))
    normw_t = es.enter_context(nc.sbuf_tensor("normw_t", [128, 128], F32))
    scw = es.enter_context(nc.sbuf_tensor("scw", [128, 8], F32))
    cbuf = es.enter_context(nc.sbuf_tensor("cbuf", [128, 48], F32))
    _stkP = ExitStack()
    PSB = _stkP.enter_context(nc.psum_tensor("PSB", [128, 512], F32))
    B_PSB = Buf()
    B_cb = [Buf() for _ in range(48)]
    B_const = Buf()

    def cb(j):
        return cbuf[:, j:j + 1]

    E("sy", lambda e: e.dma_start(out=ident[:], in_=identin), w=[B_const], dma=True)
    E("sy", lambda e: e.dma_start(out=mnegst[:], in_=masksin[0]), w=[B_const], dma=True)
    E("sy", lambda e: e.dma_start(out=mtriu[:], in_=masksin[1]), w=[B_const], dma=True)
    E("sy", lambda e: e.dma_start(out=sel8[:], in_=sel8in), w=[B_const], dma=True)
    E("sy", lambda e: e.dma_start(out=normw_t[0:1, :], in_=normwin), w=[B_const], dma=True)
    E("v", lambda e: e.memset(onesrow[:], 1.0), w=[B_const])
    E("v", lambda e: e.memset(onescol[:], 1.0), w=[B_const])
    E("v", lambda e: e.tensor_copy(identb[:], ident[:]), r=[B_const], w=[B_const])
    E("v", lambda e: e.memset(probef[:], 1.5), w=[B_const])
    E("v", lambda e: e.tensor_copy(probei[:], probef[:]), r=[B_const], w=[B_const])
    E("v", lambda e: e.tensor_scalar(betaP[:], probei[:], -2.0, -256.5, ALU.add, ALU.mult),
      r=[B_const], w=[B_const])
    E("v", lambda e: e.tensor_scalar(betaN[:], probei[:], -2.0, 256.0, ALU.add, ALU.mult),
      r=[B_const], w=[B_const])
    E("t", lambda e: e.matmul(PSB[:, 0:128], onesrow[:], normw_t[0:1, :], start=True, stop=True),
      r=[B_const], w=[B_PSB])
    cdep = E("v", lambda e: e.tensor_copy(normw_t[:], PSB[:, 0:128]), r=[B_PSB], w=[B_const])

    wtiles = []
    for wi, src in ((0, w1Ts), (1, w2Ts)):
        for ch in range(2):
            for c4 in range(4):
                wtiles.append((src[ch, :, c4 * 2048:(c4 + 1) * 2048], wi * 8 + ch * 4 + c4, wi))
    for rc in range(8):
        wtiles.append((w3Ts[rc, :, :], 16 + rc, 2))

    B_scw = Buf()
    _stk1 = ExitStack()
    wld = _stk1.enter_context(nc.sbuf_tensor("wld", [128, 2, 2048], F32))
    wq1 = _stk1.enter_context(nc.sbuf_tensor("wq1", [128, 2048], F32))
    wq2 = _stk1.enter_context(nc.sbuf_tensor("wq2", [128, 2048], F32))
    wqi = _stk1.enter_context(nc.sbuf_tensor("wqi", [128, 2048], I32))
    wqb = _stk1.enter_context(nc.sbuf_tensor("wqb", [128, 2048], BF16))
    wdmp = _stk1.enter_context(nc.sbuf_tensor("wdmp", [128, 2048], F32))
    acc3 = _stk1.enter_context(nc.sbuf_tensor("acc3", [128, 4], F32))
    arb = _stk1.enter_context(nc.sbuf_tensor("arb", [1, 4], F32))
    if True:
        B_wld = [Buf(), Buf()]
        B_acc = Buf()
        B_tmp = Buf()
        B_arb = Buf()
        E("v", lambda e: e.memset(acc3[:], 0.0), w=[B_acc])
        for i, (src, entry, m) in enumerate(wtiles):
            b = wld[:, i % 2, :]
            E("sy", lambda e, s=src, b=b: e.dma_start(out=b, in_=s), w=[B_wld[i % 2]], dma=True)
            E("sq", lambda e, b=b, m=m: e.activation(wdmp[:], b, AF.Abs, accum_out=cb(8 + m)),
              r=[B_wld[i % 2]], w=[B_tmp, B_cb[8 + m]])
            E("v", lambda e, m=m: e.tensor_tensor(acc3[:, m:m + 1], acc3[:, m:m + 1], cb(8 + m), ALU.add),
              r=[B_cb[8 + m], B_acc], w=[B_acc])
        E("t", lambda e: e.matmul(PSB[:4, 0:1], acc3[:], onescol[:], start=True, stop=True),
          r=[B_acc, B_const], w=[B_PSB])
        E("v", lambda e: e.tensor_copy(acc3[:4, 0:1], PSB[:4, 0:1]), r=[B_PSB], w=[B_acc])
        E("t", lambda e: e.transpose(PSB[0:1, 128:132], acc3[:4, 0:1], ident[0:4, 0:4]),
          r=[B_acc, B_const], w=[B_PSB])
        E("v", lambda e: e.tensor_copy(arb[:], PSB[0:1, 128:132]), r=[B_PSB], w=[B_arb])
        h0 = E("sy", lambda e: e.dma_start(out=ar_in, in_=arb[:]), r=[B_arb], dma=True)
        h = seq.cc(lambda e: e.collective_compute("AllReduce", ALU.add, replica_groups=RG8,
                                                  ins=[ar_in.opt()], outs=[ar_out.opt()]), [h0])
        E("sy", lambda e: e.dma_start(out=arb[:], in_=ar_out), w=[B_arb], x=[h], dma=True)
        E("t", lambda e: e.matmul(PSB[:, 128:132], onesrow[:], arb[:], start=True, stop=True),
          r=[B_arb, B_const], w=[B_PSB])
        E("v", lambda e: e.tensor_scalar(cbuf[:, 0:4], PSB[:, 128:132], 1.0 / (2048.0 * 8192.0),
                                         None, ALU.mult), r=[B_PSB], w=[B_cb[0]])
        E("v", lambda e: e.tensor_scalar(cbuf[:, 4:8], cbuf[:, 0:4], 1e-8, ATANH2, ALU.add, ALU.mult),
          r=[B_cb[0]], w=[B_cb[4]])
        E("v", lambda e: e.reciprocal(scw[:, 0:4], cbuf[:, 4:8]), r=[B_cb[4]], w=[B_scw])
        E("v", lambda e: e.tensor_scalar(cbuf[:, 8:12], cbuf[:, 0:4], -1.0, 1.0, ALU.mult, ALU.add),
          r=[B_cb[0]], w=[B_cb[8]])
        E("v", lambda e: e.reciprocal(cbuf[:, 12:16], cbuf[:, 8:12]), r=[B_cb[8]], w=[B_cb[12]])
        E("v", lambda e: e.tensor_scalar(cbuf[:, 16:20], cbuf[:, 0:4], 1.0, None, ALU.add),
          r=[B_cb[0]], w=[B_cb[16]])
        E("v", lambda e: e.tensor_tensor(cbuf[:, 20:24], cbuf[:, 16:20], cbuf[:, 12:16], ALU.mult),
          r=[B_cb[16], B_cb[12]], w=[B_cb[20]])
        E("sq", lambda e: e.activation(cbuf[:, 24:28], cbuf[:, 20:24], AF.Ln), r=[B_cb[20]], w=[B_cb[24]])
        E("v", lambda e: e.tensor_scalar(scw[:, 4:8], cbuf[:, 24:28], 0.5, None, ALU.mult),
          r=[B_cb[24], B_scw], w=[B_scw])

        prev_store = None
        for i, (src, entry, m) in enumerate(wtiles):
            b = wld[:, i % 2, :]
            E("sy", lambda e, s=src, b=b: e.dma_start(out=b, in_=s), w=[B_wld[i % 2]], dma=True)
            E("sq", lambda e, b=b, m=m: e.activation(wq1[:], b, AF.Copy, scale=scw[:, m:m + 1]),
              r=[B_wld[i % 2], B_scw], w=[B_tmp])
            E("v", lambda e: e.tensor_scalar(wq2[:], wq1[:], 1.4, -1.4, ALU.min, ALU.max),
              r=[B_tmp], w=[B_tmp])
            E("sq", lambda e: e.activation(wq1[:], wq2[:], AF.Identity, bias=betaP[:, 0:1]),
              r=[B_tmp, B_const], w=[B_tmp])
            E("v", lambda e: e.tensor_copy(wqi[:], wq1[:]), r=[B_tmp], w=[B_tmp])
            E("v", lambda e: e.tensor_scalar(wqb[:], wqi[:], betaN[:, 0:1], None, ALU.add),
              r=[B_tmp, B_const], w=[B_tmp])
            prev_store = E("sy", lambda e, entry=entry: e.dma_start(out=agt_in[entry], in_=wqb[:]),
                           r=[B_tmp], x=[prev_store], dma=True)
        g_agt = seq.cc(lambda e: e.collective_compute("AllGather", ALU.bypass, replica_groups=RG8,
                                                      ins=[agt_in.opt()], outs=[agt_out.opt()]), [prev_store])
        _ = g_agt
    _stk1.close()
    _stkP.close()
    seq.barrier()

    def w_up_tile(wi, ic, oc):
        return agt_out[ic // 2, wi * 8 + (ic % 2) * 4 + oc // 4, :, (oc % 4) * 512:(oc % 4) * 512 + 512]

    def w_dn_tile(icd, od):
        return agt_out[icd // 8, 16 + icd % 8, :, od * 512:od * 512 + 512]

    def ln_chain(src_ap, r_bufs, dst_ap, w_bufs, dmp_ap, dmp_buf, base, eps, x=()):
        E("v", lambda e: e.tensor_reduce(cb(base), src_ap, AX.X, ALU.add),
          r=r_bufs, w=[B_cb[base]], x=list(x))
        E("v", lambda e: e.tensor_scalar(cb(base + 1), cb(base), -1.0 / D, None, ALU.mult),
          r=[B_cb[base]], w=[B_cb[base + 1]])
        E("sq", lambda e: e.activation(dmp_ap, src_ap, AF.Square, bias=cb(base + 1),
                                       accum_out=cb(base + 2)),
          r=r_bufs + [B_cb[base + 1]], w=[dmp_buf, B_cb[base + 2]])
        E("v", lambda e: e.tensor_scalar(cb(base + 3), cb(base + 2), 1.0 / D, eps, ALU.mult, ALU.add),
          r=[B_cb[base + 2]], w=[B_cb[base + 3]])
        E("sq", lambda e: e.activation(cb(base + 4), cb(base + 3), AF.Ln),
          r=[B_cb[base + 3]], w=[B_cb[base + 4]])
        E("sq", lambda e: e.activation(cb(base + 5), cb(base + 4), AF.Exp, scale=-0.5),
          r=[B_cb[base + 4]], w=[B_cb[base + 5]])
        return E("v", lambda e: e.tensor_scalar(dst_ap, src_ap, cb(base + 1), cb(base + 5),
                                                ALU.add, ALU.mult),
                 r=r_bufs + [B_cb[base + 1], B_cb[base + 5]], w=w_bufs)

    last_sh = [None]
    last_so = [None]
    c1s = es.enter_context(nc.sbuf_tensor("c1s", [128, NT], F32))
    c2s = es.enter_context(nc.sbuf_tensor("c2s", [128, NT], F32))
    c3s = es.enter_context(nc.sbuf_tensor("c3s", [128, NT], F32))
    ssqr = es.enter_context(nc.sbuf_tensor("ssqr", [128, NT], F32))
    amaxr = es.enter_context(nc.sbuf_tensor("amaxr", [128, NT], F32))
    B_c1s = [Buf() for _ in range(NT)]
    B_c2s = [Buf() for _ in range(NT)]
    B_c3s = [Buf() for _ in range(NT)]
    B_ssqr = [Buf() for _ in range(NT)]
    B_amaxr = [Buf() for _ in range(NT)]

    def emit_ffn(call):
        x_src = x0 if call == 1 else x2sp
        ag1_last = [None]
        out_last = [None]
        hsp_done = {}
        _stk2 = ExitStack()
        gt3 = _stk2.enter_context(nc.sbuf_tensor(f"f{call}_gt3", [128, HID], BF16))
        g12t = _stk2.enter_context(nc.sbuf_tensor(f"f{call}_g12t", [128, 2, D], BF16))
        xin = _stk2.enter_context(nc.sbuf_tensor(f"f{call}_xin", [128, 2, D], F32))
        xln = _stk2.enter_context(nc.sbuf_tensor(f"f{call}_xln", [128, D], F32))
        ytmp = _stk2.enter_context(nc.sbuf_tensor(f"f{call}_ytmp", [128, D], F32))
        kint = _stk2.enter_context(nc.sbuf_tensor(f"f{call}_kint", [128, D], I32))
        kbf = _stk2.enter_context(nc.sbuf_tensor(f"f{call}_kbf", [128, 2, D], BF16))
        fdmp = _stk2.enter_context(nc.sbuf_tensor(f"f{call}_fdmp", [128, D], F32))
        k1T = _stk2.enter_context(nc.sbuf_tensor(f"f{call}_k1T", [128, 16, TBU], BF16))
        k2T = _stk2.enter_context(nc.sbuf_tensor(f"f{call}_k2T", [128, 16, TBU], BF16))
        wu0 = _stk2.enter_context(nc.sbuf_tensor(f"f{call}_wu0", [128, 8, 2, 512], BF16))
        wu1 = _stk2.enter_context(nc.sbuf_tensor(f"f{call}_wu1", [128, 8, 2, 512], BF16))
        wut = (wu0, wu1)
        silu1 = _stk2.enter_context(nc.sbuf_tensor(f"f{call}_silu1", [128, 512], F32))
        htile = _stk2.enter_context(nc.sbuf_tensor(f"f{call}_htile", [128, 512], BF16))
        hgt = _stk2.enter_context(nc.sbuf_tensor(f"f{call}_hgt", [128, 2, 512], BF16))
        ps1a = _stk2.enter_context(nc.psum_tensor(f"f{call}_ps1a", [128, 512], F32))
        ps1b = _stk2.enter_context(nc.psum_tensor(f"f{call}_ps1b", [128, 512], F32))
        ps2a = _stk2.enter_context(nc.psum_tensor(f"f{call}_ps2a", [128, 512], F32))
        ps2b = _stk2.enter_context(nc.psum_tensor(f"f{call}_ps2b", [128, 512], F32))
        psg = _stk2.enter_context(nc.psum_tensor(f"f{call}_psg", [128, 512], F32))
        if True:

            B_g = Buf()
            B_gs = Buf()
            B_psg = Buf()
            with nc.sbuf_tensor(f"f{call}_gstg", [1, 512], F32) as gstg:
                jobs = ([(g3in, gt3[:, j * 512:(j + 1) * 512], j) for j in range(16)] +
                        [(g1in, g12t[:, 0, j * 512:(j + 1) * 512], j) for j in range(4)] +
                        [(g2in, g12t[:, 1, j * 512:(j + 1) * 512], j) for j in range(4)])
                for srcT, dst, j in jobs:
                    E("sy", lambda e, srcT=srcT, j=j: e.dma_start(
                        out=gstg[:], in_=srcT[0:1, j * 512:(j + 1) * 512]), w=[B_gs], dma=True)
                    E("t", lambda e: e.matmul(psg[:, 0:512], onesrow[:], gstg[:],
                                              start=True, stop=True),
                      r=[B_const, B_gs], w=[B_psg])
                    E("v", lambda e, dst=dst: e.tensor_copy(dst, psg[:, 0:512]),
                      r=[B_psg], w=[B_g])

            B_xin = [Buf(), Buf()]
            B_xln = Buf()
            B_y = Buf()
            B_ki = Buf()
            B_kb = [Buf(), Buf()]
            B_kT = [[Buf() for _ in range(NTU)], [Buf() for _ in range(NTU)]]
            last_tk = {0: None, 1: None}
            B_wu = [Buf(), Buf()]
            last_wu = {0: None, 1: None}
            B_psy = [[Buf(), Buf()], [Buf(), Buf()]]
            B_si = Buf()
            B_ht = Buf()
            B_hg = [Buf(), Buf()]
            B_fd = Buf()
            E("v", lambda e: e.memset(ssqr[:], 0.0), w=B_ssqr)
            E("g", lambda e: e.memset(amaxr[:], 0.0), w=B_amaxr)

            for tb in range(NBU):
                for tl in range(NTU):
                    tt = tb * NTU + tl
                    xb = xin[:, tt % 2, :]
                    E("sy", lambda e, tt=tt, xb=xb: e.dma_start(
                        out=xb, in_=x_src[tt * 128:(tt + 1) * 128, :]), w=[B_xin[tt % 2]], dma=True,
                      chan="x" + str(tt % 2))
                    ln_chain(xb, [B_xin[tt % 2]], xln[:], [B_xln], fdmp[:], B_fd, 0, 1e-5)
                    E("sq", lambda e: e.activation(fdmp[:], xln[:], AF.Square, accum_out=cb(6)),
                      r=[B_xln], w=[B_fd, B_cb[6]])
                    E("v", lambda e: e.tensor_scalar(cb(7), cb(6), 1.0 / D, 1e-8, ALU.mult, ALU.add),
                      r=[B_cb[6]], w=[B_cb[7]])
                    E("sq", lambda e: e.activation(cb(8), cb(7), AF.Ln), r=[B_cb[7]], w=[B_cb[8]])
                    E("sq", lambda e: e.activation(cb(9), cb(8), AF.Exp, scale=-0.5),
                      r=[B_cb[8]], w=[B_cb[9]])
                    for gi in range(2):
                        kb = kbf[:, gi, :]
                        kT = (k1T, k2T)[gi]
                        cs = (c1s, c2s)[gi]
                        Bcs = (B_c1s, B_c2s)[gi][tt]
                        E("v", lambda e, gi=gi: e.scalar_tensor_tensor(
                            ytmp[:], xln[:], cb(9), g12t[:, gi, :], ALU.mult, ALU.mult),
                          r=[B_xln, B_cb[9], B_g], w=[B_y])
                        E("v", lambda e: e.tensor_reduce(cb(10), ytmp[:], AX.X, ALU.max,
                                                         apply_absolute_value=True),
                          r=[B_y], w=[B_cb[10]])
                        E("v", lambda e: e.tensor_scalar(cb(11), cb(10), 1e-4, None, ALU.max),
                          r=[B_cb[10]], w=[B_cb[11]])
                        E("v", lambda e: e.reciprocal(cb(12), cb(11)), r=[B_cb[11]], w=[B_cb[12]])
                        E("v", lambda e: e.tensor_scalar(cb(13), cb(12), 127.0, None, ALU.mult),
                          r=[B_cb[12]], w=[B_cb[13]])
                        E("v", lambda e, cs=cs, tt=tt, gi=gi: e.scalar_tensor_tensor(
                            cs[:, tt:tt + 1], cb(11), 1.0 / 127.0, scw[:, 4 + gi:5 + gi],
                            ALU.mult, ALU.mult), r=[B_cb[11], B_scw], w=[Bcs])
                        E("sq", lambda e: e.activation(ytmp[:], ytmp[:], AF.Identity, scale=cb(13),
                                                       bias=betaP[:, 0:1]),
                          r=[B_y, B_cb[13], B_const], w=[B_y])
                        E("v", lambda e: e.tensor_copy(kint[:], ytmp[:]), r=[B_y], w=[B_ki])
                        E("v", lambda e, kb=kb: e.tensor_scalar(kb, kint[:], betaN[:, 0:1], None, ALU.add),
                          r=[B_ki, B_const], w=[B_kb[gi]])
                        for ic in range(16):
                            h_tk = E("sy",
                              lambda e, kb=kb, kT=kT, ic=ic, tl=tl: e.dma_start(
                                out=kT[:, ic, tl * 128:(tl + 1) * 128],
                                in_=kb[:, ic * 128:(ic + 1) * 128], transpose=True),
                              r=[B_kb[gi]] if ic == 15 else [], w=[B_kT[gi][tl]],
                              x=([last_tk[gi]] if ic == 0 else []) +
                                ([B_kb[gi].writer] if ic < 15 else []),
                              dma=True, chan="tk" + str(gi), serial=False)
                        last_tk[gi] = h_tk
                for oc in range(16):
                    for wi in range(2):
                        for ch in range(2):
                            h_wu = E("g", lambda e, wi=wi, oc=oc, ch=ch: e.dma_start(
                                out=wut[wi][:, :, ch, :],
                                in_=agt_out[:, wi * 8 + ch * 4 + oc // 4, :,
                                            (oc % 4) * 512:(oc % 4) * 512 + 512]
                                    .rearrange("r p n -> p r n")),
                              w=[B_wu[wi]],
                              x=[g_agt] + ([last_wu[wi]] if ch == 0 else []),
                              dma=True, chan="wu" + str(wi), serial=False)
                            last_wu[wi] = h_wu
                    for tl in range(NTU):
                        tt = tb * NTU + tl
                        psy1 = (ps1a, ps1b)[tl % 2]
                        psy2 = (ps2a, ps2b)[tl % 2]

                        def mk_mm(tl=tl, psy1=psy1, psy2=psy2):
                            def f(e):
                                ins = []
                                for ic in range(16):
                                    ins.append(e.matmul(psy1[:], k1T[:, ic, tl * 128:(tl + 1) * 128],
                                                        wu0[:, ic // 2, ic % 2, :],
                                                        start=(ic == 0), stop=(ic == 15)))
                                for ic in range(16):
                                    ins.append(e.matmul(psy2[:], k2T[:, ic, tl * 128:(tl + 1) * 128],
                                                        wu1[:, ic // 2, ic % 2, :],
                                                        start=(ic == 0), stop=(ic == 15)))
                                return ins
                            return f
                        E("t", mk_mm(), r=[B_kT[0][tl], B_kT[1][tl], B_wu[0], B_wu[1]],
                          w=[B_psy[0][tl % 2], B_psy[1][tl % 2]])
                        E("sq", lambda e, psy1=psy1, tt=tt: e.activation(
                            silu1[:], psy1[:], AF.Silu, scale=c1s[:, tt:tt + 1]),
                          r=[B_psy[0][tl % 2], B_c1s[tt]], w=[B_si])
                        E("v", lambda e, psy2=psy2, tt=tt: e.scalar_tensor_tensor(
                            htile[:], psy2[:], c2s[:, tt:tt + 1], silu1[:], ALU.mult, ALU.mult),
                          r=[B_psy[1][tl % 2], B_c2s[tt], B_si], w=[B_ht])
                        E("sq", lambda e: e.activation(fdmp[:, 0:512], htile[:], AF.Square,
                                                       accum_out=cb(14)),
                          r=[B_ht], w=[B_fd, B_cb[14]])
                        E("v", lambda e, tt=tt: e.tensor_tensor(
                            ssqr[:, tt:tt + 1], ssqr[:, tt:tt + 1], cb(14), ALU.add),
                          r=[B_cb[14], B_ssqr[tt]], w=[B_ssqr[tt]])
                        hb = hgt[:, tl % 2, :]
                        E("v", lambda e, oc=oc, hb=hb: e.tensor_tensor(
                            hb, htile[:], gt3[:, oc * 512:(oc + 1) * 512], ALU.mult),
                          r=[B_ht, B_g], w=[B_hg[tl % 2]])
                        E("v", lambda e, hb=hb: e.tensor_reduce(cb(15), hb, AX.X, ALU.max,
                                                                apply_absolute_value=True),
                          r=[B_hg[tl % 2]], w=[B_cb[15]])
                        E("v", lambda e, tt=tt: e.tensor_tensor(
                            amaxr[:, tt:tt + 1], amaxr[:, tt:tt + 1], cb(15), ALU.max),
                          r=[B_cb[15], B_amaxr[tt]], w=[B_amaxr[tt]])
                        hsp_done[(oc, tt)] = E("sy", lambda e, oc=oc, tt=tt, hb=hb: e.dma_start(
                            out=hgsp[tt, :, oc * 512:(oc + 1) * 512], in_=hb),
                          r=[B_hg[tl % 2]],
                          x=[last_sh[0]] if (oc == 0 and tt == tb * NTU) else [],
                          dma=True, chan="sh")
                        last_sh[0] = hsp_done[(oc, tt)]

        _stk2.close()
        seq.barrier()
        _stk3 = ExitStack()
        khT = _stk3.enter_context(nc.sbuf_tensor(f"f{call}_khT", [128, 64, 512], BF16))
        wdt = _stk3.enter_context(nc.sbuf_tensor(f"f{call}_wdt", [128, 8, 8, 512], BF16))
        hgl = _stk3.enter_context(nc.sbuf_tensor(f"f{call}_hgl", [128, 2048], BF16))
        uh = _stk3.enter_context(nc.sbuf_tensor(f"f{call}_uh", [128, 2048], F32))
        kih = _stk3.enter_context(nc.sbuf_tensor(f"f{call}_kih", [128, 2048], I32))
        khb = _stk3.enter_context(nc.sbuf_tensor(f"f{call}_khb", [128, 2048], BF16))
        oacc = _stk3.enter_context(nc.sbuf_tensor(f"f{call}_oacc", [128, 512], F32))
        xres = _stk3.enter_context(nc.sbuf_tensor(f"f{call}_xres", [128, D], F32))
        x1t = _stk3.enter_context(nc.sbuf_tensor(f"f{call}_x1t", [128, D], F32))
        cdmp = _stk3.enter_context(nc.sbuf_tensor(f"f{call}_cdmp", [128, D], F32))
        x2nb = _stk3.enter_context(nc.sbuf_tensor(f"f{call}_x2nb", [128, D], BF16))
        x2nT = _stk3.enter_context(nc.sbuf_tensor(f"f{call}_x2nT", [128, 16, 128], BF16))
        psoa = _stk3.enter_context(nc.psum_tensor(f"f{call}_psoa", [128, 512], F32))
        psob = _stk3.enter_context(nc.psum_tensor(f"f{call}_psob", [128, 512], F32))
        if True:
            B_khT = [Buf() for _ in range(4)]
            B_khTb = [Buf() for _ in range(4)]
            last_tkhb = [None]
            B_wd = Buf()
            last_wd = [None]
            last_tkh = [None]
            last_tx = [None]
            B_hgl = Buf()
            B_u = Buf()
            B_k2 = Buf()
            B_k3 = Buf()
            B_oa = Buf()
            B_xr = Buf()
            B_pso = [Buf(), Buf()]
            B_x1t = Buf()
            B_x2n = Buf()
            B_x2T = Buf()
            B_cd = Buf()
            for bd in range(LT // 512):
                for tl in range(4):
                    tt = bd * 4 + tl
                    E("v", lambda e, tt=tt: e.tensor_scalar(cb(16), ssqr[:, tt:tt + 1], 1.0 / HID,
                                                            1e-8, ALU.mult, ALU.add),
                      r=[B_ssqr[tt]], w=[B_cb[16]])
                    E("sq", lambda e: e.activation(cb(17), cb(16), AF.Ln), r=[B_cb[16]], w=[B_cb[17]])
                    E("sq", lambda e: e.activation(cb(18), cb(17), AF.Exp, scale=-0.5),
                      r=[B_cb[17]], w=[B_cb[18]])
                    E("v", lambda e, tt=tt: e.tensor_tensor(cb(19), amaxr[:, tt:tt + 1], cb(18), ALU.mult),
                      r=[B_amaxr[tt], B_cb[18]], w=[B_cb[19]])
                    E("v", lambda e: e.tensor_scalar(cb(20), cb(19), 1e-4, None, ALU.max),
                      r=[B_cb[19]], w=[B_cb[20]])
                    E("v", lambda e: e.reciprocal(cb(21), cb(20)), r=[B_cb[20]], w=[B_cb[21]])
                    E("v", lambda e: e.tensor_scalar(cb(22), cb(21), 127.0, None, ALU.mult),
                      r=[B_cb[21]], w=[B_cb[22]])
                    E("v", lambda e: e.tensor_tensor(cb(23), cb(22), cb(18), ALU.mult),
                      r=[B_cb[22], B_cb[18]], w=[B_cb[23]])
                    E("v", lambda e, tt=tt: e.scalar_tensor_tensor(
                        c3s[:, tt:tt + 1], cb(20), 0.5 / 127.0, scw[:, 6:7], ALU.mult, ALU.mult),
                      r=[B_cb[20], B_scw], w=[B_c3s[tt]])
                    for q4 in range(4):
                        E("sy", lambda e, tt=tt, q4=q4: e.dma_start(
                            out=hgl[:], in_=hgsp[tt, :, q4 * 2048:(q4 + 1) * 2048]),
                          w=[B_hgl], x=[hsp_done[(15, NT - 1)]], dma=True, chan="hl")
                        E("sq", lambda e: e.activation(uh[:], hgl[:], AF.Identity, scale=cb(23),
                                                       bias=betaP[:, 0:1]),
                          r=[B_hgl, B_cb[23], B_const], w=[B_u])
                        E("v", lambda e: e.tensor_copy(kih[:], uh[:]), r=[B_u], w=[B_k2])
                        E("v", lambda e: e.tensor_scalar(khb[:], kih[:], betaN[:, 0:1], None, ALU.add),
                          r=[B_k2, B_const], w=[B_k3])
                        for ic in range(16):
                            icd = q4 * 16 + ic
                            h_kh = E("sy",
                              lambda e, icd=icd, ic=ic, tl=tl: e.dma_start(
                                out=khT[:, icd, tl * 128:(tl + 1) * 128],
                                in_=khb[:, ic * 128:(ic + 1) * 128], transpose=True),
                              r=[B_k3] if ic >= 14 else [],
                              w=[(B_khT if ic % 2 == 0 else B_khTb)[tl]],
                              x=(([last_tkh[0]] if (tl == 0 and q4 == 0 and ic == 0) else []) +
                                 ([last_tkhb[0]] if (tl == 0 and q4 == 0 and ic == 1) else []) +
                                 ([B_k3.writer] if ic < 14 else [])),
                              dma=True, chan="tkh", serial=False)

                last_tkh[0] = B_khT[3].writer
                last_tkhb[0] = B_khTb[3].writer
                ostores = {}
                for od in range(4):
                    wls = []
                    for rc in range(8):
                        h_wd = E("g", lambda e, od=od, rc=rc: e.dma_start(
                            out=wdt[:, :, rc, :],
                            in_=agt_out[:, 16 + rc, :, od * 512:od * 512 + 512]
                                .rearrange("r p n -> p r n")),
                          w=[B_wd], x=[g_agt] + ([last_wd[0]] if rc == 0 else []),
                          dma=True, chan="wd", serial=False)
                        wls.append(h_wd)
                    last_wd[0] = wls[-1]
                    for tl in range(4):
                        tt = bd * 4 + tl
                        pso = (psoa, psob)[tl % 2]

                        def mk_dn(tl=tl, pso=pso):
                            def f(e):
                                ins = []
                                for icd in range(64):
                                    ins.append(e.matmul(pso[:], khT[:, icd, tl * 128:(tl + 1) * 128],
                                                        wdt[:, icd // 8, icd % 8, :],
                                                        start=(icd == 0), stop=(icd == 63)))
                                return ins
                            return f
                        E("t", mk_dn(), r=[B_khT[tl], B_khTb[tl], B_wd], w=[B_pso[tl % 2]], x=wls)
                        E("sy", lambda e, tt=tt, od=od: e.dma_start(
                            out=xres[:, od * 512:(od + 1) * 512],
                            in_=x_src[tt * 128:(tt + 1) * 128, od * 512:(od + 1) * 512]),
                          w=[B_xr], dma=True, chan="xr")
                        E("v", lambda e, tt=tt, od=od, pso=pso: e.scalar_tensor_tensor(
                            oacc[:], pso[:], c3s[:, tt:tt + 1], xres[:, od * 512:(od + 1) * 512],
                            ALU.mult, ALU.add), r=[B_pso[tl % 2], B_c3s[tt], B_xr], w=[B_oa])
                        dst = x1sp if call == 1 else out
                        st = E("sy", lambda e, tt=tt, od=od, dst=dst: e.dma_start(
                            out=dst[tt * 128:(tt + 1) * 128, od * 512:(od + 1) * 512], in_=oacc[:]),
                          r=[B_oa], x=[last_so[0]] if (od == 0 and tl == 0) else [],
                          dma=True, chan="so", serial=False)
                        last_so[0] = st
                        ostores[(od, tl)] = st
                        out_last[0] = st
                if call == 1:
                    for tl in range(4):
                        tt = bd * 4 + tl
                        E("sy", lambda e, tt=tt: e.dma_start(
                            out=x1t[:], in_=x1sp[tt * 128:(tt + 1) * 128, :]),
                          w=[B_x1t], x=[ostores[(3, 3)]], dma=True, chan="x1")
                        ln_chain(x1t[:], [B_x1t], x2nb[:], [B_x2n], cdmp[:], B_cd, 24, 1e-5)
                        for ic in range(16):
                            h_tx = E("sy", lambda e, ic=ic: e.dma_start(
                                out=x2nT[:, ic, :], in_=x2nb[:, ic * 128:(ic + 1) * 128], transpose=True),
                              r=[B_x2n] if ic == 15 else [], w=[B_x2T],
                              x=([last_tx[0]] if ic == 0 else []) +
                                ([B_x2n.writer] if ic < 15 else []),
                              dma=True, chan="tx", serial=False)
                        last_tx[0] = h_tx
                        ag1_last[0] = E("sy", lambda e, tt=tt: e.dma_start(
                            out=ag1_in[:, :, tt * 128:(tt + 1) * 128].rearrange("a p b -> p a b"),
                            in_=x2nT[:]), r=[B_x2T], dma=True, chan="sa", serial=False)
        _stk3.close()
        seq.barrier()
        return ag1_last[0] if call == 1 else out_last[0]

    def emit_zsweep():
        zstores = {}
        _stk5 = ExitStack()
        wzsb = _stk5.enter_context(nc.sbuf_tensor("wzsb", [128, 16, 1024], BF16))
        wbasb = _stk5.enter_context(nc.sbuf_tensor("wbasb", [128, 16, 16], BF16))
        batile = _stk5.enter_context(nc.sbuf_tensor("batile", [128, 16], F32))
        zt2 = _stk5.enter_context(nc.sbuf_tensor("zt2", [128, 1024], BF16))
        xTz = _stk5.enter_context(nc.sbuf_tensor("xTz", [128, 16, 128], BF16))
        bafm = _stk5.enter_context(nc.sbuf_tensor("bafm", [8, 2, T], F32))
        bg2 = _stk5.enter_context(nc.sbuf_tensor("bg2", [8, 2, T], F32))
        zcol = _stk5.enter_context(nc.sbuf_tensor("zcol", [8, 1], F32))
        psZ1 = _stk5.enter_context(nc.psum_tensor("psZ1", [128, 512], F32))
        psZ2 = _stk5.enter_context(nc.psum_tensor("psZ2", [128, 512], F32))
        psZ3 = _stk5.enter_context(nc.psum_tensor("psZ3", [128, 512], F32))
        B_zt2 = Buf(); B_bat = Buf(); B_xTz = Buf(); B_ba = Buf(); B_bg = Buf(); B_zc = Buf()
        B_pz1 = Buf(); B_pz2 = Buf(); B_pz3 = Buf()
        wz_l = []
        for ic in range(16):
            wz_l.append(E("g", lambda e, ic=ic: e.dma_start(
                out=wzsb[:, ic, :], in_=wzin[ic * 128:(ic + 1) * 128, :]), dma=True))
        wba_l = E("g", lambda e: e.dma_start(
            out=wbasb[:], in_=wbain), dma=True)
        for tt2 in range(T // 128):
            E("sy", lambda e, tt2=tt2: e.dma_start(
                out=xTz[:],
                in_=ag1_out[(tt2 * 128) // LT, :, :,
                            (tt2 * 128) % LT:(tt2 * 128) % LT + 128].rearrange("a p b -> p a b")),
              w=[B_xTz], x=[g_ag1], dma=True, chan="zx")

            def mk_zh(half):
                def f(e):
                    ins = []
                    for ic in range(16):
                        ins.append(e.matmul((psZ1 if half == 0 else psZ2)[:],
                                            xTz[:, ic, :],
                                            wzsb[:, ic, half * 512:half * 512 + 512],
                                            start=(ic == 0), stop=(ic == 15)))
                    return ins
                return f
            E("t", mk_zh(0), r=[B_xTz], w=[B_pz1], x=wz_l)
            E("t", mk_zh(1), r=[B_xTz], w=[B_pz2], x=wz_l)

            def mk_ba(e):
                ins = []
                for ic in range(16):
                    ins.append(e.matmul(psZ3[:, 0:16], xTz[:, ic, :],
                                        wbasb[:, ic, :], start=(ic == 0), stop=(ic == 15)))
                return ins
            E("t", mk_ba, r=[B_xTz], w=[B_pz3], x=[wba_l])
            E("sq", lambda e: e.activation(zt2[:, 0:512], psZ1[:], AF.Silu), r=[B_pz1], w=[B_zt2])
            E("sq", lambda e: e.activation(zt2[:, 512:1024], psZ2[:], AF.Silu), r=[B_pz2], w=[B_zt2])
            zstores[tt2] = E("sy", lambda e, tt2=tt2: e.dma_start(out=zsp[tt2], in_=zt2[:]),
                             r=[B_zt2], dma=True, chan="sz", serial=False)
            E("v", lambda e: e.tensor_copy(batile[:], psZ3[:, 0:16]), r=[B_pz3], w=[B_bat])
            for half in range(2):
                E("t", lambda e, half=half: e.transpose(psZ3[0:8, 128 + half * 128:256 + half * 128],
                                                        batile[:, half * 8:half * 8 + 8], ident[:]),
                  r=[B_bat, B_const], w=[B_pz3])
                E("v", lambda e, tt2=tt2, half=half: e.tensor_copy(
                    bafm[:, half, tt2 * 128:(tt2 + 1) * 128],
                    psZ3[0:8, 128 + half * 128:256 + half * 128]),
                  r=[B_pz3], w=[B_ba])
        l_dtb = E("sy", lambda e: e.dma_start(out=cbuf[0:8, 28:29], in_=dtbin), dma=True)
        l_alog = E("sy", lambda e: e.dma_start(out=cbuf[0:8, 29:30], in_=alogin), dma=True)
        E("sq", lambda e: e.activation(bg2[:, 0, :], bafm[:, 0, :], AF.Sigmoid), r=[B_ba], w=[B_bg])
        E("v", lambda e: e.tensor_scalar(bafm[:, 1, :], bafm[:, 1, :], cbuf[0:8, 28:29],
                                         None, ALU.add), r=[B_ba], w=[B_ba], x=[l_dtb])
        E("sq", lambda e: e.activation(bg2[:, 1, :], bafm[:, 1, :], AF.Abs), r=[B_ba], w=[B_bg])
        E("sq", lambda e: e.activation(bg2[:, 1, :], bg2[:, 1, :], AF.Exp, scale=-1.0),
          r=[B_bg], w=[B_bg])
        E("sq", lambda e: e.activation(bg2[:, 1, :], bg2[:, 1, :], AF.Ln, bias=1.0),
          r=[B_bg], w=[B_bg])
        E("v", lambda e: e.tensor_scalar(bafm[:, 1, :], bafm[:, 1, :], 0.0, None, ALU.max),
          r=[B_ba], w=[B_ba])
        E("v", lambda e: e.tensor_tensor(bg2[:, 1, :], bg2[:, 1, :], bafm[:, 1, :], ALU.add),
          r=[B_bg, B_ba], w=[B_bg])
        E("sq", lambda e: e.activation(cbuf[0:8, 30:31], cbuf[0:8, 29:30], AF.Exp),
          x=[l_alog], w=[B_cb[30]])
        E("v", lambda e: e.tensor_scalar(cbuf[0:8, 31:32], cbuf[0:8, 30:31], -1.0, None,
                                         ALU.mult), r=[B_cb[30]], w=[B_cb[31]])
        E("v", lambda e: e.tensor_scalar(bg2[:, 1, :], bg2[:, 1, :], cbuf[0:8, 31:32], None,
                                         ALU.mult), r=[B_bg, B_cb[31]], w=[B_bg])
        E("v", lambda e: e.tensor_tensor_scan(bg2[:, 1, :], bg2[:, 1, :], bafm[:, 1, :], 0.0,
                                              ALU.add, ALU.bypass), r=[B_bg, B_ba], w=[B_bg])
        E("v", lambda e: e.memset(zcol[:], 0.0), w=[B_zc])
        E("sy", lambda e: e.dma_start(out=betagsp[0, :, 0:1], in_=zcol[:], allow_slow_non_contiguous=True), r=[B_zc], dma=True)
        E("sy", lambda e: e.dma_start(out=betagsp[1, :, 0:1], in_=zcol[:], allow_slow_non_contiguous=True), r=[B_zc], dma=True)
        E("sy", lambda e: e.dma_start(out=betagsp[0, :, 1:T + 1], in_=bg2[:, 0, :]),
          r=[B_bg], dma=True)
        bgst = E("sy", lambda e: e.dma_start(out=betagsp[1, :, 1:T + 1], in_=bg2[:, 1, :]),
                 r=[B_bg], dma=True)
        _stk5.close()
        seq.barrier()
        return zstores, bgst

    def emit_attention(zstores, bgst):
        _stk4 = ExitStack()
        wq = _stk4.enter_context(nc.sbuf_tensor("wq", [128, 16, 2048], BF16))
        wo = _stk4.enter_context(nc.sbuf_tensor("wo", [128, 8, 2048], BF16))
        cdg = _stk4.enter_context(nc.sbuf_tensor("cdg", [128, 16, 4, 128], BF16))
        xT = _stk4.enter_context(nc.sbuf_tensor("xT", [128, 1, 16, 512], BF16))
        mxs = _stk4.enter_context(nc.sbuf_tensor("mxs", [128, 515], BF16))
        tails = _stk4.enter_context(nc.sbuf_tensor("tails", [128, 16, 4], BF16))
        qkv = _stk4.enter_context(nc.sbuf_tensor("qkv", [128, 2, 16, 512], BF16))
        bg8 = _stk4.enter_context(nc.sbuf_tensor("bg8", [8, 2, 129], F32))
        hrow = _stk4.enter_context(nc.sbuf_tensor("hrow", [8, 2, 128], F32))
        hcolT = _stk4.enter_context(nc.sbuf_tensor("hcolT", [128, 4, 8], F32))
        grow = _stk4.enter_context(nc.sbuf_tensor("grow", [1, 128], F32))
        gamr = _stk4.enter_context(nc.sbuf_tensor("gamr", [8, 1], F32))
        gamd = _stk4.enter_context(nc.sbuf_tensor("gamd", [8, 8], F32))
        gamc = _stk4.enter_context(nc.sbuf_tensor("gamc", [128, 8], F32))
        lrow = _stk4.enter_context(nc.sbuf_tensor("lrow", [1, 1024], F32))
        rcolk = _stk4.enter_context(nc.sbuf_tensor("rcolk", [128, 4], F32))
        qn = _stk4.enter_context(nc.sbuf_tensor("qn", [128, 4, 128], BF16))
        kn = _stk4.enter_context(nc.sbuf_tensor("kn", [128, 4, 128], BF16))
        sq2t = _stk4.enter_context(nc.sbuf_tensor("sq2t", [128, 128], F32))
        ktok = _stk4.enter_context(nc.sbuf_tensor("ktok", [128, 4, 128], BF16))
        vtok = _stk4.enter_context(nc.sbuf_tensor("vtok", [128, 8, 128], BF16))
        kbgt = _stk4.enter_context(nc.sbuf_tensor("kbgt", [128, 128], BF16))
        vbt = _stk4.enter_context(nc.sbuf_tensor("vbt", [128, 128], BF16))
        kdect = _stk4.enter_context(nc.sbuf_tensor("kdect", [128, 128], BF16))
        dect = _stk4.enter_context(nc.sbuf_tensor("dect", [128, 2, 128], F32))
        dmneg = _stk4.enter_context(nc.sbuf_tensor("dmneg", [128, 128], F32))
        dmTin = _stk4.enter_context(nc.sbuf_tensor("dmTin", [128, 128], F32))
        Amat = _stk4.enter_context(nc.sbuf_tensor("Amat", [128, 128], BF16))
        Bmat = _stk4.enter_context(nc.sbuf_tensor("Bmat", [128, 128], BF16))
        Apw = _stk4.enter_context(nc.sbuf_tensor("Apw", [128, 2, 128], BF16))
        Bpw = _stk4.enter_context(nc.sbuf_tensor("Bpw", [128, 2, 128], BF16))
        Prod = _stk4.enter_context(nc.sbuf_tensor("Prod", [128, 2, 128], BF16))
        vcorr = _stk4.enter_context(nc.sbuf_tensor("vcorr", [128, 128], BF16))
        kcdf = _stk4.enter_context(nc.sbuf_tensor("kcdf", [128, 128], BF16))
        vnew = _stk4.enter_context(nc.sbuf_tensor("vnew", [128, 128], BF16))
        intraT = _stk4.enter_context(nc.sbuf_tensor("intraT", [128, 128], BF16))
        o1s = _stk4.enter_context(nc.sbuf_tensor("o1s", [128, 128], F32))
        oc2 = _stk4.enter_context(nc.sbuf_tensor("oc2", [128, 128], F32))
        xg = _stk4.enter_context(nc.sbuf_tensor("xg", [128, 128], F32))
        nrm = _stk4.enter_context(nc.sbuf_tensor("nrm", [128, 128], BF16))
        nrmT = _stk4.enter_context(nc.sbuf_tensor("nrmT", [128, 8, 128], BF16))
        stm = _stk4.enter_context(nc.sbuf_tensor("stm", [128, 2, 8, 128], F32))
        stb = _stk4.enter_context(nc.sbuf_tensor("stb", [128, 128], BF16))
        ztile = _stk4.enter_context(nc.sbuf_tensor("ztile", [128, 1024], BF16))
        rst = _stk4.enter_context(nc.sbuf_tensor("rst", [128, 512], BF16))
        grow1 = _stk4.enter_context(nc.sbuf_tensor("grow1", [1, 128], F32))
        sq2t1 = _stk4.enter_context(nc.sbuf_tensor("sq2t1", [128, 128], F32))
        kbgt1 = _stk4.enter_context(nc.sbuf_tensor("kbgt1", [128, 128], BF16))
        vbt1 = _stk4.enter_context(nc.sbuf_tensor("vbt1", [128, 128], BF16))
        kdect1 = _stk4.enter_context(nc.sbuf_tensor("kdect1", [128, 128], BF16))
        dect1 = _stk4.enter_context(nc.sbuf_tensor("dect1", [128, 2, 128], F32))
        dmneg1 = _stk4.enter_context(nc.sbuf_tensor("dmneg1", [128, 128], F32))
        dmTin1 = _stk4.enter_context(nc.sbuf_tensor("dmTin1", [128, 128], F32))
        Amat1 = _stk4.enter_context(nc.sbuf_tensor("Amat1", [128, 128], BF16))
        Bmat1 = _stk4.enter_context(nc.sbuf_tensor("Bmat1", [128, 128], BF16))
        Apw1 = _stk4.enter_context(nc.sbuf_tensor("Apw1", [128, 2, 128], BF16))
        Bpw1 = _stk4.enter_context(nc.sbuf_tensor("Bpw1", [128, 2, 128], BF16))
        Prod1 = _stk4.enter_context(nc.sbuf_tensor("Prod1", [128, 2, 128], BF16))
        vcorr1 = _stk4.enter_context(nc.sbuf_tensor("vcorr1", [128, 128], BF16))
        kcdf1 = _stk4.enter_context(nc.sbuf_tensor("kcdf1", [128, 128], BF16))
        vnew1 = _stk4.enter_context(nc.sbuf_tensor("vnew1", [128, 128], BF16))
        intraT1 = _stk4.enter_context(nc.sbuf_tensor("intraT1", [128, 128], BF16))
        o1s1 = _stk4.enter_context(nc.sbuf_tensor("o1s1", [128, 128], F32))
        oc21 = _stk4.enter_context(nc.sbuf_tensor("oc21", [128, 128], F32))
        xg1 = _stk4.enter_context(nc.sbuf_tensor("xg1", [128, 128], F32))
        nrm1 = _stk4.enter_context(nc.sbuf_tensor("nrm1", [128, 128], BF16))
        stb1 = _stk4.enter_context(nc.sbuf_tensor("stb1", [128, 128], BF16))
        psP = _stk4.enter_context(nc.psum_tensor("psP", [128, 512], F32))
        psC = _stk4.enter_context(nc.psum_tensor("psC", [128, 512], F32))
        psK1 = _stk4.enter_context(nc.psum_tensor("psK1", [128, 512], F32))
        psK2 = _stk4.enter_context(nc.psum_tensor("psK2", [128, 512], F32))
        psE = _stk4.enter_context(nc.psum_tensor("psE", [128, 512], F32))
        psF = _stk4.enter_context(nc.psum_tensor("psF", [128, 512], F32))
        psE1 = _stk4.enter_context(nc.psum_tensor("psE1", [128, 512], F32))
        psF1 = _stk4.enter_context(nc.psum_tensor("psF1", [128, 512], F32))
        if True:

            BN = {}
            for nm in ("psP", "psC", "psK1", "psK2", "hrow", "hcolT", "grow", "gamr", "gamd",
                       "gamc", "lrow", "rcolk", "sq2t", "kbgt", "vbt", "kdect", "dect", "dmneg",
                       "dmTin", "Amat", "Bmat", "vcorr", "kcdf", "vnew", "intraT", "o1s", "oc2",
                       "xg", "nrm", "stb", "ztile", "mxs", "tails", "rst", "wcst"):
                BN[nm] = Buf()
            _bE = Buf(); _bF = Buf()
            _bE1 = Buf(); _bF1 = Buf()
            B_psE = [_bE, _bE, _bE, _bE]
            B_psF = [_bF, _bF, _bF, _bF]
            B_xT = [Buf()]
            B_qkv = [Buf(), Buf()]
            B_qn = [Buf() for _ in range(4)]
            last_tkv = [None]
            last_tb = [None]
            last_tn = [None]
            B_kn = [Buf() for _ in range(4)]
            B_kvtok = Buf()
            B_ktok = [B_kvtok] * 4
            B_vtok = [B_kvtok] * 8
            B_Apw = [Buf(), Buf()]
            B_Bpw = [Buf(), Buf()]
            B_Prod = [Buf(), Buf()]
            B_Apw1 = [Buf(), Buf()]
            B_Bpw1 = [Buf(), Buf()]
            B_Prod1 = [Buf(), Buf()]
            B_st = [[Buf() for _ in range(8)], [Buf() for _ in range(8)]]
            B_nrmT = [Buf(), Buf()]
            B_bg8 = Buf()
            VRS = [dict(grow=grow, dect=dect, dmneg=dmneg, dmTin=dmTin, Amat=Amat, Bmat=Bmat,
                        Apw=Apw, Bpw=Bpw, Prod=Prod, kbgt=kbgt, vbt=vbt, kdect=kdect,
                        vcorr=vcorr, kcdf=kcdf, stb=stb, vnew=vnew, o1s=o1s, oc2=oc2,
                        xg=xg, nrm=nrm, sq2t=sq2t, intraT=intraT),
                   dict(grow=grow1, dect=dect1, dmneg=dmneg1, dmTin=dmTin1, Amat=Amat1, Bmat=Bmat1,
                        Apw=Apw1, Bpw=Bpw1, Prod=Prod1, kbgt=kbgt1, vbt=vbt1, kdect=kdect1,
                        vcorr=vcorr1, kcdf=kcdf1, stb=stb1, vnew=vnew1, o1s=o1s1, oc2=oc21,
                        xg=xg1, nrm=nrm1, sq2t=sq2t1, intraT=intraT1)]
            VRB = [{k: BN[k] for k in VRS[0] if k in BN},
                   {k: Buf() for k in VRS[1] if k in BN}]

            pe = lambda q, w=128: psE[:, q * 128:q * 128 + w]
            pf = lambda q, w=128: psF[:, q * 128:q * 128 + w]

            B_wo = Buf()
            for ic in range(16):
                E("g", lambda e, ic=ic: e.dma_start(out=wq[:, ic, :],
                                                    in_=wqkvin[ic * 128:(ic + 1) * 128, :]),
                  w=[BN["wcst"]], dma=True)
            for kc in range(8):
                E("g", lambda e, kc=kc: e.dma_start(out=wo[:, kc, :],
                                                    in_=woutin[kc * 128:(kc + 1) * 128, :]),
                  w=[B_wo], dma=True)
            cd_l = E("g", lambda e: e.dma_start(
                out=cdg[:], in_=convdin), dma=True)
            E("v", lambda e: e.memset(tails[:], 0.0), w=[BN["tails"]])

            for vh in range(8):
                E("v", lambda e, vh=vh: e.memset(stm[:, 0, vh, :], 0.0), w=[B_st[0][vh]])

            rs_done = []
            NGx = ATTN_CFG["groups"] if ATTN_CFG["groups"] is not None else NG
            for gidx in range(NGx):
                gb = gidx % 2
                E("sy", lambda e, gidx=gidx, gb=gb: e.dma_start(
                    out=xT[:, 0].rearrange("p a b -> p a b"),
                    in_=ag1_out[(gidx * 512) // LT, :, :,
                                (gidx * 512) % LT:(gidx * 512) % LT + 512].rearrange("a p b -> p a b")),
                  w=[B_xT[0]], x=[g_ag1], dma=True, chan="zx")
                for oc in range(16):
                    def mk_p(oc=oc, gb=gb):
                        def f(e):
                            ins = []
                            for ic in range(16):
                                ins.append(e.matmul(psP[:], wq[:, ic, oc * 128:(oc + 1) * 128],
                                                    xT[:, 0, ic, :], start=(ic == 0), stop=(ic == 15)))
                            return ins
                        return f
                    E("t", mk_p(), r=[B_xT[0], BN["wcst"]], w=[BN["psP"]])
                    E("v", lambda e: e.tensor_copy(mxs[:, 3:515], psP[:]), r=[BN["psP"]], w=[BN["mxs"]])
                    E("v", lambda e, oc=oc: e.tensor_copy(mxs[:, 0:3], tails[:, oc, 0:3]),
                      r=[BN["tails"]], w=[BN["mxs"]])
                    E("v", lambda e, oc=oc: e.tensor_copy(tails[:, oc, 0:3], mxs[:, 512:515]),
                      r=[BN["mxs"]], w=[BN["tails"]])

                    def mk_c(oc=oc):
                        def f(e):
                            ins = []
                            for k in range(4):
                                ins.append(e.matmul(psC[:], cdg[:, oc, k, :], mxs[:, k:k + 512],
                                                    start=(k == 0), stop=(k == 3)))
                            return ins
                        return f
                    E("t", mk_c(), r=[BN["mxs"]], w=[BN["psC"]], x=[cd_l])
                    E("sq", lambda e, oc=oc, gb=gb: e.activation(qkv[:, gb, oc, :], psC[:], AF.Silu),
                      r=[BN["psC"]], w=[B_qkv[gb]])

                for ch in range(ATTN_CFG["chunks"] if ATTN_CFG["chunks"] is not None else 4):
                    j = gidx * 4 + ch
                    t0 = j * 128
                    sl = slice(ch * 128, ch * 128 + 128)
                    pm = j % 2
                    pn = 1 - pm
                    Bq = B_qkv[gb]
                    E("sy", lambda e, t0=t0: e.dma_start(out=bg8[:, 0, 0:129],
                                                         in_=betagsp[1, :, t0:t0 + 129]),
                      w=[B_bg8], x=[bgst], dma=True, chan="bg")
                    E("sy", lambda e, t0=t0: e.dma_start(out=bg8[:, 1, 0:128],
                                                         in_=betagsp[0, :, t0 + 1:t0 + 129]),
                      w=[B_bg8], x=[bgst], dma=True, chan="bg")
                    E("v", lambda e: e.tensor_scalar(
                        hrow[:, 0, :], bg8[:, 0, 1:129], bg8[:, 0, 0:1], None, ALU.subtract),
                      r=[B_bg8], w=[BN["hrow"]])
                    E("v", lambda e: e.tensor_scalar(hrow[:, 1, :], hrow[:, 0, :],
                                                     hrow[:, 0, 127:128], None, ALU.subtract),
                      r=[BN["hrow"]], w=[BN["hrow"]])
                    E("sq", lambda e: e.activation(gamr[:], hrow[:, 0, 127:128], AF.Exp),
                      r=[BN["hrow"]], w=[BN["gamr"]])
                    E("t", lambda e: e.transpose(pe(0, 8), hrow[:, 0, :], ident[0:8, 0:8]),
                      r=[BN["hrow"], B_const], w=[B_psE[0]])
                    E("v", lambda e: e.tensor_copy(hcolT[:, 0, :], psE[0:128, 0:8]),
                      r=[B_psE[0]], w=[BN["hcolT"]])
                    E("t", lambda e: e.transpose(pe(0, 8), hrow[:, 1, :], ident[0:8, 0:8]),
                      r=[BN["hrow"], B_const], w=[B_psE[0]])
                    E("v", lambda e: e.tensor_copy(hcolT[:, 1, :], psE[0:128, 0:8]),
                      r=[B_psE[0]], w=[BN["hcolT"]])
                    E("t", lambda e: e.transpose(pe(0, 8), bg8[:, 1, 0:128], ident[0:8, 0:8]),
                      r=[B_bg8, B_const], w=[B_psE[0]])
                    E("v", lambda e: e.tensor_copy(hcolT[:, 2, :], psE[0:128, 0:8]),
                      r=[B_psE[0]], w=[BN["hcolT"]])
                    E("sq", lambda e: e.activation(hcolT[:, 3, :], hcolT[:, 0, :], AF.Exp),
                      r=[BN["hcolT"]], w=[BN["hcolT"]])
                    E("sq", lambda e: e.activation(hcolT[:, 1, :], hcolT[:, 1, :], AF.Exp, scale=-1.0),
                      r=[BN["hcolT"]], w=[BN["hcolT"]])
                    E("v", lambda e: e.tensor_scalar(gamd[:], identb[0:8, 0:8], gamr[:], None, ALU.mult),
                      r=[BN["gamr"], B_const], w=[BN["gamd"]])
                    E("t", lambda e: e.matmul(pe(0, 8), sel8[:], gamd[:], start=True, stop=True),
                      r=[BN["gamd"], B_const], w=[B_psE[0]])
                    E("v", lambda e: e.tensor_copy(gamc[:], psE[0:128, 0:8]), r=[B_psE[0]], w=[BN["gamc"]])
                    for kh in range(4):
                        for qk in range(2):
                            src_oc = kh if qk == 0 else 4 + kh
                            ps_t = psE if qk == 0 else psF
                            Bp = B_psE[0] if qk == 0 else B_psF[0]
                            E("sq", lambda e, src_oc=src_oc, gb=gb, sl=sl: e.activation(
                                sq2t[:], qkv[:, gb, src_oc, sl], AF.Square), r=[Bq], w=[BN["sq2t"]])
                            E("t", lambda e, kh=kh, ps_t=ps_t: e.matmul(
                                ps_t[0:1, kh * 128:kh * 128 + 128], onescol[:], sq2t[:],
                                start=True, stop=True),
                              r=[BN["sq2t"], B_const], w=[Bp])
                    E("v", lambda e: e.tensor_copy(lrow[0:1, 0:512], psE[0:1, 0:512]),
                      r=[B_psE[0]], w=[BN["lrow"]])
                    E("v", lambda e: e.tensor_copy(lrow[0:1, 512:1024], psF[0:1, 0:512]),
                      r=[B_psF[0]], w=[BN["lrow"]])
                    E("v", lambda e: e.tensor_scalar(lrow[:], lrow[:], 1e-6, None, ALU.add),
                      r=[BN["lrow"]], w=[BN["lrow"]])
                    E("sq", lambda e: e.activation(lrow[:], lrow[:], AF.Ln),
                      r=[BN["lrow"]], w=[BN["lrow"]])
                    E("sq", lambda e: e.activation(lrow[:], lrow[:], AF.Exp, scale=-0.5),
                      r=[BN["lrow"]], w=[BN["lrow"]])
                    for kh in range(4):
                        E("t", lambda e, kh=kh: e.matmul(
                            pf(1)[:, kh:kh + 1], lrow[0:1, 512 + kh * 128:512 + (kh + 1) * 128],
                            onescol[0:1, 0:1], start=True, stop=True),
                          r=[BN["lrow"], B_const], w=[B_psF[1]])
                    E("v", lambda e: e.tensor_copy(rcolk[:], psF[:, 128:132]),
                      r=[B_psF[1]], w=[BN["rcolk"]])
                    for kh in range(4):
                        for qk in range(2):
                            src_oc = kh if qk == 0 else 4 + kh
                            dstt = (qn if qk == 0 else kn)
                            Bd = (B_qn if qk == 0 else B_kn)[kh]
                            E("t", lambda e, kh=kh, qk=qk: e.matmul(
                                pf(2), onesrow[:], lrow[0:1, qk * 512 + kh * 128:qk * 512 + (kh + 1) * 128],
                                start=True, stop=True), r=[BN["lrow"], B_const], w=[B_psF[2]])
                            scl = RSQ2 if qk == 0 else 1.0
                            E("v", lambda e, src_oc=src_oc, dstt=dstt, kh=kh, gb=gb, sl=sl, scl=scl:
                              e.scalar_tensor_tensor(dstt[:, kh, :], qkv[:, gb, src_oc, sl], scl,
                                                     pf(2), ALU.mult, ALU.mult),
                              r=[Bq, B_psF[2]], w=[Bd])
                    for kh in range(4):
                        E("sy", lambda e, kh=kh, gb=gb, sl=sl: e.dma_start(
                            out=ktok[:, kh, :], in_=qkv[:, gb, 4 + kh, sl], transpose=True),
                          r=[], w=[B_kvtok],
                          x=([last_tkv[0]] if kh == 0 else []) + [Bq.writer],
                          dma=True, chan="tkv", serial=False)
                    tpkv = None
                    for vh in range(8):
                        tpkv = E("sy", lambda e, vh=vh, gb=gb, sl=sl: e.dma_start(
                            out=vtok[:, vh, :], in_=qkv[:, gb, 8 + vh, sl], transpose=True),
                          r=[Bq] if vh == 7 else [], w=[B_kvtok],
                          x=[Bq.writer] if vh < 7 else [], dma=True, chan="tkv", serial=False)
                    last_tkv[0] = tpkv
                    for kh in range(4):
                        E("t", lambda e, kh=kh: e.matmul(
                            psK1[:, kh * 128:(kh + 1) * 128], kn[:, kh, :], kn[:, kh, :],
                            start=True, stop=True), r=[B_kn[kh]], w=[BN["psK1"]])
                        E("t", lambda e, kh=kh: e.matmul(
                            psK2[:, kh * 128:(kh + 1) * 128], kn[:, kh, :], qn[:, kh, :],
                            start=True, stop=True), r=[B_kn[kh], B_qn[kh]], w=[BN["psK2"]])
                    E("sy", lambda e, j=j: e.dma_start(out=ztile[:], in_=zsp[j]),
                      w=[BN["ztile"]], dma=True, chan="zt")

                    _ven = [0]

                    def _VE(*aa, **kk):
                        _ven[0] += 1
                        if ATTN_CFG.get("vhsteps") is not None and _ven[0] > ATTN_CFG["vhsteps"]:
                            return None
                        return E(*aa, **kk)
                    def vh_chain(vh):
                        s = vh % 2
                        kh = vh // 2
                        R = VRS[s]
                        RB = VRB[s]
                        bE = (_bE, _bE1)[s]
                        bF = (_bF, _bF1)[s]
                        psE_s = (psE, psE1)[s]
                        psF_s = (psF, psF1)[s]
                        BPw = (B_Apw, B_Apw1)[s]
                        BBw = (B_Bpw, B_Bpw1)[s]
                        BPr = (B_Prod, B_Prod1)[s]
                        peS = lambda q, w=128: psE_s[:, q * 128:q * 128 + w]
                        pfS = lambda q, w=128: psF_s[:, q * 128:q * 128 + w]
                        co = 32 if s == 0 else 40
                        cbS = lambda j: cb(co + j)
                        grow_ = R["grow"]; dect_ = R["dect"]; dmneg_ = R["dmneg"]; dmTin_ = R["dmTin"]
                        Amat_ = R["Amat"]; Bmat_ = R["Bmat"]; Apw_ = R["Apw"]; Bpw_ = R["Bpw"]
                        Prod_ = R["Prod"]; kbgt_ = R["kbgt"]; vbt_ = R["vbt"]; kdect_ = R["kdect"]
                        vcorr_ = R["vcorr"]; kcdf_ = R["kcdf"]; stb_ = R["stb"]; vnew_ = R["vnew"]
                        o1s_ = R["o1s"]; oc2_ = R["oc2"]; xg_ = R["xg"]; nrm_ = R["nrm"]
                        sq2t_ = R["sq2t"]; intraT_ = R["intraT"]
                        _VE("t", lambda e, vh=vh: e.transpose(psE_s[0:1, 384:512],
                                                            hcolT[:, 0, vh:vh + 1], ident[:]),
                          r=[BN["hcolT"], B_const], w=[bE]); yield
                        _VE("v", lambda e: e.tensor_copy(grow_[:], psE_s[0:1, 384:512]),
                          r=[bE], w=[RB["grow"]]); yield
                        _VE("t", lambda e: e.matmul(psE_s[:, 384:512], onesrow[:], grow_[:],
                                                  start=True, stop=True),
                          r=[RB["grow"], B_const], w=[bE]); yield
                        _VE("v", lambda e, vh=vh: e.tensor_scalar(
                            dect_[:, 0, :], psE_s[:, 384:512], hcolT[:, 0, vh:vh + 1], 0.0,
                            ALU.subtract, ALU.max), r=[bE, BN["hcolT"]], w=[RB["dect"]]); yield
                        _VE("v", lambda e, vh=vh: e.tensor_scalar(
                            dect_[:, 1, :], psE_s[:, 384:512], hcolT[:, 0, vh:vh + 1], 0.0,
                            ALU.subtract, ALU.min), r=[bE, BN["hcolT"]], w=[RB["dect"]]); yield
                        _VE("sq", lambda e: e.activation(dect_[:, 0, :], dect_[:, 0, :], AF.Exp, scale=-1.0),
                          r=[RB["dect"]], w=[RB["dect"]]); yield
                        _VE("sq", lambda e: e.activation(dect_[:, 1, :], dect_[:, 1, :], AF.Exp),
                          r=[RB["dect"]], w=[RB["dect"]]); yield
                        _VE("v", lambda e: e.tensor_tensor(dmneg_[:], dect_[:, 0, :], mnegst[:], ALU.mult),
                          r=[RB["dect"], B_const], w=[RB["dmneg"]]); yield
                        _VE("v", lambda e: e.tensor_tensor(dmTin_[:], dect_[:, 1, :], mtriu[:], ALU.mult),
                          r=[RB["dect"], B_const], w=[RB["dmTin"]]); yield
                        _VE("v", lambda e, kh=kh, vh=vh: e.scalar_tensor_tensor(
                            Amat_[:], psK1[:, kh * 128:(kh + 1) * 128], hcolT[:, 2, vh:vh + 1],
                            dmneg_[:], ALU.mult, ALU.mult),
                          r=[BN["psK1"], BN["hcolT"], RB["dmneg"]], w=[RB["Amat"]]); yield
                        _VE("sy", lambda e: e.dma_start(out=Bmat_[:], in_=Amat_[:], transpose=True),
                          r=[RB["Amat"]], w=[RB["Bmat"]], dma=True, chan="tb" + str(s)); yield
                        _VE("v", lambda e: e.tensor_tensor(Prod_[:, 0, :], identb[:], Bmat_[:], ALU.add),
                          r=[RB["Bmat"], B_const], w=[BPr[0]]); yield
                        lastP = 0
                        AsrcAP, BsrcAP = Amat_[:], Bmat_[:]
                        AsrcB, BsrcB = RB["Amat"], RB["Bmat"]
                        for lev in range(1, 7):
                            pA = lev % 2
                            _VE("t", lambda e, B_=BsrcAP, A_=AsrcAP: e.matmul(peS(1), B_, A_,
                                                                            start=True, stop=True),
                              r=[AsrcB, BsrcB], w=[bE]); yield
                            _VE("sq", lambda e, pA=pA: e.activation(Apw_[:, pA, :], peS(1), AF.Copy),
                              r=[bE], w=[BPw[pA]]); yield
                            if lev < 6:
                                _VE("t", lambda e, B_=BsrcAP, A_=AsrcAP: e.matmul(peS(2), A_, B_,
                                                                                start=True, stop=True),
                                  r=[AsrcB, BsrcB], w=[bE]); yield
                                _VE("sq", lambda e, pA=pA: e.activation(Bpw_[:, pA, :], peS(2), AF.Copy),
                                  r=[bE], w=[BBw[pA]]); yield
                            _VE("t", lambda e, pA=pA, lastP=lastP: e.matmul(
                                peS(3), Apw_[:, pA, :], Prod_[:, lastP, :], start=True, stop=True),
                              r=[BPw[pA], BPr[lastP]], w=[bE]); yield
                            _VE("v", lambda e, lastP=lastP: e.tensor_tensor(
                                Prod_[:, 1 - lastP, :], Prod_[:, lastP, :], peS(3), ALU.add),
                              r=[bE, BPr[lastP]], w=[BPr[1 - lastP]]); yield
                            lastP = 1 - lastP
                            AsrcAP, BsrcAP = Apw_[:, pA, :], Bpw_[:, pA, :]
                            AsrcB, BsrcB = BPw[pA], BBw[pA]
                        _VE("v", lambda e, kh=kh, vh=vh: e.tensor_tensor(
                            cbS(0), rcolk[:, kh:kh + 1], hcolT[:, 2, vh:vh + 1], ALU.mult),
                          r=[BN["rcolk"], BN["hcolT"]], w=[B_cb[co + 0]]); yield
                        _VE("v", lambda e, vh=vh: e.tensor_tensor(
                            cbS(1), cbS(0), hcolT[:, 3, vh:vh + 1], ALU.mult),
                          r=[B_cb[co + 0], BN["hcolT"]], w=[B_cb[co + 1]]); yield
                        _VE("v", lambda e, kh=kh, vh=vh: e.tensor_tensor(
                            cbS(2), rcolk[:, kh:kh + 1], hcolT[:, 1, vh:vh + 1], ALU.mult),
                          r=[BN["rcolk"], BN["hcolT"]], w=[B_cb[co + 2]]); yield
                        _VE("g", lambda e, kh=kh: e.tensor_scalar(kbgt_[:], ktok[:, kh, :], cbS(1),
                                                                None, ALU.mult),
                          r=[B_ktok[kh], B_cb[co + 1]], w=[RB["kbgt"]], x=[tpkv]); yield
                        _VE("g", lambda e, vh=vh: e.tensor_scalar(vbt_[:], vtok[:, vh, :],
                                                                hcolT[:, 2, vh:vh + 1], None, ALU.mult),
                          r=[B_vtok[vh], BN["hcolT"]], w=[RB["vbt"]], x=[tpkv]); yield
                        _VE("g", lambda e, kh=kh: e.tensor_scalar(kdect_[:], ktok[:, kh, :], cbS(2),
                                                                None, ALU.mult),
                          r=[B_ktok[kh], B_cb[co + 2]], w=[RB["kdect"]], x=[tpkv]); yield
                        _VE("t", lambda e, lastP=lastP: e.matmul(pfS(0), Prod_[:, lastP, :], vbt_[:],
                                                               start=True, stop=True),
                          r=[BPr[lastP], RB["vbt"]], w=[bF]); yield
                        _VE("v", lambda e: e.tensor_copy(vcorr_[:], pfS(0)), r=[bF], w=[RB["vcorr"]]); yield
                        _VE("t", lambda e, lastP=lastP: e.matmul(pfS(1), kbgt_[:], Prod_[:, lastP, :],
                                                               start=True, stop=True),
                          r=[BPr[lastP], RB["kbgt"]], w=[bF]); yield
                        _VE("sq", lambda e: e.activation(kcdf_[:], pfS(1), AF.Copy),
                          r=[bF], w=[RB["kcdf"]]); yield
                        _VE("sq", lambda e, pm=pm, vh=vh: e.activation(stb_[:], stm[:, pm, vh, :], AF.Copy),
                          r=[B_st[pm][vh]], w=[RB["stb"]]); yield
                        _VE("t", lambda e: e.matmul(pfS(2), kcdf_[:], stb_[:], start=True, stop=True),
                          r=[RB["kcdf"], RB["stb"]], w=[bF]); yield
                        _VE("v", lambda e: e.tensor_tensor(vnew_[:], vcorr_[:], pfS(2), ALU.subtract),
                          r=[RB["vcorr"], bF], w=[RB["vnew"]]); yield
                        _VE("t", lambda e, kh=kh: e.matmul(pfS(3), qn[:, kh, :], stb_[:],
                                                         start=True, stop=True),
                          r=[B_qn[kh], RB["stb"]], w=[bF]); yield
                        _VE("sq", lambda e, vh=vh: e.activation(o1s_[:], pfS(3), AF.Copy,
                                                              scale=hcolT[:, 3, vh:vh + 1]),
                          r=[bF, BN["hcolT"]], w=[RB["o1s"]]); yield
                        _VE("v", lambda e, kh=kh: e.tensor_tensor(
                            intraT_[:], psK2[:, kh * 128:(kh + 1) * 128], dmTin_[:], ALU.mult),
                          r=[BN["psK2"], RB["dmTin"]], w=[RB["intraT"]]); yield
                        _VE("t", lambda e: e.matmul(pfS(0), intraT_[:], vnew_[:], start=True, stop=True),
                          r=[RB["intraT"], RB["vnew"]], w=[bF]); yield
                        _VE("v", lambda e: e.tensor_tensor(oc2_[:], o1s_[:], pfS(0), ALU.add),
                          r=[RB["o1s"], bF], w=[RB["oc2"]]); yield
                        _VE("v", lambda e, vh=vh: e.tensor_tensor(
                            xg_[:], oc2_[:], ztile[:, vh * 128:(vh + 1) * 128], ALU.mult),
                          r=[RB["oc2"], BN["ztile"]], w=[RB["xg"]]); yield
                        _VE("sq", lambda e: e.activation(sq2t_[:], xg_[:], AF.Square, accum_out=cbS(3)),
                          r=[RB["xg"]], w=[RB["sq2t"], B_cb[co + 3]]); yield
                        _VE("v", lambda e: e.tensor_scalar(cbS(4), cbS(3), 1.0 / 128.0, 1e-6,
                                                         ALU.mult, ALU.add),
                          r=[B_cb[co + 3]], w=[B_cb[co + 4]]); yield
                        _VE("sq", lambda e: e.activation(cbS(5), cbS(4), AF.Ln),
                          r=[B_cb[co + 4]], w=[B_cb[co + 5]]); yield
                        _VE("sq", lambda e: e.activation(cbS(6), cbS(5), AF.Exp, scale=-0.5),
                          r=[B_cb[co + 5]], w=[B_cb[co + 6]]); yield
                        _VE("v", lambda e: e.scalar_tensor_tensor(nrm_[:], xg_[:], cbS(6), normw_t[:],
                                                                ALU.mult, ALU.mult),
                          r=[RB["xg"], B_cb[co + 6], B_const], w=[RB["nrm"]]); yield
                        _VE("sy", lambda e, vh=vh: e.dma_start(out=nrmT[:, vh, :], in_=nrm_[:],
                                                             transpose=True),
                          r=[RB["nrm"]], w=[B_nrmT[s]], dma=True, chan="tn" + str(s)); yield
                        _VE("t", lambda e: e.matmul(pfS(1), kdect_[:], vnew_[:], start=True, stop=True),
                          r=[RB["kdect"], RB["vnew"]], w=[bF]); yield
                        _VE("v", lambda e, pm=pm, pn=pn, vh=vh: e.scalar_tensor_tensor(
                            stm[:, pn, vh, :], stm[:, pm, vh, :], gamc[:, vh:vh + 1], pfS(1),
                            ALU.mult, ALU.add),
                          r=[bF, B_st[pm][vh], BN["gamc"]], w=[B_st[pn][vh]]); yield

                    VHN = ATTN_CFG["vh"] if ATTN_CFG["vh"] is not None else 8

                    def stream_gen(s):
                        for vh in range(s, VHN, 2):
                            for _ in vh_chain(vh):
                                yield

                    alive = [stream_gen(0), stream_gen(1)]
                    while alive:
                        for g_ in list(alive):
                            try:
                                next(g_)
                            except StopIteration:
                                alive.remove(g_)
                    for od in range(4 if (ATTN_CFG["rs"] and (ATTN_CFG["vh"] is None or ATTN_CFG["vh"] == 8)) else 0):
                        ps_t = psK1 if od % 2 == 0 else psK2
                        Bp = BN["psK1"] if od % 2 == 0 else BN["psK2"]

                        def mk_wo(od=od, ps_t=ps_t):
                            def f(e):
                                ins = []
                                for vh in range(8):
                                    ins.append(e.matmul(ps_t[:], nrmT[:, vh, :],
                                                        wo[:, vh, od * 512:(od + 1) * 512],
                                                        start=(vh == 0), stop=(vh == 7)))
                                return ins
                            return f
                        E("t", mk_wo(), r=[B_nrmT[0], B_nrmT[1], B_wo], w=[Bp])
                        E("v", lambda e, od=od, ps_t=ps_t: e.tensor_copy(rst[:, 0:512], ps_t[:]),
                          r=[Bp], w=[BN["rst"]])
                        rs_done.append(E("sy", lambda e, t0=t0, od=od: e.dma_start(
                            out=rs_in[t0:t0 + 128, od * 512:(od + 1) * 512], in_=rst[:, 0:512]),
                          r=[BN["rst"]], dma=True, chan="sr", serial=False))
            if ATTN_CFG["rs"] and rs_done:
                g_rs = seq.cc(lambda e: e.collective_compute("ReduceScatter", ALU.add, replica_groups=RG4,
                                                             ins=[rs_in.opt()], outs=[rs_out.opt()]),
                              [rs_done[-1]])
            else:
                g_rs = E("g", lambda e: e.nop())
            _stk4.close()
            seq.barrier()
            return g_rs

    def emit_x2():
        _stk6 = ExitStack()
        xa = _stk6.enter_context(nc.sbuf_tensor("xa", [128, D], F32))
        xb2 = _stk6.enter_context(nc.sbuf_tensor("xb2", [128, D], BF16))
        xc = _stk6.enter_context(nc.sbuf_tensor("xc", [128, D], F32))
        if True:
            B_a, B_b, B_c2 = Buf(), Buf(), Buf()
            for tt in range(NT):
                E("sy", lambda e, tt=tt: e.dma_start(out=xa[:], in_=x1sp[tt * 128:(tt + 1) * 128, :]),
                  w=[B_a], dma=True, chan="xa")
                E("sy", lambda e, tt=tt: e.dma_start(out=xb2[:], in_=rs_out[tt * 128:(tt + 1) * 128, :]),
                  w=[B_b], dma=True, chan="xb")
                E("v", lambda e: e.tensor_tensor(xc[:], xa[:], xb2[:], ALU.add),
                  r=[B_a, B_b], w=[B_c2])
                E("sy", lambda e, tt=tt: e.dma_start(out=x2sp[tt * 128:(tt + 1) * 128, :], in_=xc[:]),
                  r=[B_c2], dma=True)
        _stk6.close()
        seq.barrier()

    def finish_dummy():
        with nc.sbuf_tensor("dz", [128, D], F32) as dz:
            B_dz = Buf()
            E("v", lambda e: e.memset(dz[:], 0.0), w=[B_dz])
            for tt in range(NT):
                E("sy", lambda e, tt=tt: e.dma_start(out=out[tt * 128:(tt + 1) * 128, :], in_=dz[:]),
                  r=[B_dz], dma=True)
        seq.barrier()
        with nc.Block() as block:
            seq.emit(block)
        es.close()
        return nc

    if upto == 0:
        return finish_dummy()
    ag1_dep = emit_ffn(1)
    if upto == 1:
        return finish_dummy()
    g_ag1 = seq.cc(lambda e: e.collective_compute("AllGather", ALU.bypass, replica_groups=RG4,
                                                  ins=[ag1_in.opt()], outs=[ag1_out.opt()]), [ag1_dep])
    seq.barrier()
    if upto == 2:
        return finish_dummy()
    zstores, bgst = emit_zsweep()
    if upto == 3:
        return finish_dummy()
    emit_attention(zstores, bgst)
    if upto == 4:
        return finish_dummy()
    emit_x2()
    if upto == 5:
        return finish_dummy()
    emit_ffn(2)
    seq.barrier()

    with nc.Block() as block:
        seq.emit(block)
    es.close()
    return nc


_prog_cache = {}


def kernel(L0_attn_hidden, L0_attn_W_qkv, L0_attn_W_z, L0_attn_W_b, L0_attn_W_a,
           L0_attn_conv_weight, L0_attn_dt_bias, L0_attn_A_log, L0_attn_norm_w,
           L0_attn_W_out, L0_ffn_w1, L0_ffn_g1, L0_ffn_w2, L0_ffn_g2, L0_ffn_w3, L0_ffn_g3):
    hid = np.ascontiguousarray(np.asarray(L0_attn_hidden, np.float32))
    B, T, _ = hid.shape
    LT = T // 4
    w1T = np.ascontiguousarray(np.asarray(L0_ffn_w1, np.float32).T)
    w2T = np.ascontiguousarray(np.asarray(L0_ffn_w2, np.float32).T)
    w3T = np.ascontiguousarray(np.asarray(L0_ffn_w3, np.float32).T)
    Wqkv = np.asarray(L0_attn_W_qkv, np.float32)
    Wz = np.asarray(L0_attn_W_z, np.float32)
    Wb = np.asarray(L0_attn_W_b, np.float32)
    Wa = np.asarray(L0_attn_W_a, np.float32)
    convw = np.asarray(L0_attn_conv_weight, np.float32)
    Wout = np.asarray(L0_attn_W_out, np.float32)
    ident = np.eye(128, dtype=np.float32)
    ones = np.ones((128, 128), np.float32)
    masks = np.stack([-np.tril(ones, -1), np.triu(ones)]).astype(np.float32)
    sel8 = np.ones((8, 128), np.float32)

    in_maps = []
    for c in range(NCORE):
        bb, hg = c // 4, c % 4
        kh0, vh0 = 4 * hg, 8 * hg
        qcols = slice(kh0 * 128, kh0 * 128 + 512)
        kcols = slice(2048 + kh0 * 128, 2048 + kh0 * 128 + 512)
        vcols = slice(4096 + vh0 * 128, 4096 + vh0 * 128 + 1024)
        wqkv_sl = np.concatenate([Wqkv[:, qcols], Wqkv[:, kcols], Wqkv[:, vcols]], axis=1)
        conv_sl = np.concatenate([convw[kh0 * 128:kh0 * 128 + 512],
                                  convw[2048 + kh0 * 128:2048 + kh0 * 128 + 512],
                                  convw[4096 + vh0 * 128:4096 + vh0 * 128 + 1024]], axis=0)
        convd = np.zeros((128, 16, 4, 128), np.float32)
        for oc in range(16):
            for k in range(4):
                np.fill_diagonal(convd[:, oc, k, :], conv_sl[oc * 128:(oc + 1) * 128, k])
        wba_full = np.concatenate([Wb[:, vh0:vh0 + 8], Wa[:, vh0:vh0 + 8]], axis=1)
        wba = np.ascontiguousarray(wba_full.reshape(16, 128, 16).transpose(1, 0, 2))
        in_maps.append({
            "x0": np.ascontiguousarray(hid[bb, (c % 4) * LT:(c % 4) * LT + LT, :]),
            "w1Ts": np.ascontiguousarray(w1T[c * 256:(c + 1) * 256].reshape(2, 128, 8192)),
            "w2Ts": np.ascontiguousarray(w2T[c * 256:(c + 1) * 256].reshape(2, 128, 8192)),
            "w3Ts": np.ascontiguousarray(w3T[c * 1024:(c + 1) * 1024].reshape(8, 128, 2048)),
            "g1in": np.asarray(L0_ffn_g1, np.float32).reshape(1, -1),
            "g2in": np.asarray(L0_ffn_g2, np.float32).reshape(1, -1),
            "g3in": np.asarray(L0_ffn_g3, np.float32).reshape(1, -1),
            "wqkvin": np.ascontiguousarray(wqkv_sl),
            "wzin": np.ascontiguousarray(Wz[:, vh0 * 128:vh0 * 128 + 1024]),
            "wbain": wba,
            "convdin": convd,
            "woutin": np.ascontiguousarray(Wout[vh0 * 128:vh0 * 128 + 1024, :]),
            "alogin": np.ascontiguousarray(np.asarray(L0_attn_A_log, np.float32)[vh0:vh0 + 8]
                                           .reshape(8, 1)),
            "dtbin": np.ascontiguousarray(np.asarray(L0_attn_dt_bias, np.float32)[vh0:vh0 + 8]
                                          .reshape(8, 1)),
            "normwin": np.asarray(L0_attn_norm_w, np.float32).reshape(1, 128),
            "identin": ident,
            "masksin": masks,
            "sel8in": sel8,
        })

    key = T
    if key not in _prog_cache:
        _prog_cache[key] = build_program(T)
    nc = _prog_cache[key]
    res = run_bass_kernel_spmd(nc, in_maps, core_ids=list(range(NCORE)))
    outp = np.empty((B, T, D), np.float32)
    for c in range(NCORE):
        outp[c // 4, (c % 4) * LT:(c % 4) * LT + LT, :] = res.results[c]["out"]
    return outp
